# version 02
# baseline (speedup 1.0000x reference)
import numpy as np
import ml_dtypes
import concourse.bass as bass
import concourse.mybir as mybir
from concourse.bass_utils import run_bass_kernel_spmd
from contextlib import ExitStack

F32 = mybir.dt.float32
BF16 = mybir.dt.bfloat16
AF = mybir.ActivationFunctionType
ALU = mybir.AluOpType
AX = mybir.AxisListType
BF = ml_dtypes.bfloat16

NCORES = 8
T = 2048
NT = T // 128
DM = 1024
NDT = DM // 128
NH = 16
HD = 64
S = 8192
ENGINES = ["sync", "vector", "scalar", "tensor", "gpsimd"]


class Prog:

    def __init__(self, nc):
        self.nc = nc
        self.stream = {e: [] for e in ENGINES}
        self.cnt = {e: 0 for e in ENGINES}
        self.dcnt = {}
        self.waitmax = {e: {} for e in ENGINES}
        self.default_waits = []
        self.prefix = ""

    def emit(self, eng, fn, waits=(), dma=None, inc=None):
        waits = list(waits) + self.default_waits
        if dma is not None:
            key = "D_" + self.prefix + dma
            self.dcnt[key] = self.dcnt.get(key, 0) + (inc if inc is not None else 16)
            tok = (key, self.dcnt[key])
        else:
            self.cnt[eng] += (inc if inc is not None else 1)
            tok = (eng, self.cnt[eng])
        wm = self.waitmax[eng]
        for (we, wc) in waits:
            if wm.get(we, 0) >= wc:
                continue
            wm[we] = wc
            self.stream[eng].append(("wait", we, wc))
        self.stream[eng].append(("op", fn, tok[0], tok[1], (16 if dma is not None else (inc if inc is not None else 1))))
        return tok

    def barrier(self, eng):
        return (eng, self.cnt[eng])

    def dma_total(self, name):
        key = "D_" + self.prefix + name
        return (key, self.dcnt.get(key, 0))

    def all_barriers(self):
        out = [(e, self.cnt[e]) for e in ENGINES if self.cnt[e] > 0]
        out += [(k, v) for k, v in self.dcnt.items()]
        return out

    def finish(self, final_tokens):
        for (we, wc) in final_tokens:
            self.stream["sync"].append(("wait", we, wc))

    def build(self, block, sems_engine):
        nc = self.nc
        sems = dict(sems_engine)
        self._dma_sem_guards = []
        for key in self.dcnt:
            g = nc.semaphore(key)
            sems[key] = g.__enter__()
            self._dma_sem_guards.append(g)
        for e in ENGINES:
            ops = self.stream[e]
            if not ops:
                continue
            engobj = getattr(nc, e)

            def make(ops=ops, engobj=engobj):
                def body(_):
                    for item in ops:
                        if item[0] == "wait":
                            _, we, wc = item
                            engobj.wait_ge(sems[we], wc)
                        else:
                            _, fn, key, c, amt = item
                            instr = fn(engobj)
                            if instr is None:
                                instr = engobj.nop()
                            instr.then_inc(sems[key], amt)
                return body
            getattr(block, e)(make())


class Ctx:

    def __init__(self, name):
        self.nc = bass.Bass("TRN2", target_bir_lowering=False)
        self.stack = ExitStack()
        self.name = name
        self.cur = None
        self.D = None
        self.pfx = ""

    def start(self):
        nc = self.nc
        self.sems = {}
        for e in ENGINES:
            self.sems[e] = self.stack.enter_context(nc.semaphore("S_" + e))
        self.block = self.stack.enter_context(nc.Block())
        self.p = Prog(nc)
        return self.p

    def sbuf(self, name, shape, dtype, stack=None, side=None):
        kw = {"side": side} if side else {}
        tgt = stack or self.cur or self.stack
        if tgt is not self.stack and not side:
            kw = {"side": "right"}
        return tgt.enter_context(self.nc.sbuf_tensor(self.pfx + name, list(shape), dtype, **kw))

    def scope(self):
        return ExitStack()

    def mkflush(self):
        self.flushv = self.sbuf("_flushv", [128, 1], F32)
        self.flushs = self.sbuf("_flushs", [128, 1], F32)
        self.flushs2 = self.sbuf("_flushs2", [128, 1], F32)

    def flush(self, eng):
        return self.p.emit(eng, lambda e: e.drain())

    def psum(self, name, shape, dtype=F32):
        return (self.cur or self.stack).enter_context(self.nc.psum_tensor(self.pfx + name, list(shape), dtype))

    def din(self, name, shape, dtype=F32):
        if self.D is not None:
            return self.D[name]
        return self.nc.dram_tensor(name, list(shape), dtype, kind="ExternalInput")

    def dout(self, name, shape, dtype=F32):
        if self.D is not None:
            return self.D[name]
        return self.nc.dram_tensor(name, list(shape), dtype, kind="ExternalOutput")

    def dint(self, name, shape, dtype=F32):
        return self.nc.dram_tensor(name, list(shape), dtype, kind="Internal")

    def phase_open(self, pfx, sem_pfx=None):
        self.cur = ExitStack()
        self.pfx = pfx
        self.p.prefix = sem_pfx if sem_pfx is not None else pfx
        self.p.default_waits = self.p.all_barriers()

    def phase_close(self):
        self.cur.close()
        self.cur = None
        self.pfx = ""
        self.p.prefix = ""

    def done(self, final_tokens):
        self.p.finish(final_tokens)
        self.p.build(self.block, self.sems)
        self.stack.close()
        return self.nc



def ln_tile(ctx, E, xt, out_bf, tmp, waits, eps=1e-5, nf=1024.0):
    t1 = E("vector", lambda e: e.tensor_scalar(tmp["junk"][:], xt[:], 1.0 / nf, 0.0, ALU.mult, ALU.add, accum_out=tmp["mu"][:]), waits=waits)
    t2 = E("vector", lambda e: e.tensor_scalar(tmp["negmu"][:], tmp["mu"][:], -1.0, None, ALU.mult), waits=[t1])
    t3 = E("scalar", lambda e: e.activation(tmp["sq"][:], xt[:], AF.Square, bias=tmp["negmu"][:], accum_out=tmp["ssq"][:]), waits=[t2])
    t4 = E("vector", lambda e: e.tensor_scalar(tmp["var"][:], tmp["ssq"][:], 1.0 / nf, tmp["eps_t"][:], ALU.mult, ALU.add), waits=[t3])
    t5 = E("scalar", lambda e: e.activation(tmp["var"][:], tmp["var"][:], AF.Ln), waits=[t4])
    t6 = E("scalar", lambda e: e.activation(tmp["rstd"][:], tmp["var"][:], AF.Exp, scale=-0.5), waits=[t5])
    t7 = E("vector", lambda e: e.tensor_scalar(out_bf[:], xt[:], tmp["mu"][:], tmp["rstd"][:], ALU.subtract, ALU.mult), waits=[t6])
    return t7


def transpose_128(ctx, E, p, src_bf, dst_bf, ps_tp, identb, waits):
    t1 = E("tensor", lambda e: e.transpose(ps_tp[:], src_bf[:], identb[:]), waits=waits)
    t2 = E("vector", lambda e: e.tensor_copy(dst_bf[:], ps_tp[:]), waits=[t1])
    return t2



def rel_pos_enc(seq_len, dim):
    positions = np.arange(seq_len, dtype=np.float64)[:, None]
    div_term = np.exp(np.arange(0, dim, 2, dtype=np.float32).astype(np.float64) * -(np.log(10000.0) / dim))
    pe = np.zeros((seq_len, dim), dtype=np.float32)
    pe[:, 0::2] = np.sin(positions * div_term).astype(np.float32)
    pe[:, 1::2] = np.cos(positions * div_term).astype(np.float32)
    return pe


_BUILD_CACHE = {}


def _cached(name, fn):
    if name not in _BUILD_CACHE:
        _BUILD_CACHE[name] = fn()
    return _BUILD_CACHE[name]



DBG_NOQ = False
DBG_SER = False


def phase1(ctx):
    p = ctx.p
    E = p.emit
    x_d = ctx.din("x", [T, DM])
    wqkv_d = ctx.din("wqkv", [DM, 3072])
    bqkv_d = ctx.din("bqkv_r", [128, 24])
    pe2_d = ctx.din("pe2", [128, T])
    bv_d = ctx.din("bv128", [128, DM])
    gwt_d = ctx.din("gate_wT", [DM, 8])
    wg_d = ctx.din("wg_t", [8, 11, 8, 128, 128])
    wu_d = ctx.din("wu_t", [8, 11, 8, 128, 128])
    wd_d = ctx.din("wd_t", [8, 11, 128, DM])
    swg_d = ctx.din("swg_t", [22, 8, 128, 128])
    swu_d = ctx.din("swu_t", [22, 8, 128, 128])
    swd_d = ctx.din("swd_t", [22, 128, DM])
    id_d = ctx.din("ident", [128, 128])

    K_o = ctx.dout("K_own", [DM, T], BF16)
    V_o = ctx.dout("V_own", [T, DM], BF16)
    QP_o = ctx.dout("QP_own", [NH, 65, T], BF16)
    g0_o = ctx.dout("g0", [T, DM])

    xn_fm = ctx.sbuf("xn_fm", [128, NDT * T], BF16)
    identt = ctx.sbuf("identt", [128, 128], F32)
    identb = ctx.sbuf("identb", [128, 128], BF16)
    ps_tp = ctx.psum("ps_tp", [128, 128], BF16)
    eps_t = ctx.sbuf("eps_t", [128, 1], F32)
    one_col = ctx.sbuf("one_col", [128, 1], F32)
    tmp = {k: ctx.sbuf("lntmp_" + k, [128, 1], F32) for k in ["mu", "negmu", "ssq", "var", "rstd"]}
    tmp["eps_t"] = eps_t

    li = E("sync", lambda e: e.dma_start(out=identt[:], in_=id_d[:]), dma="const")
    ci = E("vector", lambda e: e.tensor_copy(identb[:], identt[:]), waits=[li])
    mz1 = E("vector", lambda e: e.memset(eps_t[:], 1e-5))
    mz2 = E("vector", lambda e: e.memset(one_col[:], 1.0))

    gwt = ctx.sbuf("gwt", [128, NDT * 8], F32)
    lg0 = E("sync", lambda e: e.dma_start(out=gwt.ap().rearrange("p (a n) -> p a n", a=NDT), in_=gwt_d.ap().rearrange("(a p) n -> p a n", p=128)), dma="const")
    gates = ctx.sbuf("gates", [128, NT * 8], F32)
    lgt = ctx.sbuf("lgt", [128, 8], F32)
    m8 = ctx.sbuf("m8", [128, 8], F32)
    d21 = ctx.sbuf("d21", [128, 1], F32)
    ez = ctx.sbuf("ez", [128, 1], F32)
    rz = ctx.sbuf("rz", [128, 1], F32)
    pex = ctx.sbuf("pex", [128, 8], F32)
    msk = ctx.sbuf("msk", [128, 8], F32)
    ps_g = ctx.psum("ps_g", [128, 8], F32)
    ps_misc = ctx.psum("ps_misc", [128, 128], F32)

    ln_scope = ctx.scope()
    tmp["sq"] = ctx.sbuf("lntmp_sq", [128, DM], F32, ln_scope, side="right")
    tmp["junk"] = ctx.sbuf("lntmp_junk", [128, DM], F32, ln_scope, side="right")
    xbuf = [ctx.sbuf(f"xbuf{i}", [128, DM], F32, ln_scope, side="right") for i in range(2)]
    xn32 = [ctx.sbuf(f"xn32{i}", [128, DM], F32, ln_scope, side="right") for i in range(2)]
    xnb = [ctx.sbuf(f"xnb{i}", [128, DM], BF16, ln_scope, side="right") for i in range(2)]
    x32t = ctx.sbuf("x32t", [128, NDT * 128], F32, ln_scope, side="right")
    tp_done = []
    tp32_done = []
    prev_use = {}
    gprev = ()
    for tt in range(NT):
        b = tt % 2
        ldw = [prev_use[b]] if b in prev_use else []
        ld = E("sync", lambda e, tt=tt, b=b: e.dma_start(out=xbuf[b][:], in_=x_d[tt * 128:(tt + 1) * 128, :]), waits=ldw, dma="xb%d" % b)
        lnt = ln_tile(ctx, E, xbuf[b], xn32[b], tmp, [ld, mz1])
        cb16 = E("vector", lambda e, b=b: e.tensor_copy(xnb[b][:], xn32[b][:]), waits=[lnt])
        last = cb16
        for dt in range(NDT):
            t1 = E("tensor", lambda e, b=b, dt=dt: e.transpose(ps_tp[:], xnb[b][:, dt * 128:(dt + 1) * 128], identb[:]), waits=[cb16, ci] + ([tp_done[-1]] if tp_done else []))
            t1d = ctx.flush("tensor")
            t2 = E("vector", lambda e, dt=dt, tt=tt: e.tensor_copy(xn_fm[:, dt * T + tt * 128: dt * T + (tt + 1) * 128], ps_tp[:]), waits=[t1d])
            tp_done.append(t2)
            t3 = E("tensor", lambda e, b=b, dt=dt: e.transpose(ps_misc[:], xn32[b][:, dt * 128:(dt + 1) * 128], identt[:]), waits=[lnt, li] + ([tp32_done[-1]] if tp32_done else []))
            t3d = ctx.flush("tensor")
            t4 = E("vector", lambda e, dt=dt: e.tensor_copy(x32t[:, dt * 128:(dt + 1) * 128], ps_misc[:]), waits=[t3d])
            tp32_done.append(t4)
            last = t4
        mms = None
        for dt in range(NDT):
            mms = E("tensor", lambda e, dt=dt: e.matmul(
                ps_g[:], x32t[:, dt * 128:(dt + 1) * 128], gwt[:, dt * 8:(dt + 1) * 8],
                start=(dt == 0), stop=(dt == NDT - 1)),
                waits=([tp32_done[-1], lg0] + list(gprev) if dt == 0 else []))
        mmsd = ctx.flush("tensor")
        c1 = E("vector", lambda e: e.tensor_copy(lgt[:], ps_g[:]), waits=[mmsd])
        c2 = E("vector", lambda e: e.max(m8[:], lgt[:]), waits=[c1])
        c3 = E("vector", lambda e: e.tensor_scalar(d21[:], m8[:, 1:2], m8[:, 0:1], None, ALU.subtract), waits=[c2])
        c4 = E("scalar", lambda e: e.activation(ez[:], d21[:], AF.Exp), waits=[c3])
        c5 = E("vector", lambda e: e.tensor_tensor(ez[:], ez[:], one_col[:], ALU.add), waits=[c4, mz2])
        c6 = E("vector", lambda e: e.reciprocal(rz[:], ez[:]), waits=[c5])
        c7 = E("vector", lambda e: e.tensor_scalar(pex[:], lgt[:], m8[:, 0:1], None, ALU.subtract), waits=[c2])
        c8 = E("scalar", lambda e: e.activation(pex[:], pex[:], AF.Exp), waits=[c7])
        c9 = E("vector", lambda e: e.tensor_scalar(msk[:], lgt[:], m8[:, 1:2], None, ALU.is_ge), waits=[c8])
        c10 = E("vector", lambda e: e.tensor_tensor(pex[:], pex[:], msk[:], ALU.mult), waits=[c9])
        c11 = E("vector", lambda e, tt=tt: e.tensor_scalar(gates[:, tt * 8:(tt + 1) * 8], pex[:], rz[:], None, ALU.mult), waits=[c10, c6])
        gprev = (c11, c2)
        prev_use[b] = last
    XN = p.barrier("vector")
    XNT = p.barrier("tensor")
    GATES = p.barrier("vector")
    LNB = p.all_barriers()
    ln_scope.close()

    qkv_scope = ctx.scope()
    pe2 = ctx.sbuf("pe2_s", [128, T], F32, qkv_scope, side="right")
    bv = ctx.sbuf("bv", [128, DM], F32, qkv_scope, side="right")
    bq = ctx.sbuf("bq", [128, 24], F32, qkv_scope, side="right")
    lb = E("sync", lambda e: e.dma_start(out=bq[:], in_=bqkv_d[:]), dma="const")
    lpe = E("sync", lambda e: e.dma_start(out=pe2[:], in_=pe2_d[:]), waits=LNB, dma="const")
    lbv = E("sync", lambda e: e.dma_start(out=bv[:], in_=bv_d[:]), waits=LNB, dma="const")
    wq_f = [ctx.sbuf(f"wq_f{i}", [128, 3072], F32, qkv_scope, side="right") for i in range(4)]
    wqkv_bf = [ctx.sbuf(f"wqkv_bf{i}", [128, 3072], BF16, qkv_scope, side="right") for i in range(NDT)]
    wq_cast = []
    prev_cast = {}
    for dt in range(NDT):
        b = dt % 4
        ldw = [prev_cast[b]] if b in prev_cast else []
        ld = E("sync", lambda e, dt=dt, b=b: e.dma_start(out=wq_f[b][:], in_=wqkv_d[dt * 128:(dt + 1) * 128, :]), waits=ldw + LNB, dma="wqf%d" % b)
        cs = E("vector", lambda e, dt=dt, b=b: e.tensor_copy(wqkv_bf[dt][:], wq_f[b][:]), waits=[ld])
        fl = ctx.flush("vector")
        prev_cast[b] = fl
        wq_cast.append(fl)

    ps_qk = [ctx.psum(f"ps_qk{i}", [128, 512], F32) for i in range(2)]
    kst = [ctx.sbuf(f"kst{i}", [128, 512], BF16, qkv_scope, side="right") for i in range(4)]
    k_stores = []
    q_stores = []
    pb = 0
    ps_free = [(), ()]
    for ot in range(16):
        for tc in range(4):
            pb ^= 1
            mms = None
            for dt in range(NDT):
                mms = E("tensor", lambda e, ot=ot, tc=tc, dt=dt, pb=pb: e.matmul(
                    ps_qk[pb][:], wqkv_bf[dt][:, ot * 128:(ot + 1) * 128],
                    xn_fm[:, dt * T + tc * 512: dt * T + (tc + 1) * 512],
                    start=(dt == 0), stop=(dt == NDT - 1)),
                    waits=(list(ps_free[pb]) + [XN, wq_cast[-1]] if dt == 0 else []))
            mms = ctx.flush("tensor")
            if ot < 8:
                h0, h1 = 2 * ot, 2 * ot + 1
                sb = tc % 2
                e1 = E("vector", lambda e, ot=ot, pb=pb, sb=sb: e.tensor_scalar(
                    kst[sb][:], ps_qk[pb][:], bq[:, ot:ot + 1], None, ALU.add),
                    waits=[mms, lb] + list(q_stores[-4:-2] if len(q_stores) >= 4 else []))
                fl = ctx.flush("vector")
                st1 = E("gpsimd", lambda e, h0=h0, tc=tc, sb=sb: e.dma_start(
                    out=QP_o[h0, 0:64, tc * 512:(tc + 1) * 512], in_=kst[sb][0:64, :]), waits=[fl], dma="qst%d" % sb) if not DBG_NOQ else ctx.flush("vector")
                st2 = E("gpsimd", lambda e, h1=h1, tc=tc, sb=sb: e.dma_start(
                    out=QP_o[h1, 0:64, tc * 512:(tc + 1) * 512], in_=kst[sb][64:128, :]), waits=[fl], dma="qst%d" % sb) if not DBG_NOQ else ctx.flush("vector")
                q_stores += [st1, st2]
                ps_free[pb] = (fl,)
            else:
                sb = 2 + tc % 2
                e1a = E("vector", lambda e, ot=ot, pb=pb, sb=sb: e.tensor_scalar(
                    kst[sb][:], ps_qk[pb][:], bq[:, ot:ot + 1], None, ALU.add),
                    waits=[mms, lb, lpe] + list(k_stores[-2:-1] if len(k_stores) >= 2 else []))
                e1 = E("vector", lambda e, tc=tc, sb=sb: e.tensor_tensor(
                    kst[sb][:], kst[sb][:], pe2[:, tc * 512:(tc + 1) * 512], ALU.add), waits=[e1a])
                fl = ctx.flush("vector")
                st = E("gpsimd", lambda e, ot=ot, tc=tc, sb=sb: e.dma_start(
                    out=K_o[(ot - 8) * 128:(ot - 7) * 128, tc * 512:(tc + 1) * 512], in_=kst[sb][:]), waits=[fl], dma="kst%d" % sb)
                k_stores.append(st)
                ps_free[pb] = (fl, st)
    KDONE = [p.dma_total("kst2"), p.dma_total("kst3")]

    vst = [ctx.sbuf(f"vst{i}", [128, 512], BF16, qkv_scope, side="right") for i in range(2)]
    v_stores = []
    ps_free = [(), ()]
    for tt in range(NT):
        for vc in range(2):
            pb ^= 1
            mms = None
            for dt in range(NDT):
                mms = E("tensor", lambda e, tt=tt, vc=vc, dt=dt, pb=pb: e.matmul(
                    ps_qk[pb][:], xn_fm[:, dt * T + tt * 128: dt * T + (tt + 1) * 128],
                    wqkv_bf[dt][:, 2048 + vc * 512: 2048 + (vc + 1) * 512],
                    start=(dt == 0), stop=(dt == NDT - 1)),
                    waits=(list(ps_free[pb]) if dt == 0 else []))
            mms = ctx.flush("tensor")
            sb = vc
            e1 = E("vector", lambda e, vc=vc, pb=pb, sb=sb: e.tensor_tensor(
                vst[sb][:], ps_qk[pb][:], bv[:, vc * 512:(vc + 1) * 512], ALU.add),
                waits=[mms, lbv] + (v_stores[-2:-1] if len(v_stores) >= 2 else []))
            fl = ctx.flush("vector")
            st = E("gpsimd", lambda e, tt=tt, vc=vc, sb=sb: e.dma_start(
                out=V_o[tt * 128:(tt + 1) * 128, vc * 512:(vc + 1) * 512], in_=vst[sb][:]), waits=[fl], dma="vst%d" % sb)
            v_stores.append(st)
            ps_free[pb] = (fl,)

    QDONE = [p.dma_total("qst0"), p.dma_total("qst1"), p.dma_total("vst0"), p.dma_total("vst1")]
    k0b = ctx.sbuf("k0b", [64, 128], BF16, qkv_scope, side="right")
    qtb = ctx.sbuf("qtb", [64, T], BF16, qkv_scope, side="right")
    ps_mrow = ctx.psum("ps_mrow", [1, 128], F32)
    mcol = ctx.sbuf("mcol", [128, 1], F32, qkv_scope, side="right")
    mrow_st = ctx.sbuf("mrow_st", [1, T], BF16, qkv_scope, side="right")
    qp_stores = []
    for h in range(NH):
        lk = E("scalar", lambda e, h=h: e.dma_start(out=k0b[:], in_=K_o[h * 64:(h + 1) * 64, 0:128]), waits=KDONE + QDONE, dma="k0b")
        lq = E("scalar", lambda e, h=h: e.dma_start(out=qtb[:], in_=QP_o[h, 0:64, :]), waits=QDONE + qp_stores[-1:], dma="qtb")
        prev = (lk, lq)
        for qt in range(NT):
            m1 = E("tensor", lambda e, h=h, qt=qt: e.matmul(
                ps_misc[:], qtb[:, qt * 128:(qt + 1) * 128], k0b[:], start=True, stop=True), waits=list(prev))
            m1d = ctx.flush("tensor")
            r1 = E("vector", lambda e: e.tensor_reduce(mcol[:], ps_misc[:], AX.X, ALU.max), waits=[m1d])
            r1d = ctx.flush("vector")
            t1 = E("tensor", lambda e: e.transpose(ps_mrow[:], mcol[:], identt[:]), waits=[r1d])
            t1d = ctx.flush("tensor")
            c1 = E("scalar", lambda e, h=h, qt=qt: e.copy(mrow_st[:, qt * 128:(qt + 1) * 128], ps_mrow[:]), waits=[t1d])
            fls = ctx.flush("scalar")
            prev = (fls,)
        qp_stores.append(E("gpsimd", lambda e, h=h: e.dma_start(out=QP_o[h, 64:65, :], in_=mrow_st[:]), waits=[prev[0]], dma="qpst"))
    QKVB = p.all_barriers()
    qkv_scope.close()


    ysum = ctx.sbuf("ysum", [128, NT * DM], F32)
    zs = E("vector", lambda e: e.memset(ysum[:], 0.0))
    wgb = [ctx.sbuf(f"wgb{i}", [128, 8 * 128], BF16) for i in range(2)]
    wub = [ctx.sbuf(f"wub{i}", [128, 8 * 128], BF16) for i in range(2)]
    wgf = [ctx.sbuf("wgf0", [128, 8 * 128], F32)] * 2
    wuf = [ctx.sbuf("wuf0", [128, 8 * 128], F32)] * 2
    wdb = ctx.sbuf("wdb", [128, 11 * DM], BF16)
    wdf = [ctx.sbuf(f"wdf{i}", [128, DM], F32) for i in range(2)]
    Pe = ctx.sbuf("Pe", [128, 11 * T], BF16)
    sg = ctx.sbuf("sg", [128, 512], F32)
    sg2 = ctx.sbuf("sg2", [128, 512], F32)
    ps_h = [ctx.psum(f"ps_h{i}", [128, 512], F32) for i in range(2)]

    def expert_block(tag, n_it, wg_src, wu_src, wd_src, gate_e, dep0):
        pg_dep = {0: ()}
        pu_dep = {1: ()}
        wprev = {0: None, 1: None}
        up_done = []
        wchain = [None]
        for it in range(n_it):
            b = it % 2
            wl = [wchain[0]] if wchain[0] is not None else []
            lw1 = E("scalar", lambda e, it=it, b=b: e.dma_start(out=wgf[b].ap().rearrange("p (a n) -> p a n", a=8), in_=wg_src(it).rearrange("a p n -> p a n")), waits=wl + list(dep0), dma="wgf")
            cw1_ = E("vector", lambda e, b=b: e.tensor_copy(wgb[b][:], wgf[b][:]), waits=[lw1])
            cw1 = ctx.flush("vector")
            lw2 = E("scalar", lambda e, it=it, b=b: e.dma_start(out=wuf[b].ap().rearrange("p (a n) -> p a n", a=8), in_=wu_src(it).rearrange("a p n -> p a n")), waits=[cw1], dma="wuf")
            cw2_ = E("vector", lambda e, b=b: e.tensor_copy(wub[b][:], wuf[b][:]), waits=[lw2])
            cw2 = ctx.flush("vector")
            wprev[b] = cw2
            for tc in range(4):
                pba = 0
                pbb = 1
                mg = None
                for dt in range(NDT):
                    mg = E("tensor", lambda e, it=it, tc=tc, dt=dt, b=b, pba=pba: e.matmul(
                        ps_h[pba][:], wgb[b][:, dt * 128:(dt + 1) * 128],
                        xn_fm[:, dt * T + tc * 512:dt * T + (tc + 1) * 512],
                        start=(dt == 0), stop=(dt == NDT - 1)),
                        waits=([cw1, XN] + list(pg_dep[pba]) if dt == 0 else []))
                mu_ = None
                for dt in range(NDT):
                    mu_ = E("tensor", lambda e, it=it, tc=tc, dt=dt, b=b, pbb=pbb: e.matmul(
                        ps_h[pbb][:], wub[b][:, dt * 128:(dt + 1) * 128],
                        xn_fm[:, dt * T + tc * 512:dt * T + (tc + 1) * 512],
                        start=(dt == 0), stop=(dt == NDT - 1)),
                        waits=([cw2] + list(pu_dep[pbb]) if dt == 0 else []))
                mgd = ctx.flush("tensor")
                a1_ = E("scalar", lambda e, pba=pba: e.activation(sg[:], ps_h[pba][:], AF.Silu), waits=[mgd])
                a1 = ctx.flush("scalar")
                mud = ctx.flush("tensor")
                a2_ = E("vector", lambda e, it=it, tc=tc, pbb=pbb: e.tensor_tensor(
                    Pe[:, it * T + tc * 512: it * T + (tc + 1) * 512], ps_h[pbb][:], sg[:], ALU.mult), waits=[mud, a1])
                a2 = ctx.flush("vector")
                pg_dep[pba] = (a1,)
                pu_dep[pbb] = (a2,)
                up_done.append(a2)
                wchain[0] = mu_
        wd_cast = []
        wdp = {0: None, 1: None}
        for it in range(n_it):
            b = it % 2
            wl = [wdp[b]] if wdp[b] is not None else []
            lw = E("scalar", lambda e, it=it, b=b: e.dma_start(out=wdf[b][:], in_=wd_src(it)), waits=wl, dma="wdf%d" % b)
            cw_ = E("vector", lambda e, it=it, b=b: e.tensor_copy(wdb[:, it * DM:(it + 1) * DM], wdf[b][:]), waits=[lw])
            cw = ctx.flush("vector")
            wdp[b] = cw
            wd_cast.append(cw)
        UP = up_done[-1]
        UPS = p.barrier("scalar")
        WD = wd_cast[-1]
        dn_dep = {0: (), 1: ()}
        outs = []
        for tt in range(NT):
            for hf in range(2):
                pb2 = hf
                md = None
                for it in range(n_it):
                    md = E("tensor", lambda e, tt=tt, hf=hf, it=it, pb2=pb2: e.matmul(
                        ps_h[pb2][:], Pe[:, it * T + tt * 128: it * T + (tt + 1) * 128],
                        wdb[:, it * DM + hf * 512: it * DM + (hf + 1) * 512],
                        start=(it == 0), stop=(it == n_it - 1)),
                        waits=([UP, UPS, WD] + list(dn_dep[pb2]) if it == 0 else []))
                mdd = ctx.flush("tensor")
                if gate_e is None:
                    o1 = E("vector", lambda e, tt=tt, hf=hf, pb2=pb2: e.tensor_tensor(
                        ysum[:, tt * DM + hf * 512: tt * DM + (hf + 1) * 512],
                        ysum[:, tt * DM + hf * 512: tt * DM + (hf + 1) * 512], ps_h[pb2][:], ALU.add), waits=[mdd, zs])
                else:
                    o0 = E("vector", lambda e, tt=tt, pb2=pb2, ge=gate_e: e.tensor_scalar(
                        sg2[:], ps_h[pb2][:], gates[:, tt * 8 + ge: tt * 8 + ge + 1], None, ALU.mult), waits=[mdd, GATES])
                    o1 = E("vector", lambda e, tt=tt, hf=hf: e.tensor_tensor(
                        ysum[:, tt * DM + hf * 512: tt * DM + (hf + 1) * 512],
                        ysum[:, tt * DM + hf * 512: tt * DM + (hf + 1) * 512], sg2[:], ALU.add), waits=[o0, zs])
                o1f = ctx.flush("vector")
                dn_dep[pb2] = (o1f,)
                outs.append(o1f)
        return outs[-1]

    dep0 = QKVB
    n_experts = 8
    for ex in range(n_experts):
        last = expert_block(f"e{ex}", 11,
                            (lambda it, ex=ex: wg_d[ex, it, :, :, :]),
                            (lambda it, ex=ex: wu_d[ex, it, :, :, :]),
                            (lambda it, ex=ex: wd_d[ex, it, :, :]),
                            ex, dep0)
        dep0 = p.all_barriers()
    for half in range(2):
        last = expert_block(f"sh{half}", 11,
                            (lambda it, half=half: swg_d[half * 11 + it, :, :, :]),
                            (lambda it, half=half: swu_d[half * 11 + it, :, :, :]),
                            (lambda it, half=half: swd_d[half * 11 + it, :, :]),
                            None, dep0)
        dep0 = p.all_barriers()

    g0st = [ctx.sbuf(f"g0st{i}", [128, DM], F32) for i in range(2)]
    g_stores = []
    for tt in range(NT):
        b = tt % 2
        wl = [g_stores[-2]] if len(g_stores) >= 2 else []
        a1 = E("scalar", lambda e, tt=tt, b=b: e.activation(g0st[b][:], ysum[:, tt * DM:(tt + 1) * DM], AF.Silu), waits=[last] + wl)
        fls = ctx.flush("scalar")
        st = E("sync", lambda e, tt=tt, b=b: e.dma_start(out=g0_o[tt * 128:(tt + 1) * 128, :], in_=g0st[b][:]), waits=[fls], dma="g0st%d" % b)
        g_stores.append(st)

    return





def phase2(ctx):
    p = ctx.p
    E = p.emit
    QP_d = ctx.din("QP", [NH, 65, T], BF16)
    K_d = ctx.din("K_all", [4 * DM, T], BF16)
    V_d = ctx.din("V_all", [S, DM], BF16)
    g0_d = ctx.din("g0", [T, DM])
    x_d = ctx.din("x", [T, DM])
    wout_d = ctx.din("wout", [DM, DM])
    bout_d = ctx.din("bout128", [128, DM])
    id_d = ctx.din("ident", [128, 128])

    x1_o = ctx.dout("x1", [T, DM])
    xn1_o = ctx.dout("xn1", [T, DM])
    xn1fm_o = ctx.dout("xn1fm", [128, NDT * T], BF16)

    identt = ctx.sbuf("identt", [128, 128], F32)
    identb = ctx.sbuf("identb", [128, 128], BF16)
    eps_t = ctx.sbuf("eps_t", [128, 1], F32)
    li = E("sync", lambda e: e.dma_start(out=identt[:], in_=id_d[:]), dma="const")
    ci = E("vector", lambda e: e.tensor_copy(identb[:], identt[:]), waits=[li])
    cid = ctx.flush("vector")
    mz1 = E("vector", lambda e: e.memset(eps_t[:], 1e-5))

    AO = [ctx.sbuf(f"AO{i}", [128, T], BF16) for i in range(8)]
    qpb = ctx.sbuf("qpb", [65, T], BF16)
    NKB = 3
    kbuf = [ctx.sbuf(f"kbuf{i}", [65, 128], BF16) for i in range(NKB)]
    vbuf = [ctx.sbuf(f"vbuf{i}", [128, 65], BF16) for i in range(NKB)]
    Pt = [ctx.sbuf(f"Pt{i}", [128, 512], BF16) for i in range(2)]
    ones64 = ctx.sbuf("ones64", [1, 64], F32)
    rzf = ctx.sbuf("rzf", [1, 512], F32)
    ofb = ctx.sbuf("ofb", [64, 512], BF16)
    ps_S = [ctx.psum(f"ps_S{i}", [128, 512], F32) for i in range(2)]
    ps_O = ctx.psum("ps_O", [65, 512], F32)
    ps_rz = ctx.psum("ps_rz", [64, 512], F32)
    ps_s0 = [ctx.psum(f"ps_s0{i}", [128, 512], F32) for i in range(2)]
    ps_tp = ctx.psum("ps_tp", [128, 128], BF16)

    mz2 = E("vector", lambda e: e.memset(ones64[:], 1.0))
    mzk = [E("vector", lambda e, i=i: e.memset(kbuf[i][64:65, :], -1.0)) for i in range(NKB)]
    mzv = [E("vector", lambda e, i=i: e.memset(vbuf[i][:, 64:65], 1.0)) for i in range(NKB)]
    mzdr = ctx.flush("vector")

    lqp_prev = None
    kb_guard = [None] * NKB
    vb_guard = [None] * NKB
    pt_guard = [None] * 2
    ao_done = []
    for h in range(NH):
        lqp = E("sync", lambda e, h=h: e.dma_start(out=qpb[0:65, :], in_=QP_d[h, :, :]),
                waits=([lqp_prev] if lqp_prev else []), dma="qpb")
        qb_last = None
        for qb in range(4):
            o_mm = None
            for kt in range(64):
                r, lc = kt // 16, (kt % 16) * 128
                kb = kt % NKB
                lk = E("sync", lambda e, h=h, r=r, lc=lc, kb=kb: e.dma_start(
                    out=kbuf[kb][0:64, :], in_=K_d[r * DM + h * 64: r * DM + (h + 1) * 64, lc:lc + 128]),
                    waits=[x for x in [kb_guard[kb], mzdr] if x is not None], dma=f"kb{kb}")
                lv = E("scalar", lambda e, kt=kt, h=h, kb=kb: e.dma_start(
                    out=vbuf[kb][:, 0:64], in_=V_d[kt * 128:(kt + 1) * 128, h * 64:(h + 1) * 64]),
                    waits=[x for x in [vb_guard[kb], mzdr] if x is not None], dma=f"vb{kb}")
                pp = kt % 2
                m1 = E("tensor", lambda e, kb=kb, qb=qb, pp=pp: e.matmul(
                    ps_S[pp][:], kbuf[kb][:], qpb[0:65, qb * 512:(qb + 1) * 512], start=True, stop=True),
                    waits=[lk, lqp] + ([pt_guard[pp]] if pt_guard[pp] else []))
                m1d = ctx.flush("tensor")
                kb_guard[kb] = m1d
                a1 = E("scalar", lambda e, pp=pp: e.activation(Pt[pp][:], ps_S[pp][:], AF.Exp, scale=0.125), waits=[m1d])
                a1d = ctx.flush("scalar")
                pt_guard[pp] = a1d
                o_mm = E("tensor", lambda e, kb=kb, pp=pp, kt=kt: e.matmul(
                    ps_O[:], vbuf[kb][:], Pt[pp][:], start=(kt == 0), stop=(kt == 63)),
                    waits=[a1d, lv])
                vb_guard[kb] = o_mm
            omd = ctx.flush("tensor")
            r1 = E("vector", lambda e: e.reciprocal(rzf[:], ps_O[64:65, :]), waits=[omd])
            r1d = ctx.flush("vector")
            mz_ = E("tensor", lambda e: e.matmul(ps_rz[:], ones64[:], rzf[:], start=True, stop=True), waits=[r1d, mz2])
            mzd = ctx.flush("tensor")
            c1 = E("scalar", lambda e: e.copy(ofb[:], ps_O[0:64, :]), waits=[omd])
            c1d = ctx.flush("scalar")
            o1 = E("vector", lambda e, h=h, qb=qb: e.tensor_tensor(
                AO[h // 2][(h % 2) * 64:(h % 2) * 64 + 64, qb * 512:(qb + 1) * 512], ofb[:], ps_rz[:], ALU.mult),
                waits=[mzd, c1d])
            o1d = ctx.flush("vector")
            ao_done.append(o1d)
            qb_last = o1d
        lqp_prev = qb_last
    AODONE = p.barrier("vector")

    wof = [ctx.sbuf(f"wof{i}", [128, DM], F32) for i in range(2)]
    wob = [ctx.sbuf(f"wob{i}", [128, DM], BF16) for i in range(8)]
    wcast = []
    wprev = {0: None, 1: None}
    for hp in range(8):
        b = hp % 2
        lw = E("sync", lambda e, hp=hp, b=b: e.dma_start(out=wof[b][:], in_=wout_d[hp * 128:(hp + 1) * 128, :]),
               waits=[x for x in [wprev[b]] if x is not None], dma=f"wof{b}")
        cw = E("vector", lambda e, hp=hp, b=b: e.tensor_copy(wob[hp][:], wof[b][:]), waits=[lw])
        cwd = ctx.flush("vector")
        wprev[b] = cwd
        wcast.append(cwd)
    bout = ctx.sbuf("bout", [128, DM], F32)
    lbo = E("sync", lambda e: e.dma_start(out=bout[:], in_=bout_d[:]), dma="const")

    tmp = {k: ctx.sbuf("lntmp_" + k, [128, 1], F32) for k in ["mu", "negmu", "ssq", "var", "rstd"]}
    tmp["sq"] = ctx.sbuf("lntmp_sq", [128, DM], F32)
    tmp["junk"] = ctx.sbuf("lntmp_junk", [128, DM], F32)
    tmp["eps_t"] = eps_t
    xb = [ctx.sbuf(f"xb{i}", [128, DM], F32) for i in range(2)]
    g0b = [ctx.sbuf(f"g0b{i}", [128, DM], F32) for i in range(2)]
    x1t = ctx.sbuf("x1t", [128, DM], F32)
    s0f = ctx.sbuf("s0f", [128, DM], F32)
    xn32 = ctx.sbuf("xn32", [128, DM], F32)
    xnb = ctx.sbuf("xnb", [128, DM], BF16)
    ctx.xnfm_st = ctx.sbuf("xnfm_st", [128, 128], BF16)
    xg_guard = {0: None, 1: None}
    st_guard = {"x1": None, "xn1": None, "s0": None, "t1last": None, "fmst": None}
    for tt in range(NT):
        b = tt % 2
        gw = [x for x in [xg_guard[b]] if x is not None]
        lx = E("sync", lambda e, tt=tt, b=b: e.dma_start(out=xb[b][:], in_=x_d[tt * 128:(tt + 1) * 128, :]), waits=gw, dma=f"xb{b}")
        lg = E("scalar", lambda e, tt=tt, b=b: e.dma_start(out=g0b[b][:], in_=g0_d[tt * 128:(tt + 1) * 128, :]), waits=gw, dma=f"g0b{b}")
        mm = {}
        for half in range(2):
            m_ = None
            for hp in range(8):
                m_ = E("tensor", lambda e, tt=tt, half=half, hp=hp: e.matmul(
                    ps_s0[half][:], AO[hp][:, tt * 128:(tt + 1) * 128], wob[hp][:, half * 512:(half + 1) * 512],
                    start=(hp == 0), stop=(hp == 7)),
                    waits=([AODONE, wcast[-1]] + [x for x in [st_guard["s0"]] if x is not None] if hp == 0 else []))
            mm[half] = m_
        mmd = ctx.flush("tensor")
        e1 = None
        for half in range(2):
            sl = slice(half * 512, (half + 1) * 512)
            e1 = E("vector", lambda e, half=half, sl=sl: e.tensor_tensor(s0f[:, sl], ps_s0[half][:], bout[:, sl], ALU.add),
                   waits=[mmd, lbo] + [x for x in [st_guard["x1"]] if x is not None])
        e2 = E("vector", lambda e, b=b: e.tensor_tensor(s0f[:], s0f[:], g0b[b][:], ALU.mult), waits=[e1, lg])
        e3 = E("vector", lambda e, b=b: e.tensor_tensor(x1t[:], s0f[:], xb[b][:], ALU.add), waits=[e2, lx])
        e3d = ctx.flush("vector")
        st_guard["s0"] = e3d
        xg_guard[b] = e3d
        sx = E("gpsimd", lambda e, tt=tt: e.dma_start(out=x1_o[tt * 128:(tt + 1) * 128, :], in_=x1t[:]),
               waits=[e3d] + [x for x in [st_guard["x1"]] if x is not None], dma="x1st")
        st_guard["x1"] = sx
        lnt = ln_tile(ctx, E, x1t, xn32, tmp, [e3d, mz1] + [x for x in [st_guard["xn1"]] if x is not None])
        lnd = ctx.flush("vector")
        sn = E("gpsimd", lambda e, tt=tt: e.dma_start(out=xn1_o[tt * 128:(tt + 1) * 128, :], in_=xn32[:]),
               waits=[lnd], dma="xn1st")
        st_guard["xn1"] = sn
        cb = E("vector", lambda e: e.tensor_copy(xnb[:], xn32[:]),
               waits=[lnd] + [x for x in [st_guard["t1last"]] if x is not None])
        cbd = ctx.flush("vector")
        for dt in range(NDT):
            t1 = E("tensor", lambda e, dt=dt: e.transpose(ps_tp[:], xnb[:, dt * 128:(dt + 1) * 128], identb[:]),
                   waits=[cbd, cid] + [x for x in [st_guard["fmst"]] if x is not None])
            t1d = ctx.flush("tensor")
            t2 = E("vector", lambda e: e.tensor_copy(ctx.xnfm_st[:], ps_tp[:]), waits=[t1d])
            t2d = ctx.flush("vector")
            t3 = E("gpsimd", lambda e, dt=dt, tt=tt: e.dma_start(
                out=xn1fm_o[:, dt * T + tt * 128: dt * T + (tt + 1) * 128], in_=ctx.xnfm_st[:]), waits=[t2d], dma="xnfmst")
            st_guard["fmst"] = t3
            st_guard["t1last"] = t1d
    return


def build_launch1():
    ctx = Ctx("l1")
    p = ctx.start(); ctx.p = p; ctx.mkflush()
    phase1(ctx)
    return ctx.done(p.all_barriers())


def build_launch2():
    ctx = Ctx("l2")
    p = ctx.start(); ctx.p = p; ctx.mkflush()
    phase2(ctx)
    return ctx.done(p.all_barriers())


def host_prep_l1(inputs):
    x = np.asarray(inputs["L0_attn_x"])
    wqkv = np.ascontiguousarray(inputs["L0_attn_W_qkv"]).astype(np.float32)
    bqkv = np.asarray(inputs["L0_attn_b_qkv"]).astype(np.float32)
    bqkv_r = np.ascontiguousarray(bqkv.reshape(24, 128).T)
    bv128 = np.ascontiguousarray(np.broadcast_to(bqkv[2048:3072][None, :], (128, 1024)))
    pe = rel_pos_enc(S, HD)
    gate_wT = np.ascontiguousarray(np.asarray(inputs["L0_ffn_gate_w"]).astype(np.float32).T)
    wg = np.asarray(inputs["L0_ffn_wg"]).astype(np.float32)
    wu = np.asarray(inputs["L0_ffn_wu"]).astype(np.float32)
    wd = np.asarray(inputs["L0_ffn_wd"]).astype(np.float32)
    wg_t = np.ascontiguousarray(wg.reshape(8, 8, 128, 11, 128).transpose(0, 3, 1, 2, 4))
    wu_t = np.ascontiguousarray(wu.reshape(8, 8, 128, 11, 128).transpose(0, 3, 1, 2, 4))
    wd_t = np.ascontiguousarray(wd.reshape(8, 11, 128, 1024))
    swg = np.asarray(inputs["L0_ffn_swg"]).astype(np.float32)
    swu = np.asarray(inputs["L0_ffn_swu"]).astype(np.float32)
    swd = np.asarray(inputs["L0_ffn_swd"]).astype(np.float32)
    swg_t = np.ascontiguousarray(swg.reshape(8, 128, 22, 128).transpose(2, 0, 1, 3))
    swu_t = np.ascontiguousarray(swu.reshape(8, 128, 22, 128).transpose(2, 0, 1, 3))
    swd_t = np.ascontiguousarray(swd.reshape(22, 128, 1024))
    ident = np.eye(128, dtype=np.float32)
    maps = []
    for c in range(NCORES):
        b, r = c // 4, c % 4
        sl = slice(r * T, (r + 1) * T)
        pe_own = pe[sl].T
        pe2 = np.ascontiguousarray(np.concatenate([pe_own, pe_own], 0))
        maps.append({
            "x": np.ascontiguousarray(x[b, sl]).astype(np.float32),
            "wqkv": wqkv, "bqkv_r": bqkv_r, "pe2": pe2, "bv128": bv128,
            "gate_wT": gate_wT, "wg_t": wg_t, "wu_t": wu_t, "wd_t": wd_t,
            "swg_t": swg_t, "swu_t": swu_t, "swd_t": swd_t, "ident": ident,
        })
    return maps





def phase3a(ctx, i):
    p = ctx.p
    E = p.emit
    mfm_d = ctx.din("m_fm", [128, NDT * T], BF16)
    w4_d = ctx.din("in_wT_t4", [4, 32, NDT, 128, 128])
    xz_o = ctx.dout("xz", [4096, T], BF16)

    mfm = ctx.sbuf("mfm", [128, NDT * T], BF16)
    lm = E("sync", lambda e: e.dma_start(out=mfm[:], in_=mfm_d[:]), dma="const")
    wf = [ctx.sbuf(f"wf{i}", [128, NDT * 128], F32) for i in range(2)]
    wb = [ctx.sbuf(f"wb{i}", [128, NDT * 128], BF16) for i in range(2)]
    stg = [ctx.sbuf(f"stg{i}", [128, 512], BF16) for i in range(2)]
    ps = [ctx.psum(f"ps{i}", [128, 512], F32) for i in range(2)]
    wprev = {0: None, 1: None}
    st_guard = {0: None, 1: None}
    ps_guard = {0: None, 1: None}
    for ot in range(32):
        b = ot % 2
        lw = E("sync", lambda e, ot=ot, b=b, i=i: e.dma_start(
            out=wf[b].ap().rearrange("p (a n) -> p a n", a=NDT), in_=w4_d[i, ot, :, :, :].rearrange("a p n -> p a n")),
            waits=[x for x in [wprev[b]] if x is not None], dma=f"wf{b}")
        cw_ = E("vector", lambda e, b=b: e.tensor_copy(wb[b][:], wf[b][:]), waits=[lw])
        cw = ctx.flush("vector")
        wprev[b] = cw
        for tc in range(4):
            pb = tc % 2
            m_ = None
            for dt in range(NDT):
                m_ = E("tensor", lambda e, ot=ot, tc=tc, dt=dt, b=b, pb=pb: e.matmul(
                    ps[pb][:], wb[b][:, dt * 128:(dt + 1) * 128], mfm[:, dt * T + tc * 512: dt * T + (tc + 1) * 512],
                    start=(dt == 0), stop=(dt == NDT - 1)),
                    waits=([cw, lm] + [x for x in [ps_guard[pb]] if x is not None] if dt == 0 else []))
            md = ctx.flush("tensor")
            sb = tc % 2
            c1 = E("scalar", lambda e, pb=pb, sb=sb: e.copy(stg[sb][:], ps[pb][:]),
                   waits=[md] + [x for x in [st_guard[sb]] if x is not None] + (p.all_barriers() if DBG_SER else []))
            c1d = ctx.flush("scalar")
            ps_guard[pb] = c1d
            st = E("gpsimd", lambda e, ot=ot, tc=tc, sb=sb: e.dma_start(
                out=xz_o[ot * 128:(ot + 1) * 128, tc * 512:(tc + 1) * 512], in_=stg[sb][:]), waits=[c1d], dma=f"st{sb}")
            st_guard[sb] = st
        wprev[b] = md
    return



def phase3b(ctx, i):
    p = ctx.p
    E = p.emit
    xz_d = ctx.din("xz", [4096, T], BF16)
    tails_d = ctx.din("tails_ag", [4 * 2048, 3], BF16)
    ohprev_d = ctx.din("ohprev", [128, 4])
    cw_d = ctx.din("cw_r4", [4, 16, 128, 4])[i]
    cb_d = ctx.din("cb_r4", [4, 128, 16])[i]
    xpwT_d = ctx.din("xp_wT_t4", [4, 16, 128, 96])[i]
    dtwT_d = ctx.din("dt_wT_t4", [4, 16, 64, 128])[i]
    dtb_d = ctx.din("dtb_r4", [4, 128, 16])[i]
    alog_d = ctx.din("alog_r4", [4, 16, 128, 16])[i]
    dskip_d = ctx.din("dskip_r4", [4, 128, 16])[i]
    id_d = ctx.din("ident", [128, 128])

    xc_o = ctx.dout("xc", [2048, T], BF16)
    cumd_o = ctx.dout("cumd", [2048, T])
    bc_o = ctx.dout("bc", [32, T], BF16)
    s_o = ctx.dout("s_sum", [2048, 16])
    pc_o = ctx.dout("p_sum", [2048, 16])
    yl_o = ctx.dout("y_local", [2048, T], BF16)

    identt = ctx.sbuf("identt", [128, 128], F32)
    identb = ctx.sbuf("identb", [128, 128], BF16)
    li = E("sync", lambda e: e.dma_start(out=identt[:], in_=id_d[:]), dma="const")
    ci_ = E("vector", lambda e: e.tensor_copy(identb[:], identt[:]), waits=[li])
    ci = ctx.flush("vector")
    cwt = ctx.sbuf("cwt", [128, 16 * 4], F32)
    cbt = ctx.sbuf("cbt", [128, 16], F32)
    xpw = ctx.sbuf("xpw", [128, 16 * 96], F32)
    xpwb = ctx.sbuf("xpwb", [128, 16 * 96], BF16)
    dtw = ctx.sbuf("dtw", [64, 16 * 128], F32)
    dtwb = ctx.sbuf("dtwb", [64, 16 * 128], BF16)
    dtb = ctx.sbuf("dtb", [128, 16], F32)
    alog = ctx.sbuf("alog", [128, 16 * 16], F32)
    at = ctx.sbuf("at", [128, 16 * 16], F32)
    dskip = ctx.sbuf("dskip", [128, 16], F32)
    lc = []
    lc.append(E("sync", lambda e: e.dma_start(out=cwt.ap().rearrange("p (a n) -> p a n", a=16), in_=cw_d.rearrange("a p n -> p a n")), dma="const"))
    lc.append(E("sync", lambda e: e.dma_start(out=cbt[:], in_=cb_d[:]), dma="const"))
    lc.append(E("sync", lambda e: e.dma_start(out=xpw.ap().rearrange("p (a n) -> p a n", a=16), in_=xpwT_d.rearrange("a p n -> p a n")), dma="const"))
    lc.append(E("sync", lambda e: e.dma_start(out=dtw.ap().rearrange("p (a n) -> p a n", a=16), in_=dtwT_d.rearrange("a p n -> p a n")), dma="const"))
    lc.append(E("sync", lambda e: e.dma_start(out=dtb[:], in_=dtb_d[:]), dma="const"))
    lc.append(E("sync", lambda e: e.dma_start(out=alog.ap().rearrange("p (a n) -> p a n", a=16), in_=alog_d.rearrange("a p n -> p a n")), dma="const"))
    lc.append(E("sync", lambda e: e.dma_start(out=dskip[:], in_=dskip_d[:]), dma="const"))
    LC = [p.dma_total("const")]
    c1 = E("vector", lambda e: e.tensor_copy(xpwb[:], xpw[:]), waits=LC)
    c2 = E("vector", lambda e: e.tensor_copy(dtwb[:], dtw[:]), waits=LC)
    CW = ctx.flush("vector")
    a1 = E("scalar", lambda e: e.activation(at[:], alog[:], AF.Exp), waits=LC)
    a2 = E("vector", lambda e: e.tensor_scalar(at[:], at[:], -1.0, None, ALU.mult), waits=[a1])
    AT = ctx.flush("vector")

    xsp = [ctx.sbuf(f"xsp{j}", [128, T + 3], BF16) for j in range(2)]
    ohp = ctx.sbuf("ohp", [128, 4], F32)
    lohp = E("sync", lambda e: e.dma_start(out=ohp[:], in_=ohprev_d[:]), dma="const")
    tl4 = ctx.sbuf("tl4", [128, 4 * 3], BF16)
    hacc = ctx.sbuf("hacc", [128, 3], F32)
    acc1 = ctx.sbuf("acc1", [128, T], F32)
    acc2 = ctx.sbuf("acc2", [128, T], F32)
    xct = [ctx.sbuf(f"xct{i}", [128, T], BF16) for i in range(2)]
    ps_dbl = [ctx.psum(f"ps_dbl{i}", [128, 512], F32) for i in range(4)]
    xs_guard = {0: None, 1: None}
    xc_guard = {0: None, 1: None}
    hguard = [None]
    xc_stores = []
    dbl_mm = None
    for cht in range(16):
        b = cht % 2
        gw = [x for x in [xs_guard[b]] if x is not None]
        lh4 = E("sync", lambda e, cht=cht: e.dma_start(
            out=tl4.ap().rearrange("p (a n) -> p a n", a=4),
            in_=tails_d.ap().rearrange("(a q) n -> a q n", a=4)[:, cht * 128:(cht + 1) * 128, :].rearrange("a p n -> p a n")),
            waits=gw + [x for x in [hguard[0]] if x is not None], dma="tl4")
        hc1 = E("vector", lambda e: e.tensor_scalar(hacc[:], tl4[:, 0:3], ohp[:, 0:1], None, ALU.mult), waits=[lh4, lohp])
        hc2 = E("vector", lambda e: e.scalar_tensor_tensor(hacc[:], tl4[:, 3:6], ohp[:, 1:2], hacc[:], ALU.mult, ALU.add), waits=[hc1])
        hc3 = E("vector", lambda e: e.scalar_tensor_tensor(hacc[:], tl4[:, 6:9], ohp[:, 2:3], hacc[:], ALU.mult, ALU.add), waits=[hc2])
        hc4 = E("vector", lambda e, b=b: e.tensor_copy(xsp[b][:, 0:3], hacc[:]), waits=[hc3])
        lh = ctx.flush("vector")
        hguard[0] = lh
        lx = E("sync", lambda e, cht=cht, b=b: e.dma_start(out=xsp[b][:, 3:T + 3], in_=xz_d[cht * 128:(cht + 1) * 128, :]), waits=gw, dma=f"xs{b}")
        e1 = E("vector", lambda e, b=b, cht=cht: e.tensor_scalar(acc1[:], xsp[b][:, 0:T], cwt[:, cht * 4:cht * 4 + 1], None, ALU.mult), waits=[lh, lx] + LC)
        e2 = E("vector", lambda e, b=b, cht=cht: e.scalar_tensor_tensor(acc2[:], xsp[b][:, 1:T + 1], cwt[:, cht * 4 + 1:cht * 4 + 2], acc1[:], ALU.mult, ALU.add), waits=[e1])
        e3 = E("vector", lambda e, b=b, cht=cht: e.scalar_tensor_tensor(acc1[:], xsp[b][:, 2:T + 2], cwt[:, cht * 4 + 2:cht * 4 + 3], acc2[:], ALU.mult, ALU.add), waits=[e2])
        e4 = E("vector", lambda e, b=b, cht=cht: e.scalar_tensor_tensor(acc2[:], xsp[b][:, 3:T + 3], cwt[:, cht * 4 + 3:cht * 4 + 4], acc1[:], ALU.mult, ALU.add), waits=[e3])
        e4d = ctx.flush("vector")
        xs_guard[b] = e4d
        a3 = E("scalar", lambda e, b=b, cht=cht: e.activation(xct[b][:], acc2[:], AF.Silu, bias=cbt[:, cht:cht + 1]),
               waits=[e4d] + [x for x in [xc_guard[b]] if x is not None])
        a3d = ctx.flush("scalar")
        stx = E("gpsimd", lambda e, cht=cht, b=b: e.dma_start(out=xc_o[cht * 128:(cht + 1) * 128, :], in_=xct[b][:]), waits=[a3d], dma=f"xcst{b}")
        xc_stores.append(stx)
        xc_guard[b] = stx
        for tc in range(4):
            dbl_mm = E("tensor", lambda e, cht=cht, tc=tc, b=b: e.matmul(
                ps_dbl[tc][0:96, :], xpwb[:, cht * 96:(cht + 1) * 96], xct[b][:, tc * 512:(tc + 1) * 512],
                start=(cht == 0), stop=(cht == 15)), waits=([a3d, CW] if True else []))
    dbld = ctx.flush("tensor")
    dts = ctx.sbuf("dts", [64, T], BF16)
    bcs = ctx.sbuf("bcs", [32, T], BF16)
    ev = []
    for tc in range(4):
        ev.append(E("scalar", lambda e, tc=tc: e.copy(dts[:, tc * 512:(tc + 1) * 512], ps_dbl[tc][0:64, :]), waits=[dbld]))
        ev.append(E("scalar", lambda e, tc=tc: e.copy(bcs[:, tc * 512:(tc + 1) * 512], ps_dbl[tc][64:96, :]), waits=[dbld]))
    evd = ctx.flush("scalar")
    stbc = E("gpsimd", lambda e: e.dma_start(out=bc_o[:, :], in_=bcs[:]), waits=[evd], dma="bcst")

    ones_t = ctx.sbuf("ones_t", [128, T], F32)
    mzo = E("vector", lambda e: e.memset(ones_t[:], 1.0))
    delta = ctx.sbuf("delta", [128, T], F32)
    cumd = ctx.sbuf("cumd_t", [128, T], F32)
    ut = ctx.sbuf("ut", [128, T], BF16)
    sp1 = ctx.sbuf("sp1", [128, T], F32)
    sp2 = ctx.sbuf("sp2", [128, T], F32)
    av = ctx.sbuf("av", [128, T], F32)
    rv = ctx.sbuf("rv", [128, T], BF16)
    hv = ctx.sbuf("hv", [128, T], BF16)
    pv = ctx.sbuf("pv", [128, T], BF16)
    udv = ctx.sbuf("udv", [128, T], BF16)
    bbb = [ctx.sbuf(f"bbb{i}", [128, T], BF16) for i in range(2)]
    cbb = [ctx.sbuf(f"cbb{i}", [128, T], BF16) for i in range(2)]
    carry = ctx.sbuf("carry", [128, 1], F32)
    ssum = ctx.sbuf("ssum", [128, 16], F32)
    psum_t = ctx.sbuf("psum_t", [128, 16], F32)
    xc2 = [ctx.sbuf(f"xc2{i}", [128, T], BF16) for i in range(2)]
    ps_del = [ctx.psum(f"ps_del{i}", [128, 512], F32) for i in range(2)]
    ps_y = ps_dbl
    totd = ctx.sbuf("totd", [128, 1], F32)
    yst = ctx.sbuf("yst", [128, T], BF16)
    xc2_guard = {0: None, 1: None}
    bcb_guard = {0: None, 1: None}
    st2_guard = {"cumd": None, "s": None, "pc": None, "yl": None}
    del_guard = [None]
    sy_guard = [None]
    for cht in range(16):
        b = cht % 2
        lxc = E("sync", lambda e, cht=cht, b=b: e.dma_start(out=xc2[b][:], in_=xc_o[cht * 128:(cht + 1) * 128, :]),
                waits=[xc_stores[cht]] + [x for x in [xc2_guard[b]] if x is not None], dma=f"xc2{b}")
        dmm = {}
        for tc in range(4):
            pb = tc % 2
            dmm[tc] = E("tensor", lambda e, cht=cht, tc=tc, pb=pb: e.matmul(
                ps_del[pb][:], dtwb[:, cht * 128:(cht + 1) * 128], dts[:, tc * 512:(tc + 1) * 512],
                start=True, stop=True), waits=([evd, CW, dbld] + [x for x in [del_guard[0]] if x is not None] if tc < 2 else ([del_guard[0]] if tc == 2 else [])))
            if tc == 1 or tc == 3:
                dd = ctx.flush("tensor")
                for tcc in ([0, 1] if tc == 1 else [2, 3]):
                    sl = slice(tcc * 512, (tcc + 1) * 512)
                    E("scalar", lambda e, cht=cht, tcc=tcc, sl=sl: e.activation(
                        sp1[:, sl], ps_del[tcc % 2][:], AF.Abs, bias=dtb[:, cht:cht + 1]), waits=[dd])
                    E("scalar", lambda e, cht=cht, tcc=tcc, sl=sl: e.activation(
                        sp2[:, sl], ps_del[tcc % 2][:], AF.Relu, bias=dtb[:, cht:cht + 1]), waits=[dd])
                del_guard[0] = ctx.flush("scalar")
        sfa = E("scalar", lambda e: e.activation(sp1[:], sp1[:], AF.Exp, scale=-1.0))
        sfb = E("scalar", lambda e: e.activation(sp1[:], sp1[:], AF.Ln, bias=1.0), waits=[sfa])
        sfd = ctx.flush("scalar")
        dl1 = E("vector", lambda e: e.tensor_tensor(delta[:], sp1[:], sp2[:], ALU.add), waits=[sfd])
        cd1 = E("vector", lambda e: e.tensor_tensor_scan(cumd[:], ones_t[:], delta[:], 0.0, ALU.mult, ALU.add), waits=[dl1, mzo])
        cdd = ctx.flush("vector")
        sc1 = E("gpsimd", lambda e, cht=cht: e.dma_start(out=cumd_o[cht * 128:(cht + 1) * 128, :], in_=cumd[:]),
                waits=[cdd] + [x for x in [st2_guard["cumd"]] if x is not None], dma="cumdst")
        st2_guard["cumd"] = sc1
        u1 = E("vector", lambda e, b=b: e.tensor_tensor(ut[:], delta[:], xc2[b][:], ALU.mult), waits=[cdd, lxc])
        tc1 = E("vector", lambda e: e.tensor_copy(totd[:], cumd[:, T - 1:T]), waits=[cdd])
        tcd = ctx.flush("vector")
        pc1 = E("scalar", lambda e, cht=cht: e.activation(psum_t[:], at[:, cht * 16:(cht + 1) * 16], AF.Exp, scale=totd[:]), waits=[tcd, AT])
        pcd = ctx.flush("scalar")
        sp_st = E("gpsimd", lambda e, cht=cht: e.dma_start(out=pc_o[cht * 128:(cht + 1) * 128, :], in_=psum_t[:]),
                  waits=[pcd] + [x for x in [st2_guard["pc"]] if x is not None], dma="pcst")
        st2_guard["pc"] = sp_st
        ymm = None
        for n in range(16):
            bb = n % 2
            lb_ = E("sync", lambda e, n=n, bb=bb: e.dma_start(out=bbb[bb].ap().unsqueeze(1), in_=bc_o[n:n + 1, :].partition_broadcast(128)),
                    waits=[stbc] + [x for x in [bcb_guard[bb]] if x is not None], dma=f"bcb{bb}")
            lc_ = E("scalar", lambda e, n=n, bb=bb: e.dma_start(out=cbb[bb].ap().unsqueeze(1), in_=bc_o[16 + n:17 + n, :].partition_broadcast(128)),
                    waits=[stbc] + [x for x in [bcb_guard[bb]] if x is not None], dma=f"ccb{bb}")
            aa1 = E("scalar", lambda e, cht=cht, n=n: e.activation(av[:], delta[:], AF.Exp, scale=at[:, cht * 16 + n:cht * 16 + n + 1]),
                    waits=[AT] + ([ymm] if ymm else []))
            aad = ctx.flush("scalar")
            rr1 = E("vector", lambda e, bb=bb: e.tensor_tensor(rv[:], ut[:], bbb[bb][:], ALU.mult), waits=[u1, lb_, aad])
            hh1 = E("vector", lambda e: e.tensor_tensor_scan(hv[:], av[:], rv[:], 0.0, ALU.mult, ALU.add), waits=[rr1])
            cc1 = E("vector", lambda e, n=n: e.tensor_copy(ssum[:, n:n + 1], hv[:, T - 1:T]), waits=[hh1])
            pp1 = E("vector", lambda e, bb=bb: e.tensor_tensor(pv[:], hv[:], cbb[bb][:], ALU.mult),
                    waits=[cc1, lc_] + ([ymm] if ymm else []))
            ppd = ctx.flush("vector")
            bcb_guard[bb] = ppd
            for tc in range(4):
                ymm = E("tensor", lambda e, n=n, tc=tc: e.matmul(
                    ps_y[tc][:], identb[:], pv[:, tc * 512:(tc + 1) * 512], start=(n == 0), stop=False),
                    waits=[ppd, ci] + ([sy_guard[0]] if (n == 0 and sy_guard[0] is not None) else []))
        ud1 = E("vector", lambda e, b=b, cht=cht: e.tensor_scalar(udv[:], xc2[b][:], dskip[:, cht:cht + 1], None, ALU.mult),
                waits=[lxc] + ([ymm] if ymm else []))
        udd = ctx.flush("vector")
        for tc in range(4):
            ymm = E("tensor", lambda e, tc=tc: e.matmul(
                ps_y[tc][:], identb[:], udv[:, tc * 512:(tc + 1) * 512], start=False, stop=True), waits=[udd])
        ymd = ctx.flush("tensor")
        xc2_guard[b] = ymd
        sy = []
        for tc in range(4):
            sy.append(E("scalar", lambda e, tc=tc: e.copy(yst[:, tc * 512:(tc + 1) * 512], ps_y[tc][:]),
                        waits=[ymd] + [x for x in [st2_guard["yl"]] if x is not None]))
        syd = ctx.flush("scalar")
        sy_guard[0] = syd
        sy_st = E("gpsimd", lambda e, cht=cht: e.dma_start(out=yl_o[cht * 128:(cht + 1) * 128, :], in_=yst[:]),
                  waits=[syd] + [x for x in [st2_guard["yl"]] if x is not None], dma="ylst")
        st2_guard["yl"] = sy_st
        ss_st = E("gpsimd", lambda e, cht=cht: e.dma_start(out=s_o[cht * 128:(cht + 1) * 128, :], in_=ssum[:]),
                  waits=[ppd] + [x for x in [st2_guard["s"]] if x is not None], dma="sst")
        st2_guard["s"] = ss_st
    return



def phase3c(ctx, i):
    p = ctx.p
    E = p.emit
    yl_d = ctx.din("y_local", [2048, T], BF16)
    cumd_d = ctx.din("cumd", [2048, T])
    bc_d = ctx.din("bc", [32, T], BF16)
    z_d = ctx.din("xz", [4096, T], BF16)
    s_all_d = ctx.din("s_ag", [4 * 2048, 16]).rearrange("(a q) n -> a q n", a=4)
    p_all_d = ctx.din("p_ag", [4 * 2048, 16]).rearrange("(a q) n -> a q n", a=4)
    oh_d = ctx.din("ohsel", [128, 4])
    alog_d = ctx.din("alog_r4", [4, 16, 128, 16])[i]
    outwT_d = ctx.din("out_wT_t4", [4, 16, 128, DM])[i]
    lnw_d = ctx.din("lnw128_4", [4, 128, DM])[i]
    lnb_d = ctx.din("lnb128_4", [4, 128, DM])[i]
    m_d = ctx.din("m_tok", [T, DM])
    id_d = ctx.din("ident", [128, 128])

    mn_o = ctx.dout("m_next", [T, DM])
    mnfm_o = ctx.dout("m_next_fm", [128, NDT * T], BF16)
    gated_o = ctx.dout("gated", [2048, T], BF16)

    identt = ctx.sbuf("identt", [128, 128], F32)
    identb = ctx.sbuf("identb", [128, 128], BF16)
    eps_t = ctx.sbuf("eps_t", [128, 1], F32)
    li = E("sync", lambda e: e.dma_start(out=identt[:], in_=id_d[:]), dma="const")
    ci_ = E("vector", lambda e: e.tensor_copy(identb[:], identt[:]), waits=[li])
    ci = ctx.flush("vector")
    mz1 = E("vector", lambda e: e.memset(eps_t[:], 1e-5))
    alog = ctx.sbuf("alog", [128, 16 * 16], F32)
    at = ctx.sbuf("at", [128, 16 * 16], F32)
    oh = ctx.sbuf("oh", [128, 4], F32)
    lnw = ctx.sbuf("lnw", [128, DM], F32)
    lnb = ctx.sbuf("lnb", [128, DM], F32)
    la = E("sync", lambda e: e.dma_start(out=alog.ap().rearrange("p (a n) -> p a n", a=16), in_=alog_d.rearrange("a p n -> p a n")), dma="const")
    lo = E("sync", lambda e: e.dma_start(out=oh[:], in_=oh_d[:]), dma="const")
    lw_ = E("sync", lambda e: e.dma_start(out=lnw[:], in_=lnw_d[:]), dma="const")
    lb_ = E("sync", lambda e: e.dma_start(out=lnb[:], in_=lnb_d[:]), dma="const")
    a1 = E("scalar", lambda e: e.activation(at[:], alog[:], AF.Exp), waits=[la])
    a2 = E("vector", lambda e: e.tensor_scalar(at[:], at[:], -1.0, None, ALU.mult), waits=[a1])
    AT = ctx.flush("vector")

    sti = [ctx.sbuf(f"sti{i}", [128, 4 * 16], F32) for i in range(2)]
    pti = [ctx.sbuf(f"pti{i}", [128, 4 * 16], F32) for i in range(2)]
    t0 = ctx.sbuf("t0", [128, 16], F32)
    t1_ = ctx.sbuf("t1_", [128, 16], F32)
    t2_ = ctx.sbuf("t2_", [128, 16], F32)
    hin = ctx.sbuf("hin", [128, 16], F32)
    tmpv = ctx.sbuf("tmpv", [128, 16], F32)
    cumd = [ctx.sbuf(f"cumd{i}", [128, T], F32) for i in range(2)]
    ylc = [ctx.sbuf(f"ylc{i}", [128, T], BF16) for i in range(2)]
    zc = [ctx.sbuf(f"zc{i}", [128, T], BF16) for i in range(2)]
    dfv = ctx.sbuf("dfv", [128, T], F32)
    termv = ctx.sbuf("termv", [128, T], BF16)
    szv = ctx.sbuf("szv", [128, T], BF16)
    gtd = ctx.sbuf("gtd", [128, T], BF16)
    cbb = [ctx.sbuf(f"cbb{i}", [128, T], BF16) for i in range(2)]
    ps_y = [ctx.psum(f"ps_y{i}", [128, 512], F32) for i in range(4)]
    ld_guard = {0: None, 1: None}
    bcb_guard = {0: None, 1: None}
    gst_guard = [None]
    for cht in range(16):
        b = cht % 2
        gw = [x for x in [ld_guard[b]] if x is not None]
        ls = E("sync", lambda e, cht=cht, b=b: e.dma_start(out=sti[b].ap().rearrange("p (a n) -> p a n", a=4), in_=s_all_d[:, cht * 128:(cht + 1) * 128, :].rearrange("a p n -> p a n")), waits=gw, dma=f"sti{b}")
        lp = E("sync", lambda e, cht=cht, b=b: e.dma_start(out=pti[b].ap().rearrange("p (a n) -> p a n", a=4), in_=p_all_d[:, cht * 128:(cht + 1) * 128, :].rearrange("a p n -> p a n")), waits=gw, dma=f"sti{b}")
        lcu = E("sync", lambda e, cht=cht, b=b: e.dma_start(out=cumd[b][:], in_=cumd_d[cht * 128:(cht + 1) * 128, :]), waits=gw, dma=f"cud{b}")
        lyl = E("scalar", lambda e, cht=cht, b=b: e.dma_start(out=ylc[b][:], in_=yl_d[cht * 128:(cht + 1) * 128, :]), waits=gw, dma=f"ylc{b}")
        lz = E("scalar", lambda e, cht=cht, b=b: e.dma_start(out=zc[b][:], in_=z_d[2048 + cht * 128: 2048 + (cht + 1) * 128, :]), waits=gw, dma=f"zc{b}")
        h1 = E("vector", lambda e, b=b: e.tensor_copy(t0[:], sti[b][:, 0:16]), waits=[ls, lp])
        h2 = E("vector", lambda e, b=b: e.tensor_tensor(tmpv[:], pti[b][:, 16:32], t0[:], ALU.mult))
        h3 = E("vector", lambda e, b=b: e.tensor_tensor(t1_[:], tmpv[:], sti[b][:, 16:32], ALU.add))
        h4 = E("vector", lambda e, b=b: e.tensor_tensor(tmpv[:], pti[b][:, 32:48], t1_[:], ALU.mult))
        h5 = E("vector", lambda e, b=b: e.tensor_tensor(t2_[:], tmpv[:], sti[b][:, 32:48], ALU.add))
        h6 = E("vector", lambda e: e.tensor_scalar(hin[:], t0[:], oh[:, 1:2], None, ALU.mult), waits=[lo])
        h7 = E("vector", lambda e: e.scalar_tensor_tensor(tmpv[:], t1_[:], oh[:, 2:3], hin[:], ALU.mult, ALU.add))
        h8 = E("vector", lambda e: e.scalar_tensor_tensor(hin[:], t2_[:], oh[:, 3:4], tmpv[:], ALU.mult, ALU.add))
        h8d = ctx.flush("vector")
        ld_guard[b] = h8d
        ymm = None
        for tc in range(4):
            ymm = E("tensor", lambda e, tc=tc, b=b: e.matmul(
                ps_y[tc][:], identb[:], ylc[b][:, tc * 512:(tc + 1) * 512], start=True, stop=False),
                waits=[lyl, ci] + [x for x in [gst_guard[0]] if x is not None])
        for n in range(16):
            bb = n % 2
            lc_ = E("scalar", lambda e, n=n, bb=bb: e.dma_start(out=cbb[bb].ap().unsqueeze(1), in_=bc_d[16 + n:17 + n, :].partition_broadcast(128)),
                    waits=[x for x in [bcb_guard[bb]] if x is not None], dma=f"ccb{bb}")
            aa1 = E("scalar", lambda e, cht=cht, n=n, b=b: e.activation(dfv[:], cumd[b][:], AF.Exp, scale=at[:, cht * 16 + n:cht * 16 + n + 1]),
                    waits=[AT, lcu] + ([ymm] if ymm else []))
            aad = ctx.flush("scalar")
            tt1 = E("vector", lambda e, n=n, bb=bb: e.scalar_tensor_tensor(termv[:], dfv[:], hin[:, n:n + 1], cbb[bb][:], ALU.mult, ALU.mult),
                    waits=[aad, lc_, h8d] + ([ymm] if ymm else []))
            ttd = ctx.flush("vector")
            bcb_guard[bb] = ttd
            for tc in range(4):
                ymm = E("tensor", lambda e, tc=tc, n=n: e.matmul(
                    ps_y[tc][:], identb[:], termv[:, tc * 512:(tc + 1) * 512], start=False, stop=(tc == 3 and n == 15)),
                    waits=[ttd])
        ymd = ctx.flush("tensor")
        sz1 = E("scalar", lambda e, b=b: e.activation(szv[:], zc[b][:], AF.Silu), waits=[lz, ymd])
        szd = ctx.flush("scalar")
        g1 = None
        for tc in range(4):
            sl = slice(tc * 512, (tc + 1) * 512)
            g1 = E("vector", lambda e, tc=tc, sl=sl: e.tensor_tensor(gtd[:, sl], ps_y[tc][:], szv[:, sl], ALU.mult), waits=[szd])
        g1d = ctx.flush("vector")
        gst = E("gpsimd", lambda e, cht=cht: e.dma_start(out=gated_o[cht * 128:(cht + 1) * 128, :], in_=gtd[:]),
                waits=[g1d] + [x for x in [gst_guard[0]] if x is not None], dma="gst")
        gst_guard[0] = gst
    GDONE = [p.dma_total("gst")]

    owf = [ctx.sbuf(f"owf{i}", [128, DM], F32) for i in range(2)]
    owb = [ctx.sbuf(f"owb{i}", [128, DM], BF16) for i in range(2)]
    gl = [ctx.sbuf(f"gl{i}", [128, 128], BF16) for i in range(2)]
    mt = [ctx.sbuf(f"mt{i}", [128, DM], F32) for i in range(2)]
    res = ctx.sbuf("res", [128, DM], F32)
    mn32 = ctx.sbuf("mn32", [128, DM], F32)
    mnb = ctx.sbuf("mnb", [128, DM], BF16)
    fm_st = ctx.sbuf("fm_st", [128, 128], BF16)
    tmp = {k: ctx.sbuf("lntmp_" + k, [128, 1], F32) for k in ["mu", "negmu", "ssq", "var", "rstd"]}
    tmp["sq"] = ctx.sbuf("lntmp_sq", [128, DM], F32)
    tmp["junk"] = ctx.sbuf("lntmp_junk", [128, DM], F32)
    tmp["eps_t"] = eps_t
    ps_o = [ctx.psum(f"ps_o{i}", [128, 512], F32) for i in range(2)]
    ps_tp = ctx.psum("ps_tp", [128, 128], BF16)
    st_guard = {"mn": None, "fm": None, "res": None, "t1last": None, "mt": {0: None, 1: None}, "gl": {0: None, 1: None}, "ow": {0: None, 1: None}, "owpe": {0: None, 1: None}}
    for tt in range(NT):
        b = tt % 2
        lm = E("sync", lambda e, tt=tt, b=b: e.dma_start(out=mt[b][:], in_=m_d[tt * 128:(tt + 1) * 128, :]),
               waits=[x for x in [st_guard["mt"][b]] if x is not None], dma=f"mt{b}")
        mm_ = {}
        for cht in range(16):
            gb = cht % 2
            lg = E("scalar", lambda e, cht=cht, tt=tt, gb=gb: e.dma_start(out=gl[gb][:], in_=gated_o[cht * 128:(cht + 1) * 128, tt * 128:(tt + 1) * 128]),
                   waits=GDONE + [x for x in [st_guard["gl"][gb]] if x is not None], dma=f"gl{gb}")
            ob = cht % 2
            low = E("sync", lambda e, cht=cht, ob=ob: e.dma_start(out=owf[ob][:], in_=outwT_d[cht, :, :]),
                    waits=[x for x in [st_guard["ow"][ob]] if x is not None], dma=f"ow{ob}")
            cow_ = E("vector", lambda e, ob=ob: e.tensor_copy(owb[ob][:], owf[ob][:]),
                     waits=[low] + [x for x in [st_guard["owpe"][ob]] if x is not None])
            cow = ctx.flush("vector")
            st_guard["ow"][ob] = cow
            gld = ctx.flush("scalar")
            for half in range(2):
                mm_[half] = E("tensor", lambda e, cht=cht, half=half, gb=gb, ob=ob: e.matmul(
                    ps_o[half][:], gl[gb][:], owb[ob][:, half * 512:(half + 1) * 512],
                    start=(cht == 0), stop=(cht == 15)),
                    waits=[lg, gld, cow] + ([x for x in [st_guard["res"]] if x is not None] if cht == 0 else []))
            st_guard["gl"][gb] = mm_[1]
            st_guard["owpe"][ob] = mm_[1]
        mmd = ctx.flush("tensor")
        r1 = None
        for half in range(2):
            sl = slice(half * 512, (half + 1) * 512)
            r1 = E("vector", lambda e, half=half, sl=sl, b=b: e.tensor_tensor(res[:, sl], ps_o[half][:], mt[b][:, sl], ALU.add), waits=[mmd, lm])
        r1d = ctx.flush("vector")
        st_guard["res"] = r1d
        st_guard["mt"][b] = r1d
        lnt = ln_tile(ctx, E, res, mn32, tmp, [r1d, mz1])
        w1_ = E("vector", lambda e: e.tensor_tensor(mn32[:], mn32[:], lnw[:], ALU.mult), waits=[lnt, lw_])
        w2_ = E("vector", lambda e: e.tensor_tensor(mn32[:], mn32[:], lnb[:], ALU.add), waits=[w1_, lb_] + [x for x in [st_guard["mn"]] if x is not None])
        w2d = ctx.flush("vector")
        smn = E("gpsimd", lambda e, tt=tt: e.dma_start(out=mn_o[tt * 128:(tt + 1) * 128, :], in_=mn32[:]), waits=[w2d], dma="mnst")
        st_guard["mn"] = smn
        cb = E("vector", lambda e: e.tensor_copy(mnb[:], mn32[:]), waits=[w2d] + [x for x in [st_guard["t1last"]] if x is not None])
        cbd = ctx.flush("vector")
        for dt in range(NDT):
            t1 = E("tensor", lambda e, dt=dt: e.transpose(ps_tp[:], mnb[:, dt * 128:(dt + 1) * 128], identb[:]),
                   waits=[cbd, ci] + [x for x in [st_guard["fm"]] if x is not None])
            t1d = ctx.flush("tensor")
            t2 = E("vector", lambda e: e.tensor_copy(fm_st[:], ps_tp[:]), waits=[t1d])
            t2d = ctx.flush("vector")
            t3 = E("gpsimd", lambda e, dt=dt, tt=tt: e.dma_start(
                out=mnfm_o[:, dt * T + tt * 128: dt * T + (tt + 1) * 128], in_=fm_st[:]), waits=[t2d], dma="fmst")
            st_guard["fm"] = t3
            st_guard["t1last"] = t1d
    return



def phase4(ctx):
    p = ctx.p
    E = p.emit
    xnfm_d = ctx.din("xn1fm", [128, NDT * T], BF16)
    xnt_d = ctx.din("xn1", [T, DM])
    x1_d = ctx.din("x1", [T, DM])
    s1_d = ctx.din("s1", [T, DM])
    rw_d = ctx.din("router_w", [DM, 8])
    w1_d = ctx.din("w1_t", [8, 32, NDT, 128, 128])
    b1_d = ctx.din("b1_r", [8, 128, 32])
    w2_d = ctx.din("w2_t", [8, 32, 128, DM])
    b2_d = ctx.din("b2", [8, DM])
    gam_d = ctx.din("gamma", [8, DM])
    bet_d = ctx.din("beta", [8, DM])
    id_d = ctx.din("ident", [128, 128])
    out_o = ctx.dout("out", [T, DM])

    identt = ctx.sbuf("identt", [128, 128], F32)
    eps_t = ctx.sbuf("eps_t", [128, 1], F32)
    li = E("sync", lambda e: e.dma_start(out=identt[:], in_=id_d[:]), dma="const")
    mz1 = E("vector", lambda e: e.memset(eps_t[:], 1e-6))
    one_col = ctx.sbuf("one_col", [128, 1], F32)
    mz2 = E("vector", lambda e: e.memset(one_col[:], 1.0))
    xnfm = ctx.sbuf("xnfm", [128, NDT * T], BF16)
    lxf = E("sync", lambda e: e.dma_start(out=xnfm[:], in_=xnfm_d[:]), dma="const")
    rw = ctx.sbuf("rw", [128, NDT * 8], F32)
    lrw = E("sync", lambda e: e.dma_start(out=rw.ap().rearrange("p (a n) -> p a n", a=NDT), in_=rw_d.ap().rearrange("(a p) n -> p a n", p=128)), dma="const")

    gates = ctx.sbuf("gates", [128, NT * 8], F32)
    xt32 = [ctx.sbuf(f"xt32{i}", [128, DM], F32) for i in range(2)]
    x32t = ctx.sbuf("x32t", [128, NDT * 128], F32)
    lgt = ctx.sbuf("lgt", [128, 8], F32)
    m8 = ctx.sbuf("m8", [128, 8], F32)
    d21 = ctx.sbuf("d21", [128, 1], F32)
    ez = ctx.sbuf("ez", [128, 1], F32)
    rz = ctx.sbuf("rz", [128, 1], F32)
    pex = ctx.sbuf("pex", [128, 8], F32)
    msk = ctx.sbuf("msk", [128, 8], F32)
    ps_misc = ctx.psum("ps_misc", [128, 128], F32)
    ps_g = ctx.psum("ps_g", [128, 512], F32)
    xl_guard = {0: None, 1: None}
    gprev = ()
    for tt in range(NT):
        b = tt % 2
        lx = E("sync", lambda e, tt=tt, b=b: e.dma_start(out=xt32[b][:], in_=xnt_d[tt * 128:(tt + 1) * 128, :]),
               waits=[x for x in [xl_guard[b]] if x is not None], dma=f"xt{b}")
        tpd = None
        for dt in range(NDT):
            t3 = E("tensor", lambda e, b=b, dt=dt: e.transpose(ps_misc[:], xt32[b][:, dt * 128:(dt + 1) * 128], identt[:]),
                   waits=[lx, li] + ([tpd] if tpd else []))
            t3d = ctx.flush("tensor")
            t4 = E("vector", lambda e, dt=dt: e.tensor_copy(x32t[:, dt * 128:(dt + 1) * 128], ps_misc[:]), waits=[t3d])
            tpd = ctx.flush("vector")
        xl_guard[b] = tpd
        mms = None
        for dt in range(NDT):
            mms = E("tensor", lambda e, dt=dt: e.matmul(
                ps_g[0:128, 0:8], x32t[:, dt * 128:(dt + 1) * 128], rw[:, dt * 8:(dt + 1) * 8],
                start=(dt == 0), stop=(dt == NDT - 1)),
                waits=([tpd, lrw] + list(gprev) if dt == 0 else []))
        mmsd = ctx.flush("tensor")
        c1 = E("vector", lambda e: e.tensor_copy(lgt[:], ps_g[0:128, 0:8]), waits=[mmsd])
        c2 = E("vector", lambda e: e.max(m8[:], lgt[:]), waits=[c1])
        c3 = E("vector", lambda e: e.tensor_scalar(d21[:], m8[:, 1:2], m8[:, 0:1], None, ALU.subtract), waits=[c2])
        c4 = E("scalar", lambda e: e.activation(ez[:], d21[:], AF.Exp), waits=[c3])
        c5 = E("vector", lambda e: e.tensor_tensor(ez[:], ez[:], one_col[:], ALU.add), waits=[c4, mz2])
        c6 = E("vector", lambda e: e.reciprocal(rz[:], ez[:]), waits=[c5])
        c7 = E("vector", lambda e: e.tensor_scalar(pex[:], lgt[:], m8[:, 0:1], None, ALU.subtract), waits=[c2])
        c8 = E("scalar", lambda e: e.activation(pex[:], pex[:], AF.Exp), waits=[c7])
        c9 = E("vector", lambda e: e.tensor_scalar(msk[:], lgt[:], m8[:, 1:2], None, ALU.is_ge), waits=[c8])
        c10 = E("vector", lambda e: e.tensor_tensor(pex[:], pex[:], msk[:], ALU.mult), waits=[c9])
        c11 = E("vector", lambda e, tt=tt: e.tensor_scalar(gates[:, tt * 8:(tt + 1) * 8], pex[:], rz[:], None, ALU.mult), waits=[c10, c6])
        gprev = (c11, c2)
    GATES = ctx.flush("vector")

    g1acc = ctx.sbuf("g1acc", [128, NT * DM], F32)
    zg = E("vector", lambda e: e.memset(g1acc[:], 0.0))
    hfm = ctx.sbuf("hfm", [128, 32 * 512], BF16)
    w1f = [ctx.sbuf(f"w1f{i}", [128, NDT * 128], F32) for i in range(2)]
    w1b = [ctx.sbuf(f"w1b{i}", [128, NDT * 128], BF16) for i in range(2)]
    w2f = [ctx.sbuf(f"w2f{i}", [128, DM], F32) for i in range(2)]
    w2b = [ctx.sbuf(f"w2b{i}", [128, DM], BF16) for i in range(2)]
    b1t = ctx.sbuf("b1t", [128, 32], F32)
    b2b = ctx.sbuf("b2b", [128, DM], F32)
    gab = ctx.sbuf("gab", [128, DM], F32)
    beb = ctx.sbuf("beb", [128, DM], F32)
    res = ctx.sbuf("res", [128, DM], F32)
    nrm = ctx.sbuf("nrm", [128, DM], F32)
    xnt2 = [ctx.sbuf(f"xnt2{i}", [128, DM], F32) for i in range(2)]
    tmp = {k: ctx.sbuf("lntmp_" + k, [128, 1], F32) for k in ["mu", "negmu", "ssq", "var", "rstd"]}
    tmp["sq"] = ctx.sbuf("lntmp_sq", [128, DM], F32)
    tmp["junk"] = ctx.sbuf("lntmp_junk", [128, DM], F32)
    tmp["eps_t"] = eps_t
    ps_up = [ctx.psum(f"ps_up{i}", [128, 512], F32) for i in range(2)]
    ps_dn = [ctx.psum(f"ps_dn{i}", [128, 512], F32) for i in range(4)]
    w1_guard = {0: None, 1: None}
    w2_guard = {0: None, 1: None}
    up_guard = {0: None, 1: None}
    xn2_guard = {0: None, 1: None}
    dn_guard = {0: None, 1: None, 2: None, 3: None}
    hf_guard = [None]
    acc_guard = [zg]
    for ex in range(8):
        lb1 = E("sync", lambda e, ex=ex: e.dma_start(out=b1t[:], in_=b1_d[ex, :, :]), waits=acc_guard, dma="b1t")
        lb2 = E("scalar", lambda e, ex=ex: e.dma_start(out=b2b.ap().unsqueeze(1), in_=b2_d[ex:ex + 1, :].partition_broadcast(128)), waits=acc_guard, dma="b2b")
        lga = E("scalar", lambda e, ex=ex: e.dma_start(out=gab.ap().unsqueeze(1), in_=gam_d[ex:ex + 1, :].partition_broadcast(128)), waits=acc_guard, dma="gab")
        lbe = E("scalar", lambda e, ex=ex: e.dma_start(out=beb.ap().unsqueeze(1), in_=bet_d[ex:ex + 1, :].partition_broadcast(128)), waits=acc_guard, dma="beb")
        for tc in range(4):
            agd = None
            for fc in range(32):
                wbuf = fc % 2
                lw1 = E("sync", lambda e, ex=ex, fc=fc, wbuf=wbuf: e.dma_start(
                    out=w1f[wbuf].ap().rearrange("p (a n) -> p a n", a=NDT), in_=w1_d[ex, fc, :, :, :].rearrange("a p n -> p a n")),
                    waits=[x for x in [w1_guard[wbuf]] if x is not None], dma=f"w1{wbuf}")
                cw1_ = E("vector", lambda e, wbuf=wbuf: e.tensor_copy(w1b[wbuf][:], w1f[wbuf][:]), waits=[lw1])
                cw1 = ctx.flush("vector")
                w1_guard[wbuf] = cw1
                pb = fc % 2
                m_ = None
                for dt in range(NDT):
                    m_ = E("tensor", lambda e, tc=tc, fc=fc, dt=dt, wbuf=wbuf, pb=pb: e.matmul(
                        ps_up[pb][:], w1b[wbuf][:, dt * 128:(dt + 1) * 128], xnfm[:, dt * T + tc * 512: dt * T + (tc + 1) * 512],
                        start=(dt == 0), stop=(dt == NDT - 1)),
                        waits=([cw1, lxf] + [x for x in [up_guard[pb]] if x is not None] if dt == 0 else []))
                md = ctx.flush("tensor")
                w1_guard[wbuf] = md
                ag = E("scalar", lambda e, fc=fc, pb=pb: e.activation(hfm[:, fc * 512:(fc + 1) * 512], ps_up[pb][:], AF.Gelu, bias=b1t[:, fc:fc + 1]),
                       waits=[md, lb1] + [x for x in [hf_guard[0]] if x is not None])
                agd = ctx.flush("scalar")
                up_guard[pb] = agd
            for tp in range(2):
                dn_last = None
                for fc in range(32):
                    wbuf = fc % 2
                    lw2 = E("sync", lambda e, ex=ex, fc=fc, wbuf=wbuf: e.dma_start(out=w2f[wbuf][:], in_=w2_d[ex, fc, :, :]),
                            waits=[x for x in [w2_guard[wbuf]] if x is not None], dma=f"w2{wbuf}")
                    cw2_ = E("vector", lambda e, wbuf=wbuf: e.tensor_copy(w2b[wbuf][:], w2f[wbuf][:]), waits=[lw2])
                    cw2 = ctx.flush("vector")
                    for sub in range(4):
                        ttl, half = tp * 2 + sub // 2, sub % 2
                        dn_last = E("tensor", lambda e, fc=fc, ttl=ttl, half=half, wbuf=wbuf, sub=sub: e.matmul(
                            ps_dn[sub][:], hfm[:, fc * 512 + ttl * 128: fc * 512 + ttl * 128 + 128],
                            w2b[wbuf][:, half * 512:(half + 1) * 512],
                            start=(fc == 0), stop=(fc == 31)),
                            waits=([cw2, agd] + [x for x in [dn_guard[sub]] if x is not None] if fc == 0 else [cw2]))
                    w2_guard[wbuf] = dn_last
                dnd = ctx.flush("tensor")
                hf_guard[0] = dnd
                for tl in range(2):
                    ttl = tp * 2 + tl
                    tt_g = tc * 4 + ttl
                    xb2 = ttl % 2
                    lxn = E("sync", lambda e, tt_g=tt_g, xb2=xb2: e.dma_start(out=xnt2[xb2][:], in_=xnt_d[tt_g * 128:(tt_g + 1) * 128, :]),
                            waits=[x for x in [xn2_guard[xb2]] if x is not None], dma=f"xn2{xb2}")
                    r1 = None
                    for half in range(2):
                        sub = tl * 2 + half
                        sl = slice(half * 512, (half + 1) * 512)
                        r1 = E("vector", lambda e, sub=sub, sl=sl: e.tensor_tensor(res[:, sl], ps_dn[sub][:], b2b[:, sl], ALU.add), waits=[dnd, lb2])
                    r2 = E("vector", lambda e, xb2=xb2: e.tensor_tensor(res[:], res[:], xnt2[xb2][:], ALU.add), waits=[r1, lxn])
                    r2d = ctx.flush("vector")
                    xn2_guard[xb2] = r2d
                    dn_guard[tl * 2] = r2d
                    dn_guard[tl * 2 + 1] = r2d
                    lnt = ln_tile(ctx, E, res, nrm, tmp, [r2d, mz1], nf=1024.0)
                    g1_ = E("vector", lambda e: e.tensor_tensor(nrm[:], nrm[:], gab[:], ALU.mult), waits=[lnt, lga])
                    g2_ = E("vector", lambda e: e.tensor_tensor(nrm[:], nrm[:], beb[:], ALU.add), waits=[g1_, lbe])
                    g3_ = E("vector", lambda e, ex=ex, tt_g=tt_g: e.tensor_scalar(nrm[:], nrm[:], gates[:, tt_g * 8 + ex: tt_g * 8 + ex + 1], None, ALU.mult), waits=[g2_, GATES])
                    g4_ = E("vector", lambda e, tt_g=tt_g: e.tensor_tensor(g1acc[:, tt_g * DM:(tt_g + 1) * DM], g1acc[:, tt_g * DM:(tt_g + 1) * DM], nrm[:], ALU.add), waits=[g3_, zg])
        acc_guard = [ctx.flush("vector")]

    x1b = [xt32[0], xt32[1]]
    s1b = [tmp["sq"], tmp["junk"]]
    sg1 = nrm
    outb = res
    fx_guard = {0: None, 1: None}
    ost_guard = [None]
    for tt in range(NT):
        b = tt % 2
        gw = [x for x in [fx_guard[b]] if x is not None]
        lx1 = E("sync", lambda e, tt=tt, b=b: e.dma_start(out=x1b[b][:], in_=x1_d[tt * 128:(tt + 1) * 128, :]), waits=gw, dma=f"fx{b}")
        ls1 = E("scalar", lambda e, tt=tt, b=b: e.dma_start(out=s1b[b][:], in_=s1_d[tt * 128:(tt + 1) * 128, :]), waits=gw, dma=f"fs{b}")
        a1 = E("scalar", lambda e, tt=tt: e.activation(sg1[:], g1acc[:, tt * DM:(tt + 1) * DM], AF.Silu), waits=acc_guard + [x for x in [ost_guard[0]] if x is not None])
        a1d = ctx.flush("scalar")
        o1 = E("vector", lambda e, b=b: e.tensor_tensor(outb[:], sg1[:], s1b[b][:], ALU.mult), waits=[a1d, ls1])
        o2 = E("vector", lambda e, b=b: e.tensor_tensor(outb[:], outb[:], x1b[b][:], ALU.add), waits=[o1, lx1])
        o2d = ctx.flush("vector")
        fx_guard[b] = o2d
        so = E("gpsimd", lambda e, tt=tt: e.dma_start(out=out_o[tt * 128:(tt + 1) * 128, :], in_=outb[:]), waits=[o2d] + [x for x in [ost_guard[0]] if x is not None], dma="ost")
        ost_guard[0] = so
    return


def host_prep_l2(inputs, r1_results):
    x = np.asarray(inputs["L0_attn_x"])
    wout = np.ascontiguousarray(np.asarray(inputs["L0_attn_W_out"]).astype(np.float32))
    bout = np.asarray(inputs["L0_attn_b_out"]).astype(np.float32)
    bout128 = np.ascontiguousarray(np.broadcast_to(bout[None, :], (128, 1024)))
    ident = np.eye(128, dtype=np.float32)
    maps = []
    for c in range(NCORES):
        b, r = c // 4, c % 4
        grp = range(b * 4, b * 4 + 4)
        K_all = np.concatenate([np.asarray(r1_results[j]["K_own"]) for j in grp], axis=0)
        V_all = np.concatenate([np.asarray(r1_results[j]["V_own"]) for j in grp], axis=0)
        maps.append({
            "QP": np.asarray(r1_results[c]["QP_own"]),
            "K_all": np.ascontiguousarray(K_all),
            "V_all": np.ascontiguousarray(V_all),
            "g0": np.asarray(r1_results[c]["g0"]),
            "x": np.ascontiguousarray(x[b, r * T:(r + 1) * T]).astype(np.float32),
            "wout": wout, "bout128": bout128, "ident": ident,
        })
    return maps




def host_prep_l3a(inputs, i, r_prev):
    in_w = np.asarray(inputs["L1_attn_in_proj_w"][i]).astype(np.float32)
    in_wT_t = np.ascontiguousarray(in_w.T.reshape(8, 128, 32, 128).transpose(2, 0, 1, 3))
    return [{"m_fm": np.asarray(r_prev[c]["m_fm"]), "in_wT_t": in_wT_t} for c in range(NCORES)]


def host_prep_l3b(inputs, i, r3a):
    cw = np.asarray(inputs["L1_attn_conv_w"][i]).astype(np.float32)
    cb = np.asarray(inputs["L1_attn_conv_b"][i]).astype(np.float32)
    xpw = np.asarray(inputs["L1_attn_x_proj_w"][i]).astype(np.float32)
    dtw = np.asarray(inputs["L1_attn_dt_proj_w"][i]).astype(np.float32)
    dtb = np.asarray(inputs["L1_attn_dt_proj_b"][i]).astype(np.float32)
    alog = np.asarray(inputs["L1_attn_A_log"][i]).astype(np.float32)
    dskip = np.asarray(inputs["L1_attn_D_skip"][i]).astype(np.float32)
    com = {
        "cw_r": np.ascontiguousarray(cw.reshape(16, 128, 4)),
        "cb_r": np.ascontiguousarray(cb.reshape(16, 128).T),
        "xp_wT_t": np.ascontiguousarray(xpw.T.reshape(16, 128, 96)),
        "dt_wT_t": np.ascontiguousarray(dtw.T.reshape(64, 16, 128).transpose(1, 0, 2)),
        "dtb_r": np.ascontiguousarray(dtb.reshape(16, 128).T),
        "alog_r": np.ascontiguousarray(alog.reshape(16, 128, 16)),
        "dskip_r": np.ascontiguousarray(dskip.reshape(16, 128).T),
        "ident": np.eye(128, dtype=np.float32),
    }
    maps = []
    for c in range(NCORES):
        xz = np.asarray(r3a[c]["xz"])
        if c % 4 == 0:
            halo = np.zeros((2048, 3), dtype=BF)
        else:
            halo = np.ascontiguousarray(np.asarray(r3a[c - 1]["xz"])[0:2048, T - 3:T])
        m = dict(com)
        m["xz"] = xz
        m["halo"] = halo
        maps.append(m)
    return maps


def host_prep_l3c(inputs, i, r3a, r3b, m_tok_list):
    outw = np.asarray(inputs["L1_attn_out_proj_w"][i]).astype(np.float32)
    lnw = np.asarray(inputs["L1_attn_ln_w"][i]).astype(np.float32)
    lnb = np.asarray(inputs["L1_attn_ln_b"][i]).astype(np.float32)
    alog = np.asarray(inputs["L1_attn_A_log"][i]).astype(np.float32)
    com = {
        "out_wT_t": np.ascontiguousarray(outw.T.reshape(16, 128, DM)),
        "lnw128": np.ascontiguousarray(np.broadcast_to(lnw[None, :], (128, DM))),
        "lnb128": np.ascontiguousarray(np.broadcast_to(lnb[None, :], (128, DM))),
        "alog_r": np.ascontiguousarray(alog.reshape(16, 128, 16)),
        "ident": np.eye(128, dtype=np.float32),
    }
    maps = []
    for c in range(NCORES):
        b, r = c // 4, c % 4
        grp = range(b * 4, b * 4 + 4)
        s_all = np.stack([np.asarray(r3b[j]["s_sum"]) for j in grp], 0)
        p_all = np.stack([np.asarray(r3b[j]["p_sum"]) for j in grp], 0)
        oh = np.zeros((128, 4), np.float32)
        oh[:, r] = 1.0
        m = dict(com)
        m.update({
            "y_local": np.asarray(r3b[c]["y_local"]),
            "cumd": np.asarray(r3b[c]["cumd"]),
            "bc": np.asarray(r3b[c]["bc"]),
            "z": np.ascontiguousarray(np.asarray(r3a[c]["xz"])[2048:4096, :]),
            "s_all": np.ascontiguousarray(s_all),
            "p_all": np.ascontiguousarray(p_all),
            "ohsel": oh,
            "m_tok": np.asarray(m_tok_list[c]),
        })
        maps.append(m)
    return maps


def host_prep_l4(inputs, r2, s1_list):
    w1 = np.asarray(inputs["L1_ffn_w1"]).astype(np.float32)
    w2 = np.asarray(inputs["L1_ffn_w2"]).astype(np.float32)
    b1 = np.asarray(inputs["L1_ffn_b1"]).astype(np.float32)
    com = {
        "router_w": np.ascontiguousarray(np.asarray(inputs["L1_ffn_router_w"]).astype(np.float32)),
        "w1_t": np.ascontiguousarray(w1.reshape(8, 8, 128, 32, 128).transpose(0, 3, 1, 2, 4)),
        "b1_r": np.ascontiguousarray(b1.reshape(8, 32, 128).transpose(0, 2, 1)),
        "w2_t": np.ascontiguousarray(w2.reshape(8, 32, 128, DM)),
        "b2": np.ascontiguousarray(np.asarray(inputs["L1_ffn_b2"]).astype(np.float32)),
        "gamma": np.ascontiguousarray(np.asarray(inputs["L1_ffn_gamma"]).astype(np.float32)),
        "beta": np.ascontiguousarray(np.asarray(inputs["L1_ffn_beta"]).astype(np.float32)),
        "ident": np.eye(128, dtype=np.float32),
    }
    maps = []
    for c in range(NCORES):
        m = dict(com)
        m.update({
            "xn1fm": np.asarray(r2[c]["xn1fm"]),
            "xn1": np.asarray(r2[c]["xn1"]),
            "x1": np.asarray(r2[c]["x1"]),
            "s1": np.asarray(s1_list[c]),
        })
        maps.append(m)
    return maps





RG4 = [[0, 1, 2, 3], [4, 5, 6, 7]]


def build_fused():
    ctx = Ctx("fused")
    nc = ctx.nc
    p = ctx.start()
    ctx.p = p
    ctx.mkflush()
    E = p.emit

    D = {}
    def ext(name, shape, dtype=F32):
        D[name] = nc.dram_tensor(name, list(shape), dtype, kind="ExternalInput")
    def itn(name, shape, dtype=F32):
        D[name] = nc.dram_tensor(name, list(shape), dtype, kind="Internal")
    ext("x", [T, DM]); ext("wqkv", [DM, 3072]); ext("bqkv_r", [128, 24]); ext("pe2", [128, T])
    ext("bv128", [128, DM]); ext("gate_wT", [DM, 8])
    ext("wg_t", [8, 11, 8, 128, 128]); ext("wu_t", [8, 11, 8, 128, 128]); ext("wd_t", [8, 11, 128, DM])
    ext("swg_t", [22, 8, 128, 128]); ext("swu_t", [22, 8, 128, 128]); ext("swd_t", [22, 128, DM])
    ext("ident", [128, 128]); ext("wout", [DM, DM]); ext("bout128", [128, DM])
    ext("in_wT_t4", [4, 32, NDT, 128, 128])
    ext("cw_r4", [4, 16, 128, 4]); ext("cb_r4", [4, 128, 16]); ext("xp_wT_t4", [4, 16, 128, 96])
    ext("dt_wT_t4", [4, 16, 64, 128]); ext("dtb_r4", [4, 128, 16]); ext("alog_r4", [4, 16, 128, 16])
    ext("dskip_r4", [4, 128, 16]); ext("out_wT_t4", [4, 16, 128, DM])
    ext("lnw128_4", [4, 128, DM]); ext("lnb128_4", [4, 128, DM])
    ext("ohsel", [128, 4]); ext("ohprev", [128, 4])
    ext("router_w", [DM, 8]); ext("w1_t", [8, 32, NDT, 128, 128]); ext("b1_r", [8, 128, 32])
    ext("w2_t", [8, 32, 128, DM]); ext("b2", [8, DM]); ext("gamma", [8, DM]); ext("beta", [8, DM])
    itn("K_own", [DM, T], BF16); itn("V_own", [T, DM], BF16); itn("QP_own", [NH, 65, T], BF16)
    itn("g0", [T, DM]); itn("K_all", [4 * DM, T], BF16); itn("V_all", [S, DM], BF16)
    itn("x1", [T, DM]); itn("xn1", [T, DM]); itn("xn1fm", [128, NDT * T], BF16)
    itn("xz", [4096, T], BF16); itn("xc", [2048, T], BF16); itn("cumd", [2048, T])
    itn("bc", [32, T], BF16); itn("s_sum", [2048, 16]); itn("p_sum", [2048, 16])
    itn("y_local", [2048, T], BF16); itn("gated", [2048, T], BF16)
    itn("tail_own", [2048, 3], BF16); itn("tails_ag", [4 * 2048, 3], BF16)
    itn("s_ag", [4 * 2048, 16]); itn("p_ag", [4 * 2048, 16])
    itn("mA", [T, DM]); itn("mB", [T, DM])
    itn("mA_fm", [128, NDT * T], BF16); itn("mB_fm", [128, NDT * T], BF16)
    D["out"] = nc.dram_tensor("out", [T, DM], F32, kind="ExternalOutput")
    ctx.D = D

    ctx.phase_open("p1_", "")
    D["QP"] = D["QP_own"]
    phase1(ctx)
    ctx.phase_close()
    bar = p.all_barriers()
    E("gpsimd", lambda e: e.collective_compute("AllGather", ALU.bypass, replica_groups=RG4,
                                               ins=[D["K_own"].ap().opt()], outs=[D["K_all"].ap().opt()]), waits=bar)
    E("gpsimd", lambda e: e.collective_compute("AllGather", ALU.bypass, replica_groups=RG4,
                                               ins=[D["V_own"].ap().opt()], outs=[D["V_all"].ap().opt()]))
    ctx.phase_open("p2_", "")
    phase2(ctx)
    ctx.phase_close()

    cur_fm, cur_tok = "xn1fm", "xn1"
    for i in range(4):
        nxt_fm, nxt_tok = ("mA_fm", "mA") if i % 2 == 0 else ("mB_fm", "mB")
        D["m_fm"] = D[cur_fm]
        D["m_tok"] = D[cur_tok]
        D["m_next"] = D[nxt_tok]
        D["m_next_fm"] = D[nxt_fm]
        ctx.phase_open(f"a{i}_", "")
        phase3a(ctx, i)
        ctx.phase_close()
        bar = p.all_barriers()
        tcp = E("gpsimd", lambda e: e.dma_start(out=D["tail_own"][:, :], in_=D["xz"][0:2048, T - 3:T]), waits=bar, dma="tailcp")
        E("gpsimd", lambda e: e.collective_compute("AllGather", ALU.bypass, replica_groups=RG4,
                                                   ins=[D["tail_own"].ap().opt()], outs=[D["tails_ag"].ap().opt()]), waits=[tcp])
        ctx.phase_open(f"b{i}_", "")
        phase3b(ctx, i)
        ctx.phase_close()
        bar = p.all_barriers()
        E("gpsimd", lambda e: e.collective_compute("AllGather", ALU.bypass, replica_groups=RG4,
                                                   ins=[D["s_sum"].ap().opt()], outs=[D["s_ag"].ap().opt()]), waits=bar)
        E("gpsimd", lambda e: e.collective_compute("AllGather", ALU.bypass, replica_groups=RG4,
                                                   ins=[D["p_sum"].ap().opt()], outs=[D["p_ag"].ap().opt()]))
        ctx.phase_open(f"c{i}_", "")
        phase3c(ctx, i)
        ctx.phase_close()
        cur_fm, cur_tok = nxt_fm, nxt_tok

    D["s1"] = D[cur_tok]
    ctx.phase_open("p4_", "")
    phase4(ctx)
    ctx.phase_close()
    return ctx.done(p.all_barriers())


def host_prep_fused(inputs):
    x = np.asarray(inputs["L0_attn_x"])
    wqkv = np.ascontiguousarray(inputs["L0_attn_W_qkv"]).astype(np.float32)
    bqkv = np.asarray(inputs["L0_attn_b_qkv"]).astype(np.float32)
    pe = rel_pos_enc(S, HD)
    wg = np.asarray(inputs["L0_ffn_wg"]).astype(np.float32)
    wu = np.asarray(inputs["L0_ffn_wu"]).astype(np.float32)
    wd = np.asarray(inputs["L0_ffn_wd"]).astype(np.float32)
    swg = np.asarray(inputs["L0_ffn_swg"]).astype(np.float32)
    swu = np.asarray(inputs["L0_ffn_swu"]).astype(np.float32)
    swd = np.asarray(inputs["L0_ffn_swd"]).astype(np.float32)
    in_w = np.asarray(inputs["L1_attn_in_proj_w"]).astype(np.float32)
    cw = np.asarray(inputs["L1_attn_conv_w"]).astype(np.float32)
    cb = np.asarray(inputs["L1_attn_conv_b"]).astype(np.float32)
    xpw = np.asarray(inputs["L1_attn_x_proj_w"]).astype(np.float32)
    dtw = np.asarray(inputs["L1_attn_dt_proj_w"]).astype(np.float32)
    dtb = np.asarray(inputs["L1_attn_dt_proj_b"]).astype(np.float32)
    alog = np.asarray(inputs["L1_attn_A_log"]).astype(np.float32)
    dskip = np.asarray(inputs["L1_attn_D_skip"]).astype(np.float32)
    outw = np.asarray(inputs["L1_attn_out_proj_w"]).astype(np.float32)
    lnw = np.asarray(inputs["L1_attn_ln_w"]).astype(np.float32)
    lnb = np.asarray(inputs["L1_attn_ln_b"]).astype(np.float32)
    w1 = np.asarray(inputs["L1_ffn_w1"]).astype(np.float32)
    b1 = np.asarray(inputs["L1_ffn_b1"]).astype(np.float32)
    w2 = np.asarray(inputs["L1_ffn_w2"]).astype(np.float32)
    com = {
        "wqkv": wqkv,
        "bqkv_r": np.ascontiguousarray(bqkv.reshape(24, 128).T),
        "bv128": np.ascontiguousarray(np.broadcast_to(bqkv[2048:3072][None, :], (128, 1024))),
        "gate_wT": np.ascontiguousarray(np.asarray(inputs["L0_ffn_gate_w"]).astype(np.float32).T),
        "wg_t": np.ascontiguousarray(wg.reshape(8, 8, 128, 11, 128).transpose(0, 3, 1, 2, 4)),
        "wu_t": np.ascontiguousarray(wu.reshape(8, 8, 128, 11, 128).transpose(0, 3, 1, 2, 4)),
        "wd_t": np.ascontiguousarray(wd.reshape(8, 11, 128, 1024)),
        "swg_t": np.ascontiguousarray(swg.reshape(8, 128, 22, 128).transpose(2, 0, 1, 3)),
        "swu_t": np.ascontiguousarray(swu.reshape(8, 128, 22, 128).transpose(2, 0, 1, 3)),
        "swd_t": np.ascontiguousarray(swd.reshape(22, 128, 1024)),
        "ident": np.eye(128, dtype=np.float32),
        "wout": np.ascontiguousarray(np.asarray(inputs["L0_attn_W_out"]).astype(np.float32)),
        "bout128": np.ascontiguousarray(np.broadcast_to(np.asarray(inputs["L0_attn_b_out"]).astype(np.float32)[None, :], (128, 1024))),
        "in_wT_t4": np.ascontiguousarray(np.stack([in_w[i].T.reshape(8, 128, 32, 128).transpose(2, 0, 1, 3) for i in range(4)])),
        "cw_r4": np.ascontiguousarray(cw.reshape(4, 16, 128, 4)),
        "cb_r4": np.ascontiguousarray(np.stack([cb[i].reshape(16, 128).T for i in range(4)])),
        "xp_wT_t4": np.ascontiguousarray(np.stack([xpw[i].T.reshape(16, 128, 96) for i in range(4)])),
        "dt_wT_t4": np.ascontiguousarray(np.stack([dtw[i].T.reshape(64, 16, 128).transpose(1, 0, 2) for i in range(4)])),
        "dtb_r4": np.ascontiguousarray(np.stack([dtb[i].reshape(16, 128).T for i in range(4)])),
        "alog_r4": np.ascontiguousarray(alog.reshape(4, 16, 128, 16)),
        "dskip_r4": np.ascontiguousarray(np.stack([dskip[i].reshape(16, 128).T for i in range(4)])),
        "out_wT_t4": np.ascontiguousarray(np.stack([outw[i].T.reshape(16, 128, DM) for i in range(4)])),
        "lnw128_4": np.ascontiguousarray(np.stack([np.broadcast_to(lnw[i][None, :], (128, DM)) for i in range(4)])),
        "lnb128_4": np.ascontiguousarray(np.stack([np.broadcast_to(lnb[i][None, :], (128, DM)) for i in range(4)])),
        "router_w": np.ascontiguousarray(np.asarray(inputs["L1_ffn_router_w"]).astype(np.float32)),
        "w1_t": np.ascontiguousarray(w1.reshape(8, 8, 128, 32, 128).transpose(0, 3, 1, 2, 4)),
        "b1_r": np.ascontiguousarray(b1.reshape(8, 32, 128).transpose(0, 2, 1)),
        "w2_t": np.ascontiguousarray(w2.reshape(8, 32, 128, DM)),
        "b2": np.ascontiguousarray(np.asarray(inputs["L1_ffn_b2"]).astype(np.float32)),
        "gamma": np.ascontiguousarray(np.asarray(inputs["L1_ffn_gamma"]).astype(np.float32)),
        "beta": np.ascontiguousarray(np.asarray(inputs["L1_ffn_beta"]).astype(np.float32)),
    }
    maps = []
    for c in range(NCORES):
        b, r = c // 4, c % 4
        sl = slice(r * T, (r + 1) * T)
        pe_own = pe[sl].T
        oh = np.zeros((128, 4), np.float32); oh[:, r] = 1.0
        ohp = np.zeros((128, 4), np.float32)
        if r > 0:
            ohp[:, r - 1] = 1.0
        m = dict(com)
        m.update({
            "x": np.ascontiguousarray(x[b, sl]).astype(np.float32),
            "pe2": np.ascontiguousarray(np.concatenate([pe_own, pe_own], 0)),
            "ohsel": oh, "ohprev": ohp,
        })
        maps.append(m)
    return maps


def kernel(**inputs):
    cores = list(range(NCORES))
    nc = _cached("fused", build_fused)
    res = run_bass_kernel_spmd(nc, host_prep_fused(inputs), core_ids=cores).results
    out = np.zeros((2, S, DM), np.float32)
    for c in range(NCORES):
        b, r = c // 4, c % 4
        out[b, r * T:(r + 1) * T] = np.asarray(res[c]["out"])
    return out



# version 03
# speedup vs baseline: 1.0363x; 1.0363x over previous
import numpy as np
import ml_dtypes
import concourse.bass as bass
import concourse.mybir as mybir
from concourse.bass_utils import run_bass_kernel_spmd
from contextlib import ExitStack

F32 = mybir.dt.float32
BF16 = mybir.dt.bfloat16
AF = mybir.ActivationFunctionType
ALU = mybir.AluOpType
AX = mybir.AxisListType
BF = ml_dtypes.bfloat16

NCORES = 8
T = 2048
NT = T // 128
DM = 1024
NDT = DM // 128
NH = 16
HD = 64
S = 8192
ENGINES = ["sync", "vector", "scalar", "tensor", "gpsimd"]


class Prog:

    def __init__(self, nc):
        self.nc = nc
        self.stream = {e: [] for e in ENGINES}
        self.cnt = {e: 0 for e in ENGINES}
        self.dcnt = {}
        self.waitmax = {e: {} for e in ENGINES}
        self.default_waits = []
        self.prefix = ""

    def emit(self, eng, fn, waits=(), dma=None, inc=None):
        waits = list(waits) + self.default_waits
        if dma is not None:
            key = "D_" + self.prefix + dma
            self.dcnt[key] = self.dcnt.get(key, 0) + (inc if inc is not None else 16)
            tok = (key, self.dcnt[key])
        else:
            self.cnt[eng] += (inc if inc is not None else 1)
            tok = (eng, self.cnt[eng])
        wm = self.waitmax[eng]
        for (we, wc) in waits:
            if wm.get(we, 0) >= wc:
                continue
            wm[we] = wc
            self.stream[eng].append(("wait", we, wc))
        self.stream[eng].append(("op", fn, tok[0], tok[1], (16 if dma is not None else (inc if inc is not None else 1))))
        return tok

    def barrier(self, eng):
        return (eng, self.cnt[eng])

    def dma_total(self, name):
        key = "D_" + self.prefix + name
        return (key, self.dcnt.get(key, 0))

    def all_barriers(self):
        out = [(e, self.cnt[e]) for e in ENGINES if self.cnt[e] > 0]
        out += [(k, v) for k, v in self.dcnt.items()]
        return out

    def finish(self, final_tokens):
        for (we, wc) in final_tokens:
            self.stream["sync"].append(("wait", we, wc))

    def build(self, block, sems_engine):
        nc = self.nc
        sems = dict(sems_engine)
        self._dma_sem_guards = []
        for key in self.dcnt:
            g = nc.semaphore(key)
            sems[key] = g.__enter__()
            self._dma_sem_guards.append(g)
        for e in ENGINES:
            ops = self.stream[e]
            if not ops:
                continue
            engobj = getattr(nc, e)

            def make(ops=ops, engobj=engobj):
                def body(_):
                    for item in ops:
                        if item[0] == "wait":
                            _, we, wc = item
                            engobj.wait_ge(sems[we], wc)
                        else:
                            _, fn, key, c, amt = item
                            instr = fn(engobj)
                            if instr is None:
                                instr = engobj.nop()
                            instr.then_inc(sems[key], amt)
                return body
            getattr(block, e)(make())


class Ctx:

    def __init__(self, name):
        self.nc = bass.Bass("TRN2", target_bir_lowering=False)
        self.stack = ExitStack()
        self.name = name
        self.cur = None
        self.D = None
        self.pfx = ""

    def start(self):
        nc = self.nc
        self.sems = {}
        for e in ENGINES:
            self.sems[e] = self.stack.enter_context(nc.semaphore("S_" + e))
        self.block = self.stack.enter_context(nc.Block())
        self.p = Prog(nc)
        return self.p

    def sbuf(self, name, shape, dtype, stack=None, side=None):
        kw = {"side": side} if side else {}
        tgt = stack or self.cur or self.stack
        if tgt is not self.stack and not side:
            kw = {"side": "right"}
        return tgt.enter_context(self.nc.sbuf_tensor(self.pfx + name, list(shape), dtype, **kw))

    def scope(self):
        return ExitStack()

    def mkflush(self):
        self.flushv = self.sbuf("_flushv", [128, 1], F32)
        self.flushs = self.sbuf("_flushs", [128, 1], F32)
        self.flushs2 = self.sbuf("_flushs2", [128, 1], F32)

    def flush(self, eng):
        return self.p.emit(eng, lambda e: e.drain())

    def psum(self, name, shape, dtype=F32):
        return (self.cur or self.stack).enter_context(self.nc.psum_tensor(self.pfx + name, list(shape), dtype))

    def din(self, name, shape, dtype=F32):
        if self.D is not None:
            return self.D[name]
        return self.nc.dram_tensor(name, list(shape), dtype, kind="ExternalInput")

    def dout(self, name, shape, dtype=F32):
        if self.D is not None:
            return self.D[name]
        return self.nc.dram_tensor(name, list(shape), dtype, kind="ExternalOutput")

    def dint(self, name, shape, dtype=F32):
        return self.nc.dram_tensor(name, list(shape), dtype, kind="Internal")

    def phase_open(self, pfx, sem_pfx=None):
        self.cur = ExitStack()
        self.pfx = pfx
        self.p.prefix = sem_pfx if sem_pfx is not None else pfx
        self.p.default_waits = self.p.all_barriers()

    def phase_close(self):
        self.cur.close()
        self.cur = None
        self.pfx = ""
        self.p.prefix = ""

    def done(self, final_tokens):
        self.p.finish(final_tokens)
        self.p.build(self.block, self.sems)
        self.stack.close()
        return self.nc



def ln_tile(ctx, E, xt, out_bf, tmp, waits, eps=1e-5, nf=1024.0):
    t1 = E("vector", lambda e: e.tensor_scalar(tmp["junk"][:], xt[:], 1.0 / nf, 0.0, ALU.mult, ALU.add, accum_out=tmp["mu"][:]), waits=waits)
    t2 = E("vector", lambda e: e.tensor_scalar(tmp["negmu"][:], tmp["mu"][:], -1.0, None, ALU.mult), waits=[t1])
    t3 = E("scalar", lambda e: e.activation(tmp["sq"][:], xt[:], AF.Square, bias=tmp["negmu"][:], accum_out=tmp["ssq"][:]), waits=[t2])
    t4 = E("vector", lambda e: e.tensor_scalar(tmp["var"][:], tmp["ssq"][:], 1.0 / nf, tmp["eps_t"][:], ALU.mult, ALU.add), waits=[t3])
    t5 = E("scalar", lambda e: e.activation(tmp["var"][:], tmp["var"][:], AF.Ln), waits=[t4])
    t6 = E("scalar", lambda e: e.activation(tmp["rstd"][:], tmp["var"][:], AF.Exp, scale=-0.5), waits=[t5])
    t7 = E("vector", lambda e: e.tensor_scalar(out_bf[:], xt[:], tmp["mu"][:], tmp["rstd"][:], ALU.subtract, ALU.mult), waits=[t6])
    return t7


def transpose_128(ctx, E, p, src_bf, dst_bf, ps_tp, identb, waits):
    t1 = E("tensor", lambda e: e.transpose(ps_tp[:], src_bf[:], identb[:]), waits=waits)
    t2 = E("vector", lambda e: e.tensor_copy(dst_bf[:], ps_tp[:]), waits=[t1])
    return t2



def rel_pos_enc(seq_len, dim):
    positions = np.arange(seq_len, dtype=np.float64)[:, None]
    div_term = np.exp(np.arange(0, dim, 2, dtype=np.float32).astype(np.float64) * -(np.log(10000.0) / dim))
    pe = np.zeros((seq_len, dim), dtype=np.float32)
    pe[:, 0::2] = np.sin(positions * div_term).astype(np.float32)
    pe[:, 1::2] = np.cos(positions * div_term).astype(np.float32)
    return pe


_BUILD_CACHE = {}


def _cached(name, fn):
    if name not in _BUILD_CACHE:
        _BUILD_CACHE[name] = fn()
    return _BUILD_CACHE[name]



DBG_NOQ = False
DBG_SER = False


def phase1(ctx):
    p = ctx.p
    E = p.emit
    x_d = ctx.din("x", [T, DM])
    wqkv_d = ctx.din("wqkv", [DM, 3072])
    bqkv_d = ctx.din("bqkv_r", [128, 24])
    pe2_d = ctx.din("pe2", [128, T])
    bv_d = ctx.din("bv128", [128, DM])
    gwt_d = ctx.din("gate_wT", [DM, 8])
    wg_d = ctx.din("wg_t", [8, 11, 8, 128, 128])
    wu_d = ctx.din("wu_t", [8, 11, 8, 128, 128])
    wd_d = ctx.din("wd_t", [8, 11, 128, DM])
    swg_d = ctx.din("swg_t", [22, 8, 128, 128])
    swu_d = ctx.din("swu_t", [22, 8, 128, 128])
    swd_d = ctx.din("swd_t", [22, 128, DM])
    id_d = ctx.din("ident", [128, 128])

    K_o = ctx.dout("K_own", [DM, T], BF16)
    V_o = ctx.dout("V_own", [T, DM], BF16)
    QP_o = ctx.dout("QP_own", [NH, 65, T], BF16)
    g0_o = ctx.dout("g0", [T, DM])

    xn_fm = ctx.sbuf("xn_fm", [128, NDT * T], BF16)
    identt = ctx.sbuf("identt", [128, 128], F32)
    identb = ctx.sbuf("identb", [128, 128], BF16)
    ps_tp = ctx.psum("ps_tp", [128, 128], BF16)
    eps_t = ctx.sbuf("eps_t", [128, 1], F32)
    one_col = ctx.sbuf("one_col", [128, 1], F32)
    tmp = {k: ctx.sbuf("lntmp_" + k, [128, 1], F32) for k in ["mu", "negmu", "ssq", "var", "rstd"]}
    tmp["eps_t"] = eps_t

    li = E("sync", lambda e: e.dma_start(out=identt[:], in_=id_d[:]), dma="const")
    ci = E("vector", lambda e: e.tensor_copy(identb[:], identt[:]), waits=[li])
    mz1 = E("vector", lambda e: e.memset(eps_t[:], 1e-5))
    mz2 = E("vector", lambda e: e.memset(one_col[:], 1.0))

    gwt = ctx.sbuf("gwt", [128, NDT * 8], F32)
    lg0 = E("sync", lambda e: e.dma_start(out=gwt.ap().rearrange("p (a n) -> p a n", a=NDT), in_=gwt_d.ap().rearrange("(a p) n -> p a n", p=128)), dma="const")
    gates = ctx.sbuf("gates", [128, NT * 8], F32)
    lgt = ctx.sbuf("lgt", [128, 8], F32)
    m8 = ctx.sbuf("m8", [128, 8], F32)
    d21 = ctx.sbuf("d21", [128, 1], F32)
    ez = ctx.sbuf("ez", [128, 1], F32)
    rz = ctx.sbuf("rz", [128, 1], F32)
    pex = ctx.sbuf("pex", [128, 8], F32)
    msk = ctx.sbuf("msk", [128, 8], F32)
    ps_g = ctx.psum("ps_g", [128, 8], F32)
    ps_misc = ctx.psum("ps_misc", [128, 128], F32)

    ln_scope = ctx.scope()
    tmp["sq"] = ctx.sbuf("lntmp_sq", [128, DM], F32, ln_scope, side="right")
    tmp["junk"] = ctx.sbuf("lntmp_junk", [128, DM], F32, ln_scope, side="right")
    xbuf = [ctx.sbuf(f"xbuf{i}", [128, DM], F32, ln_scope, side="right") for i in range(2)]
    xn32 = [ctx.sbuf(f"xn32{i}", [128, DM], F32, ln_scope, side="right") for i in range(2)]
    xnb = [ctx.sbuf(f"xnb{i}", [128, DM], BF16, ln_scope, side="right") for i in range(2)]
    x32t = ctx.sbuf("x32t", [128, NDT * 128], F32, ln_scope, side="right")
    tp_done = []
    tp32_done = []
    prev_use = {}
    gprev = ()
    for tt in range(NT):
        b = tt % 2
        ldw = [prev_use[b]] if b in prev_use else []
        ld = E("sync", lambda e, tt=tt, b=b: e.dma_start(out=xbuf[b][:], in_=x_d[tt * 128:(tt + 1) * 128, :]), waits=ldw, dma="xb%d" % b)
        lnt = ln_tile(ctx, E, xbuf[b], xn32[b], tmp, [ld, mz1])
        cb16 = E("vector", lambda e, b=b: e.tensor_copy(xnb[b][:], xn32[b][:]), waits=[lnt])
        last = cb16
        for dt in range(NDT):
            t1 = E("tensor", lambda e, b=b, dt=dt: e.transpose(ps_tp[:], xnb[b][:, dt * 128:(dt + 1) * 128], identb[:]), waits=[cb16, ci] + ([tp_done[-1]] if tp_done else []))
            t1d = ctx.flush("tensor")
            t2 = E("vector", lambda e, dt=dt, tt=tt: e.tensor_copy(xn_fm[:, dt * T + tt * 128: dt * T + (tt + 1) * 128], ps_tp[:]), waits=[t1d])
            tp_done.append(t2)
            t3 = E("tensor", lambda e, b=b, dt=dt: e.transpose(ps_misc[:], xn32[b][:, dt * 128:(dt + 1) * 128], identt[:]), waits=[lnt, li] + ([tp32_done[-1]] if tp32_done else []))
            t3d = ctx.flush("tensor")
            t4 = E("vector", lambda e, dt=dt: e.tensor_copy(x32t[:, dt * 128:(dt + 1) * 128], ps_misc[:]), waits=[t3d])
            tp32_done.append(t4)
            last = t4
        mms = None
        for dt in range(NDT):
            mms = E("tensor", lambda e, dt=dt: e.matmul(
                ps_g[:], x32t[:, dt * 128:(dt + 1) * 128], gwt[:, dt * 8:(dt + 1) * 8],
                start=(dt == 0), stop=(dt == NDT - 1)),
                waits=([tp32_done[-1], lg0] + list(gprev) if dt == 0 else []))
        mmsd = ctx.flush("tensor")
        c1 = E("vector", lambda e: e.tensor_copy(lgt[:], ps_g[:]), waits=[mmsd])
        c2 = E("vector", lambda e: e.max(m8[:], lgt[:]), waits=[c1])
        c3 = E("vector", lambda e: e.tensor_scalar(d21[:], m8[:, 1:2], m8[:, 0:1], None, ALU.subtract), waits=[c2])
        c4 = E("scalar", lambda e: e.activation(ez[:], d21[:], AF.Exp), waits=[c3])
        c5 = E("vector", lambda e: e.tensor_tensor(ez[:], ez[:], one_col[:], ALU.add), waits=[c4, mz2])
        c6 = E("vector", lambda e: e.reciprocal(rz[:], ez[:]), waits=[c5])
        c7 = E("vector", lambda e: e.tensor_scalar(pex[:], lgt[:], m8[:, 0:1], None, ALU.subtract), waits=[c2])
        c8 = E("scalar", lambda e: e.activation(pex[:], pex[:], AF.Exp), waits=[c7])
        c9 = E("vector", lambda e: e.tensor_scalar(msk[:], lgt[:], m8[:, 1:2], None, ALU.is_ge), waits=[c8])
        c10 = E("vector", lambda e: e.tensor_tensor(pex[:], pex[:], msk[:], ALU.mult), waits=[c9])
        c11 = E("vector", lambda e, tt=tt: e.tensor_scalar(gates[:, tt * 8:(tt + 1) * 8], pex[:], rz[:], None, ALU.mult), waits=[c10, c6])
        gprev = (c11, c2)
        prev_use[b] = last
    XN = p.barrier("vector")
    XNT = p.barrier("tensor")
    GATES = p.barrier("vector")
    LNB = p.all_barriers()
    ln_scope.close()

    qkv_scope = ctx.scope()
    pe2 = ctx.sbuf("pe2_s", [128, T], F32, qkv_scope, side="right")
    bv = ctx.sbuf("bv", [128, DM], F32, qkv_scope, side="right")
    bq = ctx.sbuf("bq", [128, 24], F32, qkv_scope, side="right")
    lb = E("sync", lambda e: e.dma_start(out=bq[:], in_=bqkv_d[:]), dma="const")
    lpe = E("sync", lambda e: e.dma_start(out=pe2[:], in_=pe2_d[:]), waits=LNB, dma="const")
    lbv = E("sync", lambda e: e.dma_start(out=bv[:], in_=bv_d[:]), waits=LNB, dma="const")
    wq_f = [ctx.sbuf(f"wq_f{i}", [128, 3072], F32, qkv_scope, side="right") for i in range(4)]
    wqkv_bf = [ctx.sbuf(f"wqkv_bf{i}", [128, 3072], BF16, qkv_scope, side="right") for i in range(NDT)]
    wq_cast = []
    prev_cast = {}
    for dt in range(NDT):
        b = dt % 4
        ldw = [prev_cast[b]] if b in prev_cast else []
        ld = E("sync", lambda e, dt=dt, b=b: e.dma_start(out=wq_f[b][:], in_=wqkv_d[dt * 128:(dt + 1) * 128, :]), waits=ldw + LNB, dma="wqf%d" % b)
        cs = E("vector", lambda e, dt=dt, b=b: e.tensor_copy(wqkv_bf[dt][:], wq_f[b][:]), waits=[ld])
        fl = ctx.flush("vector")
        prev_cast[b] = fl
        wq_cast.append(fl)

    ps_qk = [ctx.psum(f"ps_qk{i}", [128, 512], F32) for i in range(2)]
    kst = [ctx.sbuf(f"kst{i}", [128, 512], BF16, qkv_scope, side="right") for i in range(4)]
    k_stores = []
    q_stores = []
    pb = 0
    ps_free = [(), ()]
    for ot in range(16):
        for tc in range(4):
            pb ^= 1
            mms = None
            for dt in range(NDT):
                mms = E("tensor", lambda e, ot=ot, tc=tc, dt=dt, pb=pb: e.matmul(
                    ps_qk[pb][:], wqkv_bf[dt][:, ot * 128:(ot + 1) * 128],
                    xn_fm[:, dt * T + tc * 512: dt * T + (tc + 1) * 512],
                    start=(dt == 0), stop=(dt == NDT - 1)),
                    waits=(list(ps_free[pb]) + [XN, wq_cast[-1]] if dt == 0 else []))
            mms = ctx.flush("tensor")
            if ot < 8:
                h0, h1 = 2 * ot, 2 * ot + 1
                sb = tc % 2
                e1 = E("vector", lambda e, ot=ot, pb=pb, sb=sb: e.tensor_scalar(
                    kst[sb][:], ps_qk[pb][:], bq[:, ot:ot + 1], None, ALU.add),
                    waits=[mms, lb] + list(q_stores[-4:-2] if len(q_stores) >= 4 else []))
                fl = ctx.flush("vector")
                st1 = E("gpsimd", lambda e, h0=h0, tc=tc, sb=sb: e.dma_start(
                    out=QP_o[h0, 0:64, tc * 512:(tc + 1) * 512], in_=kst[sb][0:64, :]), waits=[fl], dma="qst%d" % sb) if not DBG_NOQ else ctx.flush("vector")
                st2 = E("gpsimd", lambda e, h1=h1, tc=tc, sb=sb: e.dma_start(
                    out=QP_o[h1, 0:64, tc * 512:(tc + 1) * 512], in_=kst[sb][64:128, :]), waits=[fl], dma="qst%d" % sb) if not DBG_NOQ else ctx.flush("vector")
                q_stores += [st1, st2]
                ps_free[pb] = (fl,)
            else:
                sb = 2 + tc % 2
                e1a = E("vector", lambda e, ot=ot, pb=pb, sb=sb: e.tensor_scalar(
                    kst[sb][:], ps_qk[pb][:], bq[:, ot:ot + 1], None, ALU.add),
                    waits=[mms, lb, lpe] + list(k_stores[-2:-1] if len(k_stores) >= 2 else []))
                e1 = E("vector", lambda e, tc=tc, sb=sb: e.tensor_tensor(
                    kst[sb][:], kst[sb][:], pe2[:, tc * 512:(tc + 1) * 512], ALU.add), waits=[e1a])
                fl = ctx.flush("vector")
                st = E("gpsimd", lambda e, ot=ot, tc=tc, sb=sb: e.dma_start(
                    out=K_o[(ot - 8) * 128:(ot - 7) * 128, tc * 512:(tc + 1) * 512], in_=kst[sb][:]), waits=[fl], dma="kst%d" % sb)
                k_stores.append(st)
                ps_free[pb] = (fl, st)
    KDONE = [p.dma_total("kst2"), p.dma_total("kst3")]

    vst = [ctx.sbuf(f"vst{i}", [128, 512], BF16, qkv_scope, side="right") for i in range(2)]
    v_stores = []
    ps_free = [(), ()]
    for tt in range(NT):
        for vc in range(2):
            pb ^= 1
            mms = None
            for dt in range(NDT):
                mms = E("tensor", lambda e, tt=tt, vc=vc, dt=dt, pb=pb: e.matmul(
                    ps_qk[pb][:], xn_fm[:, dt * T + tt * 128: dt * T + (tt + 1) * 128],
                    wqkv_bf[dt][:, 2048 + vc * 512: 2048 + (vc + 1) * 512],
                    start=(dt == 0), stop=(dt == NDT - 1)),
                    waits=(list(ps_free[pb]) if dt == 0 else []))
            mms = ctx.flush("tensor")
            sb = vc
            e1 = E("vector", lambda e, vc=vc, pb=pb, sb=sb: e.tensor_tensor(
                vst[sb][:], ps_qk[pb][:], bv[:, vc * 512:(vc + 1) * 512], ALU.add),
                waits=[mms, lbv] + (v_stores[-2:-1] if len(v_stores) >= 2 else []))
            fl = ctx.flush("vector")
            st = E("gpsimd", lambda e, tt=tt, vc=vc, sb=sb: e.dma_start(
                out=V_o[tt * 128:(tt + 1) * 128, vc * 512:(vc + 1) * 512], in_=vst[sb][:]), waits=[fl], dma="vst%d" % sb)
            v_stores.append(st)
            ps_free[pb] = (fl,)

    QDONE = [p.dma_total("qst0"), p.dma_total("qst1"), p.dma_total("vst0"), p.dma_total("vst1")]
    k0b = ctx.sbuf("k0b", [64, 128], BF16, qkv_scope, side="right")
    qtb = ctx.sbuf("qtb", [64, T], BF16, qkv_scope, side="right")
    ps_mrow = ctx.psum("ps_mrow", [1, 128], F32)
    mcol = ctx.sbuf("mcol", [128, 1], F32, qkv_scope, side="right")
    mrow_st = ctx.sbuf("mrow_st", [1, T], BF16, qkv_scope, side="right")
    qp_stores = []
    for h in range(NH):
        lk = E("scalar", lambda e, h=h: e.dma_start(out=k0b[:], in_=K_o[h * 64:(h + 1) * 64, 0:128]), waits=KDONE + QDONE, dma="k0b")
        lq = E("scalar", lambda e, h=h: e.dma_start(out=qtb[:], in_=QP_o[h, 0:64, :]), waits=QDONE + qp_stores[-1:], dma="qtb")
        prev = (lk, lq)
        for qt in range(NT):
            m1 = E("tensor", lambda e, h=h, qt=qt: e.matmul(
                ps_misc[:], qtb[:, qt * 128:(qt + 1) * 128], k0b[:], start=True, stop=True), waits=list(prev))
            m1d = ctx.flush("tensor")
            r1 = E("vector", lambda e: e.tensor_reduce(mcol[:], ps_misc[:], AX.X, ALU.max), waits=[m1d])
            r1d = ctx.flush("vector")
            t1 = E("tensor", lambda e: e.transpose(ps_mrow[:], mcol[:], identt[:]), waits=[r1d])
            t1d = ctx.flush("tensor")
            c1 = E("scalar", lambda e, h=h, qt=qt: e.copy(mrow_st[:, qt * 128:(qt + 1) * 128], ps_mrow[:]), waits=[t1d])
            fls = ctx.flush("scalar")
            prev = (fls,)
        qp_stores.append(E("gpsimd", lambda e, h=h: e.dma_start(out=QP_o[h, 64:65, :], in_=mrow_st[:]), waits=[prev[0]], dma="qpst"))
    QKVB = p.all_barriers()
    qkv_scope.close()


    ysum = ctx.sbuf("ysum", [128, NT * DM], F32)
    zs = E("vector", lambda e: e.memset(ysum[:], 0.0))
    wgb = [ctx.sbuf(f"wgb{i}", [128, 8 * 128], BF16) for i in range(2)]
    wub = [ctx.sbuf(f"wub{i}", [128, 8 * 128], BF16) for i in range(2)]
    wgf = [ctx.sbuf("wgf0", [128, 8 * 128], F32)] * 2
    wuf = [ctx.sbuf("wuf0", [128, 8 * 128], F32)] * 2
    wdb = ctx.sbuf("wdb", [128, 11 * DM], BF16)
    wdf = [ctx.sbuf(f"wdf{i}", [128, DM], F32) for i in range(2)]
    Pe = ctx.sbuf("Pe", [128, 11 * T], BF16)
    sg = ctx.sbuf("sg", [128, 512], F32)
    sg2 = ctx.sbuf("sg2", [128, 512], F32)
    ps_h = [ctx.psum(f"ps_h{i}", [128, 512], F32) for i in range(2)]

    def expert_block(tag, n_it, wg_src, wu_src, wd_src, gate_e, dep0):
        pg_dep = {0: ()}
        pu_dep = {1: ()}
        wprev = {0: None, 1: None}
        up_done = []
        wchain = [None]
        for it in range(n_it):
            b = it % 2
            wl = [wchain[0]] if wchain[0] is not None else []
            lw1 = E("scalar", lambda e, it=it, b=b: e.dma_start(out=wgf[b].ap().rearrange("p (a n) -> p a n", a=8), in_=wg_src(it).rearrange("a p n -> p a n")), waits=wl + list(dep0), dma="wgf")
            cw1_ = E("vector", lambda e, b=b: e.tensor_copy(wgb[b][:], wgf[b][:]), waits=[lw1])
            cw1 = ctx.flush("vector")
            lw2 = E("scalar", lambda e, it=it, b=b: e.dma_start(out=wuf[b].ap().rearrange("p (a n) -> p a n", a=8), in_=wu_src(it).rearrange("a p n -> p a n")), waits=[cw1], dma="wuf")
            cw2_ = E("vector", lambda e, b=b: e.tensor_copy(wub[b][:], wuf[b][:]), waits=[lw2])
            cw2 = ctx.flush("vector")
            wprev[b] = cw2
            for tc in range(4):
                pba = 0
                pbb = 1
                mg = None
                for dt in range(NDT):
                    mg = E("tensor", lambda e, it=it, tc=tc, dt=dt, b=b, pba=pba: e.matmul(
                        ps_h[pba][:], wgb[b][:, dt * 128:(dt + 1) * 128],
                        xn_fm[:, dt * T + tc * 512:dt * T + (tc + 1) * 512],
                        start=(dt == 0), stop=(dt == NDT - 1)),
                        waits=([cw1, XN] + list(pg_dep[pba]) if dt == 0 else []))
                mu_ = None
                for dt in range(NDT):
                    mu_ = E("tensor", lambda e, it=it, tc=tc, dt=dt, b=b, pbb=pbb: e.matmul(
                        ps_h[pbb][:], wub[b][:, dt * 128:(dt + 1) * 128],
                        xn_fm[:, dt * T + tc * 512:dt * T + (tc + 1) * 512],
                        start=(dt == 0), stop=(dt == NDT - 1)),
                        waits=([cw2] + list(pu_dep[pbb]) if dt == 0 else []))
                mgd = ctx.flush("tensor")
                a1_ = E("scalar", lambda e, pba=pba: e.activation(sg[:], ps_h[pba][:], AF.Silu), waits=[mgd])
                a1 = ctx.flush("scalar")
                mud = ctx.flush("tensor")
                a2_ = E("vector", lambda e, it=it, tc=tc, pbb=pbb: e.tensor_tensor(
                    Pe[:, it * T + tc * 512: it * T + (tc + 1) * 512], ps_h[pbb][:], sg[:], ALU.mult), waits=[mud, a1])
                a2 = ctx.flush("vector")
                pg_dep[pba] = (a1,)
                pu_dep[pbb] = (a2,)
                up_done.append(a2)
                wchain[0] = mu_
        wd_cast = []
        wdp = {0: None, 1: None}
        for it in range(n_it):
            b = it % 2
            wl = [wdp[b]] if wdp[b] is not None else []
            lw = E("scalar", lambda e, it=it, b=b: e.dma_start(out=wdf[b][:], in_=wd_src(it)), waits=wl, dma="wdf%d" % b)
            cw_ = E("vector", lambda e, it=it, b=b: e.tensor_copy(wdb[:, it * DM:(it + 1) * DM], wdf[b][:]), waits=[lw])
            cw = ctx.flush("vector")
            wdp[b] = cw
            wd_cast.append(cw)
        UP = up_done[-1]
        UPS = p.barrier("scalar")
        WD = wd_cast[-1]
        dn_dep = {0: (), 1: ()}
        outs = []
        for tt in range(NT):
            for hf in range(2):
                pb2 = hf
                md = None
                for it in range(n_it):
                    md = E("tensor", lambda e, tt=tt, hf=hf, it=it, pb2=pb2: e.matmul(
                        ps_h[pb2][:], Pe[:, it * T + tt * 128: it * T + (tt + 1) * 128],
                        wdb[:, it * DM + hf * 512: it * DM + (hf + 1) * 512],
                        start=(it == 0), stop=(it == n_it - 1)),
                        waits=([UP, UPS, WD] + list(dn_dep[pb2]) if it == 0 else []))
                mdd = ctx.flush("tensor")
                if gate_e is None:
                    o1 = E("vector", lambda e, tt=tt, hf=hf, pb2=pb2: e.tensor_tensor(
                        ysum[:, tt * DM + hf * 512: tt * DM + (hf + 1) * 512],
                        ysum[:, tt * DM + hf * 512: tt * DM + (hf + 1) * 512], ps_h[pb2][:], ALU.add), waits=[mdd, zs])
                else:
                    o0 = E("vector", lambda e, tt=tt, pb2=pb2, ge=gate_e: e.tensor_scalar(
                        sg2[:], ps_h[pb2][:], gates[:, tt * 8 + ge: tt * 8 + ge + 1], None, ALU.mult), waits=[mdd, GATES])
                    o1 = E("vector", lambda e, tt=tt, hf=hf: e.tensor_tensor(
                        ysum[:, tt * DM + hf * 512: tt * DM + (hf + 1) * 512],
                        ysum[:, tt * DM + hf * 512: tt * DM + (hf + 1) * 512], sg2[:], ALU.add), waits=[o0, zs])
                o1f = ctx.flush("vector")
                dn_dep[pb2] = (o1f,)
                outs.append(o1f)
        return outs[-1]

    dep0 = QKVB
    n_experts = 8
    for ex in range(n_experts):
        last = expert_block(f"e{ex}", 11,
                            (lambda it, ex=ex: wg_d[ex, it, :, :, :]),
                            (lambda it, ex=ex: wu_d[ex, it, :, :, :]),
                            (lambda it, ex=ex: wd_d[ex, it, :, :]),
                            ex, dep0)
        dep0 = p.all_barriers()
    for half in range(2):
        last = expert_block(f"sh{half}", 11,
                            (lambda it, half=half: swg_d[half * 11 + it, :, :, :]),
                            (lambda it, half=half: swu_d[half * 11 + it, :, :, :]),
                            (lambda it, half=half: swd_d[half * 11 + it, :, :]),
                            None, dep0)
        dep0 = p.all_barriers()

    g0st = [ctx.sbuf(f"g0st{i}", [128, DM], F32) for i in range(2)]
    g_stores = []
    for tt in range(NT):
        b = tt % 2
        wl = [g_stores[-2]] if len(g_stores) >= 2 else []
        a1 = E("scalar", lambda e, tt=tt, b=b: e.activation(g0st[b][:], ysum[:, tt * DM:(tt + 1) * DM], AF.Silu), waits=[last] + wl)
        fls = ctx.flush("scalar")
        st = E("sync", lambda e, tt=tt, b=b: e.dma_start(out=g0_o[tt * 128:(tt + 1) * 128, :], in_=g0st[b][:]), waits=[fls], dma="g0st%d" % b)
        g_stores.append(st)

    return





def phase2(ctx):
    p = ctx.p
    E = p.emit
    QP_d = ctx.din("QP", [NH, 65, T], BF16)
    K_d = ctx.din("K_all", [4 * DM, T], BF16)
    V_d = ctx.din("V_all", [S, DM], BF16)
    g0_d = ctx.din("g0", [T, DM])
    x_d = ctx.din("x", [T, DM])
    wout_d = ctx.din("wout", [DM, DM])
    bout_d = ctx.din("bout128", [128, DM])
    id_d = ctx.din("ident", [128, 128])

    x1_o = ctx.dout("x1", [T, DM])
    xn1_o = ctx.dout("xn1", [T, DM])
    xn1fm_o = ctx.dout("xn1fm", [128, NDT * T], BF16)

    identt = ctx.sbuf("identt", [128, 128], F32)
    identb = ctx.sbuf("identb", [128, 128], BF16)
    eps_t = ctx.sbuf("eps_t", [128, 1], F32)
    li = E("sync", lambda e: e.dma_start(out=identt[:], in_=id_d[:]), dma="const")
    ci = E("vector", lambda e: e.tensor_copy(identb[:], identt[:]), waits=[li])
    cid = ctx.flush("vector")
    mz1 = E("vector", lambda e: e.memset(eps_t[:], 1e-5))

    pt_guard = [None, None]
    AO = [ctx.sbuf(f"AO{i}", [128, T], BF16) for i in range(8)]
    qpb = [ctx.sbuf(f"qpb{i}", [65, T], BF16) for i in range(2)]
    Kh = [ctx.sbuf(f"Kh{i}", [65, S], BF16) for i in range(2)]
    vtm = [ctx.sbuf(f"vtm{i}", [128, 64 * 65], BF16) for i in range(2)]
    Pt = [ctx.sbuf(f"Pt{i}", [128, 512], BF16) for i in range(2)]
    ones64 = ctx.sbuf("ones64", [1, 64], F32)
    rzf = ctx.sbuf("rzf", [1, 512], F32)
    ofb = ctx.sbuf("ofb", [64, 512], BF16)
    ps_S = [ctx.psum(f"ps_S{i}", [128, 512], F32) for i in range(2)]
    ps_O = ctx.psum("ps_O", [65, 512], F32)
    ps_rz = ctx.psum("ps_rz", [64, 512], F32)
    ps_s0 = [ctx.psum(f"ps_s0{i}", [128, 512], F32) for i in range(2)]
    ps_tp = ctx.psum("ps_tp", [128, 128], BF16)

    mz2 = E("vector", lambda e: e.memset(ones64[:], 1.0))
    for i in range(2):
        E("vector", lambda e, i=i: e.memset(Kh[i][64:65, :], -1.0))
        E("vector", lambda e, i=i: e.memset(vtm[i].ap().rearrange("p (a n) -> p a n", a=64)[:, :, 64:65], 1.0))
    mzdr = ctx.flush("vector")

    V_r = V_d.ap().rearrange("(a p) n -> p a n", p=128)
    head_guard = [None, None]
    qb_guard = None
    ao_done = []
    for h in range(NH):
        hb = h % 2
        hw = [x for x in [head_guard[hb], mzdr] if x is not None]
        lqp = E("sync", lambda e, h=h, hb=hb: e.dma_start(out=qpb[hb][0:65, :], in_=QP_d[h, :, :]),
                waits=hw, dma=f"qp{hb}")
        lks = []
        for r in range(4):
            lks.append(E("sync", lambda e, h=h, r=r, hb=hb: e.dma_start(
                out=Kh[hb][0:64, r * T:(r + 1) * T], in_=K_d[r * DM + h * 64: r * DM + (h + 1) * 64, :]),
                waits=hw, dma=f"kh{hb}"))
        lvs = []
        for q4 in range(4):
            lvs.append(E("gpsimd", lambda e, h=h, q4=q4, hb=hb: e.dma_start(
                out=vtm[hb].ap().rearrange("p (a n) -> p a n", a=64)[:, q4 * 16:(q4 + 1) * 16, 0:64],
                in_=V_r[:, q4 * 16:(q4 + 1) * 16, h * 64:(h + 1) * 64]),
                waits=hw, dma=f"vh{hb}"))
        for qb in range(4):
            for kt2 in range(32):
                mw = []
                for j in range(2):
                    kt = kt2 * 2 + j
                    mw.append(E("tensor", lambda e, kt=kt, qb=qb, j=j, hb=hb: e.matmul(
                        ps_S[j][:], Kh[hb][:, kt * 128:(kt + 1) * 128], qpb[hb][0:65, qb * 512:(qb + 1) * 512],
                        start=True, stop=True),
                        waits=[lks[kt // 16], lqp] + [x for x in [pt_guard[j]] if x is not None]))
                m1d = ctx.flush("tensor")
                for j in range(2):
                    E("scalar", lambda e, j=j: e.activation(Pt[j][:], ps_S[j][:], AF.Exp, scale=0.125), waits=[m1d])
                a1d = ctx.flush("scalar")
                pt_guard[0] = a1d
                pt_guard[1] = a1d
                for j in range(2):
                    kt = kt2 * 2 + j
                    E("tensor", lambda e, kt=kt, j=j, hb=hb: e.matmul(
                        ps_O[:], vtm[hb].ap().rearrange("p (a n) -> p a n", a=64)[:, kt:kt + 1, :].squeeze(1), Pt[j][:],
                        start=(kt == 0), stop=(kt == 63)),
                        waits=[a1d, lvs[kt // 16]] + [x for x in [qb_guard] if (kt == 0 and x is not None)])
            omd = ctx.flush("tensor")
            r1 = E("vector", lambda e: e.reciprocal(rzf[:], ps_O[64:65, :]), waits=[omd])
            r1d = ctx.flush("vector")
            mz_ = E("tensor", lambda e: e.matmul(ps_rz[:], ones64[:], rzf[:], start=True, stop=True), waits=[r1d, mz2])
            mzd = ctx.flush("tensor")
            c1 = E("scalar", lambda e: e.copy(ofb[:], ps_O[0:64, :]), waits=[omd])
            c1d = ctx.flush("scalar")
            o1 = E("vector", lambda e, h=h, qb=qb: e.tensor_tensor(
                AO[h // 2][(h % 2) * 64:(h % 2) * 64 + 64, qb * 512:(qb + 1) * 512], ofb[:], ps_rz[:], ALU.mult),
                waits=[mzd, c1d])
            o1d = ctx.flush("vector")
            ao_done.append(o1d)
            qb_guard = o1d
        head_guard[hb] = qb_guard
    AODONE = p.barrier("vector")

    wof = [ctx.sbuf(f"wof{i}", [128, DM], F32) for i in range(2)]
    wob = [ctx.sbuf(f"wob{i}", [128, DM], BF16) for i in range(8)]
    wcast = []
    wprev = {0: None, 1: None}
    for hp in range(8):
        b = hp % 2
        lw = E("sync", lambda e, hp=hp, b=b: e.dma_start(out=wof[b][:], in_=wout_d[hp * 128:(hp + 1) * 128, :]),
               waits=[x for x in [wprev[b]] if x is not None], dma=f"wof{b}")
        cw = E("vector", lambda e, hp=hp, b=b: e.tensor_copy(wob[hp][:], wof[b][:]), waits=[lw])
        cwd = ctx.flush("vector")
        wprev[b] = cwd
        wcast.append(cwd)
    bout = ctx.sbuf("bout", [128, DM], F32)
    lbo = E("sync", lambda e: e.dma_start(out=bout[:], in_=bout_d[:]), dma="const")

    tmp = {k: ctx.sbuf("lntmp_" + k, [128, 1], F32) for k in ["mu", "negmu", "ssq", "var", "rstd"]}
    tmp["sq"] = ctx.sbuf("lntmp_sq", [128, DM], F32)
    tmp["junk"] = ctx.sbuf("lntmp_junk", [128, DM], F32)
    tmp["eps_t"] = eps_t
    xb = [ctx.sbuf(f"xb{i}", [128, DM], F32) for i in range(2)]
    g0b = [ctx.sbuf(f"g0b{i}", [128, DM], F32) for i in range(2)]
    x1t = ctx.sbuf("x1t", [128, DM], F32)
    s0f = ctx.sbuf("s0f", [128, DM], F32)
    xn32 = ctx.sbuf("xn32", [128, DM], F32)
    xnb = ctx.sbuf("xnb", [128, DM], BF16)
    ctx.xnfm_st = ctx.sbuf("xnfm_st", [128, 128], BF16)
    xg_guard = {0: None, 1: None}
    st_guard = {"x1": None, "xn1": None, "s0": None, "t1last": None, "fmst": None}
    for tt in range(NT):
        b = tt % 2
        gw = [x for x in [xg_guard[b]] if x is not None]
        lx = E("sync", lambda e, tt=tt, b=b: e.dma_start(out=xb[b][:], in_=x_d[tt * 128:(tt + 1) * 128, :]), waits=gw, dma=f"xb{b}")
        lg = E("scalar", lambda e, tt=tt, b=b: e.dma_start(out=g0b[b][:], in_=g0_d[tt * 128:(tt + 1) * 128, :]), waits=gw, dma=f"g0b{b}")
        mm = {}
        for half in range(2):
            m_ = None
            for hp in range(8):
                m_ = E("tensor", lambda e, tt=tt, half=half, hp=hp: e.matmul(
                    ps_s0[half][:], AO[hp][:, tt * 128:(tt + 1) * 128], wob[hp][:, half * 512:(half + 1) * 512],
                    start=(hp == 0), stop=(hp == 7)),
                    waits=([AODONE, wcast[-1]] + [x for x in [st_guard["s0"]] if x is not None] if hp == 0 else []))
            mm[half] = m_
        mmd = ctx.flush("tensor")
        e1 = None
        for half in range(2):
            sl = slice(half * 512, (half + 1) * 512)
            e1 = E("vector", lambda e, half=half, sl=sl: e.tensor_tensor(s0f[:, sl], ps_s0[half][:], bout[:, sl], ALU.add),
                   waits=[mmd, lbo] + [x for x in [st_guard["x1"]] if x is not None])
        e2 = E("vector", lambda e, b=b: e.tensor_tensor(s0f[:], s0f[:], g0b[b][:], ALU.mult), waits=[e1, lg])
        e3 = E("vector", lambda e, b=b: e.tensor_tensor(x1t[:], s0f[:], xb[b][:], ALU.add), waits=[e2, lx])
        e3d = ctx.flush("vector")
        st_guard["s0"] = e3d
        xg_guard[b] = e3d
        sx = E("gpsimd", lambda e, tt=tt: e.dma_start(out=x1_o[tt * 128:(tt + 1) * 128, :], in_=x1t[:]),
               waits=[e3d] + [x for x in [st_guard["x1"]] if x is not None], dma="x1st")
        st_guard["x1"] = sx
        lnt = ln_tile(ctx, E, x1t, xn32, tmp, [e3d, mz1] + [x for x in [st_guard["xn1"]] if x is not None])
        lnd = ctx.flush("vector")
        sn = E("gpsimd", lambda e, tt=tt: e.dma_start(out=xn1_o[tt * 128:(tt + 1) * 128, :], in_=xn32[:]),
               waits=[lnd], dma="xn1st")
        st_guard["xn1"] = sn
        cb = E("vector", lambda e: e.tensor_copy(xnb[:], xn32[:]),
               waits=[lnd] + [x for x in [st_guard["t1last"]] if x is not None])
        cbd = ctx.flush("vector")
        for dt in range(NDT):
            t1 = E("tensor", lambda e, dt=dt: e.transpose(ps_tp[:], xnb[:, dt * 128:(dt + 1) * 128], identb[:]),
                   waits=[cbd, cid] + [x for x in [st_guard["fmst"]] if x is not None])
            t1d = ctx.flush("tensor")
            t2 = E("vector", lambda e: e.tensor_copy(ctx.xnfm_st[:], ps_tp[:]), waits=[t1d])
            t2d = ctx.flush("vector")
            t3 = E("gpsimd", lambda e, dt=dt, tt=tt: e.dma_start(
                out=xn1fm_o[:, dt * T + tt * 128: dt * T + (tt + 1) * 128], in_=ctx.xnfm_st[:]), waits=[t2d], dma="xnfmst")
            st_guard["fmst"] = t3
            st_guard["t1last"] = t1d
    return


def build_launch1():
    ctx = Ctx("l1")
    p = ctx.start(); ctx.p = p; ctx.mkflush()
    phase1(ctx)
    return ctx.done(p.all_barriers())


def build_launch2():
    ctx = Ctx("l2")
    p = ctx.start(); ctx.p = p; ctx.mkflush()
    phase2(ctx)
    return ctx.done(p.all_barriers())


def host_prep_l1(inputs):
    x = np.asarray(inputs["L0_attn_x"])
    wqkv = np.ascontiguousarray(inputs["L0_attn_W_qkv"]).astype(np.float32)
    bqkv = np.asarray(inputs["L0_attn_b_qkv"]).astype(np.float32)
    bqkv_r = np.ascontiguousarray(bqkv.reshape(24, 128).T)
    bv128 = np.ascontiguousarray(np.broadcast_to(bqkv[2048:3072][None, :], (128, 1024)))
    pe = rel_pos_enc(S, HD)
    gate_wT = np.ascontiguousarray(np.asarray(inputs["L0_ffn_gate_w"]).astype(np.float32).T)
    wg = np.asarray(inputs["L0_ffn_wg"]).astype(np.float32)
    wu = np.asarray(inputs["L0_ffn_wu"]).astype(np.float32)
    wd = np.asarray(inputs["L0_ffn_wd"]).astype(np.float32)
    wg_t = np.ascontiguousarray(wg.reshape(8, 8, 128, 11, 128).transpose(0, 3, 1, 2, 4))
    wu_t = np.ascontiguousarray(wu.reshape(8, 8, 128, 11, 128).transpose(0, 3, 1, 2, 4))
    wd_t = np.ascontiguousarray(wd.reshape(8, 11, 128, 1024))
    swg = np.asarray(inputs["L0_ffn_swg"]).astype(np.float32)
    swu = np.asarray(inputs["L0_ffn_swu"]).astype(np.float32)
    swd = np.asarray(inputs["L0_ffn_swd"]).astype(np.float32)
    swg_t = np.ascontiguousarray(swg.reshape(8, 128, 22, 128).transpose(2, 0, 1, 3))
    swu_t = np.ascontiguousarray(swu.reshape(8, 128, 22, 128).transpose(2, 0, 1, 3))
    swd_t = np.ascontiguousarray(swd.reshape(22, 128, 1024))
    ident = np.eye(128, dtype=np.float32)
    maps = []
    for c in range(NCORES):
        b, r = c // 4, c % 4
        sl = slice(r * T, (r + 1) * T)
        pe_own = pe[sl].T
        pe2 = np.ascontiguousarray(np.concatenate([pe_own, pe_own], 0))
        maps.append({
            "x": np.ascontiguousarray(x[b, sl]).astype(np.float32),
            "wqkv": wqkv, "bqkv_r": bqkv_r, "pe2": pe2, "bv128": bv128,
            "gate_wT": gate_wT, "wg_t": wg_t, "wu_t": wu_t, "wd_t": wd_t,
            "swg_t": swg_t, "swu_t": swu_t, "swd_t": swd_t, "ident": ident,
        })
    return maps





def phase3a(ctx, i):
    p = ctx.p
    E = p.emit
    mfm_d = ctx.din("m_fm", [128, NDT * T], BF16)
    w4_d = ctx.din("in_wT_t4", [4, 32, NDT, 128, 128])
    xz_o = ctx.dout("xz", [4096, T], BF16)

    mfm = ctx.sbuf("mfm", [128, NDT * T], BF16)
    lm = E("sync", lambda e: e.dma_start(out=mfm[:], in_=mfm_d[:]), dma="const")
    wf = [ctx.sbuf(f"wf{i}", [128, NDT * 128], F32) for i in range(2)]
    wb = [ctx.sbuf(f"wb{i}", [128, NDT * 128], BF16) for i in range(2)]
    stg = [ctx.sbuf(f"stg{i}", [128, 512], BF16) for i in range(2)]
    ps = [ctx.psum(f"ps{i}", [128, 512], F32) for i in range(2)]
    wprev = {0: None, 1: None}
    st_guard = {0: None, 1: None}
    ps_guard = {0: None, 1: None}
    for ot in range(32):
        b = ot % 2
        lw = E("sync", lambda e, ot=ot, b=b, i=i: e.dma_start(
            out=wf[b].ap().rearrange("p (a n) -> p a n", a=NDT), in_=w4_d[i, ot, :, :, :].rearrange("a p n -> p a n")),
            waits=[x for x in [wprev[b]] if x is not None], dma=f"wf{b}")
        cw_ = E("vector", lambda e, b=b: e.tensor_copy(wb[b][:], wf[b][:]), waits=[lw])
        cw = ctx.flush("vector")
        wprev[b] = cw
        for tc in range(4):
            pb = tc % 2
            m_ = None
            for dt in range(NDT):
                m_ = E("tensor", lambda e, ot=ot, tc=tc, dt=dt, b=b, pb=pb: e.matmul(
                    ps[pb][:], wb[b][:, dt * 128:(dt + 1) * 128], mfm[:, dt * T + tc * 512: dt * T + (tc + 1) * 512],
                    start=(dt == 0), stop=(dt == NDT - 1)),
                    waits=([cw, lm] + [x for x in [ps_guard[pb]] if x is not None] if dt == 0 else []))
            md = ctx.flush("tensor")
            sb = tc % 2
            c1 = E("scalar", lambda e, pb=pb, sb=sb: e.copy(stg[sb][:], ps[pb][:]),
                   waits=[md] + [x for x in [st_guard[sb]] if x is not None] + (p.all_barriers() if DBG_SER else []))
            c1d = ctx.flush("scalar")
            ps_guard[pb] = c1d
            st = E("gpsimd", lambda e, ot=ot, tc=tc, sb=sb: e.dma_start(
                out=xz_o[ot * 128:(ot + 1) * 128, tc * 512:(tc + 1) * 512], in_=stg[sb][:]), waits=[c1d], dma=f"st{sb}")
            st_guard[sb] = st
        wprev[b] = md
    return



def phase3b(ctx, i):
    p = ctx.p
    E = p.emit
    xz_d = ctx.din("xz", [4096, T], BF16)
    tails_d = ctx.din("tails_ag", [4 * 2048, 3], BF16)
    ohprev_d = ctx.din("ohprev", [128, 4])
    cw_d = ctx.din("cw_r4", [4, 16, 128, 4])[i]
    cb_d = ctx.din("cb_r4", [4, 128, 16])[i]
    xpwT_d = ctx.din("xp_wT_t4", [4, 16, 128, 96])[i]
    dtwT_d = ctx.din("dt_wT_t4", [4, 16, 64, 128])[i]
    dtb_d = ctx.din("dtb_r4", [4, 128, 16])[i]
    alog_d = ctx.din("alog_r4", [4, 16, 128, 16])[i]
    dskip_d = ctx.din("dskip_r4", [4, 128, 16])[i]
    id_d = ctx.din("ident", [128, 128])

    xc_o = ctx.dout("xc", [2048, T], BF16)
    cumd_o = ctx.dout("cumd", [2048, T])
    bc_o = ctx.dout("bc", [32, T], BF16)
    s_o = ctx.dout("s_sum", [2048, 16])
    pc_o = ctx.dout("p_sum", [2048, 16])
    yl_o = ctx.dout("y_local", [2048, T], BF16)

    identt = ctx.sbuf("identt", [128, 128], F32)
    identb = ctx.sbuf("identb", [128, 128], BF16)
    li = E("sync", lambda e: e.dma_start(out=identt[:], in_=id_d[:]), dma="const")
    ci_ = E("vector", lambda e: e.tensor_copy(identb[:], identt[:]), waits=[li])
    ci = ctx.flush("vector")
    cwt = ctx.sbuf("cwt", [128, 16 * 4], F32)
    cbt = ctx.sbuf("cbt", [128, 16], F32)
    xpw = ctx.sbuf("xpw", [128, 16 * 96], F32)
    xpwb = ctx.sbuf("xpwb", [128, 16 * 96], BF16)
    dtw = ctx.sbuf("dtw", [64, 16 * 128], F32)
    dtwb = ctx.sbuf("dtwb", [64, 16 * 128], BF16)
    dtb = ctx.sbuf("dtb", [128, 16], F32)
    alog = ctx.sbuf("alog", [128, 16 * 16], F32)
    at = ctx.sbuf("at", [128, 16 * 16], F32)
    dskip = ctx.sbuf("dskip", [128, 16], F32)
    lc = []
    lc.append(E("sync", lambda e: e.dma_start(out=cwt.ap().rearrange("p (a n) -> p a n", a=16), in_=cw_d.rearrange("a p n -> p a n")), dma="const"))
    lc.append(E("sync", lambda e: e.dma_start(out=cbt[:], in_=cb_d[:]), dma="const"))
    lc.append(E("sync", lambda e: e.dma_start(out=xpw.ap().rearrange("p (a n) -> p a n", a=16), in_=xpwT_d.rearrange("a p n -> p a n")), dma="const"))
    lc.append(E("sync", lambda e: e.dma_start(out=dtw.ap().rearrange("p (a n) -> p a n", a=16), in_=dtwT_d.rearrange("a p n -> p a n")), dma="const"))
    lc.append(E("sync", lambda e: e.dma_start(out=dtb[:], in_=dtb_d[:]), dma="const"))
    lc.append(E("sync", lambda e: e.dma_start(out=alog.ap().rearrange("p (a n) -> p a n", a=16), in_=alog_d.rearrange("a p n -> p a n")), dma="const"))
    lc.append(E("sync", lambda e: e.dma_start(out=dskip[:], in_=dskip_d[:]), dma="const"))
    LC = [p.dma_total("const")]
    c1 = E("vector", lambda e: e.tensor_copy(xpwb[:], xpw[:]), waits=LC)
    c2 = E("vector", lambda e: e.tensor_copy(dtwb[:], dtw[:]), waits=LC)
    CW = ctx.flush("vector")
    a1 = E("scalar", lambda e: e.activation(at[:], alog[:], AF.Exp), waits=LC)
    a2 = E("vector", lambda e: e.tensor_scalar(at[:], at[:], -1.0, None, ALU.mult), waits=[a1])
    AT = ctx.flush("vector")

    xsp = [ctx.sbuf(f"xsp{j}", [128, T + 3], BF16) for j in range(2)]
    ohp = ctx.sbuf("ohp", [128, 4], F32)
    lohp = E("sync", lambda e: e.dma_start(out=ohp[:], in_=ohprev_d[:]), dma="const")
    tl4 = ctx.sbuf("tl4", [128, 4 * 3], BF16)
    hacc = ctx.sbuf("hacc", [128, 3], F32)
    acc1 = ctx.sbuf("acc1", [128, T], F32)
    acc2 = ctx.sbuf("acc2", [128, T], F32)
    xct = [ctx.sbuf(f"xct{i}", [128, T], BF16) for i in range(2)]
    ps_dbl = [ctx.psum(f"ps_dbl{i}", [128, 512], F32) for i in range(4)]
    xs_guard = {0: None, 1: None}
    xc_guard = {0: None, 1: None}
    hguard = [None]
    xc_stores = []
    dbl_mm = None
    for cht in range(16):
        b = cht % 2
        gw = [x for x in [xs_guard[b]] if x is not None]
        lh4 = E("sync", lambda e, cht=cht: e.dma_start(
            out=tl4.ap().rearrange("p (a n) -> p a n", a=4),
            in_=tails_d.ap().rearrange("(a q) n -> a q n", a=4)[:, cht * 128:(cht + 1) * 128, :].rearrange("a p n -> p a n")),
            waits=gw + [x for x in [hguard[0]] if x is not None], dma="tl4")
        hc1 = E("vector", lambda e: e.tensor_scalar(hacc[:], tl4[:, 0:3], ohp[:, 0:1], None, ALU.mult), waits=[lh4, lohp])
        hc2 = E("vector", lambda e: e.scalar_tensor_tensor(hacc[:], tl4[:, 3:6], ohp[:, 1:2], hacc[:], ALU.mult, ALU.add), waits=[hc1])
        hc3 = E("vector", lambda e: e.scalar_tensor_tensor(hacc[:], tl4[:, 6:9], ohp[:, 2:3], hacc[:], ALU.mult, ALU.add), waits=[hc2])
        hc4 = E("vector", lambda e, b=b: e.tensor_copy(xsp[b][:, 0:3], hacc[:]), waits=[hc3])
        lh = ctx.flush("vector")
        hguard[0] = lh
        lx = E("sync", lambda e, cht=cht, b=b: e.dma_start(out=xsp[b][:, 3:T + 3], in_=xz_d[cht * 128:(cht + 1) * 128, :]), waits=gw, dma=f"xs{b}")
        e1 = E("vector", lambda e, b=b, cht=cht: e.tensor_scalar(acc1[:], xsp[b][:, 0:T], cwt[:, cht * 4:cht * 4 + 1], None, ALU.mult), waits=[lh, lx] + LC)
        e2 = E("vector", lambda e, b=b, cht=cht: e.scalar_tensor_tensor(acc2[:], xsp[b][:, 1:T + 1], cwt[:, cht * 4 + 1:cht * 4 + 2], acc1[:], ALU.mult, ALU.add), waits=[e1])
        e3 = E("vector", lambda e, b=b, cht=cht: e.scalar_tensor_tensor(acc1[:], xsp[b][:, 2:T + 2], cwt[:, cht * 4 + 2:cht * 4 + 3], acc2[:], ALU.mult, ALU.add), waits=[e2])
        e4 = E("vector", lambda e, b=b, cht=cht: e.scalar_tensor_tensor(acc2[:], xsp[b][:, 3:T + 3], cwt[:, cht * 4 + 3:cht * 4 + 4], acc1[:], ALU.mult, ALU.add), waits=[e3])
        e4d = ctx.flush("vector")
        xs_guard[b] = e4d
        a3 = E("scalar", lambda e, b=b, cht=cht: e.activation(xct[b][:], acc2[:], AF.Silu, bias=cbt[:, cht:cht + 1]),
               waits=[e4d] + [x for x in [xc_guard[b]] if x is not None])
        a3d = ctx.flush("scalar")
        stx = E("gpsimd", lambda e, cht=cht, b=b: e.dma_start(out=xc_o[cht * 128:(cht + 1) * 128, :], in_=xct[b][:]), waits=[a3d], dma=f"xcst{b}")
        xc_stores.append(stx)
        xc_guard[b] = stx
        for tc in range(4):
            dbl_mm = E("tensor", lambda e, cht=cht, tc=tc, b=b: e.matmul(
                ps_dbl[tc][0:96, :], xpwb[:, cht * 96:(cht + 1) * 96], xct[b][:, tc * 512:(tc + 1) * 512],
                start=(cht == 0), stop=(cht == 15)), waits=([a3d, CW] if True else []))
    dbld = ctx.flush("tensor")
    dts = ctx.sbuf("dts", [64, T], BF16)
    bcs = ctx.sbuf("bcs", [32, T], BF16)
    ev = []
    for tc in range(4):
        ev.append(E("scalar", lambda e, tc=tc: e.copy(dts[:, tc * 512:(tc + 1) * 512], ps_dbl[tc][0:64, :]), waits=[dbld]))
        ev.append(E("scalar", lambda e, tc=tc: e.copy(bcs[:, tc * 512:(tc + 1) * 512], ps_dbl[tc][64:96, :]), waits=[dbld]))
    evd = ctx.flush("scalar")
    stbc = E("gpsimd", lambda e: e.dma_start(out=bc_o[:, :], in_=bcs[:]), waits=[evd], dma="bcst")

    ones_t = ctx.sbuf("ones_t", [128, T], F32)
    mzo = E("vector", lambda e: e.memset(ones_t[:], 1.0))
    delta = ctx.sbuf("delta", [128, T], F32)
    cumd = ctx.sbuf("cumd_t", [128, T], F32)
    ut = ctx.sbuf("ut", [128, T], BF16)
    sp1 = ctx.sbuf("sp1", [128, T], F32)
    sp2 = ctx.sbuf("sp2", [128, T], F32)
    av = ctx.sbuf("av", [128, T], F32)
    rv = ctx.sbuf("rv", [128, T], BF16)
    hv = ctx.sbuf("hv", [128, T], BF16)
    pv = ctx.sbuf("pv", [128, T], BF16)
    udv = ctx.sbuf("udv", [128, T], BF16)
    bbb = [ctx.sbuf(f"bbb{i}", [128, T], BF16) for i in range(2)]
    cbb = [ctx.sbuf(f"cbb{i}", [128, T], BF16) for i in range(2)]
    carry = ctx.sbuf("carry", [128, 1], F32)
    ssum = ctx.sbuf("ssum", [128, 16], F32)
    psum_t = ctx.sbuf("psum_t", [128, 16], F32)
    xc2 = [ctx.sbuf(f"xc2{i}", [128, T], BF16) for i in range(2)]
    ps_del = [ctx.psum(f"ps_del{i}", [128, 512], F32) for i in range(2)]
    ps_y = ps_dbl
    totd = ctx.sbuf("totd", [128, 1], F32)
    yst = ctx.sbuf("yst", [128, T], BF16)
    xc2_guard = {0: None, 1: None}
    bcb_guard = {0: None, 1: None}
    st2_guard = {"cumd": None, "s": None, "pc": None, "yl": None}
    del_guard = [None]
    sy_guard = [None]
    for cht in range(16):
        b = cht % 2
        lxc = E("sync", lambda e, cht=cht, b=b: e.dma_start(out=xc2[b][:], in_=xc_o[cht * 128:(cht + 1) * 128, :]),
                waits=[xc_stores[cht]] + [x for x in [xc2_guard[b]] if x is not None], dma=f"xc2{b}")
        dmm = {}
        for tc in range(4):
            pb = tc % 2
            dmm[tc] = E("tensor", lambda e, cht=cht, tc=tc, pb=pb: e.matmul(
                ps_del[pb][:], dtwb[:, cht * 128:(cht + 1) * 128], dts[:, tc * 512:(tc + 1) * 512],
                start=True, stop=True), waits=([evd, CW, dbld] + [x for x in [del_guard[0]] if x is not None] if tc < 2 else ([del_guard[0]] if tc == 2 else [])))
            if tc == 1 or tc == 3:
                dd = ctx.flush("tensor")
                for tcc in ([0, 1] if tc == 1 else [2, 3]):
                    sl = slice(tcc * 512, (tcc + 1) * 512)
                    E("scalar", lambda e, cht=cht, tcc=tcc, sl=sl: e.activation(
                        sp1[:, sl], ps_del[tcc % 2][:], AF.Abs, bias=dtb[:, cht:cht + 1]), waits=[dd])
                    E("scalar", lambda e, cht=cht, tcc=tcc, sl=sl: e.activation(
                        sp2[:, sl], ps_del[tcc % 2][:], AF.Relu, bias=dtb[:, cht:cht + 1]), waits=[dd])
                del_guard[0] = ctx.flush("scalar")
        sfa = E("scalar", lambda e: e.activation(sp1[:], sp1[:], AF.Exp, scale=-1.0))
        sfb = E("scalar", lambda e: e.activation(sp1[:], sp1[:], AF.Ln, bias=1.0), waits=[sfa])
        sfd = ctx.flush("scalar")
        dl1 = E("vector", lambda e: e.tensor_tensor(delta[:], sp1[:], sp2[:], ALU.add), waits=[sfd])
        cd1 = E("vector", lambda e: e.tensor_tensor_scan(cumd[:], ones_t[:], delta[:], 0.0, ALU.mult, ALU.add), waits=[dl1, mzo])
        cdd = ctx.flush("vector")
        sc1 = E("gpsimd", lambda e, cht=cht: e.dma_start(out=cumd_o[cht * 128:(cht + 1) * 128, :], in_=cumd[:]),
                waits=[cdd] + [x for x in [st2_guard["cumd"]] if x is not None], dma="cumdst")
        st2_guard["cumd"] = sc1
        u1 = E("vector", lambda e, b=b: e.tensor_tensor(ut[:], delta[:], xc2[b][:], ALU.mult), waits=[cdd, lxc])
        tc1 = E("vector", lambda e: e.tensor_copy(totd[:], cumd[:, T - 1:T]), waits=[cdd])
        tcd = ctx.flush("vector")
        pc1 = E("scalar", lambda e, cht=cht: e.activation(psum_t[:], at[:, cht * 16:(cht + 1) * 16], AF.Exp, scale=totd[:]), waits=[tcd, AT])
        pcd = ctx.flush("scalar")
        sp_st = E("gpsimd", lambda e, cht=cht: e.dma_start(out=pc_o[cht * 128:(cht + 1) * 128, :], in_=psum_t[:]),
                  waits=[pcd] + [x for x in [st2_guard["pc"]] if x is not None], dma="pcst")
        st2_guard["pc"] = sp_st
        ymm = None
        for n in range(16):
            bb = n % 2
            lb_ = E("sync", lambda e, n=n, bb=bb: e.dma_start(out=bbb[bb].ap().unsqueeze(1), in_=bc_o[n:n + 1, :].partition_broadcast(128)),
                    waits=[stbc] + [x for x in [bcb_guard[bb]] if x is not None], dma=f"bcb{bb}")
            lc_ = E("scalar", lambda e, n=n, bb=bb: e.dma_start(out=cbb[bb].ap().unsqueeze(1), in_=bc_o[16 + n:17 + n, :].partition_broadcast(128)),
                    waits=[stbc] + [x for x in [bcb_guard[bb]] if x is not None], dma=f"ccb{bb}")
            aa1 = E("scalar", lambda e, cht=cht, n=n: e.activation(av[:], delta[:], AF.Exp, scale=at[:, cht * 16 + n:cht * 16 + n + 1]),
                    waits=[AT] + ([ymm] if ymm else []))
            aad = ctx.flush("scalar")
            rr1 = E("vector", lambda e, bb=bb: e.tensor_tensor(rv[:], ut[:], bbb[bb][:], ALU.mult), waits=[u1, lb_, aad])
            hh1 = E("vector", lambda e: e.tensor_tensor_scan(hv[:], av[:], rv[:], 0.0, ALU.mult, ALU.add), waits=[rr1])
            cc1 = E("vector", lambda e, n=n: e.tensor_copy(ssum[:, n:n + 1], hv[:, T - 1:T]), waits=[hh1])
            pp1 = E("vector", lambda e, bb=bb: e.tensor_tensor(pv[:], hv[:], cbb[bb][:], ALU.mult),
                    waits=[cc1, lc_] + ([ymm] if ymm else []))
            ppd = ctx.flush("vector")
            bcb_guard[bb] = ppd
            for tc in range(4):
                ymm = E("tensor", lambda e, n=n, tc=tc: e.matmul(
                    ps_y[tc][:], identb[:], pv[:, tc * 512:(tc + 1) * 512], start=(n == 0), stop=False),
                    waits=[ppd, ci] + ([sy_guard[0]] if (n == 0 and sy_guard[0] is not None) else []))
        ud1 = E("vector", lambda e, b=b, cht=cht: e.tensor_scalar(udv[:], xc2[b][:], dskip[:, cht:cht + 1], None, ALU.mult),
                waits=[lxc] + ([ymm] if ymm else []))
        udd = ctx.flush("vector")
        for tc in range(4):
            ymm = E("tensor", lambda e, tc=tc: e.matmul(
                ps_y[tc][:], identb[:], udv[:, tc * 512:(tc + 1) * 512], start=False, stop=True), waits=[udd])
        ymd = ctx.flush("tensor")
        xc2_guard[b] = ymd
        sy = []
        for tc in range(4):
            sy.append(E("scalar", lambda e, tc=tc: e.copy(yst[:, tc * 512:(tc + 1) * 512], ps_y[tc][:]),
                        waits=[ymd] + [x for x in [st2_guard["yl"]] if x is not None]))
        syd = ctx.flush("scalar")
        sy_guard[0] = syd
        sy_st = E("gpsimd", lambda e, cht=cht: e.dma_start(out=yl_o[cht * 128:(cht + 1) * 128, :], in_=yst[:]),
                  waits=[syd] + [x for x in [st2_guard["yl"]] if x is not None], dma="ylst")
        st2_guard["yl"] = sy_st
        ss_st = E("gpsimd", lambda e, cht=cht: e.dma_start(out=s_o[cht * 128:(cht + 1) * 128, :], in_=ssum[:]),
                  waits=[ppd] + [x for x in [st2_guard["s"]] if x is not None], dma="sst")
        st2_guard["s"] = ss_st
    return



def phase3c(ctx, i):
    p = ctx.p
    E = p.emit
    yl_d = ctx.din("y_local", [2048, T], BF16)
    cumd_d = ctx.din("cumd", [2048, T])
    bc_d = ctx.din("bc", [32, T], BF16)
    z_d = ctx.din("xz", [4096, T], BF16)
    s_all_d = ctx.din("s_ag", [4 * 2048, 16]).rearrange("(a q) n -> a q n", a=4)
    p_all_d = ctx.din("p_ag", [4 * 2048, 16]).rearrange("(a q) n -> a q n", a=4)
    oh_d = ctx.din("ohsel", [128, 4])
    alog_d = ctx.din("alog_r4", [4, 16, 128, 16])[i]
    outwT_d = ctx.din("out_wT_t4", [4, 16, 128, DM])[i]
    lnw_d = ctx.din("lnw128_4", [4, 128, DM])[i]
    lnb_d = ctx.din("lnb128_4", [4, 128, DM])[i]
    m_d = ctx.din("m_tok", [T, DM])
    id_d = ctx.din("ident", [128, 128])

    mn_o = ctx.dout("m_next", [T, DM])
    mnfm_o = ctx.dout("m_next_fm", [128, NDT * T], BF16)
    gated_o = ctx.dout("gated", [2048, T], BF16)

    identt = ctx.sbuf("identt", [128, 128], F32)
    identb = ctx.sbuf("identb", [128, 128], BF16)
    eps_t = ctx.sbuf("eps_t", [128, 1], F32)
    li = E("sync", lambda e: e.dma_start(out=identt[:], in_=id_d[:]), dma="const")
    ci_ = E("vector", lambda e: e.tensor_copy(identb[:], identt[:]), waits=[li])
    ci = ctx.flush("vector")
    mz1 = E("vector", lambda e: e.memset(eps_t[:], 1e-5))
    alog = ctx.sbuf("alog", [128, 16 * 16], F32)
    at = ctx.sbuf("at", [128, 16 * 16], F32)
    oh = ctx.sbuf("oh", [128, 4], F32)
    lnw = ctx.sbuf("lnw", [128, DM], F32)
    lnb = ctx.sbuf("lnb", [128, DM], F32)
    la = E("sync", lambda e: e.dma_start(out=alog.ap().rearrange("p (a n) -> p a n", a=16), in_=alog_d.rearrange("a p n -> p a n")), dma="const")
    lo = E("sync", lambda e: e.dma_start(out=oh[:], in_=oh_d[:]), dma="const")
    lw_ = E("sync", lambda e: e.dma_start(out=lnw[:], in_=lnw_d[:]), dma="const")
    lb_ = E("sync", lambda e: e.dma_start(out=lnb[:], in_=lnb_d[:]), dma="const")
    a1 = E("scalar", lambda e: e.activation(at[:], alog[:], AF.Exp), waits=[la])
    a2 = E("vector", lambda e: e.tensor_scalar(at[:], at[:], -1.0, None, ALU.mult), waits=[a1])
    AT = ctx.flush("vector")

    sti = [ctx.sbuf(f"sti{i}", [128, 4 * 16], F32) for i in range(2)]
    pti = [ctx.sbuf(f"pti{i}", [128, 4 * 16], F32) for i in range(2)]
    t0 = ctx.sbuf("t0", [128, 16], F32)
    t1_ = ctx.sbuf("t1_", [128, 16], F32)
    t2_ = ctx.sbuf("t2_", [128, 16], F32)
    hin = ctx.sbuf("hin", [128, 16], F32)
    tmpv = ctx.sbuf("tmpv", [128, 16], F32)
    cumd = [ctx.sbuf(f"cumd{i}", [128, T], F32) for i in range(2)]
    ylc = [ctx.sbuf(f"ylc{i}", [128, T], BF16) for i in range(2)]
    zc = [ctx.sbuf(f"zc{i}", [128, T], BF16) for i in range(2)]
    dfv = ctx.sbuf("dfv", [128, T], F32)
    termv = ctx.sbuf("termv", [128, T], BF16)
    szv = ctx.sbuf("szv", [128, T], BF16)
    gtd = ctx.sbuf("gtd", [128, T], BF16)
    cbb = [ctx.sbuf(f"cbb{i}", [128, T], BF16) for i in range(2)]
    ps_y = [ctx.psum(f"ps_y{i}", [128, 512], F32) for i in range(4)]
    ld_guard = {0: None, 1: None}
    bcb_guard = {0: None, 1: None}
    gst_guard = [None]
    for cht in range(16):
        b = cht % 2
        gw = [x for x in [ld_guard[b]] if x is not None]
        ls = E("sync", lambda e, cht=cht, b=b: e.dma_start(out=sti[b].ap().rearrange("p (a n) -> p a n", a=4), in_=s_all_d[:, cht * 128:(cht + 1) * 128, :].rearrange("a p n -> p a n")), waits=gw, dma=f"sti{b}")
        lp = E("sync", lambda e, cht=cht, b=b: e.dma_start(out=pti[b].ap().rearrange("p (a n) -> p a n", a=4), in_=p_all_d[:, cht * 128:(cht + 1) * 128, :].rearrange("a p n -> p a n")), waits=gw, dma=f"sti{b}")
        lcu = E("sync", lambda e, cht=cht, b=b: e.dma_start(out=cumd[b][:], in_=cumd_d[cht * 128:(cht + 1) * 128, :]), waits=gw, dma=f"cud{b}")
        lyl = E("scalar", lambda e, cht=cht, b=b: e.dma_start(out=ylc[b][:], in_=yl_d[cht * 128:(cht + 1) * 128, :]), waits=gw, dma=f"ylc{b}")
        lz = E("scalar", lambda e, cht=cht, b=b: e.dma_start(out=zc[b][:], in_=z_d[2048 + cht * 128: 2048 + (cht + 1) * 128, :]), waits=gw, dma=f"zc{b}")
        h1 = E("vector", lambda e, b=b: e.tensor_copy(t0[:], sti[b][:, 0:16]), waits=[ls, lp])
        h2 = E("vector", lambda e, b=b: e.tensor_tensor(tmpv[:], pti[b][:, 16:32], t0[:], ALU.mult))
        h3 = E("vector", lambda e, b=b: e.tensor_tensor(t1_[:], tmpv[:], sti[b][:, 16:32], ALU.add))
        h4 = E("vector", lambda e, b=b: e.tensor_tensor(tmpv[:], pti[b][:, 32:48], t1_[:], ALU.mult))
        h5 = E("vector", lambda e, b=b: e.tensor_tensor(t2_[:], tmpv[:], sti[b][:, 32:48], ALU.add))
        h6 = E("vector", lambda e: e.tensor_scalar(hin[:], t0[:], oh[:, 1:2], None, ALU.mult), waits=[lo])
        h7 = E("vector", lambda e: e.scalar_tensor_tensor(tmpv[:], t1_[:], oh[:, 2:3], hin[:], ALU.mult, ALU.add))
        h8 = E("vector", lambda e: e.scalar_tensor_tensor(hin[:], t2_[:], oh[:, 3:4], tmpv[:], ALU.mult, ALU.add))
        h8d = ctx.flush("vector")
        ld_guard[b] = h8d
        ymm = None
        for tc in range(4):
            ymm = E("tensor", lambda e, tc=tc, b=b: e.matmul(
                ps_y[tc][:], identb[:], ylc[b][:, tc * 512:(tc + 1) * 512], start=True, stop=False),
                waits=[lyl, ci] + [x for x in [gst_guard[0]] if x is not None])
        for n in range(16):
            bb = n % 2
            lc_ = E("scalar", lambda e, n=n, bb=bb: e.dma_start(out=cbb[bb].ap().unsqueeze(1), in_=bc_d[16 + n:17 + n, :].partition_broadcast(128)),
                    waits=[x for x in [bcb_guard[bb]] if x is not None], dma=f"ccb{bb}")
            aa1 = E("scalar", lambda e, cht=cht, n=n, b=b: e.activation(dfv[:], cumd[b][:], AF.Exp, scale=at[:, cht * 16 + n:cht * 16 + n + 1]),
                    waits=[AT, lcu] + ([ymm] if ymm else []))
            aad = ctx.flush("scalar")
            tt1 = E("vector", lambda e, n=n, bb=bb: e.scalar_tensor_tensor(termv[:], dfv[:], hin[:, n:n + 1], cbb[bb][:], ALU.mult, ALU.mult),
                    waits=[aad, lc_, h8d] + ([ymm] if ymm else []))
            ttd = ctx.flush("vector")
            bcb_guard[bb] = ttd
            for tc in range(4):
                ymm = E("tensor", lambda e, tc=tc, n=n: e.matmul(
                    ps_y[tc][:], identb[:], termv[:, tc * 512:(tc + 1) * 512], start=False, stop=(tc == 3 and n == 15)),
                    waits=[ttd])
        ymd = ctx.flush("tensor")
        sz1 = E("scalar", lambda e, b=b: e.activation(szv[:], zc[b][:], AF.Silu), waits=[lz, ymd])
        szd = ctx.flush("scalar")
        g1 = None
        for tc in range(4):
            sl = slice(tc * 512, (tc + 1) * 512)
            g1 = E("vector", lambda e, tc=tc, sl=sl: e.tensor_tensor(gtd[:, sl], ps_y[tc][:], szv[:, sl], ALU.mult), waits=[szd])
        g1d = ctx.flush("vector")
        gst = E("gpsimd", lambda e, cht=cht: e.dma_start(out=gated_o[cht * 128:(cht + 1) * 128, :], in_=gtd[:]),
                waits=[g1d] + [x for x in [gst_guard[0]] if x is not None], dma="gst")
        gst_guard[0] = gst
    GDONE = [p.dma_total("gst")]

    owf = [ctx.sbuf(f"owf{i}", [128, DM], F32) for i in range(2)]
    owb = [ctx.sbuf(f"owb{i}", [128, DM], BF16) for i in range(2)]
    gl = [ctx.sbuf(f"gl{i}", [128, 128], BF16) for i in range(2)]
    mt = [ctx.sbuf(f"mt{i}", [128, DM], F32) for i in range(2)]
    res = ctx.sbuf("res", [128, DM], F32)
    mn32 = ctx.sbuf("mn32", [128, DM], F32)
    mnb = ctx.sbuf("mnb", [128, DM], BF16)
    fm_st = ctx.sbuf("fm_st", [128, 128], BF16)
    tmp = {k: ctx.sbuf("lntmp_" + k, [128, 1], F32) for k in ["mu", "negmu", "ssq", "var", "rstd"]}
    tmp["sq"] = ctx.sbuf("lntmp_sq", [128, DM], F32)
    tmp["junk"] = ctx.sbuf("lntmp_junk", [128, DM], F32)
    tmp["eps_t"] = eps_t
    ps_o = [ctx.psum(f"ps_o{i}", [128, 512], F32) for i in range(2)]
    ps_tp = ctx.psum("ps_tp", [128, 128], BF16)
    st_guard = {"mn": None, "fm": None, "res": None, "t1last": None, "mt": {0: None, 1: None}, "gl": {0: None, 1: None}, "ow": {0: None, 1: None}, "owpe": {0: None, 1: None}}
    for tt in range(NT):
        b = tt % 2
        lm = E("sync", lambda e, tt=tt, b=b: e.dma_start(out=mt[b][:], in_=m_d[tt * 128:(tt + 1) * 128, :]),
               waits=[x for x in [st_guard["mt"][b]] if x is not None], dma=f"mt{b}")
        mm_ = {}
        for cht in range(16):
            gb = cht % 2
            lg = E("scalar", lambda e, cht=cht, tt=tt, gb=gb: e.dma_start(out=gl[gb][:], in_=gated_o[cht * 128:(cht + 1) * 128, tt * 128:(tt + 1) * 128]),
                   waits=GDONE + [x for x in [st_guard["gl"][gb]] if x is not None], dma=f"gl{gb}")
            ob = cht % 2
            low = E("sync", lambda e, cht=cht, ob=ob: e.dma_start(out=owf[ob][:], in_=outwT_d[cht, :, :]),
                    waits=[x for x in [st_guard["ow"][ob]] if x is not None], dma=f"ow{ob}")
            cow_ = E("vector", lambda e, ob=ob: e.tensor_copy(owb[ob][:], owf[ob][:]),
                     waits=[low] + [x for x in [st_guard["owpe"][ob]] if x is not None])
            cow = ctx.flush("vector")
            st_guard["ow"][ob] = cow
            gld = ctx.flush("scalar")
            for half in range(2):
                mm_[half] = E("tensor", lambda e, cht=cht, half=half, gb=gb, ob=ob: e.matmul(
                    ps_o[half][:], gl[gb][:], owb[ob][:, half * 512:(half + 1) * 512],
                    start=(cht == 0), stop=(cht == 15)),
                    waits=[lg, gld, cow] + ([x for x in [st_guard["res"]] if x is not None] if cht == 0 else []))
            st_guard["gl"][gb] = mm_[1]
            st_guard["owpe"][ob] = mm_[1]
        mmd = ctx.flush("tensor")
        r1 = None
        for half in range(2):
            sl = slice(half * 512, (half + 1) * 512)
            r1 = E("vector", lambda e, half=half, sl=sl, b=b: e.tensor_tensor(res[:, sl], ps_o[half][:], mt[b][:, sl], ALU.add), waits=[mmd, lm])
        r1d = ctx.flush("vector")
        st_guard["res"] = r1d
        st_guard["mt"][b] = r1d
        lnt = ln_tile(ctx, E, res, mn32, tmp, [r1d, mz1])
        w1_ = E("vector", lambda e: e.tensor_tensor(mn32[:], mn32[:], lnw[:], ALU.mult), waits=[lnt, lw_])
        w2_ = E("vector", lambda e: e.tensor_tensor(mn32[:], mn32[:], lnb[:], ALU.add), waits=[w1_, lb_] + [x for x in [st_guard["mn"]] if x is not None])
        w2d = ctx.flush("vector")
        smn = E("gpsimd", lambda e, tt=tt: e.dma_start(out=mn_o[tt * 128:(tt + 1) * 128, :], in_=mn32[:]), waits=[w2d], dma="mnst")
        st_guard["mn"] = smn
        cb = E("vector", lambda e: e.tensor_copy(mnb[:], mn32[:]), waits=[w2d] + [x for x in [st_guard["t1last"]] if x is not None])
        cbd = ctx.flush("vector")
        for dt in range(NDT):
            t1 = E("tensor", lambda e, dt=dt: e.transpose(ps_tp[:], mnb[:, dt * 128:(dt + 1) * 128], identb[:]),
                   waits=[cbd, ci] + [x for x in [st_guard["fm"]] if x is not None])
            t1d = ctx.flush("tensor")
            t2 = E("vector", lambda e: e.tensor_copy(fm_st[:], ps_tp[:]), waits=[t1d])
            t2d = ctx.flush("vector")
            t3 = E("gpsimd", lambda e, dt=dt, tt=tt: e.dma_start(
                out=mnfm_o[:, dt * T + tt * 128: dt * T + (tt + 1) * 128], in_=fm_st[:]), waits=[t2d], dma="fmst")
            st_guard["fm"] = t3
            st_guard["t1last"] = t1d
    return



def phase4(ctx):
    p = ctx.p
    E = p.emit
    xnfm_d = ctx.din("xn1fm", [128, NDT * T], BF16)
    xnt_d = ctx.din("xn1", [T, DM])
    x1_d = ctx.din("x1", [T, DM])
    s1_d = ctx.din("s1", [T, DM])
    rw_d = ctx.din("router_w", [DM, 8])
    w1_d = ctx.din("w1_t", [8, 32, NDT, 128, 128])
    b1_d = ctx.din("b1_r", [8, 128, 32])
    w2_d = ctx.din("w2_t", [8, 32, 128, DM])
    b2_d = ctx.din("b2", [8, DM])
    gam_d = ctx.din("gamma", [8, DM])
    bet_d = ctx.din("beta", [8, DM])
    id_d = ctx.din("ident", [128, 128])
    out_o = ctx.dout("out", [T, DM])

    identt = ctx.sbuf("identt", [128, 128], F32)
    eps_t = ctx.sbuf("eps_t", [128, 1], F32)
    li = E("sync", lambda e: e.dma_start(out=identt[:], in_=id_d[:]), dma="const")
    mz1 = E("vector", lambda e: e.memset(eps_t[:], 1e-6))
    one_col = ctx.sbuf("one_col", [128, 1], F32)
    mz2 = E("vector", lambda e: e.memset(one_col[:], 1.0))
    xnfm = ctx.sbuf("xnfm", [128, NDT * T], BF16)
    lxf = E("sync", lambda e: e.dma_start(out=xnfm[:], in_=xnfm_d[:]), dma="const")
    rw = ctx.sbuf("rw", [128, NDT * 8], F32)
    lrw = E("sync", lambda e: e.dma_start(out=rw.ap().rearrange("p (a n) -> p a n", a=NDT), in_=rw_d.ap().rearrange("(a p) n -> p a n", p=128)), dma="const")

    gates = ctx.sbuf("gates", [128, NT * 8], F32)
    xt32 = [ctx.sbuf(f"xt32{i}", [128, DM], F32) for i in range(2)]
    x32t = ctx.sbuf("x32t", [128, NDT * 128], F32)
    lgt = ctx.sbuf("lgt", [128, 8], F32)
    m8 = ctx.sbuf("m8", [128, 8], F32)
    d21 = ctx.sbuf("d21", [128, 1], F32)
    ez = ctx.sbuf("ez", [128, 1], F32)
    rz = ctx.sbuf("rz", [128, 1], F32)
    pex = ctx.sbuf("pex", [128, 8], F32)
    msk = ctx.sbuf("msk", [128, 8], F32)
    ps_misc = ctx.psum("ps_misc", [128, 128], F32)
    ps_g = ctx.psum("ps_g", [128, 512], F32)
    xl_guard = {0: None, 1: None}
    gprev = ()
    for tt in range(NT):
        b = tt % 2
        lx = E("sync", lambda e, tt=tt, b=b: e.dma_start(out=xt32[b][:], in_=xnt_d[tt * 128:(tt + 1) * 128, :]),
               waits=[x for x in [xl_guard[b]] if x is not None], dma=f"xt{b}")
        tpd = None
        for dt in range(NDT):
            t3 = E("tensor", lambda e, b=b, dt=dt: e.transpose(ps_misc[:], xt32[b][:, dt * 128:(dt + 1) * 128], identt[:]),
                   waits=[lx, li] + ([tpd] if tpd else []))
            t3d = ctx.flush("tensor")
            t4 = E("vector", lambda e, dt=dt: e.tensor_copy(x32t[:, dt * 128:(dt + 1) * 128], ps_misc[:]), waits=[t3d])
            tpd = ctx.flush("vector")
        xl_guard[b] = tpd
        mms = None
        for dt in range(NDT):
            mms = E("tensor", lambda e, dt=dt: e.matmul(
                ps_g[0:128, 0:8], x32t[:, dt * 128:(dt + 1) * 128], rw[:, dt * 8:(dt + 1) * 8],
                start=(dt == 0), stop=(dt == NDT - 1)),
                waits=([tpd, lrw] + list(gprev) if dt == 0 else []))
        mmsd = ctx.flush("tensor")
        c1 = E("vector", lambda e: e.tensor_copy(lgt[:], ps_g[0:128, 0:8]), waits=[mmsd])
        c2 = E("vector", lambda e: e.max(m8[:], lgt[:]), waits=[c1])
        c3 = E("vector", lambda e: e.tensor_scalar(d21[:], m8[:, 1:2], m8[:, 0:1], None, ALU.subtract), waits=[c2])
        c4 = E("scalar", lambda e: e.activation(ez[:], d21[:], AF.Exp), waits=[c3])
        c5 = E("vector", lambda e: e.tensor_tensor(ez[:], ez[:], one_col[:], ALU.add), waits=[c4, mz2])
        c6 = E("vector", lambda e: e.reciprocal(rz[:], ez[:]), waits=[c5])
        c7 = E("vector", lambda e: e.tensor_scalar(pex[:], lgt[:], m8[:, 0:1], None, ALU.subtract), waits=[c2])
        c8 = E("scalar", lambda e: e.activation(pex[:], pex[:], AF.Exp), waits=[c7])
        c9 = E("vector", lambda e: e.tensor_scalar(msk[:], lgt[:], m8[:, 1:2], None, ALU.is_ge), waits=[c8])
        c10 = E("vector", lambda e: e.tensor_tensor(pex[:], pex[:], msk[:], ALU.mult), waits=[c9])
        c11 = E("vector", lambda e, tt=tt: e.tensor_scalar(gates[:, tt * 8:(tt + 1) * 8], pex[:], rz[:], None, ALU.mult), waits=[c10, c6])
        gprev = (c11, c2)
    GATES = ctx.flush("vector")

    g1acc = ctx.sbuf("g1acc", [128, NT * DM], F32)
    zg = E("vector", lambda e: e.memset(g1acc[:], 0.0))
    hfm = ctx.sbuf("hfm", [128, 32 * 512], BF16)
    w1f = [ctx.sbuf(f"w1f{i}", [128, NDT * 128], F32) for i in range(2)]
    w1b = [ctx.sbuf(f"w1b{i}", [128, NDT * 128], BF16) for i in range(2)]
    w2f = [ctx.sbuf(f"w2f{i}", [128, DM], F32) for i in range(2)]
    w2b = [ctx.sbuf(f"w2b{i}", [128, DM], BF16) for i in range(2)]
    b1t = ctx.sbuf("b1t", [128, 32], F32)
    b2b = ctx.sbuf("b2b", [128, DM], F32)
    gab = ctx.sbuf("gab", [128, DM], F32)
    beb = ctx.sbuf("beb", [128, DM], F32)
    res = ctx.sbuf("res", [128, DM], F32)
    nrm = ctx.sbuf("nrm", [128, DM], F32)
    xnt2 = [ctx.sbuf(f"xnt2{i}", [128, DM], F32) for i in range(2)]
    tmp = {k: ctx.sbuf("lntmp_" + k, [128, 1], F32) for k in ["mu", "negmu", "ssq", "var", "rstd"]}
    tmp["sq"] = ctx.sbuf("lntmp_sq", [128, DM], F32)
    tmp["junk"] = ctx.sbuf("lntmp_junk", [128, DM], F32)
    tmp["eps_t"] = eps_t
    ps_up = [ctx.psum(f"ps_up{i}", [128, 512], F32) for i in range(2)]
    ps_dn = [ctx.psum(f"ps_dn{i}", [128, 512], F32) for i in range(4)]
    w1_guard = {0: None, 1: None}
    w2_guard = {0: None, 1: None}
    up_guard = {0: None, 1: None}
    xn2_guard = {0: None, 1: None}
    dn_guard = {0: None, 1: None, 2: None, 3: None}
    hf_guard = [None]
    acc_guard = [zg]
    for ex in range(8):
        lb1 = E("sync", lambda e, ex=ex: e.dma_start(out=b1t[:], in_=b1_d[ex, :, :]), waits=acc_guard, dma="b1t")
        lb2 = E("scalar", lambda e, ex=ex: e.dma_start(out=b2b.ap().unsqueeze(1), in_=b2_d[ex:ex + 1, :].partition_broadcast(128)), waits=acc_guard, dma="b2b")
        lga = E("scalar", lambda e, ex=ex: e.dma_start(out=gab.ap().unsqueeze(1), in_=gam_d[ex:ex + 1, :].partition_broadcast(128)), waits=acc_guard, dma="gab")
        lbe = E("scalar", lambda e, ex=ex: e.dma_start(out=beb.ap().unsqueeze(1), in_=bet_d[ex:ex + 1, :].partition_broadcast(128)), waits=acc_guard, dma="beb")
        for tc in range(4):
            agd = None
            for fc in range(32):
                wbuf = fc % 2
                lw1 = E("sync", lambda e, ex=ex, fc=fc, wbuf=wbuf: e.dma_start(
                    out=w1f[wbuf].ap().rearrange("p (a n) -> p a n", a=NDT), in_=w1_d[ex, fc, :, :, :].rearrange("a p n -> p a n")),
                    waits=[x for x in [w1_guard[wbuf]] if x is not None], dma=f"w1{wbuf}")
                cw1_ = E("vector", lambda e, wbuf=wbuf: e.tensor_copy(w1b[wbuf][:], w1f[wbuf][:]), waits=[lw1])
                cw1 = ctx.flush("vector")
                w1_guard[wbuf] = cw1
                pb = fc % 2
                m_ = None
                for dt in range(NDT):
                    m_ = E("tensor", lambda e, tc=tc, fc=fc, dt=dt, wbuf=wbuf, pb=pb: e.matmul(
                        ps_up[pb][:], w1b[wbuf][:, dt * 128:(dt + 1) * 128], xnfm[:, dt * T + tc * 512: dt * T + (tc + 1) * 512],
                        start=(dt == 0), stop=(dt == NDT - 1)),
                        waits=([cw1, lxf] + [x for x in [up_guard[pb]] if x is not None] if dt == 0 else []))
                md = ctx.flush("tensor")
                w1_guard[wbuf] = md
                ag = E("scalar", lambda e, fc=fc, pb=pb: e.activation(hfm[:, fc * 512:(fc + 1) * 512], ps_up[pb][:], AF.Gelu, bias=b1t[:, fc:fc + 1]),
                       waits=[md, lb1] + [x for x in [hf_guard[0]] if x is not None])
                agd = ctx.flush("scalar")
                up_guard[pb] = agd
            for tp in range(2):
                dn_last = None
                for fc in range(32):
                    wbuf = fc % 2
                    lw2 = E("sync", lambda e, ex=ex, fc=fc, wbuf=wbuf: e.dma_start(out=w2f[wbuf][:], in_=w2_d[ex, fc, :, :]),
                            waits=[x for x in [w2_guard[wbuf]] if x is not None], dma=f"w2{wbuf}")
                    cw2_ = E("vector", lambda e, wbuf=wbuf: e.tensor_copy(w2b[wbuf][:], w2f[wbuf][:]), waits=[lw2])
                    cw2 = ctx.flush("vector")
                    for sub in range(4):
                        ttl, half = tp * 2 + sub // 2, sub % 2
                        dn_last = E("tensor", lambda e, fc=fc, ttl=ttl, half=half, wbuf=wbuf, sub=sub: e.matmul(
                            ps_dn[sub][:], hfm[:, fc * 512 + ttl * 128: fc * 512 + ttl * 128 + 128],
                            w2b[wbuf][:, half * 512:(half + 1) * 512],
                            start=(fc == 0), stop=(fc == 31)),
                            waits=([cw2, agd] + [x for x in [dn_guard[sub]] if x is not None] if fc == 0 else [cw2]))
                    w2_guard[wbuf] = dn_last
                dnd = ctx.flush("tensor")
                hf_guard[0] = dnd
                for tl in range(2):
                    ttl = tp * 2 + tl
                    tt_g = tc * 4 + ttl
                    xb2 = ttl % 2
                    lxn = E("sync", lambda e, tt_g=tt_g, xb2=xb2: e.dma_start(out=xnt2[xb2][:], in_=xnt_d[tt_g * 128:(tt_g + 1) * 128, :]),
                            waits=[x for x in [xn2_guard[xb2]] if x is not None], dma=f"xn2{xb2}")
                    r1 = None
                    for half in range(2):
                        sub = tl * 2 + half
                        sl = slice(half * 512, (half + 1) * 512)
                        r1 = E("vector", lambda e, sub=sub, sl=sl: e.tensor_tensor(res[:, sl], ps_dn[sub][:], b2b[:, sl], ALU.add), waits=[dnd, lb2])
                    r2 = E("vector", lambda e, xb2=xb2: e.tensor_tensor(res[:], res[:], xnt2[xb2][:], ALU.add), waits=[r1, lxn])
                    r2d = ctx.flush("vector")
                    xn2_guard[xb2] = r2d
                    dn_guard[tl * 2] = r2d
                    dn_guard[tl * 2 + 1] = r2d
                    lnt = ln_tile(ctx, E, res, nrm, tmp, [r2d, mz1], nf=1024.0)
                    g1_ = E("vector", lambda e: e.tensor_tensor(nrm[:], nrm[:], gab[:], ALU.mult), waits=[lnt, lga])
                    g2_ = E("vector", lambda e: e.tensor_tensor(nrm[:], nrm[:], beb[:], ALU.add), waits=[g1_, lbe])
                    g3_ = E("vector", lambda e, ex=ex, tt_g=tt_g: e.tensor_scalar(nrm[:], nrm[:], gates[:, tt_g * 8 + ex: tt_g * 8 + ex + 1], None, ALU.mult), waits=[g2_, GATES])
                    g4_ = E("vector", lambda e, tt_g=tt_g: e.tensor_tensor(g1acc[:, tt_g * DM:(tt_g + 1) * DM], g1acc[:, tt_g * DM:(tt_g + 1) * DM], nrm[:], ALU.add), waits=[g3_, zg])
        acc_guard = [ctx.flush("vector")]

    x1b = [xt32[0], xt32[1]]
    s1b = [tmp["sq"], tmp["junk"]]
    sg1 = nrm
    outb = res
    fx_guard = {0: None, 1: None}
    ost_guard = [None]
    for tt in range(NT):
        b = tt % 2
        gw = [x for x in [fx_guard[b]] if x is not None]
        lx1 = E("sync", lambda e, tt=tt, b=b: e.dma_start(out=x1b[b][:], in_=x1_d[tt * 128:(tt + 1) * 128, :]), waits=gw, dma=f"fx{b}")
        ls1 = E("scalar", lambda e, tt=tt, b=b: e.dma_start(out=s1b[b][:], in_=s1_d[tt * 128:(tt + 1) * 128, :]), waits=gw, dma=f"fs{b}")
        a1 = E("scalar", lambda e, tt=tt: e.activation(sg1[:], g1acc[:, tt * DM:(tt + 1) * DM], AF.Silu), waits=acc_guard + [x for x in [ost_guard[0]] if x is not None])
        a1d = ctx.flush("scalar")
        o1 = E("vector", lambda e, b=b: e.tensor_tensor(outb[:], sg1[:], s1b[b][:], ALU.mult), waits=[a1d, ls1])
        o2 = E("vector", lambda e, b=b: e.tensor_tensor(outb[:], outb[:], x1b[b][:], ALU.add), waits=[o1, lx1])
        o2d = ctx.flush("vector")
        fx_guard[b] = o2d
        so = E("gpsimd", lambda e, tt=tt: e.dma_start(out=out_o[tt * 128:(tt + 1) * 128, :], in_=outb[:]), waits=[o2d] + [x for x in [ost_guard[0]] if x is not None], dma="ost")
        ost_guard[0] = so
    return


def host_prep_l2(inputs, r1_results):
    x = np.asarray(inputs["L0_attn_x"])
    wout = np.ascontiguousarray(np.asarray(inputs["L0_attn_W_out"]).astype(np.float32))
    bout = np.asarray(inputs["L0_attn_b_out"]).astype(np.float32)
    bout128 = np.ascontiguousarray(np.broadcast_to(bout[None, :], (128, 1024)))
    ident = np.eye(128, dtype=np.float32)
    maps = []
    for c in range(NCORES):
        b, r = c // 4, c % 4
        grp = range(b * 4, b * 4 + 4)
        K_all = np.concatenate([np.asarray(r1_results[j]["K_own"]) for j in grp], axis=0)
        V_all = np.concatenate([np.asarray(r1_results[j]["V_own"]) for j in grp], axis=0)
        maps.append({
            "QP": np.asarray(r1_results[c]["QP_own"]),
            "K_all": np.ascontiguousarray(K_all),
            "V_all": np.ascontiguousarray(V_all),
            "g0": np.asarray(r1_results[c]["g0"]),
            "x": np.ascontiguousarray(x[b, r * T:(r + 1) * T]).astype(np.float32),
            "wout": wout, "bout128": bout128, "ident": ident,
        })
    return maps




def host_prep_l3a(inputs, i, r_prev):
    in_w = np.asarray(inputs["L1_attn_in_proj_w"][i]).astype(np.float32)
    in_wT_t = np.ascontiguousarray(in_w.T.reshape(8, 128, 32, 128).transpose(2, 0, 1, 3))
    return [{"m_fm": np.asarray(r_prev[c]["m_fm"]), "in_wT_t": in_wT_t} for c in range(NCORES)]


def host_prep_l3b(inputs, i, r3a):
    cw = np.asarray(inputs["L1_attn_conv_w"][i]).astype(np.float32)
    cb = np.asarray(inputs["L1_attn_conv_b"][i]).astype(np.float32)
    xpw = np.asarray(inputs["L1_attn_x_proj_w"][i]).astype(np.float32)
    dtw = np.asarray(inputs["L1_attn_dt_proj_w"][i]).astype(np.float32)
    dtb = np.asarray(inputs["L1_attn_dt_proj_b"][i]).astype(np.float32)
    alog = np.asarray(inputs["L1_attn_A_log"][i]).astype(np.float32)
    dskip = np.asarray(inputs["L1_attn_D_skip"][i]).astype(np.float32)
    com = {
        "cw_r": np.ascontiguousarray(cw.reshape(16, 128, 4)),
        "cb_r": np.ascontiguousarray(cb.reshape(16, 128).T),
        "xp_wT_t": np.ascontiguousarray(xpw.T.reshape(16, 128, 96)),
        "dt_wT_t": np.ascontiguousarray(dtw.T.reshape(64, 16, 128).transpose(1, 0, 2)),
        "dtb_r": np.ascontiguousarray(dtb.reshape(16, 128).T),
        "alog_r": np.ascontiguousarray(alog.reshape(16, 128, 16)),
        "dskip_r": np.ascontiguousarray(dskip.reshape(16, 128).T),
        "ident": np.eye(128, dtype=np.float32),
    }
    maps = []
    for c in range(NCORES):
        xz = np.asarray(r3a[c]["xz"])
        if c % 4 == 0:
            halo = np.zeros((2048, 3), dtype=BF)
        else:
            halo = np.ascontiguousarray(np.asarray(r3a[c - 1]["xz"])[0:2048, T - 3:T])
        m = dict(com)
        m["xz"] = xz
        m["halo"] = halo
        maps.append(m)
    return maps


def host_prep_l3c(inputs, i, r3a, r3b, m_tok_list):
    outw = np.asarray(inputs["L1_attn_out_proj_w"][i]).astype(np.float32)
    lnw = np.asarray(inputs["L1_attn_ln_w"][i]).astype(np.float32)
    lnb = np.asarray(inputs["L1_attn_ln_b"][i]).astype(np.float32)
    alog = np.asarray(inputs["L1_attn_A_log"][i]).astype(np.float32)
    com = {
        "out_wT_t": np.ascontiguousarray(outw.T.reshape(16, 128, DM)),
        "lnw128": np.ascontiguousarray(np.broadcast_to(lnw[None, :], (128, DM))),
        "lnb128": np.ascontiguousarray(np.broadcast_to(lnb[None, :], (128, DM))),
        "alog_r": np.ascontiguousarray(alog.reshape(16, 128, 16)),
        "ident": np.eye(128, dtype=np.float32),
    }
    maps = []
    for c in range(NCORES):
        b, r = c // 4, c % 4
        grp = range(b * 4, b * 4 + 4)
        s_all = np.stack([np.asarray(r3b[j]["s_sum"]) for j in grp], 0)
        p_all = np.stack([np.asarray(r3b[j]["p_sum"]) for j in grp], 0)
        oh = np.zeros((128, 4), np.float32)
        oh[:, r] = 1.0
        m = dict(com)
        m.update({
            "y_local": np.asarray(r3b[c]["y_local"]),
            "cumd": np.asarray(r3b[c]["cumd"]),
            "bc": np.asarray(r3b[c]["bc"]),
            "z": np.ascontiguousarray(np.asarray(r3a[c]["xz"])[2048:4096, :]),
            "s_all": np.ascontiguousarray(s_all),
            "p_all": np.ascontiguousarray(p_all),
            "ohsel": oh,
            "m_tok": np.asarray(m_tok_list[c]),
        })
        maps.append(m)
    return maps


def host_prep_l4(inputs, r2, s1_list):
    w1 = np.asarray(inputs["L1_ffn_w1"]).astype(np.float32)
    w2 = np.asarray(inputs["L1_ffn_w2"]).astype(np.float32)
    b1 = np.asarray(inputs["L1_ffn_b1"]).astype(np.float32)
    com = {
        "router_w": np.ascontiguousarray(np.asarray(inputs["L1_ffn_router_w"]).astype(np.float32)),
        "w1_t": np.ascontiguousarray(w1.reshape(8, 8, 128, 32, 128).transpose(0, 3, 1, 2, 4)),
        "b1_r": np.ascontiguousarray(b1.reshape(8, 32, 128).transpose(0, 2, 1)),
        "w2_t": np.ascontiguousarray(w2.reshape(8, 32, 128, DM)),
        "b2": np.ascontiguousarray(np.asarray(inputs["L1_ffn_b2"]).astype(np.float32)),
        "gamma": np.ascontiguousarray(np.asarray(inputs["L1_ffn_gamma"]).astype(np.float32)),
        "beta": np.ascontiguousarray(np.asarray(inputs["L1_ffn_beta"]).astype(np.float32)),
        "ident": np.eye(128, dtype=np.float32),
    }
    maps = []
    for c in range(NCORES):
        m = dict(com)
        m.update({
            "xn1fm": np.asarray(r2[c]["xn1fm"]),
            "xn1": np.asarray(r2[c]["xn1"]),
            "x1": np.asarray(r2[c]["x1"]),
            "s1": np.asarray(s1_list[c]),
        })
        maps.append(m)
    return maps





RG4 = [[0, 1, 2, 3], [4, 5, 6, 7]]


def build_fused():
    ctx = Ctx("fused")
    nc = ctx.nc
    p = ctx.start()
    ctx.p = p
    ctx.mkflush()
    E = p.emit

    D = {}
    def ext(name, shape, dtype=F32):
        D[name] = nc.dram_tensor(name, list(shape), dtype, kind="ExternalInput")
    def itn(name, shape, dtype=F32):
        D[name] = nc.dram_tensor(name, list(shape), dtype, kind="Internal")
    ext("x", [T, DM]); ext("wqkv", [DM, 3072]); ext("bqkv_r", [128, 24]); ext("pe2", [128, T])
    ext("bv128", [128, DM]); ext("gate_wT", [DM, 8])
    ext("wg_t", [8, 11, 8, 128, 128]); ext("wu_t", [8, 11, 8, 128, 128]); ext("wd_t", [8, 11, 128, DM])
    ext("swg_t", [22, 8, 128, 128]); ext("swu_t", [22, 8, 128, 128]); ext("swd_t", [22, 128, DM])
    ext("ident", [128, 128]); ext("wout", [DM, DM]); ext("bout128", [128, DM])
    ext("in_wT_t4", [4, 32, NDT, 128, 128])
    ext("cw_r4", [4, 16, 128, 4]); ext("cb_r4", [4, 128, 16]); ext("xp_wT_t4", [4, 16, 128, 96])
    ext("dt_wT_t4", [4, 16, 64, 128]); ext("dtb_r4", [4, 128, 16]); ext("alog_r4", [4, 16, 128, 16])
    ext("dskip_r4", [4, 128, 16]); ext("out_wT_t4", [4, 16, 128, DM])
    ext("lnw128_4", [4, 128, DM]); ext("lnb128_4", [4, 128, DM])
    ext("ohsel", [128, 4]); ext("ohprev", [128, 4])
    ext("router_w", [DM, 8]); ext("w1_t", [8, 32, NDT, 128, 128]); ext("b1_r", [8, 128, 32])
    ext("w2_t", [8, 32, 128, DM]); ext("b2", [8, DM]); ext("gamma", [8, DM]); ext("beta", [8, DM])
    itn("K_own", [DM, T], BF16); itn("V_own", [T, DM], BF16); itn("QP_own", [NH, 65, T], BF16)
    itn("g0", [T, DM]); itn("K_all", [4 * DM, T], BF16); itn("V_all", [S, DM], BF16)
    itn("x1", [T, DM]); itn("xn1", [T, DM]); itn("xn1fm", [128, NDT * T], BF16)
    itn("xz", [4096, T], BF16); itn("xc", [2048, T], BF16); itn("cumd", [2048, T])
    itn("bc", [32, T], BF16); itn("s_sum", [2048, 16]); itn("p_sum", [2048, 16])
    itn("y_local", [2048, T], BF16); itn("gated", [2048, T], BF16)
    itn("tail_own", [2048, 3], BF16); itn("tails_ag", [4 * 2048, 3], BF16)
    itn("s_ag", [4 * 2048, 16]); itn("p_ag", [4 * 2048, 16])
    itn("mA", [T, DM]); itn("mB", [T, DM])
    itn("mA_fm", [128, NDT * T], BF16); itn("mB_fm", [128, NDT * T], BF16)
    D["out"] = nc.dram_tensor("out", [T, DM], F32, kind="ExternalOutput")
    ctx.D = D

    ctx.phase_open("p1_", "")
    D["QP"] = D["QP_own"]
    phase1(ctx)
    ctx.phase_close()
    bar = p.all_barriers()
    E("gpsimd", lambda e: e.collective_compute("AllGather", ALU.bypass, replica_groups=RG4,
                                               ins=[D["K_own"].ap().opt()], outs=[D["K_all"].ap().opt()]), waits=bar)
    E("gpsimd", lambda e: e.collective_compute("AllGather", ALU.bypass, replica_groups=RG4,
                                               ins=[D["V_own"].ap().opt()], outs=[D["V_all"].ap().opt()]))
    ctx.phase_open("p2_", "")
    phase2(ctx)
    ctx.phase_close()

    cur_fm, cur_tok = "xn1fm", "xn1"
    for i in range(4):
        nxt_fm, nxt_tok = ("mA_fm", "mA") if i % 2 == 0 else ("mB_fm", "mB")
        D["m_fm"] = D[cur_fm]
        D["m_tok"] = D[cur_tok]
        D["m_next"] = D[nxt_tok]
        D["m_next_fm"] = D[nxt_fm]
        ctx.phase_open(f"a{i}_", "")
        phase3a(ctx, i)
        ctx.phase_close()
        bar = p.all_barriers()
        tcp = E("gpsimd", lambda e: e.dma_start(out=D["tail_own"][:, :], in_=D["xz"][0:2048, T - 3:T]), waits=bar, dma="tailcp")
        E("gpsimd", lambda e: e.collective_compute("AllGather", ALU.bypass, replica_groups=RG4,
                                                   ins=[D["tail_own"].ap().opt()], outs=[D["tails_ag"].ap().opt()]), waits=[tcp])
        ctx.phase_open(f"b{i}_", "")
        phase3b(ctx, i)
        ctx.phase_close()
        bar = p.all_barriers()
        E("gpsimd", lambda e: e.collective_compute("AllGather", ALU.bypass, replica_groups=RG4,
                                                   ins=[D["s_sum"].ap().opt()], outs=[D["s_ag"].ap().opt()]), waits=bar)
        E("gpsimd", lambda e: e.collective_compute("AllGather", ALU.bypass, replica_groups=RG4,
                                                   ins=[D["p_sum"].ap().opt()], outs=[D["p_ag"].ap().opt()]))
        ctx.phase_open(f"c{i}_", "")
        phase3c(ctx, i)
        ctx.phase_close()
        cur_fm, cur_tok = nxt_fm, nxt_tok

    D["s1"] = D[cur_tok]
    ctx.phase_open("p4_", "")
    phase4(ctx)
    ctx.phase_close()
    return ctx.done(p.all_barriers())


def host_prep_fused(inputs):
    x = np.asarray(inputs["L0_attn_x"])
    wqkv = np.ascontiguousarray(inputs["L0_attn_W_qkv"]).astype(np.float32)
    bqkv = np.asarray(inputs["L0_attn_b_qkv"]).astype(np.float32)
    pe = rel_pos_enc(S, HD)
    wg = np.asarray(inputs["L0_ffn_wg"]).astype(np.float32)
    wu = np.asarray(inputs["L0_ffn_wu"]).astype(np.float32)
    wd = np.asarray(inputs["L0_ffn_wd"]).astype(np.float32)
    swg = np.asarray(inputs["L0_ffn_swg"]).astype(np.float32)
    swu = np.asarray(inputs["L0_ffn_swu"]).astype(np.float32)
    swd = np.asarray(inputs["L0_ffn_swd"]).astype(np.float32)
    in_w = np.asarray(inputs["L1_attn_in_proj_w"]).astype(np.float32)
    cw = np.asarray(inputs["L1_attn_conv_w"]).astype(np.float32)
    cb = np.asarray(inputs["L1_attn_conv_b"]).astype(np.float32)
    xpw = np.asarray(inputs["L1_attn_x_proj_w"]).astype(np.float32)
    dtw = np.asarray(inputs["L1_attn_dt_proj_w"]).astype(np.float32)
    dtb = np.asarray(inputs["L1_attn_dt_proj_b"]).astype(np.float32)
    alog = np.asarray(inputs["L1_attn_A_log"]).astype(np.float32)
    dskip = np.asarray(inputs["L1_attn_D_skip"]).astype(np.float32)
    outw = np.asarray(inputs["L1_attn_out_proj_w"]).astype(np.float32)
    lnw = np.asarray(inputs["L1_attn_ln_w"]).astype(np.float32)
    lnb = np.asarray(inputs["L1_attn_ln_b"]).astype(np.float32)
    w1 = np.asarray(inputs["L1_ffn_w1"]).astype(np.float32)
    b1 = np.asarray(inputs["L1_ffn_b1"]).astype(np.float32)
    w2 = np.asarray(inputs["L1_ffn_w2"]).astype(np.float32)
    com = {
        "wqkv": wqkv,
        "bqkv_r": np.ascontiguousarray(bqkv.reshape(24, 128).T),
        "bv128": np.ascontiguousarray(np.broadcast_to(bqkv[2048:3072][None, :], (128, 1024))),
        "gate_wT": np.ascontiguousarray(np.asarray(inputs["L0_ffn_gate_w"]).astype(np.float32).T),
        "wg_t": np.ascontiguousarray(wg.reshape(8, 8, 128, 11, 128).transpose(0, 3, 1, 2, 4)),
        "wu_t": np.ascontiguousarray(wu.reshape(8, 8, 128, 11, 128).transpose(0, 3, 1, 2, 4)),
        "wd_t": np.ascontiguousarray(wd.reshape(8, 11, 128, 1024)),
        "swg_t": np.ascontiguousarray(swg.reshape(8, 128, 22, 128).transpose(2, 0, 1, 3)),
        "swu_t": np.ascontiguousarray(swu.reshape(8, 128, 22, 128).transpose(2, 0, 1, 3)),
        "swd_t": np.ascontiguousarray(swd.reshape(22, 128, 1024)),
        "ident": np.eye(128, dtype=np.float32),
        "wout": np.ascontiguousarray(np.asarray(inputs["L0_attn_W_out"]).astype(np.float32)),
        "bout128": np.ascontiguousarray(np.broadcast_to(np.asarray(inputs["L0_attn_b_out"]).astype(np.float32)[None, :], (128, 1024))),
        "in_wT_t4": np.ascontiguousarray(np.stack([in_w[i].T.reshape(8, 128, 32, 128).transpose(2, 0, 1, 3) for i in range(4)])),
        "cw_r4": np.ascontiguousarray(cw.reshape(4, 16, 128, 4)),
        "cb_r4": np.ascontiguousarray(np.stack([cb[i].reshape(16, 128).T for i in range(4)])),
        "xp_wT_t4": np.ascontiguousarray(np.stack([xpw[i].T.reshape(16, 128, 96) for i in range(4)])),
        "dt_wT_t4": np.ascontiguousarray(np.stack([dtw[i].T.reshape(64, 16, 128).transpose(1, 0, 2) for i in range(4)])),
        "dtb_r4": np.ascontiguousarray(np.stack([dtb[i].reshape(16, 128).T for i in range(4)])),
        "alog_r4": np.ascontiguousarray(alog.reshape(4, 16, 128, 16)),
        "dskip_r4": np.ascontiguousarray(np.stack([dskip[i].reshape(16, 128).T for i in range(4)])),
        "out_wT_t4": np.ascontiguousarray(np.stack([outw[i].T.reshape(16, 128, DM) for i in range(4)])),
        "lnw128_4": np.ascontiguousarray(np.stack([np.broadcast_to(lnw[i][None, :], (128, DM)) for i in range(4)])),
        "lnb128_4": np.ascontiguousarray(np.stack([np.broadcast_to(lnb[i][None, :], (128, DM)) for i in range(4)])),
        "router_w": np.ascontiguousarray(np.asarray(inputs["L1_ffn_router_w"]).astype(np.float32)),
        "w1_t": np.ascontiguousarray(w1.reshape(8, 8, 128, 32, 128).transpose(0, 3, 1, 2, 4)),
        "b1_r": np.ascontiguousarray(b1.reshape(8, 32, 128).transpose(0, 2, 1)),
        "w2_t": np.ascontiguousarray(w2.reshape(8, 32, 128, DM)),
        "b2": np.ascontiguousarray(np.asarray(inputs["L1_ffn_b2"]).astype(np.float32)),
        "gamma": np.ascontiguousarray(np.asarray(inputs["L1_ffn_gamma"]).astype(np.float32)),
        "beta": np.ascontiguousarray(np.asarray(inputs["L1_ffn_beta"]).astype(np.float32)),
    }
    maps = []
    for c in range(NCORES):
        b, r = c // 4, c % 4
        sl = slice(r * T, (r + 1) * T)
        pe_own = pe[sl].T
        oh = np.zeros((128, 4), np.float32); oh[:, r] = 1.0
        ohp = np.zeros((128, 4), np.float32)
        if r > 0:
            ohp[:, r - 1] = 1.0
        m = dict(com)
        m.update({
            "x": np.ascontiguousarray(x[b, sl]).astype(np.float32),
            "pe2": np.ascontiguousarray(np.concatenate([pe_own, pe_own], 0)),
            "ohsel": oh, "ohprev": ohp,
        })
        maps.append(m)
    return maps


def kernel(**inputs):
    cores = list(range(NCORES))
    nc = _cached("fused", build_fused)
    res = run_bass_kernel_spmd(nc, host_prep_fused(inputs), core_ids=cores).results
    out = np.zeros((2, S, DM), np.float32)
    for c in range(NCORES):
        b, r = c // 4, c % 4
        out[b, r * T:(r + 1) * T] = np.asarray(res[c]["out"])
    return out



# version 04
# speedup vs baseline: 1.0950x; 1.0566x over previous
import numpy as np
import ml_dtypes
import concourse.bass as bass
import concourse.mybir as mybir
from concourse.bass_utils import run_bass_kernel_spmd
from contextlib import ExitStack

F32 = mybir.dt.float32
BF16 = mybir.dt.bfloat16
AF = mybir.ActivationFunctionType
ALU = mybir.AluOpType
AX = mybir.AxisListType
BF = ml_dtypes.bfloat16

NCORES = 8
T = 2048
NT = T // 128
DM = 1024
NDT = DM // 128
NH = 16
HD = 64
S = 8192
ENGINES = ["sync", "vector", "scalar", "tensor", "gpsimd"]


class Prog:

    def __init__(self, nc):
        self.nc = nc
        self.stream = {e: [] for e in ENGINES}
        self.cnt = {e: 0 for e in ENGINES}
        self.dcnt = {}
        self.waitmax = {e: {} for e in ENGINES}
        self.default_waits = []
        self.prefix = ""

    def emit(self, eng, fn, waits=(), dma=None, inc=None):
        waits = list(waits) + self.default_waits
        if dma is not None:
            key = "D_" + self.prefix + dma
            self.dcnt[key] = self.dcnt.get(key, 0) + (inc if inc is not None else 16)
            tok = (key, self.dcnt[key])
        else:
            self.cnt[eng] += (inc if inc is not None else 1)
            tok = (eng, self.cnt[eng])
        wm = self.waitmax[eng]
        for (we, wc) in waits:
            if wm.get(we, 0) >= wc:
                continue
            wm[we] = wc
            self.stream[eng].append(("wait", we, wc))
        self.stream[eng].append(("op", fn, tok[0], tok[1], (16 if dma is not None else (inc if inc is not None else 1))))
        return tok

    def barrier(self, eng):
        return (eng, self.cnt[eng])

    def dma_total(self, name):
        key = "D_" + self.prefix + name
        return (key, self.dcnt.get(key, 0))

    def all_barriers(self):
        out = [(e, self.cnt[e]) for e in ENGINES if self.cnt[e] > 0]
        out += [(k, v) for k, v in self.dcnt.items()]
        return out

    def finish(self, final_tokens):
        for (we, wc) in final_tokens:
            self.stream["sync"].append(("wait", we, wc))

    def build(self, block, sems_engine):
        nc = self.nc
        sems = dict(sems_engine)
        self._dma_sem_guards = []
        for key in self.dcnt:
            g = nc.semaphore(key)
            sems[key] = g.__enter__()
            self._dma_sem_guards.append(g)
        for e in ENGINES:
            ops = self.stream[e]
            if not ops:
                continue
            engobj = getattr(nc, e)

            def make(ops=ops, engobj=engobj):
                def body(_):
                    for item in ops:
                        if item[0] == "wait":
                            _, we, wc = item
                            engobj.wait_ge(sems[we], wc)
                        else:
                            _, fn, key, c, amt = item
                            instr = fn(engobj)
                            if instr is None:
                                instr = engobj.nop()
                            instr.then_inc(sems[key], amt)
                return body
            getattr(block, e)(make())


class Ctx:

    def __init__(self, name):
        self.nc = bass.Bass("TRN2", target_bir_lowering=False)
        self.stack = ExitStack()
        self.name = name
        self.cur = None
        self.D = None
        self.pfx = ""

    def start(self):
        nc = self.nc
        self.sems = {}
        for e in ENGINES:
            self.sems[e] = self.stack.enter_context(nc.semaphore("S_" + e))
        self.block = self.stack.enter_context(nc.Block())
        self.p = Prog(nc)
        return self.p

    def sbuf(self, name, shape, dtype, stack=None, side=None):
        kw = {"side": side} if side else {}
        tgt = stack or self.cur or self.stack
        if tgt is not self.stack and not side:
            kw = {"side": "right"}
        return tgt.enter_context(self.nc.sbuf_tensor(self.pfx + name, list(shape), dtype, **kw))

    def scope(self):
        return ExitStack()

    def mkflush(self):
        self.flushv = self.sbuf("_flushv", [128, 1], F32)
        self.flushs = self.sbuf("_flushs", [128, 1], F32)
        self.flushs2 = self.sbuf("_flushs2", [128, 1], F32)

    def flush(self, eng):
        return self.p.emit(eng, lambda e: e.drain())

    def psum(self, name, shape, dtype=F32):
        return (self.cur or self.stack).enter_context(self.nc.psum_tensor(self.pfx + name, list(shape), dtype))

    def din(self, name, shape, dtype=F32):
        if self.D is not None:
            return self.D[name]
        return self.nc.dram_tensor(name, list(shape), dtype, kind="ExternalInput")

    def dout(self, name, shape, dtype=F32):
        if self.D is not None:
            return self.D[name]
        return self.nc.dram_tensor(name, list(shape), dtype, kind="ExternalOutput")

    def dint(self, name, shape, dtype=F32):
        return self.nc.dram_tensor(name, list(shape), dtype, kind="Internal")

    def phase_open(self, pfx, sem_pfx=None):
        self.cur = ExitStack()
        self.pfx = pfx
        self.p.prefix = sem_pfx if sem_pfx is not None else pfx
        self.p.default_waits = self.p.all_barriers()

    def phase_close(self):
        self.cur.close()
        self.cur = None
        self.pfx = ""
        self.p.prefix = ""

    def done(self, final_tokens):
        self.p.finish(final_tokens)
        self.p.build(self.block, self.sems)
        self.stack.close()
        return self.nc



def ln_tile(ctx, E, xt, out_bf, tmp, waits, eps=1e-5, nf=1024.0):
    t1 = E("vector", lambda e: e.tensor_scalar(tmp["junk"][:], xt[:], 1.0 / nf, 0.0, ALU.mult, ALU.add, accum_out=tmp["mu"][:]), waits=waits)
    t2 = E("vector", lambda e: e.tensor_scalar(tmp["negmu"][:], tmp["mu"][:], -1.0, None, ALU.mult), waits=[t1])
    t3 = E("scalar", lambda e: e.activation(tmp["sq"][:], xt[:], AF.Square, bias=tmp["negmu"][:], accum_out=tmp["ssq"][:]), waits=[t2])
    t4 = E("vector", lambda e: e.tensor_scalar(tmp["var"][:], tmp["ssq"][:], 1.0 / nf, tmp["eps_t"][:], ALU.mult, ALU.add), waits=[t3])
    t5 = E("scalar", lambda e: e.activation(tmp["var"][:], tmp["var"][:], AF.Ln), waits=[t4])
    t6 = E("scalar", lambda e: e.activation(tmp["rstd"][:], tmp["var"][:], AF.Exp, scale=-0.5), waits=[t5])
    t7 = E("vector", lambda e: e.tensor_scalar(out_bf[:], xt[:], tmp["mu"][:], tmp["rstd"][:], ALU.subtract, ALU.mult), waits=[t6])
    return t7


def transpose_128(ctx, E, p, src_bf, dst_bf, ps_tp, identb, waits):
    t1 = E("tensor", lambda e: e.transpose(ps_tp[:], src_bf[:], identb[:]), waits=waits)
    t2 = E("vector", lambda e: e.tensor_copy(dst_bf[:], ps_tp[:]), waits=[t1])
    return t2



def rel_pos_enc(seq_len, dim):
    positions = np.arange(seq_len, dtype=np.float64)[:, None]
    div_term = np.exp(np.arange(0, dim, 2, dtype=np.float32).astype(np.float64) * -(np.log(10000.0) / dim))
    pe = np.zeros((seq_len, dim), dtype=np.float32)
    pe[:, 0::2] = np.sin(positions * div_term).astype(np.float32)
    pe[:, 1::2] = np.cos(positions * div_term).astype(np.float32)
    return pe


_BUILD_CACHE = {}


def _cached(name, fn):
    if name not in _BUILD_CACHE:
        _BUILD_CACHE[name] = fn()
    return _BUILD_CACHE[name]



DBG_NOQ = False
DBG_SER = False


def phase1(ctx):
    p = ctx.p
    E = p.emit
    x_d = ctx.din("x", [T, DM])
    wqkv_d = ctx.din("wqkv", [DM, 3072])
    bqkv_d = ctx.din("bqkv_r", [128, 24])
    pe2_d = ctx.din("pe2", [128, T])
    bv_d = ctx.din("bv128", [128, DM])
    gwt_d = ctx.din("gate_wT", [DM, 8])
    wg_d = ctx.din("wg_t", [8, 11, 8, 128, 128])
    wu_d = ctx.din("wu_t", [8, 11, 8, 128, 128])
    wd_d = ctx.din("wd_t", [8, 11, 128, DM])
    swg_d = ctx.din("swg_t", [22, 8, 128, 128])
    swu_d = ctx.din("swu_t", [22, 8, 128, 128])
    swd_d = ctx.din("swd_t", [22, 128, DM])
    id_d = ctx.din("ident", [128, 128])

    K_o = ctx.dout("K_own", [DM, T], BF16)
    V_o = ctx.dout("V_own", [T, DM], BF16)
    QP_o = ctx.dout("QP_own", [NH, 65, T], BF16)
    g0_o = ctx.dout("g0", [T, DM])

    xn_fm = ctx.sbuf("xn_fm", [128, NDT * T], BF16)
    identt = ctx.sbuf("identt", [128, 128], F32)
    identb = ctx.sbuf("identb", [128, 128], BF16)
    ps_tp = ctx.psum("ps_tp", [128, 128], BF16)
    eps_t = ctx.sbuf("eps_t", [128, 1], F32)
    one_col = ctx.sbuf("one_col", [128, 1], F32)
    tmp = {k: ctx.sbuf("lntmp_" + k, [128, 1], F32) for k in ["mu", "negmu", "ssq", "var", "rstd"]}
    tmp["eps_t"] = eps_t

    li = E("sync", lambda e: e.dma_start(out=identt[:], in_=id_d[:]), dma="const")
    ci = E("vector", lambda e: e.tensor_copy(identb[:], identt[:]), waits=[li])
    mz1 = E("vector", lambda e: e.memset(eps_t[:], 1e-5))
    mz2 = E("vector", lambda e: e.memset(one_col[:], 1.0))

    gwt = ctx.sbuf("gwt", [128, NDT * 8], F32)
    lg0 = E("sync", lambda e: e.dma_start(out=gwt.ap().rearrange("p (a n) -> p a n", a=NDT), in_=gwt_d.ap().rearrange("(a p) n -> p a n", p=128)), dma="const")
    gates = ctx.sbuf("gates", [128, NT * 8], F32)
    lgt = ctx.sbuf("lgt", [128, 8], F32)
    m8 = ctx.sbuf("m8", [128, 8], F32)
    d21 = ctx.sbuf("d21", [128, 1], F32)
    ez = ctx.sbuf("ez", [128, 1], F32)
    rz = ctx.sbuf("rz", [128, 1], F32)
    pex = ctx.sbuf("pex", [128, 8], F32)
    msk = ctx.sbuf("msk", [128, 8], F32)
    ps_g = ctx.psum("ps_g", [128, 8], F32)
    ps_misc = ctx.psum("ps_misc", [128, 128], F32)

    ln_scope = ctx.scope()
    tmp["sq"] = ctx.sbuf("lntmp_sq", [128, DM], F32, ln_scope, side="right")
    tmp["junk"] = ctx.sbuf("lntmp_junk", [128, DM], F32, ln_scope, side="right")
    xbuf = [ctx.sbuf(f"xbuf{i}", [128, DM], F32, ln_scope, side="right") for i in range(2)]
    xn32 = [ctx.sbuf(f"xn32{i}", [128, DM], F32, ln_scope, side="right") for i in range(2)]
    xnb = [ctx.sbuf(f"xnb{i}", [128, DM], BF16, ln_scope, side="right") for i in range(2)]
    x32t = ctx.sbuf("x32t", [128, NDT * 128], F32, ln_scope, side="right")
    tp_done = []
    tp32_done = []
    prev_use = {}
    gprev = ()
    for tt in range(NT):
        b = tt % 2
        ldw = [prev_use[b]] if b in prev_use else []
        ld = E("sync", lambda e, tt=tt, b=b: e.dma_start(out=xbuf[b][:], in_=x_d[tt * 128:(tt + 1) * 128, :]), waits=ldw, dma="xb%d" % b)
        lnt = ln_tile(ctx, E, xbuf[b], xn32[b], tmp, [ld, mz1])
        cb16 = E("vector", lambda e, b=b: e.tensor_copy(xnb[b][:], xn32[b][:]), waits=[lnt])
        last = cb16
        for dt in range(NDT):
            t1 = E("tensor", lambda e, b=b, dt=dt: e.transpose(ps_tp[:], xnb[b][:, dt * 128:(dt + 1) * 128], identb[:]), waits=[cb16, ci] + ([tp_done[-1]] if tp_done else []))
            t1d = ctx.flush("tensor")
            t2 = E("vector", lambda e, dt=dt, tt=tt: e.tensor_copy(xn_fm[:, dt * T + tt * 128: dt * T + (tt + 1) * 128], ps_tp[:]), waits=[t1d])
            tp_done.append(t2)
            t3 = E("tensor", lambda e, b=b, dt=dt: e.transpose(ps_misc[:], xn32[b][:, dt * 128:(dt + 1) * 128], identt[:]), waits=[lnt, li] + ([tp32_done[-1]] if tp32_done else []))
            t3d = ctx.flush("tensor")
            t4 = E("vector", lambda e, dt=dt: e.tensor_copy(x32t[:, dt * 128:(dt + 1) * 128], ps_misc[:]), waits=[t3d])
            tp32_done.append(t4)
            last = t4
        mms = None
        for dt in range(NDT):
            mms = E("tensor", lambda e, dt=dt: e.matmul(
                ps_g[:], x32t[:, dt * 128:(dt + 1) * 128], gwt[:, dt * 8:(dt + 1) * 8],
                start=(dt == 0), stop=(dt == NDT - 1)),
                waits=([tp32_done[-1], lg0] + list(gprev) if dt == 0 else []))
        mmsd = ctx.flush("tensor")
        c1 = E("vector", lambda e: e.tensor_copy(lgt[:], ps_g[:]), waits=[mmsd])
        c2 = E("vector", lambda e: e.max(m8[:], lgt[:]), waits=[c1])
        c3 = E("vector", lambda e: e.tensor_scalar(d21[:], m8[:, 1:2], m8[:, 0:1], None, ALU.subtract), waits=[c2])
        c4 = E("scalar", lambda e: e.activation(ez[:], d21[:], AF.Exp), waits=[c3])
        c5 = E("vector", lambda e: e.tensor_tensor(ez[:], ez[:], one_col[:], ALU.add), waits=[c4, mz2])
        c6 = E("vector", lambda e: e.reciprocal(rz[:], ez[:]), waits=[c5])
        c7 = E("vector", lambda e: e.tensor_scalar(pex[:], lgt[:], m8[:, 0:1], None, ALU.subtract), waits=[c2])
        c8 = E("scalar", lambda e: e.activation(pex[:], pex[:], AF.Exp), waits=[c7])
        c9 = E("vector", lambda e: e.tensor_scalar(msk[:], lgt[:], m8[:, 1:2], None, ALU.is_ge), waits=[c8])
        c10 = E("vector", lambda e: e.tensor_tensor(pex[:], pex[:], msk[:], ALU.mult), waits=[c9])
        c11 = E("vector", lambda e, tt=tt: e.tensor_scalar(gates[:, tt * 8:(tt + 1) * 8], pex[:], rz[:], None, ALU.mult), waits=[c10, c6])
        gprev = (c11, c2)
        prev_use[b] = last
    XN = p.barrier("vector")
    XNT = p.barrier("tensor")
    GATES = p.barrier("vector")
    LNB = p.all_barriers()
    ln_scope.close()

    qkv_scope = ctx.scope()
    pe2 = ctx.sbuf("pe2_s", [128, T], F32, qkv_scope, side="right")
    bv = ctx.sbuf("bv", [128, DM], F32, qkv_scope, side="right")
    bq = ctx.sbuf("bq", [128, 24], F32, qkv_scope, side="right")
    lb = E("sync", lambda e: e.dma_start(out=bq[:], in_=bqkv_d[:]), dma="const")
    lpe = E("sync", lambda e: e.dma_start(out=pe2[:], in_=pe2_d[:]), waits=LNB, dma="const")
    lbv = E("sync", lambda e: e.dma_start(out=bv[:], in_=bv_d[:]), waits=LNB, dma="const")
    wq_f = [ctx.sbuf(f"wq_f{i}", [128, 3072], F32, qkv_scope, side="right") for i in range(4)]
    wqkv_bf = [ctx.sbuf(f"wqkv_bf{i}", [128, 3072], BF16, qkv_scope, side="right") for i in range(NDT)]
    wq_cast = []
    prev_cast = {}
    for dt in range(NDT):
        b = dt % 4
        ldw = [prev_cast[b]] if b in prev_cast else []
        ld = E("sync", lambda e, dt=dt, b=b: e.dma_start(out=wq_f[b][:], in_=wqkv_d[dt * 128:(dt + 1) * 128, :]), waits=ldw + LNB, dma="wqf%d" % b)
        cs = E("vector", lambda e, dt=dt, b=b: e.tensor_copy(wqkv_bf[dt][:], wq_f[b][:]), waits=[ld])
        fl = ctx.flush("vector")
        prev_cast[b] = fl
        wq_cast.append(fl)

    ps_qk = [ctx.psum(f"ps_qk{i}", [128, 512], F32) for i in range(2)]
    kst = [ctx.sbuf(f"kst{i}", [128, 512], BF16, qkv_scope, side="right") for i in range(4)]
    k_stores = []
    q_stores = []
    pb = 0
    ps_free = [(), ()]
    for ot in range(16):
        for tc in range(4):
            pb ^= 1
            mms = None
            for dt in range(NDT):
                mms = E("tensor", lambda e, ot=ot, tc=tc, dt=dt, pb=pb: e.matmul(
                    ps_qk[pb][:], wqkv_bf[dt][:, ot * 128:(ot + 1) * 128],
                    xn_fm[:, dt * T + tc * 512: dt * T + (tc + 1) * 512],
                    start=(dt == 0), stop=(dt == NDT - 1)),
                    waits=(list(ps_free[pb]) + [XN, wq_cast[-1]] if dt == 0 else []))
            mms = ctx.flush("tensor")
            if ot < 8:
                h0, h1 = 2 * ot, 2 * ot + 1
                sb = tc % 2
                e1 = E("vector", lambda e, ot=ot, pb=pb, sb=sb: e.tensor_scalar(
                    kst[sb][:], ps_qk[pb][:], bq[:, ot:ot + 1], None, ALU.add),
                    waits=[mms, lb] + list(q_stores[-4:-2] if len(q_stores) >= 4 else []))
                fl = ctx.flush("vector")
                st1 = E("gpsimd", lambda e, h0=h0, tc=tc, sb=sb: e.dma_start(
                    out=QP_o[h0, 0:64, tc * 512:(tc + 1) * 512], in_=kst[sb][0:64, :]), waits=[fl], dma="qst%d" % sb) if not DBG_NOQ else ctx.flush("vector")
                st2 = E("gpsimd", lambda e, h1=h1, tc=tc, sb=sb: e.dma_start(
                    out=QP_o[h1, 0:64, tc * 512:(tc + 1) * 512], in_=kst[sb][64:128, :]), waits=[fl], dma="qst%d" % sb) if not DBG_NOQ else ctx.flush("vector")
                q_stores += [st1, st2]
                ps_free[pb] = (fl,)
            else:
                sb = 2 + tc % 2
                e1a = E("vector", lambda e, ot=ot, pb=pb, sb=sb: e.tensor_scalar(
                    kst[sb][:], ps_qk[pb][:], bq[:, ot:ot + 1], None, ALU.add),
                    waits=[mms, lb, lpe] + list(k_stores[-2:-1] if len(k_stores) >= 2 else []))
                e1 = E("vector", lambda e, tc=tc, sb=sb: e.tensor_tensor(
                    kst[sb][:], kst[sb][:], pe2[:, tc * 512:(tc + 1) * 512], ALU.add), waits=[e1a])
                fl = ctx.flush("vector")
                st = E("gpsimd", lambda e, ot=ot, tc=tc, sb=sb: e.dma_start(
                    out=K_o[(ot - 8) * 128:(ot - 7) * 128, tc * 512:(tc + 1) * 512], in_=kst[sb][:]), waits=[fl], dma="kst%d" % sb)
                k_stores.append(st)
                ps_free[pb] = (fl, st)
    KDONE = [p.dma_total("kst2"), p.dma_total("kst3")]

    vst = [ctx.sbuf(f"vst{i}", [128, 512], BF16, qkv_scope, side="right") for i in range(2)]
    v_stores = []
    ps_free = [(), ()]
    for tt in range(NT):
        for vc in range(2):
            pb ^= 1
            mms = None
            for dt in range(NDT):
                mms = E("tensor", lambda e, tt=tt, vc=vc, dt=dt, pb=pb: e.matmul(
                    ps_qk[pb][:], xn_fm[:, dt * T + tt * 128: dt * T + (tt + 1) * 128],
                    wqkv_bf[dt][:, 2048 + vc * 512: 2048 + (vc + 1) * 512],
                    start=(dt == 0), stop=(dt == NDT - 1)),
                    waits=(list(ps_free[pb]) if dt == 0 else []))
            mms = ctx.flush("tensor")
            sb = vc
            e1 = E("vector", lambda e, vc=vc, pb=pb, sb=sb: e.tensor_tensor(
                vst[sb][:], ps_qk[pb][:], bv[:, vc * 512:(vc + 1) * 512], ALU.add),
                waits=[mms, lbv] + (v_stores[-2:-1] if len(v_stores) >= 2 else []))
            fl = ctx.flush("vector")
            st = E("gpsimd", lambda e, tt=tt, vc=vc, sb=sb: e.dma_start(
                out=V_o[tt * 128:(tt + 1) * 128, vc * 512:(vc + 1) * 512], in_=vst[sb][:]), waits=[fl], dma="vst%d" % sb)
            v_stores.append(st)
            ps_free[pb] = (fl,)

    QDONE = [p.dma_total("qst0"), p.dma_total("qst1"), p.dma_total("vst0"), p.dma_total("vst1")]
    k0b = ctx.sbuf("k0b", [64, 128], BF16, qkv_scope, side="right")
    qtb = ctx.sbuf("qtb", [64, T], BF16, qkv_scope, side="right")
    ps_mrow = ctx.psum("ps_mrow", [1, 128], F32)
    mcol = ctx.sbuf("mcol", [128, 1], F32, qkv_scope, side="right")
    mrow_st = ctx.sbuf("mrow_st", [1, T], BF16, qkv_scope, side="right")
    qp_stores = []
    for h in range(NH):
        lk = E("scalar", lambda e, h=h: e.dma_start(out=k0b[:], in_=K_o[h * 64:(h + 1) * 64, 0:128]), waits=KDONE + QDONE, dma="k0b")
        lq = E("scalar", lambda e, h=h: e.dma_start(out=qtb[:], in_=QP_o[h, 0:64, :]), waits=QDONE + qp_stores[-1:], dma="qtb")
        prev = (lk, lq)
        for qt in range(NT):
            m1 = E("tensor", lambda e, h=h, qt=qt: e.matmul(
                ps_misc[:], qtb[:, qt * 128:(qt + 1) * 128], k0b[:], start=True, stop=True), waits=list(prev))
            m1d = ctx.flush("tensor")
            r1 = E("vector", lambda e: e.tensor_reduce(mcol[:], ps_misc[:], AX.X, ALU.max), waits=[m1d])
            r1d = ctx.flush("vector")
            t1 = E("tensor", lambda e: e.transpose(ps_mrow[:], mcol[:], identt[:]), waits=[r1d])
            t1d = ctx.flush("tensor")
            c1 = E("scalar", lambda e, h=h, qt=qt: e.copy(mrow_st[:, qt * 128:(qt + 1) * 128], ps_mrow[:]), waits=[t1d])
            fls = ctx.flush("scalar")
            prev = (fls,)
        qp_stores.append(E("gpsimd", lambda e, h=h: e.dma_start(out=QP_o[h, 64:65, :], in_=mrow_st[:]), waits=[prev[0]], dma="qpst"))
    QKVB = p.all_barriers()
    qkv_scope.close()


    ysum = ctx.sbuf("ysum", [128, NT * DM], F32)
    zs = E("vector", lambda e: e.memset(ysum[:], 0.0))
    wgb = [ctx.sbuf(f"wgb{i}", [128, 8 * 128], BF16) for i in range(2)]
    wub = [ctx.sbuf(f"wub{i}", [128, 8 * 128], BF16) for i in range(2)]
    wgf = [ctx.sbuf("wgf0", [128, 8 * 128], F32)] * 2
    wuf = [ctx.sbuf("wuf0", [128, 8 * 128], F32)] * 2
    wdb = ctx.sbuf("wdb", [128, 11 * DM], BF16)
    wdf = [ctx.sbuf(f"wdf{i}", [128, DM], F32) for i in range(2)]
    Pe = ctx.sbuf("Pe", [128, 11 * T], BF16)
    sg = ctx.sbuf("sg", [128, 512], F32)
    sg2 = ctx.sbuf("sg2", [128, 512], F32)
    ps_h = [ctx.psum(f"ps_h{i}", [128, 512], F32) for i in range(2)]

    def expert_block(tag, n_it, wg_src, wu_src, wd_src, gate_e, dep0):
        pg_dep = {0: ()}
        pu_dep = {1: ()}
        wprev = {0: None, 1: None}
        up_done = []
        wchain = [None]
        for it in range(n_it):
            b = it % 2
            wl = [wchain[0]] if wchain[0] is not None else []
            lw1 = E("scalar", lambda e, it=it, b=b: e.dma_start(out=wgf[b].ap().rearrange("p (a n) -> p a n", a=8), in_=wg_src(it).rearrange("a p n -> p a n")), waits=wl + list(dep0), dma="wgf")
            cw1_ = E("vector", lambda e, b=b: e.tensor_copy(wgb[b][:], wgf[b][:]), waits=[lw1])
            cw1 = ctx.flush("vector")
            lw2 = E("scalar", lambda e, it=it, b=b: e.dma_start(out=wuf[b].ap().rearrange("p (a n) -> p a n", a=8), in_=wu_src(it).rearrange("a p n -> p a n")), waits=[cw1], dma="wuf")
            cw2_ = E("vector", lambda e, b=b: e.tensor_copy(wub[b][:], wuf[b][:]), waits=[lw2])
            cw2 = ctx.flush("vector")
            wprev[b] = cw2
            for tc in range(4):
                pba = 0
                pbb = 1
                mg = None
                for dt in range(NDT):
                    mg = E("tensor", lambda e, it=it, tc=tc, dt=dt, b=b, pba=pba: e.matmul(
                        ps_h[pba][:], wgb[b][:, dt * 128:(dt + 1) * 128],
                        xn_fm[:, dt * T + tc * 512:dt * T + (tc + 1) * 512],
                        start=(dt == 0), stop=(dt == NDT - 1)),
                        waits=([cw1, XN] + list(pg_dep[pba]) if dt == 0 else []))
                mu_ = None
                for dt in range(NDT):
                    mu_ = E("tensor", lambda e, it=it, tc=tc, dt=dt, b=b, pbb=pbb: e.matmul(
                        ps_h[pbb][:], wub[b][:, dt * 128:(dt + 1) * 128],
                        xn_fm[:, dt * T + tc * 512:dt * T + (tc + 1) * 512],
                        start=(dt == 0), stop=(dt == NDT - 1)),
                        waits=([cw2] + list(pu_dep[pbb]) if dt == 0 else []))
                mgd = ctx.flush("tensor")
                a1_ = E("scalar", lambda e, pba=pba: e.activation(sg[:], ps_h[pba][:], AF.Silu), waits=[mgd])
                a1 = ctx.flush("scalar")
                mud = ctx.flush("tensor")
                a2_ = E("vector", lambda e, it=it, tc=tc, pbb=pbb: e.tensor_tensor(
                    Pe[:, it * T + tc * 512: it * T + (tc + 1) * 512], ps_h[pbb][:], sg[:], ALU.mult), waits=[mud, a1])
                a2 = ctx.flush("vector")
                pg_dep[pba] = (a1,)
                pu_dep[pbb] = (a2,)
                up_done.append(a2)
                wchain[0] = mu_
        wd_cast = []
        wdp = {0: None, 1: None}
        for it in range(n_it):
            b = it % 2
            wl = [wdp[b]] if wdp[b] is not None else []
            lw = E("scalar", lambda e, it=it, b=b: e.dma_start(out=wdf[b][:], in_=wd_src(it)), waits=wl, dma="wdf%d" % b)
            cw_ = E("vector", lambda e, it=it, b=b: e.tensor_copy(wdb[:, it * DM:(it + 1) * DM], wdf[b][:]), waits=[lw])
            cw = ctx.flush("vector")
            wdp[b] = cw
            wd_cast.append(cw)
        UP = up_done[-1]
        UPS = p.barrier("scalar")
        WD = wd_cast[-1]
        dn_dep = {0: (), 1: ()}
        outs = []
        for tt in range(NT):
            for hf in range(2):
                pb2 = hf
                md = None
                for it in range(n_it):
                    md = E("tensor", lambda e, tt=tt, hf=hf, it=it, pb2=pb2: e.matmul(
                        ps_h[pb2][:], Pe[:, it * T + tt * 128: it * T + (tt + 1) * 128],
                        wdb[:, it * DM + hf * 512: it * DM + (hf + 1) * 512],
                        start=(it == 0), stop=(it == n_it - 1)),
                        waits=([UP, UPS, WD] + list(dn_dep[pb2]) if it == 0 else []))
                mdd = ctx.flush("tensor")
                if gate_e is None:
                    o1 = E("vector", lambda e, tt=tt, hf=hf, pb2=pb2: e.tensor_tensor(
                        ysum[:, tt * DM + hf * 512: tt * DM + (hf + 1) * 512],
                        ysum[:, tt * DM + hf * 512: tt * DM + (hf + 1) * 512], ps_h[pb2][:], ALU.add), waits=[mdd, zs])
                else:
                    o0 = E("vector", lambda e, tt=tt, pb2=pb2, ge=gate_e: e.tensor_scalar(
                        sg2[:], ps_h[pb2][:], gates[:, tt * 8 + ge: tt * 8 + ge + 1], None, ALU.mult), waits=[mdd, GATES])
                    o1 = E("vector", lambda e, tt=tt, hf=hf: e.tensor_tensor(
                        ysum[:, tt * DM + hf * 512: tt * DM + (hf + 1) * 512],
                        ysum[:, tt * DM + hf * 512: tt * DM + (hf + 1) * 512], sg2[:], ALU.add), waits=[o0, zs])
                o1f = ctx.flush("vector")
                dn_dep[pb2] = (o1f,)
                outs.append(o1f)
        return outs[-1]

    dep0 = QKVB
    n_experts = 8
    for ex in range(n_experts):
        last = expert_block(f"e{ex}", 11,
                            (lambda it, ex=ex: wg_d[ex, it, :, :, :]),
                            (lambda it, ex=ex: wu_d[ex, it, :, :, :]),
                            (lambda it, ex=ex: wd_d[ex, it, :, :]),
                            ex, dep0)
        dep0 = p.all_barriers()
    for half in range(2):
        last = expert_block(f"sh{half}", 11,
                            (lambda it, half=half: swg_d[half * 11 + it, :, :, :]),
                            (lambda it, half=half: swu_d[half * 11 + it, :, :, :]),
                            (lambda it, half=half: swd_d[half * 11 + it, :, :]),
                            None, dep0)
        dep0 = p.all_barriers()

    g0st = [ctx.sbuf(f"g0st{i}", [128, DM], F32) for i in range(2)]
    g_stores = []
    for tt in range(NT):
        b = tt % 2
        wl = [g_stores[-2]] if len(g_stores) >= 2 else []
        a1 = E("scalar", lambda e, tt=tt, b=b: e.activation(g0st[b][:], ysum[:, tt * DM:(tt + 1) * DM], AF.Silu), waits=[last] + wl)
        fls = ctx.flush("scalar")
        st = E("sync", lambda e, tt=tt, b=b: e.dma_start(out=g0_o[tt * 128:(tt + 1) * 128, :], in_=g0st[b][:]), waits=[fls], dma="g0st%d" % b)
        g_stores.append(st)

    return





def phase2(ctx):
    p = ctx.p
    E = p.emit
    QP_d = ctx.din("QP", [NH, 65, T], BF16)
    K_d = ctx.din("K_all", [4 * DM, T], BF16)
    V_d = ctx.din("V_all", [S, DM], BF16)
    g0_d = ctx.din("g0", [T, DM])
    x_d = ctx.din("x", [T, DM])
    wout_d = ctx.din("wout", [DM, DM])
    bout_d = ctx.din("bout128", [128, DM])
    id_d = ctx.din("ident", [128, 128])

    x1_o = ctx.dout("x1", [T, DM])
    xn1_o = ctx.dout("xn1", [T, DM])
    xn1fm_o = ctx.dout("xn1fm", [128, NDT * T], BF16)

    identt = ctx.sbuf("identt", [128, 128], F32)
    identb = ctx.sbuf("identb", [128, 128], BF16)
    eps_t = ctx.sbuf("eps_t", [128, 1], F32)
    li = E("sync", lambda e: e.dma_start(out=identt[:], in_=id_d[:]), dma="const")
    ci = E("vector", lambda e: e.tensor_copy(identb[:], identt[:]), waits=[li])
    cid = ctx.flush("vector")
    mz1 = E("vector", lambda e: e.memset(eps_t[:], 1e-5))

    pt_guard = [None, None]
    AO = [ctx.sbuf(f"AO{i}", [128, T], BF16) for i in range(8)]
    qpb = [ctx.sbuf(f"qpb{i}", [65, T], BF16) for i in range(2)]
    Kh = [ctx.sbuf(f"Kh{i}", [65, S], BF16) for i in range(2)]
    vtm = [ctx.sbuf(f"vtm{i}", [128, 64 * 65], BF16) for i in range(2)]
    Pt = [ctx.sbuf(f"Pt{i}", [128, 512], BF16) for i in range(2)]
    ones64 = ctx.sbuf("ones64", [1, 64], F32)
    rzf = ctx.sbuf("rzf", [1, 512], F32)
    ofb = ctx.sbuf("ofb", [64, 512], BF16)
    ps_S = [ctx.psum(f"ps_S{i}", [128, 512], F32) for i in range(2)]
    ps_O = ctx.psum("ps_O", [65, 512], F32)
    ps_rz = ctx.psum("ps_rz", [64, 512], F32)
    ps_s0 = [ctx.psum(f"ps_s0{i}", [128, 512], F32) for i in range(2)]
    ps_tp = ctx.psum("ps_tp", [128, 128], BF16)

    mz2 = E("vector", lambda e: e.memset(ones64[:], 1.0))
    for i in range(2):
        E("vector", lambda e, i=i: e.memset(Kh[i][64:65, :], -1.0))
        E("vector", lambda e, i=i: e.memset(vtm[i].ap().rearrange("p (a n) -> p a n", a=64)[:, :, 64:65], 1.0))
    mzdr = ctx.flush("vector")

    V_r = V_d.ap().rearrange("(a p) n -> p a n", p=128)
    head_guard = [None, None]
    qb_guard = None
    ao_done = []
    for h in range(NH):
        hb = h % 2
        hw = [x for x in [head_guard[hb], mzdr] if x is not None]
        lqp = E("sync", lambda e, h=h, hb=hb: e.dma_start(out=qpb[hb][0:65, :], in_=QP_d[h, :, :]),
                waits=hw, dma=f"qp{hb}")
        lks = []
        for r in range(4):
            lks.append(E("sync", lambda e, h=h, r=r, hb=hb: e.dma_start(
                out=Kh[hb][0:64, r * T:(r + 1) * T], in_=K_d[r * DM + h * 64: r * DM + (h + 1) * 64, :]),
                waits=hw, dma=f"kh{hb}"))
        lvs = []
        for q4 in range(4):
            lvs.append(E("gpsimd", lambda e, h=h, q4=q4, hb=hb: e.dma_start(
                out=vtm[hb].ap().rearrange("p (a n) -> p a n", a=64)[:, q4 * 16:(q4 + 1) * 16, 0:64],
                in_=V_r[:, q4 * 16:(q4 + 1) * 16, h * 64:(h + 1) * 64]),
                waits=hw, dma=f"vh{hb}"))
        for qb in range(4):
            for kt2 in range(32):
                mw = []
                for j in range(2):
                    kt = kt2 * 2 + j
                    mw.append(E("tensor", lambda e, kt=kt, qb=qb, j=j, hb=hb: e.matmul(
                        ps_S[j][:], Kh[hb][:, kt * 128:(kt + 1) * 128], qpb[hb][0:65, qb * 512:(qb + 1) * 512],
                        start=True, stop=True),
                        waits=[lks[kt // 16], lqp] + [x for x in [pt_guard[j]] if x is not None]))
                m1d = ctx.flush("tensor")
                for j in range(2):
                    E("scalar", lambda e, j=j: e.activation(Pt[j][:], ps_S[j][:], AF.Exp, scale=0.125), waits=[m1d])
                a1d = ctx.flush("scalar")
                pt_guard[0] = a1d
                pt_guard[1] = a1d
                for j in range(2):
                    kt = kt2 * 2 + j
                    E("tensor", lambda e, kt=kt, j=j, hb=hb: e.matmul(
                        ps_O[:], vtm[hb].ap().rearrange("p (a n) -> p a n", a=64)[:, kt:kt + 1, :].squeeze(1), Pt[j][:],
                        start=(kt == 0), stop=(kt == 63)),
                        waits=[a1d, lvs[kt // 16]] + [x for x in [qb_guard] if (kt == 0 and x is not None)])
            omd = ctx.flush("tensor")
            r1 = E("vector", lambda e: e.reciprocal(rzf[:], ps_O[64:65, :]), waits=[omd])
            r1d = ctx.flush("vector")
            mz_ = E("tensor", lambda e: e.matmul(ps_rz[:], ones64[:], rzf[:], start=True, stop=True), waits=[r1d, mz2])
            mzd = ctx.flush("tensor")
            c1 = E("scalar", lambda e: e.copy(ofb[:], ps_O[0:64, :]), waits=[omd])
            c1d = ctx.flush("scalar")
            o1 = E("vector", lambda e, h=h, qb=qb: e.tensor_tensor(
                AO[h // 2][(h % 2) * 64:(h % 2) * 64 + 64, qb * 512:(qb + 1) * 512], ofb[:], ps_rz[:], ALU.mult),
                waits=[mzd, c1d])
            o1d = ctx.flush("vector")
            ao_done.append(o1d)
            qb_guard = o1d
        head_guard[hb] = qb_guard
    AODONE = p.barrier("vector")

    wof = [ctx.sbuf(f"wof{i}", [128, DM], F32) for i in range(2)]
    wob = [ctx.sbuf(f"wob{i}", [128, DM], BF16) for i in range(8)]
    wcast = []
    wprev = {0: None, 1: None}
    for hp in range(8):
        b = hp % 2
        lw = E("sync", lambda e, hp=hp, b=b: e.dma_start(out=wof[b][:], in_=wout_d[hp * 128:(hp + 1) * 128, :]),
               waits=[x for x in [wprev[b]] if x is not None], dma=f"wof{b}")
        cw = E("vector", lambda e, hp=hp, b=b: e.tensor_copy(wob[hp][:], wof[b][:]), waits=[lw])
        cwd = ctx.flush("vector")
        wprev[b] = cwd
        wcast.append(cwd)
    bout = ctx.sbuf("bout", [128, DM], F32)
    lbo = E("sync", lambda e: e.dma_start(out=bout[:], in_=bout_d[:]), dma="const")

    tmp = {k: ctx.sbuf("lntmp_" + k, [128, 1], F32) for k in ["mu", "negmu", "ssq", "var", "rstd"]}
    tmp["sq"] = ctx.sbuf("lntmp_sq", [128, DM], F32)
    tmp["junk"] = ctx.sbuf("lntmp_junk", [128, DM], F32)
    tmp["eps_t"] = eps_t
    xb = [ctx.sbuf(f"xb{i}", [128, DM], F32) for i in range(2)]
    g0b = [ctx.sbuf(f"g0b{i}", [128, DM], F32) for i in range(2)]
    x1t = ctx.sbuf("x1t", [128, DM], F32)
    s0f = ctx.sbuf("s0f", [128, DM], F32)
    xn32 = ctx.sbuf("xn32", [128, DM], F32)
    xnb = ctx.sbuf("xnb", [128, DM], BF16)
    ctx.xnfm_st = ctx.sbuf("xnfm_st", [128, 128], BF16)
    xg_guard = {0: None, 1: None}
    st_guard = {"x1": None, "xn1": None, "s0": None, "t1last": None, "fmst": None}
    for tt in range(NT):
        b = tt % 2
        gw = [x for x in [xg_guard[b]] if x is not None]
        lx = E("sync", lambda e, tt=tt, b=b: e.dma_start(out=xb[b][:], in_=x_d[tt * 128:(tt + 1) * 128, :]), waits=gw, dma=f"xb{b}")
        lg = E("scalar", lambda e, tt=tt, b=b: e.dma_start(out=g0b[b][:], in_=g0_d[tt * 128:(tt + 1) * 128, :]), waits=gw, dma=f"g0b{b}")
        mm = {}
        for half in range(2):
            m_ = None
            for hp in range(8):
                m_ = E("tensor", lambda e, tt=tt, half=half, hp=hp: e.matmul(
                    ps_s0[half][:], AO[hp][:, tt * 128:(tt + 1) * 128], wob[hp][:, half * 512:(half + 1) * 512],
                    start=(hp == 0), stop=(hp == 7)),
                    waits=([AODONE, wcast[-1]] + [x for x in [st_guard["s0"]] if x is not None] if hp == 0 else []))
            mm[half] = m_
        mmd = ctx.flush("tensor")
        e1 = None
        for half in range(2):
            sl = slice(half * 512, (half + 1) * 512)
            e1 = E("vector", lambda e, half=half, sl=sl: e.tensor_tensor(s0f[:, sl], ps_s0[half][:], bout[:, sl], ALU.add),
                   waits=[mmd, lbo] + [x for x in [st_guard["x1"]] if x is not None])
        e2 = E("vector", lambda e, b=b: e.tensor_tensor(s0f[:], s0f[:], g0b[b][:], ALU.mult), waits=[e1, lg])
        e3 = E("vector", lambda e, b=b: e.tensor_tensor(x1t[:], s0f[:], xb[b][:], ALU.add), waits=[e2, lx])
        e3d = ctx.flush("vector")
        st_guard["s0"] = e3d
        xg_guard[b] = e3d
        sx = E("gpsimd", lambda e, tt=tt: e.dma_start(out=x1_o[tt * 128:(tt + 1) * 128, :], in_=x1t[:]),
               waits=[e3d] + [x for x in [st_guard["x1"]] if x is not None], dma="x1st")
        st_guard["x1"] = sx
        lnt = ln_tile(ctx, E, x1t, xn32, tmp, [e3d, mz1] + [x for x in [st_guard["xn1"]] if x is not None])
        lnd = ctx.flush("vector")
        sn = E("gpsimd", lambda e, tt=tt: e.dma_start(out=xn1_o[tt * 128:(tt + 1) * 128, :], in_=xn32[:]),
               waits=[lnd], dma="xn1st")
        st_guard["xn1"] = sn
        cb = E("vector", lambda e: e.tensor_copy(xnb[:], xn32[:]),
               waits=[lnd] + [x for x in [st_guard["t1last"]] if x is not None])
        cbd = ctx.flush("vector")
        for dt in range(NDT):
            t1 = E("tensor", lambda e, dt=dt: e.transpose(ps_tp[:], xnb[:, dt * 128:(dt + 1) * 128], identb[:]),
                   waits=[cbd, cid] + [x for x in [st_guard["fmst"]] if x is not None])
            t1d = ctx.flush("tensor")
            t2 = E("vector", lambda e: e.tensor_copy(ctx.xnfm_st[:], ps_tp[:]), waits=[t1d])
            t2d = ctx.flush("vector")
            t3 = E("gpsimd", lambda e, dt=dt, tt=tt: e.dma_start(
                out=xn1fm_o[:, dt * T + tt * 128: dt * T + (tt + 1) * 128], in_=ctx.xnfm_st[:]), waits=[t2d], dma="xnfmst")
            st_guard["fmst"] = t3
            st_guard["t1last"] = t1d
    return


def build_launch1():
    ctx = Ctx("l1")
    p = ctx.start(); ctx.p = p; ctx.mkflush()
    phase1(ctx)
    return ctx.done(p.all_barriers())


def build_launch2():
    ctx = Ctx("l2")
    p = ctx.start(); ctx.p = p; ctx.mkflush()
    phase2(ctx)
    return ctx.done(p.all_barriers())


def host_prep_l1(inputs):
    x = np.asarray(inputs["L0_attn_x"])
    wqkv = np.ascontiguousarray(inputs["L0_attn_W_qkv"]).astype(np.float32)
    bqkv = np.asarray(inputs["L0_attn_b_qkv"]).astype(np.float32)
    bqkv_r = np.ascontiguousarray(bqkv.reshape(24, 128).T)
    bv128 = np.ascontiguousarray(np.broadcast_to(bqkv[2048:3072][None, :], (128, 1024)))
    pe = rel_pos_enc(S, HD)
    gate_wT = np.ascontiguousarray(np.asarray(inputs["L0_ffn_gate_w"]).astype(np.float32).T)
    wg = np.asarray(inputs["L0_ffn_wg"]).astype(np.float32)
    wu = np.asarray(inputs["L0_ffn_wu"]).astype(np.float32)
    wd = np.asarray(inputs["L0_ffn_wd"]).astype(np.float32)
    wg_t = np.ascontiguousarray(wg.reshape(8, 8, 128, 11, 128).transpose(0, 3, 1, 2, 4))
    wu_t = np.ascontiguousarray(wu.reshape(8, 8, 128, 11, 128).transpose(0, 3, 1, 2, 4))
    wd_t = np.ascontiguousarray(wd.reshape(8, 11, 128, 1024))
    swg = np.asarray(inputs["L0_ffn_swg"]).astype(np.float32)
    swu = np.asarray(inputs["L0_ffn_swu"]).astype(np.float32)
    swd = np.asarray(inputs["L0_ffn_swd"]).astype(np.float32)
    swg_t = np.ascontiguousarray(swg.reshape(8, 128, 22, 128).transpose(2, 0, 1, 3))
    swu_t = np.ascontiguousarray(swu.reshape(8, 128, 22, 128).transpose(2, 0, 1, 3))
    swd_t = np.ascontiguousarray(swd.reshape(22, 128, 1024))
    ident = np.eye(128, dtype=np.float32)
    maps = []
    for c in range(NCORES):
        b, r = c // 4, c % 4
        sl = slice(r * T, (r + 1) * T)
        pe_own = pe[sl].T
        pe2 = np.ascontiguousarray(np.concatenate([pe_own, pe_own], 0))
        maps.append({
            "x": np.ascontiguousarray(x[b, sl]).astype(np.float32),
            "wqkv": wqkv, "bqkv_r": bqkv_r, "pe2": pe2, "bv128": bv128,
            "gate_wT": gate_wT, "wg_t": wg_t, "wu_t": wu_t, "wd_t": wd_t,
            "swg_t": swg_t, "swu_t": swu_t, "swd_t": swd_t, "ident": ident,
        })
    return maps





def phase3a(ctx, i):
    p = ctx.p
    E = p.emit
    mfm_d = ctx.din("m_fm", [128, NDT * T], BF16)
    w4_d = ctx.din("in_wT_t4", [4, 32, NDT, 128, 128])
    xz_o = ctx.dout("xz", [4096, T], BF16)

    mfm = ctx.sbuf("mfm", [128, NDT * T], BF16)
    lm = E("sync", lambda e: e.dma_start(out=mfm[:], in_=mfm_d[:]), dma="const")
    wf = [ctx.sbuf(f"wf{i}", [128, NDT * 128], F32) for i in range(2)]
    wb = [ctx.sbuf(f"wb{i}", [128, NDT * 128], BF16) for i in range(2)]
    stg = [ctx.sbuf(f"stg{i}", [128, 512], BF16) for i in range(2)]
    ps = [ctx.psum(f"ps{i}", [128, 512], F32) for i in range(2)]
    wprev = {0: None, 1: None}
    st_guard = {0: None, 1: None}
    ps_guard = {0: None, 1: None}
    for ot in range(32):
        b = ot % 2
        lw = E("sync", lambda e, ot=ot, b=b, i=i: e.dma_start(
            out=wf[b].ap().rearrange("p (a n) -> p a n", a=NDT), in_=w4_d[i, ot, :, :, :].rearrange("a p n -> p a n")),
            waits=[x for x in [wprev[b]] if x is not None], dma=f"wf{b}")
        cw_ = E("vector", lambda e, b=b: e.tensor_copy(wb[b][:], wf[b][:]), waits=[lw])
        cw = ctx.flush("vector")
        wprev[b] = cw
        for tc in range(4):
            pb = tc % 2
            m_ = None
            for dt in range(NDT):
                m_ = E("tensor", lambda e, ot=ot, tc=tc, dt=dt, b=b, pb=pb: e.matmul(
                    ps[pb][:], wb[b][:, dt * 128:(dt + 1) * 128], mfm[:, dt * T + tc * 512: dt * T + (tc + 1) * 512],
                    start=(dt == 0), stop=(dt == NDT - 1)),
                    waits=([cw, lm] + [x for x in [ps_guard[pb]] if x is not None] if dt == 0 else []))
            md = ctx.flush("tensor")
            sb = tc % 2
            c1 = E("scalar", lambda e, pb=pb, sb=sb: e.copy(stg[sb][:], ps[pb][:]),
                   waits=[md] + [x for x in [st_guard[sb]] if x is not None] + (p.all_barriers() if DBG_SER else []))
            c1d = ctx.flush("scalar")
            ps_guard[pb] = c1d
            st = E("gpsimd", lambda e, ot=ot, tc=tc, sb=sb: e.dma_start(
                out=xz_o[ot * 128:(ot + 1) * 128, tc * 512:(tc + 1) * 512], in_=stg[sb][:]), waits=[c1d], dma=f"st{sb}")
            st_guard[sb] = st
        wprev[b] = md
    return



def phase3b(ctx, i):
    p = ctx.p
    E = p.emit
    xz_d = ctx.din("xz", [4096, T], BF16)
    tails_d = ctx.din("tails_ag", [4 * 2048, 3], BF16)
    ohprev_d = ctx.din("ohprev", [128, 4])
    cw_d = ctx.din("cw_r4", [4, 16, 128, 4])[i]
    cb_d = ctx.din("cb_r4", [4, 128, 16])[i]
    xpwT_d = ctx.din("xp_wT_t4", [4, 16, 128, 96])[i]
    dtwT_d = ctx.din("dt_wT_t4", [4, 16, 64, 128])[i]
    dtb_d = ctx.din("dtb_r4", [4, 128, 16])[i]
    alog_d = ctx.din("alog_r4", [4, 16, 128, 16])[i]
    dskip_d = ctx.din("dskip_r4", [4, 128, 16])[i]
    id_d = ctx.din("ident", [128, 128])

    xc_o = ctx.dout("xc", [2048, T], BF16)
    cumd_o = ctx.dout("cumd", [2048, T])
    bc_o = ctx.dout("bc", [32, T], BF16)
    s_o = ctx.dout("s_sum", [2048, 16])
    pc_o = ctx.dout("p_sum", [2048, 16])
    yl_o = ctx.dout("y_local", [2048, T], BF16)

    identt = ctx.sbuf("identt", [128, 128], F32)
    identb = ctx.sbuf("identb", [128, 128], BF16)
    li = E("sync", lambda e: e.dma_start(out=identt[:], in_=id_d[:]), dma="const")
    ci_ = E("vector", lambda e: e.tensor_copy(identb[:], identt[:]), waits=[li])
    ci = ctx.flush("vector")
    cwt = ctx.sbuf("cwt", [128, 16 * 4], F32)
    cbt = ctx.sbuf("cbt", [128, 16], F32)
    xpw = ctx.sbuf("xpw", [128, 16 * 96], F32)
    xpwb = ctx.sbuf("xpwb", [128, 16 * 96], BF16)
    dtw = ctx.sbuf("dtw", [64, 16 * 128], F32)
    dtwb = ctx.sbuf("dtwb", [64, 16 * 128], BF16)
    dtb = ctx.sbuf("dtb", [128, 16], F32)
    alog = ctx.sbuf("alog", [128, 16 * 16], F32)
    at = ctx.sbuf("at", [128, 16 * 16], F32)
    dskip = ctx.sbuf("dskip", [128, 16], F32)
    lc = []
    lc.append(E("sync", lambda e: e.dma_start(out=cwt.ap().rearrange("p (a n) -> p a n", a=16), in_=cw_d.rearrange("a p n -> p a n")), dma="const"))
    lc.append(E("sync", lambda e: e.dma_start(out=cbt[:], in_=cb_d[:]), dma="const"))
    lc.append(E("sync", lambda e: e.dma_start(out=xpw.ap().rearrange("p (a n) -> p a n", a=16), in_=xpwT_d.rearrange("a p n -> p a n")), dma="const"))
    lc.append(E("sync", lambda e: e.dma_start(out=dtw.ap().rearrange("p (a n) -> p a n", a=16), in_=dtwT_d.rearrange("a p n -> p a n")), dma="const"))
    lc.append(E("sync", lambda e: e.dma_start(out=dtb[:], in_=dtb_d[:]), dma="const"))
    lc.append(E("sync", lambda e: e.dma_start(out=alog.ap().rearrange("p (a n) -> p a n", a=16), in_=alog_d.rearrange("a p n -> p a n")), dma="const"))
    lc.append(E("sync", lambda e: e.dma_start(out=dskip[:], in_=dskip_d[:]), dma="const"))
    LC = [p.dma_total("const")]
    c1 = E("vector", lambda e: e.tensor_copy(xpwb[:], xpw[:]), waits=LC)
    c2 = E("vector", lambda e: e.tensor_copy(dtwb[:], dtw[:]), waits=LC)
    CW = ctx.flush("vector")
    a1 = E("scalar", lambda e: e.activation(at[:], alog[:], AF.Exp), waits=LC)
    a2 = E("vector", lambda e: e.tensor_scalar(at[:], at[:], -1.0, None, ALU.mult), waits=[a1])
    AT = ctx.flush("vector")

    xsp = [ctx.sbuf(f"xsp{j}", [128, T + 3], BF16) for j in range(2)]
    ohp = ctx.sbuf("ohp", [128, 4], F32)
    lohp = E("sync", lambda e: e.dma_start(out=ohp[:], in_=ohprev_d[:]), dma="const")
    tl4 = ctx.sbuf("tl4", [128, 4 * 3], BF16)
    hacc = ctx.sbuf("hacc", [128, 3], F32)
    acc1 = ctx.sbuf("acc1", [128, T], F32)
    acc2 = ctx.sbuf("acc2", [128, T], F32)
    xct = [ctx.sbuf(f"xct{i}", [128, T], BF16) for i in range(2)]
    ps_dbl = [ctx.psum(f"ps_dbl{i}", [128, 512], F32) for i in range(4)]
    xs_guard = {0: None, 1: None}
    xc_guard = {0: None, 1: None}
    hguard = [None]
    xc_stores = []
    dbl_mm = None
    for cht in range(16):
        b = cht % 2
        gw = [x for x in [xs_guard[b]] if x is not None]
        lh4 = E("sync", lambda e, cht=cht: e.dma_start(
            out=tl4.ap().rearrange("p (a n) -> p a n", a=4),
            in_=tails_d.ap().rearrange("(a q) n -> a q n", a=4)[:, cht * 128:(cht + 1) * 128, :].rearrange("a p n -> p a n")),
            waits=gw + [x for x in [hguard[0]] if x is not None], dma="tl4")
        hc1 = E("vector", lambda e: e.tensor_scalar(hacc[:], tl4[:, 0:3], ohp[:, 0:1], None, ALU.mult), waits=[lh4, lohp])
        hc2 = E("vector", lambda e: e.scalar_tensor_tensor(hacc[:], tl4[:, 3:6], ohp[:, 1:2], hacc[:], ALU.mult, ALU.add), waits=[hc1])
        hc3 = E("vector", lambda e: e.scalar_tensor_tensor(hacc[:], tl4[:, 6:9], ohp[:, 2:3], hacc[:], ALU.mult, ALU.add), waits=[hc2])
        hc4 = E("vector", lambda e, b=b: e.tensor_copy(xsp[b][:, 0:3], hacc[:]), waits=[hc3])
        lh = ctx.flush("vector")
        hguard[0] = lh
        lx = E("sync", lambda e, cht=cht, b=b: e.dma_start(out=xsp[b][:, 3:T + 3], in_=xz_d[cht * 128:(cht + 1) * 128, :]), waits=gw, dma=f"xs{b}")
        e1 = E("vector", lambda e, b=b, cht=cht: e.tensor_scalar(acc1[:], xsp[b][:, 0:T], cwt[:, cht * 4:cht * 4 + 1], None, ALU.mult), waits=[lh, lx] + LC)
        e2 = E("vector", lambda e, b=b, cht=cht: e.scalar_tensor_tensor(acc2[:], xsp[b][:, 1:T + 1], cwt[:, cht * 4 + 1:cht * 4 + 2], acc1[:], ALU.mult, ALU.add), waits=[e1])
        e3 = E("vector", lambda e, b=b, cht=cht: e.scalar_tensor_tensor(acc1[:], xsp[b][:, 2:T + 2], cwt[:, cht * 4 + 2:cht * 4 + 3], acc2[:], ALU.mult, ALU.add), waits=[e2])
        e4 = E("vector", lambda e, b=b, cht=cht: e.scalar_tensor_tensor(acc2[:], xsp[b][:, 3:T + 3], cwt[:, cht * 4 + 3:cht * 4 + 4], acc1[:], ALU.mult, ALU.add), waits=[e3])
        e4d = ctx.flush("vector")
        xs_guard[b] = e4d
        a3 = E("scalar", lambda e, b=b, cht=cht: e.activation(xct[b][:], acc2[:], AF.Silu, bias=cbt[:, cht:cht + 1]),
               waits=[e4d] + [x for x in [xc_guard[b]] if x is not None])
        a3d = ctx.flush("scalar")
        stx = E("gpsimd", lambda e, cht=cht, b=b: e.dma_start(out=xc_o[cht * 128:(cht + 1) * 128, :], in_=xct[b][:]), waits=[a3d], dma=f"xcst{b}")
        xc_stores.append(stx)
        xc_guard[b] = stx
        for tc in range(4):
            dbl_mm = E("tensor", lambda e, cht=cht, tc=tc, b=b: e.matmul(
                ps_dbl[tc][0:96, :], xpwb[:, cht * 96:(cht + 1) * 96], xct[b][:, tc * 512:(tc + 1) * 512],
                start=(cht == 0), stop=(cht == 15)), waits=([a3d, CW] if True else []))
    dbld = ctx.flush("tensor")
    dts = ctx.sbuf("dts", [64, T], BF16)
    bcs = ctx.sbuf("bcs", [32, T], BF16)
    ev = []
    for tc in range(4):
        ev.append(E("scalar", lambda e, tc=tc: e.copy(dts[:, tc * 512:(tc + 1) * 512], ps_dbl[tc][0:64, :]), waits=[dbld]))
        ev.append(E("scalar", lambda e, tc=tc: e.copy(bcs[:, tc * 512:(tc + 1) * 512], ps_dbl[tc][64:96, :]), waits=[dbld]))
    evd = ctx.flush("scalar")
    stbc = E("gpsimd", lambda e: e.dma_start(out=bc_o[:, :], in_=bcs[:]), waits=[evd], dma="bcst")

    ones_t = ctx.sbuf("ones_t", [128, T], F32)
    mzo = E("vector", lambda e: e.memset(ones_t[:], 1.0))
    delta = ctx.sbuf("delta", [128, T], F32)
    cumd = ctx.sbuf("cumd_t", [128, T], F32)
    ut = ctx.sbuf("ut", [128, T], BF16)
    sp1 = ctx.sbuf("sp1", [128, T], F32)
    sp2 = ctx.sbuf("sp2", [128, T], F32)
    av = ctx.sbuf("av", [128, T], F32)
    rv = ctx.sbuf("rv", [128, T], BF16)
    hv = ctx.sbuf("hv", [128, T], BF16)
    pv = ctx.sbuf("pv", [128, T], BF16)
    udv = ctx.sbuf("udv", [128, T], BF16)
    bbb = [ctx.sbuf(f"bbb{i}", [128, T], BF16) for i in range(2)]
    cbb = [ctx.sbuf(f"cbb{i}", [128, T], BF16) for i in range(2)]
    carry = ctx.sbuf("carry", [128, 1], F32)
    ssum = ctx.sbuf("ssum", [128, 16], F32)
    psum_t = ctx.sbuf("psum_t", [128, 16], F32)
    xc2 = [ctx.sbuf(f"xc2{i}", [128, T], BF16) for i in range(2)]
    ps_del = [ctx.psum(f"ps_del{i}", [128, 512], F32) for i in range(2)]
    ps_y = ps_dbl
    totd = ctx.sbuf("totd", [128, 1], F32)
    yst = ctx.sbuf("yst", [128, T], BF16)
    xc2_guard = {0: None, 1: None}
    bcb_guard = {0: None, 1: None}
    st2_guard = {"cumd": None, "s": None, "pc": None, "yl": None}
    del_guard = [None]
    sy_guard = [None]
    for cht in range(16):
        b = cht % 2
        lxc = E("sync", lambda e, cht=cht, b=b: e.dma_start(out=xc2[b][:], in_=xc_o[cht * 128:(cht + 1) * 128, :]),
                waits=[xc_stores[cht]] + [x for x in [xc2_guard[b]] if x is not None], dma=f"xc2{b}")
        dmm = {}
        for tc in range(4):
            pb = tc % 2
            dmm[tc] = E("tensor", lambda e, cht=cht, tc=tc, pb=pb: e.matmul(
                ps_del[pb][:], dtwb[:, cht * 128:(cht + 1) * 128], dts[:, tc * 512:(tc + 1) * 512],
                start=True, stop=True), waits=([evd, CW, dbld] + [x for x in [del_guard[0]] if x is not None] if tc < 2 else ([del_guard[0]] if tc == 2 else [])))
            if tc == 1 or tc == 3:
                dd = ctx.flush("tensor")
                for tcc in ([0, 1] if tc == 1 else [2, 3]):
                    sl = slice(tcc * 512, (tcc + 1) * 512)
                    E("scalar", lambda e, cht=cht, tcc=tcc, sl=sl: e.activation(
                        sp1[:, sl], ps_del[tcc % 2][:], AF.Abs, bias=dtb[:, cht:cht + 1]), waits=[dd])
                    E("scalar", lambda e, cht=cht, tcc=tcc, sl=sl: e.activation(
                        sp2[:, sl], ps_del[tcc % 2][:], AF.Relu, bias=dtb[:, cht:cht + 1]), waits=[dd])
                del_guard[0] = ctx.flush("scalar")
        sfa = E("scalar", lambda e: e.activation(sp1[:], sp1[:], AF.Exp, scale=-1.0))
        sfb = E("scalar", lambda e: e.activation(sp1[:], sp1[:], AF.Ln, bias=1.0), waits=[sfa])
        sfd = ctx.flush("scalar")
        dl1 = E("vector", lambda e: e.tensor_tensor(delta[:], sp1[:], sp2[:], ALU.add), waits=[sfd])
        cd1 = E("vector", lambda e: e.tensor_tensor_scan(cumd[:], ones_t[:], delta[:], 0.0, ALU.mult, ALU.add), waits=[dl1, mzo])
        cdd = ctx.flush("vector")
        sc1 = E("gpsimd", lambda e, cht=cht: e.dma_start(out=cumd_o[cht * 128:(cht + 1) * 128, :], in_=cumd[:]),
                waits=[cdd] + [x for x in [st2_guard["cumd"]] if x is not None], dma="cumdst")
        st2_guard["cumd"] = sc1
        u1 = E("vector", lambda e, b=b: e.tensor_tensor(ut[:], delta[:], xc2[b][:], ALU.mult), waits=[cdd, lxc])
        tc1 = E("vector", lambda e: e.tensor_copy(totd[:], cumd[:, T - 1:T]), waits=[cdd])
        tcd = ctx.flush("vector")
        pc1 = E("scalar", lambda e, cht=cht: e.activation(psum_t[:], at[:, cht * 16:(cht + 1) * 16], AF.Exp, scale=totd[:]), waits=[tcd, AT])
        pcd = ctx.flush("scalar")
        sp_st = E("gpsimd", lambda e, cht=cht: e.dma_start(out=pc_o[cht * 128:(cht + 1) * 128, :], in_=psum_t[:]),
                  waits=[pcd] + [x for x in [st2_guard["pc"]] if x is not None], dma="pcst")
        st2_guard["pc"] = sp_st
        ymm = None
        for n in range(16):
            bb = n % 2
            lb_ = E("sync", lambda e, n=n, bb=bb: e.dma_start(out=bbb[bb].ap().unsqueeze(1), in_=bc_o[n:n + 1, :].partition_broadcast(128)),
                    waits=[stbc] + [x for x in [bcb_guard[bb]] if x is not None], dma=f"bcb{bb}")
            lc_ = E("scalar", lambda e, n=n, bb=bb: e.dma_start(out=cbb[bb].ap().unsqueeze(1), in_=bc_o[16 + n:17 + n, :].partition_broadcast(128)),
                    waits=[stbc] + [x for x in [bcb_guard[bb]] if x is not None], dma=f"ccb{bb}")
            aa1 = E("scalar", lambda e, cht=cht, n=n: e.activation(av[:], delta[:], AF.Exp, scale=at[:, cht * 16 + n:cht * 16 + n + 1]),
                    waits=[AT] + ([ymm] if ymm else []))
            aad = ctx.flush("scalar")
            rr1 = E("vector", lambda e, bb=bb: e.tensor_tensor(rv[:], ut[:], bbb[bb][:], ALU.mult), waits=[u1, lb_, aad])
            hh1 = E("vector", lambda e: e.tensor_tensor_scan(hv[:], av[:], rv[:], 0.0, ALU.mult, ALU.add), waits=[rr1])
            cc1 = E("vector", lambda e, n=n: e.tensor_copy(ssum[:, n:n + 1], hv[:, T - 1:T]), waits=[hh1])
            pp1 = E("vector", lambda e, bb=bb: e.tensor_tensor(pv[:], hv[:], cbb[bb][:], ALU.mult),
                    waits=[cc1, lc_] + ([ymm] if ymm else []))
            ppd = ctx.flush("vector")
            bcb_guard[bb] = ppd
            for tc in range(4):
                ymm = E("tensor", lambda e, n=n, tc=tc: e.matmul(
                    ps_y[tc][:], identb[:], pv[:, tc * 512:(tc + 1) * 512], start=(n == 0), stop=False),
                    waits=[ppd, ci] + ([sy_guard[0]] if (n == 0 and sy_guard[0] is not None) else []))
        ud1 = E("vector", lambda e, b=b, cht=cht: e.tensor_scalar(udv[:], xc2[b][:], dskip[:, cht:cht + 1], None, ALU.mult),
                waits=[lxc] + ([ymm] if ymm else []))
        udd = ctx.flush("vector")
        for tc in range(4):
            ymm = E("tensor", lambda e, tc=tc: e.matmul(
                ps_y[tc][:], identb[:], udv[:, tc * 512:(tc + 1) * 512], start=False, stop=True), waits=[udd])
        ymd = ctx.flush("tensor")
        xc2_guard[b] = ymd
        sy = []
        for tc in range(4):
            sy.append(E("scalar", lambda e, tc=tc: e.copy(yst[:, tc * 512:(tc + 1) * 512], ps_y[tc][:]),
                        waits=[ymd] + [x for x in [st2_guard["yl"]] if x is not None]))
        syd = ctx.flush("scalar")
        sy_guard[0] = syd
        sy_st = E("gpsimd", lambda e, cht=cht: e.dma_start(out=yl_o[cht * 128:(cht + 1) * 128, :], in_=yst[:]),
                  waits=[syd] + [x for x in [st2_guard["yl"]] if x is not None], dma="ylst")
        st2_guard["yl"] = sy_st
        ss_st = E("gpsimd", lambda e, cht=cht: e.dma_start(out=s_o[cht * 128:(cht + 1) * 128, :], in_=ssum[:]),
                  waits=[ppd] + [x for x in [st2_guard["s"]] if x is not None], dma="sst")
        st2_guard["s"] = ss_st
    return



def phase3c(ctx, i):
    p = ctx.p
    E = p.emit
    yl_d = ctx.din("y_local", [2048, T], BF16)
    cumd_d = ctx.din("cumd", [2048, T])
    bc_d = ctx.din("bc", [32, T], BF16)
    z_d = ctx.din("xz", [4096, T], BF16)
    s_all_d = ctx.din("s_ag", [4 * 2048, 16]).rearrange("(a q) n -> a q n", a=4)
    p_all_d = ctx.din("p_ag", [4 * 2048, 16]).rearrange("(a q) n -> a q n", a=4)
    oh_d = ctx.din("ohsel", [128, 4])
    alog_d = ctx.din("alog_r4", [4, 16, 128, 16])[i]
    outwT_d = ctx.din("out_wT_t4", [4, 16, 128, DM])[i]
    lnw_d = ctx.din("lnw128_4", [4, 128, DM])[i]
    lnb_d = ctx.din("lnb128_4", [4, 128, DM])[i]
    m_d = ctx.din("m_tok", [T, DM])
    id_d = ctx.din("ident", [128, 128])

    mn_o = ctx.dout("m_next", [T, DM])
    mnfm_o = ctx.dout("m_next_fm", [128, NDT * T], BF16)
    gated_o = ctx.dout("gated", [2048, T], BF16)

    identt = ctx.sbuf("identt", [128, 128], F32)
    identb = ctx.sbuf("identb", [128, 128], BF16)
    eps_t = ctx.sbuf("eps_t", [128, 1], F32)
    li = E("sync", lambda e: e.dma_start(out=identt[:], in_=id_d[:]), dma="const")
    ci_ = E("vector", lambda e: e.tensor_copy(identb[:], identt[:]), waits=[li])
    ci = ctx.flush("vector")
    mz1 = E("vector", lambda e: e.memset(eps_t[:], 1e-5))
    alog = ctx.sbuf("alog", [128, 16 * 16], F32)
    at = ctx.sbuf("at", [128, 16 * 16], F32)
    oh = ctx.sbuf("oh", [128, 4], F32)
    lnw = ctx.sbuf("lnw", [128, DM], F32)
    lnb = ctx.sbuf("lnb", [128, DM], F32)
    la = E("sync", lambda e: e.dma_start(out=alog.ap().rearrange("p (a n) -> p a n", a=16), in_=alog_d.rearrange("a p n -> p a n")), dma="const")
    lo = E("sync", lambda e: e.dma_start(out=oh[:], in_=oh_d[:]), dma="const")
    lw_ = E("sync", lambda e: e.dma_start(out=lnw[:], in_=lnw_d[:]), dma="const")
    lb_ = E("sync", lambda e: e.dma_start(out=lnb[:], in_=lnb_d[:]), dma="const")
    a1 = E("scalar", lambda e: e.activation(at[:], alog[:], AF.Exp), waits=[la])
    a2 = E("vector", lambda e: e.tensor_scalar(at[:], at[:], -1.0, None, ALU.mult), waits=[a1])
    AT = ctx.flush("vector")

    sti = [ctx.sbuf(f"sti{i}", [128, 4 * 16], F32) for i in range(2)]
    pti = [ctx.sbuf(f"pti{i}", [128, 4 * 16], F32) for i in range(2)]
    t0 = ctx.sbuf("t0", [128, 16], F32)
    t1_ = ctx.sbuf("t1_", [128, 16], F32)
    t2_ = ctx.sbuf("t2_", [128, 16], F32)
    hin = ctx.sbuf("hin", [128, 16], F32)
    tmpv = ctx.sbuf("tmpv", [128, 16], F32)
    cumd = [ctx.sbuf(f"cumd{i}", [128, T], F32) for i in range(2)]
    ylc = [ctx.sbuf(f"ylc{i}", [128, T], BF16) for i in range(2)]
    zc = [ctx.sbuf(f"zc{i}", [128, T], BF16) for i in range(2)]
    dfv = ctx.sbuf("dfv", [128, T], F32)
    termv = ctx.sbuf("termv", [128, T], BF16)
    szv = ctx.sbuf("szv", [128, T], BF16)
    gtd = ctx.sbuf("gtd", [128, T], BF16)
    cbb = [ctx.sbuf(f"cbb{i}", [128, T], BF16) for i in range(2)]
    ps_y = [ctx.psum(f"ps_y{i}", [128, 512], F32) for i in range(4)]
    ld_guard = {0: None, 1: None}
    bcb_guard = {0: None, 1: None}
    gst_guard = [None]
    for cht in range(16):
        b = cht % 2
        gw = [x for x in [ld_guard[b]] if x is not None]
        ls = E("sync", lambda e, cht=cht, b=b: e.dma_start(out=sti[b].ap().rearrange("p (a n) -> p a n", a=4), in_=s_all_d[:, cht * 128:(cht + 1) * 128, :].rearrange("a p n -> p a n")), waits=gw, dma=f"sti{b}")
        lp = E("sync", lambda e, cht=cht, b=b: e.dma_start(out=pti[b].ap().rearrange("p (a n) -> p a n", a=4), in_=p_all_d[:, cht * 128:(cht + 1) * 128, :].rearrange("a p n -> p a n")), waits=gw, dma=f"sti{b}")
        lcu = E("sync", lambda e, cht=cht, b=b: e.dma_start(out=cumd[b][:], in_=cumd_d[cht * 128:(cht + 1) * 128, :]), waits=gw, dma=f"cud{b}")
        lyl = E("scalar", lambda e, cht=cht, b=b: e.dma_start(out=ylc[b][:], in_=yl_d[cht * 128:(cht + 1) * 128, :]), waits=gw, dma=f"ylc{b}")
        lz = E("scalar", lambda e, cht=cht, b=b: e.dma_start(out=zc[b][:], in_=z_d[2048 + cht * 128: 2048 + (cht + 1) * 128, :]), waits=gw, dma=f"zc{b}")
        h1 = E("vector", lambda e, b=b: e.tensor_copy(t0[:], sti[b][:, 0:16]), waits=[ls, lp])
        h2 = E("vector", lambda e, b=b: e.tensor_tensor(tmpv[:], pti[b][:, 16:32], t0[:], ALU.mult))
        h3 = E("vector", lambda e, b=b: e.tensor_tensor(t1_[:], tmpv[:], sti[b][:, 16:32], ALU.add))
        h4 = E("vector", lambda e, b=b: e.tensor_tensor(tmpv[:], pti[b][:, 32:48], t1_[:], ALU.mult))
        h5 = E("vector", lambda e, b=b: e.tensor_tensor(t2_[:], tmpv[:], sti[b][:, 32:48], ALU.add))
        h6 = E("vector", lambda e: e.tensor_scalar(hin[:], t0[:], oh[:, 1:2], None, ALU.mult), waits=[lo])
        h7 = E("vector", lambda e: e.scalar_tensor_tensor(tmpv[:], t1_[:], oh[:, 2:3], hin[:], ALU.mult, ALU.add))
        h8 = E("vector", lambda e: e.scalar_tensor_tensor(hin[:], t2_[:], oh[:, 3:4], tmpv[:], ALU.mult, ALU.add))
        h8d = ctx.flush("vector")
        ld_guard[b] = h8d
        ymm = None
        for tc in range(4):
            ymm = E("tensor", lambda e, tc=tc, b=b: e.matmul(
                ps_y[tc][:], identb[:], ylc[b][:, tc * 512:(tc + 1) * 512], start=True, stop=False),
                waits=[lyl, ci] + [x for x in [gst_guard[0]] if x is not None])
        for n in range(16):
            bb = n % 2
            lc_ = E("scalar", lambda e, n=n, bb=bb: e.dma_start(out=cbb[bb].ap().unsqueeze(1), in_=bc_d[16 + n:17 + n, :].partition_broadcast(128)),
                    waits=[x for x in [bcb_guard[bb]] if x is not None], dma=f"ccb{bb}")
            aa1 = E("scalar", lambda e, cht=cht, n=n, b=b: e.activation(dfv[:], cumd[b][:], AF.Exp, scale=at[:, cht * 16 + n:cht * 16 + n + 1]),
                    waits=[AT, lcu] + ([ymm] if ymm else []))
            aad = ctx.flush("scalar")
            tt1 = E("vector", lambda e, n=n, bb=bb: e.scalar_tensor_tensor(termv[:], dfv[:], hin[:, n:n + 1], cbb[bb][:], ALU.mult, ALU.mult),
                    waits=[aad, lc_, h8d] + ([ymm] if ymm else []))
            ttd = ctx.flush("vector")
            bcb_guard[bb] = ttd
            for tc in range(4):
                ymm = E("tensor", lambda e, tc=tc, n=n: e.matmul(
                    ps_y[tc][:], identb[:], termv[:, tc * 512:(tc + 1) * 512], start=False, stop=(tc == 3 and n == 15)),
                    waits=[ttd])
        ymd = ctx.flush("tensor")
        sz1 = E("scalar", lambda e, b=b: e.activation(szv[:], zc[b][:], AF.Silu), waits=[lz, ymd])
        szd = ctx.flush("scalar")
        g1 = None
        for tc in range(4):
            sl = slice(tc * 512, (tc + 1) * 512)
            g1 = E("vector", lambda e, tc=tc, sl=sl: e.tensor_tensor(gtd[:, sl], ps_y[tc][:], szv[:, sl], ALU.mult), waits=[szd])
        g1d = ctx.flush("vector")
        gst = E("gpsimd", lambda e, cht=cht: e.dma_start(out=gated_o[cht * 128:(cht + 1) * 128, :], in_=gtd[:]),
                waits=[g1d] + [x for x in [gst_guard[0]] if x is not None], dma="gst")
        gst_guard[0] = gst
    GDONE = [p.dma_total("gst")]

    owf = [ctx.sbuf(f"owf{i}", [128, DM], F32) for i in range(2)]
    owb = [ctx.sbuf(f"owb{i}", [128, DM], BF16) for i in range(2)]
    gl = [ctx.sbuf(f"gl{i}", [128, 128], BF16) for i in range(2)]
    mt = [ctx.sbuf(f"mt{i}", [128, DM], F32) for i in range(2)]
    res = ctx.sbuf("res", [128, DM], F32)
    mn32 = ctx.sbuf("mn32", [128, DM], F32)
    mnb = ctx.sbuf("mnb", [128, DM], BF16)
    fm_st = ctx.sbuf("fm_st", [128, 128], BF16)
    tmp = {k: ctx.sbuf("lntmp_" + k, [128, 1], F32) for k in ["mu", "negmu", "ssq", "var", "rstd"]}
    tmp["sq"] = ctx.sbuf("lntmp_sq", [128, DM], F32)
    tmp["junk"] = ctx.sbuf("lntmp_junk", [128, DM], F32)
    tmp["eps_t"] = eps_t
    ps_o = [ctx.psum(f"ps_o{i}", [128, 512], F32) for i in range(2)]
    ps_tp = ctx.psum("ps_tp", [128, 128], BF16)
    st_guard = {"mn": None, "fm": None, "res": None, "t1last": None, "mt": {0: None, 1: None}, "gl": {0: None, 1: None}, "ow": {0: None, 1: None}, "owpe": {0: None, 1: None}}
    for tt in range(NT):
        b = tt % 2
        lm = E("sync", lambda e, tt=tt, b=b: e.dma_start(out=mt[b][:], in_=m_d[tt * 128:(tt + 1) * 128, :]),
               waits=[x for x in [st_guard["mt"][b]] if x is not None], dma=f"mt{b}")
        mm_ = {}
        for cht in range(16):
            gb = cht % 2
            lg = E("scalar", lambda e, cht=cht, tt=tt, gb=gb: e.dma_start(out=gl[gb][:], in_=gated_o[cht * 128:(cht + 1) * 128, tt * 128:(tt + 1) * 128]),
                   waits=GDONE + [x for x in [st_guard["gl"][gb]] if x is not None], dma=f"gl{gb}")
            ob = cht % 2
            low = E("sync", lambda e, cht=cht, ob=ob: e.dma_start(out=owf[ob][:], in_=outwT_d[cht, :, :]),
                    waits=[x for x in [st_guard["ow"][ob]] if x is not None], dma=f"ow{ob}")
            cow_ = E("vector", lambda e, ob=ob: e.tensor_copy(owb[ob][:], owf[ob][:]),
                     waits=[low] + [x for x in [st_guard["owpe"][ob]] if x is not None])
            cow = ctx.flush("vector")
            st_guard["ow"][ob] = cow
            gld = ctx.flush("scalar")
            for half in range(2):
                mm_[half] = E("tensor", lambda e, cht=cht, half=half, gb=gb, ob=ob: e.matmul(
                    ps_o[half][:], gl[gb][:], owb[ob][:, half * 512:(half + 1) * 512],
                    start=(cht == 0), stop=(cht == 15)),
                    waits=[lg, gld, cow] + ([x for x in [st_guard["res"]] if x is not None] if cht == 0 else []))
            st_guard["gl"][gb] = mm_[1]
            st_guard["owpe"][ob] = mm_[1]
        mmd = ctx.flush("tensor")
        r1 = None
        for half in range(2):
            sl = slice(half * 512, (half + 1) * 512)
            r1 = E("vector", lambda e, half=half, sl=sl, b=b: e.tensor_tensor(res[:, sl], ps_o[half][:], mt[b][:, sl], ALU.add), waits=[mmd, lm])
        r1d = ctx.flush("vector")
        st_guard["res"] = r1d
        st_guard["mt"][b] = r1d
        lnt = ln_tile(ctx, E, res, mn32, tmp, [r1d, mz1])
        w1_ = E("vector", lambda e: e.tensor_tensor(mn32[:], mn32[:], lnw[:], ALU.mult), waits=[lnt, lw_])
        w2_ = E("vector", lambda e: e.tensor_tensor(mn32[:], mn32[:], lnb[:], ALU.add), waits=[w1_, lb_] + [x for x in [st_guard["mn"]] if x is not None])
        w2d = ctx.flush("vector")
        smn = E("gpsimd", lambda e, tt=tt: e.dma_start(out=mn_o[tt * 128:(tt + 1) * 128, :], in_=mn32[:]), waits=[w2d], dma="mnst")
        st_guard["mn"] = smn
        cb = E("vector", lambda e: e.tensor_copy(mnb[:], mn32[:]), waits=[w2d] + [x for x in [st_guard["t1last"]] if x is not None])
        cbd = ctx.flush("vector")
        for dt in range(NDT):
            t1 = E("tensor", lambda e, dt=dt: e.transpose(ps_tp[:], mnb[:, dt * 128:(dt + 1) * 128], identb[:]),
                   waits=[cbd, ci] + [x for x in [st_guard["fm"]] if x is not None])
            t1d = ctx.flush("tensor")
            t2 = E("vector", lambda e: e.tensor_copy(fm_st[:], ps_tp[:]), waits=[t1d])
            t2d = ctx.flush("vector")
            t3 = E("gpsimd", lambda e, dt=dt, tt=tt: e.dma_start(
                out=mnfm_o[:, dt * T + tt * 128: dt * T + (tt + 1) * 128], in_=fm_st[:]), waits=[t2d], dma="fmst")
            st_guard["fm"] = t3
            st_guard["t1last"] = t1d
    return



def phase4(ctx):
    p = ctx.p
    E = p.emit
    xnfm_d = ctx.din("xn1fm", [128, NDT * T], BF16)
    xnt_d = ctx.din("xn1", [T, DM])
    x1_d = ctx.din("x1", [T, DM])
    s1_d = ctx.din("s1", [T, DM])
    rw_d = ctx.din("router_w", [DM, 8])
    w1_d = ctx.din("w1_t", [8, 32, NDT, 128, 128])
    b1_d = ctx.din("b1_r", [8, 128, 32])
    w2_d = ctx.din("w2_t", [8, 32, 128, DM])
    b2_d = ctx.din("b2", [8, DM])
    gam_d = ctx.din("gamma", [8, DM])
    bet_d = ctx.din("beta", [8, DM])
    id_d = ctx.din("ident", [128, 128])
    out_o = ctx.dout("out", [T, DM])

    identt = ctx.sbuf("identt", [128, 128], F32)
    eps_t = ctx.sbuf("eps_t", [128, 1], F32)
    li = E("sync", lambda e: e.dma_start(out=identt[:], in_=id_d[:]), dma="const")
    mz1 = E("vector", lambda e: e.memset(eps_t[:], 1e-6))
    one_col = ctx.sbuf("one_col", [128, 1], F32)
    mz2 = E("vector", lambda e: e.memset(one_col[:], 1.0))
    xnfm = ctx.sbuf("xnfm", [128, NDT * T], BF16)
    lxf = E("sync", lambda e: e.dma_start(out=xnfm[:], in_=xnfm_d[:]), dma="const")
    rw = ctx.sbuf("rw", [128, NDT * 8], F32)
    lrw = E("sync", lambda e: e.dma_start(out=rw.ap().rearrange("p (a n) -> p a n", a=NDT), in_=rw_d.ap().rearrange("(a p) n -> p a n", p=128)), dma="const")

    gates = ctx.sbuf("gates", [128, NT * 8], F32)
    xt32 = [ctx.sbuf(f"xt32{i}", [128, DM], F32) for i in range(2)]
    x32t = ctx.sbuf("x32t", [128, NDT * 128], F32)
    lgt = ctx.sbuf("lgt", [128, 8], F32)
    m8 = ctx.sbuf("m8", [128, 8], F32)
    d21 = ctx.sbuf("d21", [128, 1], F32)
    ez = ctx.sbuf("ez", [128, 1], F32)
    rz = ctx.sbuf("rz", [128, 1], F32)
    pex = ctx.sbuf("pex", [128, 8], F32)
    msk = ctx.sbuf("msk", [128, 8], F32)
    ps_dn = [ctx.psum(f"ps_dn{i}", [128, 512], F32) for i in range(8)]
    xl_guard = {0: None, 1: None}
    gprev = ()
    for tt in range(NT):
        b = tt % 2
        lx = E("sync", lambda e, tt=tt, b=b: e.dma_start(out=xt32[b][:], in_=xnt_d[tt * 128:(tt + 1) * 128, :]),
               waits=[x for x in [xl_guard[b]] if x is not None], dma=f"xt{b}")
        tpd = None
        for dt in range(NDT):
            t3 = E("tensor", lambda e, b=b, dt=dt: e.transpose(ps_dn[0][0:128, 0:128], xt32[b][:, dt * 128:(dt + 1) * 128], identt[:]),
                   waits=[lx, li] + ([tpd] if tpd else []))
            t3d = ctx.flush("tensor")
            t4 = E("vector", lambda e, dt=dt: e.tensor_copy(x32t[:, dt * 128:(dt + 1) * 128], ps_dn[0][0:128, 0:128]), waits=[t3d])
            tpd = ctx.flush("vector")
        xl_guard[b] = tpd
        mms = None
        for dt in range(NDT):
            mms = E("tensor", lambda e, dt=dt: e.matmul(
                ps_dn[1][0:128, 0:8], x32t[:, dt * 128:(dt + 1) * 128], rw[:, dt * 8:(dt + 1) * 8],
                start=(dt == 0), stop=(dt == NDT - 1)),
                waits=([tpd, lrw] + list(gprev) if dt == 0 else []))
        mmsd = ctx.flush("tensor")
        c1 = E("vector", lambda e: e.tensor_copy(lgt[:], ps_dn[1][0:128, 0:8]), waits=[mmsd])
        c2 = E("vector", lambda e: e.max(m8[:], lgt[:]), waits=[c1])
        c3 = E("vector", lambda e: e.tensor_scalar(d21[:], m8[:, 1:2], m8[:, 0:1], None, ALU.subtract), waits=[c2])
        c4 = E("scalar", lambda e: e.activation(ez[:], d21[:], AF.Exp), waits=[c3])
        c5 = E("vector", lambda e: e.tensor_tensor(ez[:], ez[:], one_col[:], ALU.add), waits=[c4, mz2])
        c6 = E("vector", lambda e: e.reciprocal(rz[:], ez[:]), waits=[c5])
        c7 = E("vector", lambda e: e.tensor_scalar(pex[:], lgt[:], m8[:, 0:1], None, ALU.subtract), waits=[c2])
        c8 = E("scalar", lambda e: e.activation(pex[:], pex[:], AF.Exp), waits=[c7])
        c9 = E("vector", lambda e: e.tensor_scalar(msk[:], lgt[:], m8[:, 1:2], None, ALU.is_ge), waits=[c8])
        c10 = E("vector", lambda e: e.tensor_tensor(pex[:], pex[:], msk[:], ALU.mult), waits=[c9])
        c11 = E("vector", lambda e, tt=tt: e.tensor_scalar(gates[:, tt * 8:(tt + 1) * 8], pex[:], rz[:], None, ALU.mult), waits=[c10, c6])
        gprev = (c11, c2)
    GATES = ctx.flush("vector")

    g1acc = ctx.sbuf("g1acc", [128, NT * DM], F32)
    zg = E("vector", lambda e: e.memset(g1acc[:], 0.0))
    hfm = ctx.sbuf("hfm", [128, 32 * 512], BF16)
    w1f = [ctx.sbuf(f"w1f{i}", [128, NDT * 128], F32) for i in range(2)]
    w1b = [ctx.sbuf(f"w1b{i}", [128, NDT * 128], BF16) for i in range(2)]
    w2b = [ctx.sbuf(f"w2b{i}", [128, DM], BF16) for i in range(2)]
    b1t = ctx.sbuf("b1t", [128, 32], F32)
    b2b = ctx.sbuf("b2b", [128, DM], F32)
    gab = ctx.sbuf("gab", [128, DM], F32)
    beb = ctx.sbuf("beb", [128, DM], F32)
    res = ctx.sbuf("res", [128, DM], F32)
    nrm = ctx.sbuf("nrm", [128, DM], F32)
    xnt2 = [ctx.sbuf(f"xnt2{i}", [128, DM], F32) for i in range(2)]
    tmp = {k: ctx.sbuf("lntmp_" + k, [128, 1], F32) for k in ["mu", "negmu", "ssq", "var", "rstd"]}
    tmp["sq"] = ctx.sbuf("lntmp_sq", [128, DM], F32)
    tmp["junk"] = ctx.sbuf("lntmp_junk", [128, DM], F32)
    tmp["eps_t"] = eps_t
    ps_up = [ps_dn[6], ps_dn[7]]
    w1_guard = {0: None, 1: None}
    w2_guard = {0: None, 1: None}
    up_guard = {0: None, 1: None}
    xn2_guard = {0: None, 1: None}
    dn_guard = {i: None for i in range(8)}
    hf_guard = [None]
    acc_guard = [zg]
    for ex in range(8):
        lb1 = E("sync", lambda e, ex=ex: e.dma_start(out=b1t[:], in_=b1_d[ex, :, :]), waits=acc_guard, dma="b1t")
        lb2 = E("scalar", lambda e, ex=ex: e.dma_start(out=b2b.ap().unsqueeze(1), in_=b2_d[ex:ex + 1, :].partition_broadcast(128)), waits=acc_guard, dma="b2b")
        lga = E("scalar", lambda e, ex=ex: e.dma_start(out=gab.ap().unsqueeze(1), in_=gam_d[ex:ex + 1, :].partition_broadcast(128)), waits=acc_guard, dma="gab")
        lbe = E("scalar", lambda e, ex=ex: e.dma_start(out=beb.ap().unsqueeze(1), in_=bet_d[ex:ex + 1, :].partition_broadcast(128)), waits=acc_guard, dma="beb")
        for tc in range(4):
            agd = None
            for fc in range(32):
                wbuf = fc % 2
                lw1 = E("sync", lambda e, ex=ex, fc=fc, wbuf=wbuf: e.dma_start(
                    out=w1f[wbuf].ap().rearrange("p (a n) -> p a n", a=NDT), in_=w1_d[ex, fc, :, :, :].rearrange("a p n -> p a n")),
                    waits=[x for x in [w1_guard[wbuf]] if x is not None], dma=f"w1{wbuf}")
                cw1_ = E("vector", lambda e, wbuf=wbuf: e.tensor_copy(w1b[wbuf][:], w1f[wbuf][:]), waits=[lw1])
                cw1 = ctx.flush("vector")
                w1_guard[wbuf] = cw1
                pb = fc % 2
                m_ = None
                for dt in range(NDT):
                    m_ = E("tensor", lambda e, tc=tc, fc=fc, dt=dt, wbuf=wbuf, pb=pb: e.matmul(
                        ps_up[pb][:], w1b[wbuf][:, dt * 128:(dt + 1) * 128], xnfm[:, dt * T + tc * 512: dt * T + (tc + 1) * 512],
                        start=(dt == 0), stop=(dt == NDT - 1)),
                        waits=([cw1, lxf] + [x for x in [up_guard[pb], dn_guard[6 + pb]] if x is not None] if dt == 0 else []))
                md = ctx.flush("tensor")
                w1_guard[wbuf] = md
                ag = E("scalar", lambda e, fc=fc, pb=pb: e.activation(hfm[:, fc * 512:(fc + 1) * 512], ps_up[pb][:], AF.Gelu, bias=b1t[:, fc:fc + 1]),
                       waits=[md, lb1] + [x for x in [hf_guard[0]] if x is not None])
                agd = ctx.flush("scalar")
                up_guard[pb] = agd
            dn_last = None
            for fc in range(32):
                wbuf = fc % 2
                lw2 = E("gpsimd", lambda e, ex=ex, fc=fc, wbuf=wbuf: e.dma_start(out=w2b[wbuf][:], in_=w2_d[ex, fc, :, :]),
                        waits=[x for x in [w2_guard[wbuf]] if x is not None], dma=f"w2{wbuf}")
                for sub in range(8):
                    ttl, half = sub // 2, sub % 2
                    dn_last = E("tensor", lambda e, fc=fc, ttl=ttl, half=half, wbuf=wbuf, sub=sub: e.matmul(
                        ps_dn[sub][:], hfm[:, fc * 512 + ttl * 128: fc * 512 + ttl * 128 + 128],
                        w2b[wbuf][:, half * 512:(half + 1) * 512],
                        start=(fc == 0), stop=(fc == 31)),
                        waits=([lw2, agd] + [x for x in [dn_guard[sub]] if x is not None] if fc == 0 else [lw2]))
                w2_guard[wbuf] = dn_last
            dnd = ctx.flush("tensor")
            hf_guard[0] = dnd
            for tl in range(4):
                tt_g = tc * 4 + tl
                xb2 = tl % 2
                lxn = E("sync", lambda e, tt_g=tt_g, xb2=xb2: e.dma_start(out=xnt2[xb2][:], in_=xnt_d[tt_g * 128:(tt_g + 1) * 128, :]),
                        waits=[x for x in [xn2_guard[xb2]] if x is not None], dma=f"xn2{xb2}")
                r1 = None
                for half in range(2):
                    sub = tl * 2 + half
                    sl = slice(half * 512, (half + 1) * 512)
                    r1 = E("vector", lambda e, sub=sub, sl=sl: e.tensor_tensor(res[:, sl], ps_dn[sub][:], b2b[:, sl], ALU.add), waits=[dnd, lb2])
                r2 = E("vector", lambda e, xb2=xb2: e.tensor_tensor(res[:], res[:], xnt2[xb2][:], ALU.add), waits=[r1, lxn])
                r2d = ctx.flush("vector")
                xn2_guard[xb2] = r2d
                dn_guard[tl * 2] = r2d
                dn_guard[tl * 2 + 1] = r2d
                lnt = ln_tile(ctx, E, res, nrm, tmp, [r2d, mz1], nf=1024.0)
                g1_ = E("vector", lambda e: e.tensor_tensor(nrm[:], nrm[:], gab[:], ALU.mult), waits=[lnt, lga])
                g2_ = E("vector", lambda e: e.tensor_tensor(nrm[:], nrm[:], beb[:], ALU.add), waits=[g1_, lbe])
                g3_ = E("vector", lambda e, ex=ex, tt_g=tt_g: e.tensor_scalar(nrm[:], nrm[:], gates[:, tt_g * 8 + ex: tt_g * 8 + ex + 1], None, ALU.mult), waits=[g2_, GATES])
                g4_ = E("vector", lambda e, tt_g=tt_g: e.tensor_tensor(g1acc[:, tt_g * DM:(tt_g + 1) * DM], g1acc[:, tt_g * DM:(tt_g + 1) * DM], nrm[:], ALU.add), waits=[g3_, zg])
        acc_guard = [ctx.flush("vector")]

    x1b = [xt32[0], xt32[1]]
    s1b = [tmp["sq"], tmp["junk"]]
    sg1 = nrm
    outb = res
    fx_guard = {0: None, 1: None}
    ost_guard = [None]
    for tt in range(NT):
        b = tt % 2
        gw = [x for x in [fx_guard[b]] if x is not None]
        lx1 = E("sync", lambda e, tt=tt, b=b: e.dma_start(out=x1b[b][:], in_=x1_d[tt * 128:(tt + 1) * 128, :]), waits=gw, dma=f"fx{b}")
        ls1 = E("scalar", lambda e, tt=tt, b=b: e.dma_start(out=s1b[b][:], in_=s1_d[tt * 128:(tt + 1) * 128, :]), waits=gw, dma=f"fs{b}")
        a1 = E("scalar", lambda e, tt=tt: e.activation(sg1[:], g1acc[:, tt * DM:(tt + 1) * DM], AF.Silu), waits=acc_guard + [x for x in [ost_guard[0]] if x is not None])
        a1d = ctx.flush("scalar")
        o1 = E("vector", lambda e, b=b: e.tensor_tensor(outb[:], sg1[:], s1b[b][:], ALU.mult), waits=[a1d, ls1])
        o2 = E("vector", lambda e, b=b: e.tensor_tensor(outb[:], outb[:], x1b[b][:], ALU.add), waits=[o1, lx1])
        o2d = ctx.flush("vector")
        fx_guard[b] = o2d
        so = E("gpsimd", lambda e, tt=tt: e.dma_start(out=out_o[tt * 128:(tt + 1) * 128, :], in_=outb[:]), waits=[o2d] + [x for x in [ost_guard[0]] if x is not None], dma="ost")
        ost_guard[0] = so
    return


def host_prep_l2(inputs, r1_results):
    x = np.asarray(inputs["L0_attn_x"])
    wout = np.ascontiguousarray(np.asarray(inputs["L0_attn_W_out"]).astype(np.float32))
    bout = np.asarray(inputs["L0_attn_b_out"]).astype(np.float32)
    bout128 = np.ascontiguousarray(np.broadcast_to(bout[None, :], (128, 1024)))
    ident = np.eye(128, dtype=np.float32)
    maps = []
    for c in range(NCORES):
        b, r = c // 4, c % 4
        grp = range(b * 4, b * 4 + 4)
        K_all = np.concatenate([np.asarray(r1_results[j]["K_own"]) for j in grp], axis=0)
        V_all = np.concatenate([np.asarray(r1_results[j]["V_own"]) for j in grp], axis=0)
        maps.append({
            "QP": np.asarray(r1_results[c]["QP_own"]),
            "K_all": np.ascontiguousarray(K_all),
            "V_all": np.ascontiguousarray(V_all),
            "g0": np.asarray(r1_results[c]["g0"]),
            "x": np.ascontiguousarray(x[b, r * T:(r + 1) * T]).astype(np.float32),
            "wout": wout, "bout128": bout128, "ident": ident,
        })
    return maps




def host_prep_l3a(inputs, i, r_prev):
    in_w = np.asarray(inputs["L1_attn_in_proj_w"][i]).astype(np.float32)
    in_wT_t = np.ascontiguousarray(in_w.T.reshape(8, 128, 32, 128).transpose(2, 0, 1, 3))
    return [{"m_fm": np.asarray(r_prev[c]["m_fm"]), "in_wT_t": in_wT_t} for c in range(NCORES)]


def host_prep_l3b(inputs, i, r3a):
    cw = np.asarray(inputs["L1_attn_conv_w"][i]).astype(np.float32)
    cb = np.asarray(inputs["L1_attn_conv_b"][i]).astype(np.float32)
    xpw = np.asarray(inputs["L1_attn_x_proj_w"][i]).astype(np.float32)
    dtw = np.asarray(inputs["L1_attn_dt_proj_w"][i]).astype(np.float32)
    dtb = np.asarray(inputs["L1_attn_dt_proj_b"][i]).astype(np.float32)
    alog = np.asarray(inputs["L1_attn_A_log"][i]).astype(np.float32)
    dskip = np.asarray(inputs["L1_attn_D_skip"][i]).astype(np.float32)
    com = {
        "cw_r": np.ascontiguousarray(cw.reshape(16, 128, 4)),
        "cb_r": np.ascontiguousarray(cb.reshape(16, 128).T),
        "xp_wT_t": np.ascontiguousarray(xpw.T.reshape(16, 128, 96)),
        "dt_wT_t": np.ascontiguousarray(dtw.T.reshape(64, 16, 128).transpose(1, 0, 2)),
        "dtb_r": np.ascontiguousarray(dtb.reshape(16, 128).T),
        "alog_r": np.ascontiguousarray(alog.reshape(16, 128, 16)),
        "dskip_r": np.ascontiguousarray(dskip.reshape(16, 128).T),
        "ident": np.eye(128, dtype=np.float32),
    }
    maps = []
    for c in range(NCORES):
        xz = np.asarray(r3a[c]["xz"])
        if c % 4 == 0:
            halo = np.zeros((2048, 3), dtype=BF)
        else:
            halo = np.ascontiguousarray(np.asarray(r3a[c - 1]["xz"])[0:2048, T - 3:T])
        m = dict(com)
        m["xz"] = xz
        m["halo"] = halo
        maps.append(m)
    return maps


def host_prep_l3c(inputs, i, r3a, r3b, m_tok_list):
    outw = np.asarray(inputs["L1_attn_out_proj_w"][i]).astype(np.float32)
    lnw = np.asarray(inputs["L1_attn_ln_w"][i]).astype(np.float32)
    lnb = np.asarray(inputs["L1_attn_ln_b"][i]).astype(np.float32)
    alog = np.asarray(inputs["L1_attn_A_log"][i]).astype(np.float32)
    com = {
        "out_wT_t": np.ascontiguousarray(outw.T.reshape(16, 128, DM)),
        "lnw128": np.ascontiguousarray(np.broadcast_to(lnw[None, :], (128, DM))),
        "lnb128": np.ascontiguousarray(np.broadcast_to(lnb[None, :], (128, DM))),
        "alog_r": np.ascontiguousarray(alog.reshape(16, 128, 16)),
        "ident": np.eye(128, dtype=np.float32),
    }
    maps = []
    for c in range(NCORES):
        b, r = c // 4, c % 4
        grp = range(b * 4, b * 4 + 4)
        s_all = np.stack([np.asarray(r3b[j]["s_sum"]) for j in grp], 0)
        p_all = np.stack([np.asarray(r3b[j]["p_sum"]) for j in grp], 0)
        oh = np.zeros((128, 4), np.float32)
        oh[:, r] = 1.0
        m = dict(com)
        m.update({
            "y_local": np.asarray(r3b[c]["y_local"]),
            "cumd": np.asarray(r3b[c]["cumd"]),
            "bc": np.asarray(r3b[c]["bc"]),
            "z": np.ascontiguousarray(np.asarray(r3a[c]["xz"])[2048:4096, :]),
            "s_all": np.ascontiguousarray(s_all),
            "p_all": np.ascontiguousarray(p_all),
            "ohsel": oh,
            "m_tok": np.asarray(m_tok_list[c]),
        })
        maps.append(m)
    return maps


def host_prep_l4(inputs, r2, s1_list):
    w1 = np.asarray(inputs["L1_ffn_w1"]).astype(np.float32)
    w2 = np.asarray(inputs["L1_ffn_w2"]).astype(np.float32)
    b1 = np.asarray(inputs["L1_ffn_b1"]).astype(np.float32)
    com = {
        "router_w": np.ascontiguousarray(np.asarray(inputs["L1_ffn_router_w"]).astype(np.float32)),
        "w1_t": np.ascontiguousarray(w1.reshape(8, 8, 128, 32, 128).transpose(0, 3, 1, 2, 4)),
        "b1_r": np.ascontiguousarray(b1.reshape(8, 32, 128).transpose(0, 2, 1)),
        "w2_t": np.ascontiguousarray(w2.reshape(8, 32, 128, DM)),
        "b2": np.ascontiguousarray(np.asarray(inputs["L1_ffn_b2"]).astype(np.float32)),
        "gamma": np.ascontiguousarray(np.asarray(inputs["L1_ffn_gamma"]).astype(np.float32)),
        "beta": np.ascontiguousarray(np.asarray(inputs["L1_ffn_beta"]).astype(np.float32)),
        "ident": np.eye(128, dtype=np.float32),
    }
    maps = []
    for c in range(NCORES):
        m = dict(com)
        m.update({
            "xn1fm": np.asarray(r2[c]["xn1fm"]),
            "xn1": np.asarray(r2[c]["xn1"]),
            "x1": np.asarray(r2[c]["x1"]),
            "s1": np.asarray(s1_list[c]),
        })
        maps.append(m)
    return maps





RG4 = [[0, 1, 2, 3], [4, 5, 6, 7]]


def build_fused():
    ctx = Ctx("fused")
    nc = ctx.nc
    p = ctx.start()
    ctx.p = p
    ctx.mkflush()
    E = p.emit

    D = {}
    def ext(name, shape, dtype=F32):
        D[name] = nc.dram_tensor(name, list(shape), dtype, kind="ExternalInput")
    def itn(name, shape, dtype=F32):
        D[name] = nc.dram_tensor(name, list(shape), dtype, kind="Internal")
    ext("x", [T, DM]); ext("wqkv", [DM, 3072]); ext("bqkv_r", [128, 24]); ext("pe2", [128, T])
    ext("bv128", [128, DM]); ext("gate_wT", [DM, 8])
    ext("wg_t", [8, 11, 8, 128, 128]); ext("wu_t", [8, 11, 8, 128, 128]); ext("wd_t", [8, 11, 128, DM])
    ext("swg_t", [22, 8, 128, 128]); ext("swu_t", [22, 8, 128, 128]); ext("swd_t", [22, 128, DM])
    ext("ident", [128, 128]); ext("wout", [DM, DM]); ext("bout128", [128, DM])
    ext("in_wT_t4", [4, 32, NDT, 128, 128])
    ext("cw_r4", [4, 16, 128, 4]); ext("cb_r4", [4, 128, 16]); ext("xp_wT_t4", [4, 16, 128, 96])
    ext("dt_wT_t4", [4, 16, 64, 128]); ext("dtb_r4", [4, 128, 16]); ext("alog_r4", [4, 16, 128, 16])
    ext("dskip_r4", [4, 128, 16]); ext("out_wT_t4", [4, 16, 128, DM])
    ext("lnw128_4", [4, 128, DM]); ext("lnb128_4", [4, 128, DM])
    ext("ohsel", [128, 4]); ext("ohprev", [128, 4])
    ext("router_w", [DM, 8]); ext("w1_t", [8, 32, NDT, 128, 128]); ext("b1_r", [8, 128, 32])
    ext("w2_t", [8, 32, 128, DM]); ext("b2", [8, DM]); ext("gamma", [8, DM]); ext("beta", [8, DM])
    itn("K_own", [DM, T], BF16); itn("V_own", [T, DM], BF16); itn("QP_own", [NH, 65, T], BF16)
    itn("g0", [T, DM]); itn("K_all", [4 * DM, T], BF16); itn("V_all", [S, DM], BF16)
    itn("x1", [T, DM]); itn("xn1", [T, DM]); itn("xn1fm", [128, NDT * T], BF16)
    itn("xz", [4096, T], BF16); itn("xc", [2048, T], BF16); itn("cumd", [2048, T])
    itn("bc", [32, T], BF16); itn("s_sum", [2048, 16]); itn("p_sum", [2048, 16])
    itn("y_local", [2048, T], BF16); itn("gated", [2048, T], BF16)
    itn("tail_own", [2048, 3], BF16); itn("tails_ag", [4 * 2048, 3], BF16)
    itn("s_ag", [4 * 2048, 16]); itn("p_ag", [4 * 2048, 16])
    itn("mA", [T, DM]); itn("mB", [T, DM])
    itn("mA_fm", [128, NDT * T], BF16); itn("mB_fm", [128, NDT * T], BF16)
    D["out"] = nc.dram_tensor("out", [T, DM], F32, kind="ExternalOutput")
    ctx.D = D

    ctx.phase_open("p1_", "")
    D["QP"] = D["QP_own"]
    phase1(ctx)
    ctx.phase_close()
    bar = p.all_barriers()
    E("gpsimd", lambda e: e.collective_compute("AllGather", ALU.bypass, replica_groups=RG4,
                                               ins=[D["K_own"].ap().opt()], outs=[D["K_all"].ap().opt()]), waits=bar)
    E("gpsimd", lambda e: e.collective_compute("AllGather", ALU.bypass, replica_groups=RG4,
                                               ins=[D["V_own"].ap().opt()], outs=[D["V_all"].ap().opt()]))
    ctx.phase_open("p2_", "")
    phase2(ctx)
    ctx.phase_close()

    cur_fm, cur_tok = "xn1fm", "xn1"
    for i in range(4):
        nxt_fm, nxt_tok = ("mA_fm", "mA") if i % 2 == 0 else ("mB_fm", "mB")
        D["m_fm"] = D[cur_fm]
        D["m_tok"] = D[cur_tok]
        D["m_next"] = D[nxt_tok]
        D["m_next_fm"] = D[nxt_fm]
        ctx.phase_open(f"a{i}_", "")
        phase3a(ctx, i)
        ctx.phase_close()
        bar = p.all_barriers()
        tcp = E("gpsimd", lambda e: e.dma_start(out=D["tail_own"][:, :], in_=D["xz"][0:2048, T - 3:T]), waits=bar, dma="tailcp")
        E("gpsimd", lambda e: e.collective_compute("AllGather", ALU.bypass, replica_groups=RG4,
                                                   ins=[D["tail_own"].ap().opt()], outs=[D["tails_ag"].ap().opt()]), waits=[tcp])
        ctx.phase_open(f"b{i}_", "")
        phase3b(ctx, i)
        ctx.phase_close()
        bar = p.all_barriers()
        E("gpsimd", lambda e: e.collective_compute("AllGather", ALU.bypass, replica_groups=RG4,
                                                   ins=[D["s_sum"].ap().opt()], outs=[D["s_ag"].ap().opt()]), waits=bar)
        E("gpsimd", lambda e: e.collective_compute("AllGather", ALU.bypass, replica_groups=RG4,
                                                   ins=[D["p_sum"].ap().opt()], outs=[D["p_ag"].ap().opt()]))
        ctx.phase_open(f"c{i}_", "")
        phase3c(ctx, i)
        ctx.phase_close()
        cur_fm, cur_tok = nxt_fm, nxt_tok

    D["s1"] = D[cur_tok]
    ctx.phase_open("p4_", "")
    phase4(ctx)
    ctx.phase_close()
    return ctx.done(p.all_barriers())


def host_prep_fused(inputs):
    x = np.asarray(inputs["L0_attn_x"])
    wqkv = np.ascontiguousarray(inputs["L0_attn_W_qkv"]).astype(np.float32)
    bqkv = np.asarray(inputs["L0_attn_b_qkv"]).astype(np.float32)
    pe = rel_pos_enc(S, HD)
    wg = np.asarray(inputs["L0_ffn_wg"]).astype(np.float32)
    wu = np.asarray(inputs["L0_ffn_wu"]).astype(np.float32)
    wd = np.asarray(inputs["L0_ffn_wd"]).astype(np.float32)
    swg = np.asarray(inputs["L0_ffn_swg"]).astype(np.float32)
    swu = np.asarray(inputs["L0_ffn_swu"]).astype(np.float32)
    swd = np.asarray(inputs["L0_ffn_swd"]).astype(np.float32)
    in_w = np.asarray(inputs["L1_attn_in_proj_w"]).astype(np.float32)
    cw = np.asarray(inputs["L1_attn_conv_w"]).astype(np.float32)
    cb = np.asarray(inputs["L1_attn_conv_b"]).astype(np.float32)
    xpw = np.asarray(inputs["L1_attn_x_proj_w"]).astype(np.float32)
    dtw = np.asarray(inputs["L1_attn_dt_proj_w"]).astype(np.float32)
    dtb = np.asarray(inputs["L1_attn_dt_proj_b"]).astype(np.float32)
    alog = np.asarray(inputs["L1_attn_A_log"]).astype(np.float32)
    dskip = np.asarray(inputs["L1_attn_D_skip"]).astype(np.float32)
    outw = np.asarray(inputs["L1_attn_out_proj_w"]).astype(np.float32)
    lnw = np.asarray(inputs["L1_attn_ln_w"]).astype(np.float32)
    lnb = np.asarray(inputs["L1_attn_ln_b"]).astype(np.float32)
    w1 = np.asarray(inputs["L1_ffn_w1"]).astype(np.float32)
    b1 = np.asarray(inputs["L1_ffn_b1"]).astype(np.float32)
    w2 = np.asarray(inputs["L1_ffn_w2"]).astype(np.float32)
    com = {
        "wqkv": wqkv,
        "bqkv_r": np.ascontiguousarray(bqkv.reshape(24, 128).T),
        "bv128": np.ascontiguousarray(np.broadcast_to(bqkv[2048:3072][None, :], (128, 1024))),
        "gate_wT": np.ascontiguousarray(np.asarray(inputs["L0_ffn_gate_w"]).astype(np.float32).T),
        "wg_t": np.ascontiguousarray(wg.reshape(8, 8, 128, 11, 128).transpose(0, 3, 1, 2, 4)),
        "wu_t": np.ascontiguousarray(wu.reshape(8, 8, 128, 11, 128).transpose(0, 3, 1, 2, 4)),
        "wd_t": np.ascontiguousarray(wd.reshape(8, 11, 128, 1024)),
        "swg_t": np.ascontiguousarray(swg.reshape(8, 128, 22, 128).transpose(2, 0, 1, 3)),
        "swu_t": np.ascontiguousarray(swu.reshape(8, 128, 22, 128).transpose(2, 0, 1, 3)),
        "swd_t": np.ascontiguousarray(swd.reshape(22, 128, 1024)),
        "ident": np.eye(128, dtype=np.float32),
        "wout": np.ascontiguousarray(np.asarray(inputs["L0_attn_W_out"]).astype(np.float32)),
        "bout128": np.ascontiguousarray(np.broadcast_to(np.asarray(inputs["L0_attn_b_out"]).astype(np.float32)[None, :], (128, 1024))),
        "in_wT_t4": np.ascontiguousarray(np.stack([in_w[i].T.reshape(8, 128, 32, 128).transpose(2, 0, 1, 3) for i in range(4)])),
        "cw_r4": np.ascontiguousarray(cw.reshape(4, 16, 128, 4)),
        "cb_r4": np.ascontiguousarray(np.stack([cb[i].reshape(16, 128).T for i in range(4)])),
        "xp_wT_t4": np.ascontiguousarray(np.stack([xpw[i].T.reshape(16, 128, 96) for i in range(4)])),
        "dt_wT_t4": np.ascontiguousarray(np.stack([dtw[i].T.reshape(64, 16, 128).transpose(1, 0, 2) for i in range(4)])),
        "dtb_r4": np.ascontiguousarray(np.stack([dtb[i].reshape(16, 128).T for i in range(4)])),
        "alog_r4": np.ascontiguousarray(alog.reshape(4, 16, 128, 16)),
        "dskip_r4": np.ascontiguousarray(np.stack([dskip[i].reshape(16, 128).T for i in range(4)])),
        "out_wT_t4": np.ascontiguousarray(np.stack([outw[i].T.reshape(16, 128, DM) for i in range(4)])),
        "lnw128_4": np.ascontiguousarray(np.stack([np.broadcast_to(lnw[i][None, :], (128, DM)) for i in range(4)])),
        "lnb128_4": np.ascontiguousarray(np.stack([np.broadcast_to(lnb[i][None, :], (128, DM)) for i in range(4)])),
        "router_w": np.ascontiguousarray(np.asarray(inputs["L1_ffn_router_w"]).astype(np.float32)),
        "w1_t": np.ascontiguousarray(w1.reshape(8, 8, 128, 32, 128).transpose(0, 3, 1, 2, 4)),
        "b1_r": np.ascontiguousarray(b1.reshape(8, 32, 128).transpose(0, 2, 1)),
        "w2_t": np.ascontiguousarray(w2.reshape(8, 32, 128, DM)),
        "b2": np.ascontiguousarray(np.asarray(inputs["L1_ffn_b2"]).astype(np.float32)),
        "gamma": np.ascontiguousarray(np.asarray(inputs["L1_ffn_gamma"]).astype(np.float32)),
        "beta": np.ascontiguousarray(np.asarray(inputs["L1_ffn_beta"]).astype(np.float32)),
    }
    maps = []
    for c in range(NCORES):
        b, r = c // 4, c % 4
        sl = slice(r * T, (r + 1) * T)
        pe_own = pe[sl].T
        oh = np.zeros((128, 4), np.float32); oh[:, r] = 1.0
        ohp = np.zeros((128, 4), np.float32)
        if r > 0:
            ohp[:, r - 1] = 1.0
        m = dict(com)
        m.update({
            "x": np.ascontiguousarray(x[b, sl]).astype(np.float32),
            "pe2": np.ascontiguousarray(np.concatenate([pe_own, pe_own], 0)),
            "ohsel": oh, "ohprev": ohp,
        })
        maps.append(m)
    return maps


def kernel(**inputs):
    cores = list(range(NCORES))
    nc = _cached("fused", build_fused)
    res = run_bass_kernel_spmd(nc, host_prep_fused(inputs), core_ids=cores).results
    out = np.zeros((2, S, DM), np.float32)
    for c in range(NCORES):
        b, r = c // 4, c % 4
        out[b, r * T:(r + 1) * T] = np.asarray(res[c]["out"])
    return out



# version 05
# speedup vs baseline: 1.1106x; 1.0143x over previous
import numpy as np
import ml_dtypes
import concourse.bass as bass
import concourse.mybir as mybir
from concourse.bass_utils import run_bass_kernel_spmd
from contextlib import ExitStack

F32 = mybir.dt.float32
BF16 = mybir.dt.bfloat16
AF = mybir.ActivationFunctionType
ALU = mybir.AluOpType
AX = mybir.AxisListType
BF = ml_dtypes.bfloat16

NCORES = 8
T = 2048
NT = T // 128
DM = 1024
NDT = DM // 128
NH = 16
HD = 64
S = 8192
ENGINES = ["sync", "vector", "scalar", "tensor", "gpsimd"]


class Prog:

    def __init__(self, nc):
        self.nc = nc
        self.stream = {e: [] for e in ENGINES}
        self.cnt = {e: 0 for e in ENGINES}
        self.dcnt = {}
        self.waitmax = {e: {} for e in ENGINES}
        self.default_waits = []
        self.prefix = ""

    def emit(self, eng, fn, waits=(), dma=None, inc=None):
        waits = list(waits) + self.default_waits
        if dma is not None:
            key = "D_" + self.prefix + dma
            self.dcnt[key] = self.dcnt.get(key, 0) + (inc if inc is not None else 16)
            tok = (key, self.dcnt[key])
        else:
            self.cnt[eng] += (inc if inc is not None else 1)
            tok = (eng, self.cnt[eng])
        wm = self.waitmax[eng]
        for (we, wc) in waits:
            if wm.get(we, 0) >= wc:
                continue
            wm[we] = wc
            self.stream[eng].append(("wait", we, wc))
        self.stream[eng].append(("op", fn, tok[0], tok[1], (16 if dma is not None else (inc if inc is not None else 1))))
        return tok

    def barrier(self, eng):
        return (eng, self.cnt[eng])

    def dma_total(self, name):
        key = "D_" + self.prefix + name
        return (key, self.dcnt.get(key, 0))

    def all_barriers(self):
        out = [(e, self.cnt[e]) for e in ENGINES if self.cnt[e] > 0]
        out += [(k, v) for k, v in self.dcnt.items()]
        return out

    def finish(self, final_tokens):
        for (we, wc) in final_tokens:
            self.stream["sync"].append(("wait", we, wc))

    def build(self, block, sems_engine):
        nc = self.nc
        sems = dict(sems_engine)
        self._dma_sem_guards = []
        for key in self.dcnt:
            g = nc.semaphore(key)
            sems[key] = g.__enter__()
            self._dma_sem_guards.append(g)
        for e in ENGINES:
            ops = self.stream[e]
            if not ops:
                continue
            engobj = getattr(nc, e)

            def make(ops=ops, engobj=engobj):
                def body(_):
                    for item in ops:
                        if item[0] == "wait":
                            _, we, wc = item
                            engobj.wait_ge(sems[we], wc)
                        else:
                            _, fn, key, c, amt = item
                            instr = fn(engobj)
                            if instr is None:
                                instr = engobj.nop()
                            instr.then_inc(sems[key], amt)
                return body
            getattr(block, e)(make())


class Ctx:

    def __init__(self, name):
        self.nc = bass.Bass("TRN2", target_bir_lowering=False)
        self.stack = ExitStack()
        self.name = name
        self.cur = None
        self.D = None
        self.pfx = ""

    def start(self):
        nc = self.nc
        self.sems = {}
        for e in ENGINES:
            self.sems[e] = self.stack.enter_context(nc.semaphore("S_" + e))
        self.block = self.stack.enter_context(nc.Block())
        self.p = Prog(nc)
        return self.p

    def sbuf(self, name, shape, dtype, stack=None, side=None):
        kw = {"side": side} if side else {}
        tgt = stack or self.cur or self.stack
        if tgt is not self.stack and not side:
            kw = {"side": "right"}
        return tgt.enter_context(self.nc.sbuf_tensor(self.pfx + name, list(shape), dtype, **kw))

    def scope(self):
        return ExitStack()

    def mkflush(self):
        self.flushv = self.sbuf("_flushv", [128, 1], F32)
        self.flushs = self.sbuf("_flushs", [128, 1], F32)
        self.flushs2 = self.sbuf("_flushs2", [128, 1], F32)

    def flush(self, eng):
        return self.p.emit(eng, lambda e: e.drain())

    def psum(self, name, shape, dtype=F32):
        return (self.cur or self.stack).enter_context(self.nc.psum_tensor(self.pfx + name, list(shape), dtype))

    def din(self, name, shape, dtype=F32):
        if self.D is not None:
            return self.D[name]
        return self.nc.dram_tensor(name, list(shape), dtype, kind="ExternalInput")

    def dout(self, name, shape, dtype=F32):
        if self.D is not None:
            return self.D[name]
        return self.nc.dram_tensor(name, list(shape), dtype, kind="ExternalOutput")

    def dint(self, name, shape, dtype=F32):
        return self.nc.dram_tensor(name, list(shape), dtype, kind="Internal")

    def phase_open(self, pfx, sem_pfx=None):
        self.cur = ExitStack()
        self.pfx = pfx
        self.p.prefix = sem_pfx if sem_pfx is not None else pfx
        self.p.default_waits = self.p.all_barriers()

    def phase_close(self):
        self.cur.close()
        self.cur = None
        self.pfx = ""
        self.p.prefix = ""

    def done(self, final_tokens):
        self.p.finish(final_tokens)
        self.p.build(self.block, self.sems)
        self.stack.close()
        return self.nc



def ln_tile(ctx, E, xt, out_bf, tmp, waits, eps=1e-5, nf=1024.0):
    t1 = E("vector", lambda e: e.tensor_scalar(tmp["junk"][:], xt[:], 1.0 / nf, 0.0, ALU.mult, ALU.add, accum_out=tmp["mu"][:]), waits=waits)
    t2 = E("vector", lambda e: e.tensor_scalar(tmp["negmu"][:], tmp["mu"][:], -1.0, None, ALU.mult), waits=[t1])
    t3 = E("scalar", lambda e: e.activation(tmp["sq"][:], xt[:], AF.Square, bias=tmp["negmu"][:], accum_out=tmp["ssq"][:]), waits=[t2])
    t4 = E("vector", lambda e: e.tensor_scalar(tmp["var"][:], tmp["ssq"][:], 1.0 / nf, tmp["eps_t"][:], ALU.mult, ALU.add), waits=[t3])
    t5 = E("scalar", lambda e: e.activation(tmp["var"][:], tmp["var"][:], AF.Ln), waits=[t4])
    t6 = E("scalar", lambda e: e.activation(tmp["rstd"][:], tmp["var"][:], AF.Exp, scale=-0.5), waits=[t5])
    t7 = E("vector", lambda e: e.tensor_scalar(out_bf[:], xt[:], tmp["mu"][:], tmp["rstd"][:], ALU.subtract, ALU.mult), waits=[t6])
    return t7


def transpose_128(ctx, E, p, src_bf, dst_bf, ps_tp, identb, waits):
    t1 = E("tensor", lambda e: e.transpose(ps_tp[:], src_bf[:], identb[:]), waits=waits)
    t2 = E("vector", lambda e: e.tensor_copy(dst_bf[:], ps_tp[:]), waits=[t1])
    return t2



def rel_pos_enc(seq_len, dim):
    positions = np.arange(seq_len, dtype=np.float64)[:, None]
    div_term = np.exp(np.arange(0, dim, 2, dtype=np.float32).astype(np.float64) * -(np.log(10000.0) / dim))
    pe = np.zeros((seq_len, dim), dtype=np.float32)
    pe[:, 0::2] = np.sin(positions * div_term).astype(np.float32)
    pe[:, 1::2] = np.cos(positions * div_term).astype(np.float32)
    return pe


_BUILD_CACHE = {}


def _cached(name, fn):
    if name not in _BUILD_CACHE:
        _BUILD_CACHE[name] = fn()
    return _BUILD_CACHE[name]



DBG_NOQ = False
DBG_SER = False


def phase1(ctx):
    p = ctx.p
    E = p.emit
    x_d = ctx.din("x", [T, DM])
    wqkv_d = ctx.din("wqkv", [DM, 3072])
    bqkv_d = ctx.din("bqkv_r", [128, 24])
    pe2_d = ctx.din("pe2", [128, T])
    bv_d = ctx.din("bv128", [128, DM])
    gwt_d = ctx.din("gate_wT", [DM, 8])
    wg_d = ctx.din("wg_t", [8, 11, 8, 128, 128])
    wu_d = ctx.din("wu_t", [8, 11, 8, 128, 128])
    wd_d = ctx.din("wd_t", [8, 11, 128, DM])
    swg_d = ctx.din("swg_t", [22, 8, 128, 128])
    swu_d = ctx.din("swu_t", [22, 8, 128, 128])
    swd_d = ctx.din("swd_t", [22, 128, DM])
    id_d = ctx.din("ident", [128, 128])

    K_o = ctx.dout("K_own", [DM, T], BF16)
    V_o = ctx.dout("V_own", [T, DM], BF16)
    QP_o = ctx.dout("QP_own", [NH, 65, T], BF16)
    g0_o = ctx.dout("g0", [T, DM])

    xn_fm = ctx.sbuf("xn_fm", [128, NDT * T], BF16)
    identt = ctx.sbuf("identt", [128, 128], F32)
    identb = ctx.sbuf("identb", [128, 128], BF16)
    ps_tp = ctx.psum("ps_tp", [128, 128], BF16)
    eps_t = ctx.sbuf("eps_t", [128, 1], F32)
    one_col = ctx.sbuf("one_col", [128, 1], F32)
    tmp = {k: ctx.sbuf("lntmp_" + k, [128, 1], F32) for k in ["mu", "negmu", "ssq", "var", "rstd"]}
    tmp["eps_t"] = eps_t

    li = E("sync", lambda e: e.dma_start(out=identt[:], in_=id_d[:]), dma="const")
    ci = E("vector", lambda e: e.tensor_copy(identb[:], identt[:]), waits=[li])
    mz1 = E("vector", lambda e: e.memset(eps_t[:], 1e-5))
    mz2 = E("vector", lambda e: e.memset(one_col[:], 1.0))

    gwt = ctx.sbuf("gwt", [128, NDT * 8], F32)
    lg0 = E("sync", lambda e: e.dma_start(out=gwt.ap().rearrange("p (a n) -> p a n", a=NDT), in_=gwt_d.ap().rearrange("(a p) n -> p a n", p=128)), dma="const")
    gates = ctx.sbuf("gates", [128, NT * 8], F32)
    lgt = ctx.sbuf("lgt", [128, 8], F32)
    m8 = ctx.sbuf("m8", [128, 8], F32)
    d21 = ctx.sbuf("d21", [128, 1], F32)
    ez = ctx.sbuf("ez", [128, 1], F32)
    rz = ctx.sbuf("rz", [128, 1], F32)
    pex = ctx.sbuf("pex", [128, 8], F32)
    msk = ctx.sbuf("msk", [128, 8], F32)
    ps_g = ctx.psum("ps_g", [128, 8], F32)
    ps_misc = ctx.psum("ps_misc", [128, 128], F32)

    ln_scope = ctx.scope()
    tmp["sq"] = ctx.sbuf("lntmp_sq", [128, DM], F32, ln_scope, side="right")
    tmp["junk"] = ctx.sbuf("lntmp_junk", [128, DM], F32, ln_scope, side="right")
    xbuf = [ctx.sbuf(f"xbuf{i}", [128, DM], F32, ln_scope, side="right") for i in range(2)]
    xn32 = [ctx.sbuf(f"xn32{i}", [128, DM], F32, ln_scope, side="right") for i in range(2)]
    xnb = [ctx.sbuf(f"xnb{i}", [128, DM], BF16, ln_scope, side="right") for i in range(2)]
    x32t = ctx.sbuf("x32t", [128, NDT * 128], F32, ln_scope, side="right")
    tp_done = []
    tp32_done = []
    prev_use = {}
    gprev = ()
    for tt in range(NT):
        b = tt % 2
        ldw = [prev_use[b]] if b in prev_use else []
        ld = E("sync", lambda e, tt=tt, b=b: e.dma_start(out=xbuf[b][:], in_=x_d[tt * 128:(tt + 1) * 128, :]), waits=ldw, dma="xb%d" % b)
        lnt = ln_tile(ctx, E, xbuf[b], xn32[b], tmp, [ld, mz1])
        cb16 = E("vector", lambda e, b=b: e.tensor_copy(xnb[b][:], xn32[b][:]), waits=[lnt])
        last = cb16
        for dt in range(NDT):
            t1 = E("tensor", lambda e, b=b, dt=dt: e.transpose(ps_tp[:], xnb[b][:, dt * 128:(dt + 1) * 128], identb[:]), waits=[cb16, ci] + ([tp_done[-1]] if tp_done else []))
            t1d = ctx.flush("tensor")
            t2 = E("vector", lambda e, dt=dt, tt=tt: e.tensor_copy(xn_fm[:, dt * T + tt * 128: dt * T + (tt + 1) * 128], ps_tp[:]), waits=[t1d])
            tp_done.append(t2)
            t3 = E("tensor", lambda e, b=b, dt=dt: e.transpose(ps_misc[:], xn32[b][:, dt * 128:(dt + 1) * 128], identt[:]), waits=[lnt, li] + ([tp32_done[-1]] if tp32_done else []))
            t3d = ctx.flush("tensor")
            t4 = E("vector", lambda e, dt=dt: e.tensor_copy(x32t[:, dt * 128:(dt + 1) * 128], ps_misc[:]), waits=[t3d])
            tp32_done.append(t4)
            last = t4
        mms = None
        for dt in range(NDT):
            mms = E("tensor", lambda e, dt=dt: e.matmul(
                ps_g[:], x32t[:, dt * 128:(dt + 1) * 128], gwt[:, dt * 8:(dt + 1) * 8],
                start=(dt == 0), stop=(dt == NDT - 1)),
                waits=([tp32_done[-1], lg0] + list(gprev) if dt == 0 else []))
        mmsd = ctx.flush("tensor")
        c1 = E("vector", lambda e: e.tensor_copy(lgt[:], ps_g[:]), waits=[mmsd])
        c2 = E("vector", lambda e: e.max(m8[:], lgt[:]), waits=[c1])
        c3 = E("vector", lambda e: e.tensor_scalar(d21[:], m8[:, 1:2], m8[:, 0:1], None, ALU.subtract), waits=[c2])
        c4 = E("scalar", lambda e: e.activation(ez[:], d21[:], AF.Exp), waits=[c3])
        c5 = E("vector", lambda e: e.tensor_tensor(ez[:], ez[:], one_col[:], ALU.add), waits=[c4, mz2])
        c6 = E("vector", lambda e: e.reciprocal(rz[:], ez[:]), waits=[c5])
        c7 = E("vector", lambda e: e.tensor_scalar(pex[:], lgt[:], m8[:, 0:1], None, ALU.subtract), waits=[c2])
        c8 = E("scalar", lambda e: e.activation(pex[:], pex[:], AF.Exp), waits=[c7])
        c9 = E("vector", lambda e: e.tensor_scalar(msk[:], lgt[:], m8[:, 1:2], None, ALU.is_ge), waits=[c8])
        c10 = E("vector", lambda e: e.tensor_tensor(pex[:], pex[:], msk[:], ALU.mult), waits=[c9])
        c11 = E("vector", lambda e, tt=tt: e.tensor_scalar(gates[:, tt * 8:(tt + 1) * 8], pex[:], rz[:], None, ALU.mult), waits=[c10, c6])
        gprev = (c11, c2)
        prev_use[b] = last
    XN = p.barrier("vector")
    XNT = p.barrier("tensor")
    GATES = p.barrier("vector")
    LNB = p.all_barriers()
    ln_scope.close()

    qkv_scope = ctx.scope()
    pe2 = ctx.sbuf("pe2_s", [128, T], F32, qkv_scope, side="right")
    bv = ctx.sbuf("bv", [128, DM], F32, qkv_scope, side="right")
    bq = ctx.sbuf("bq", [128, 24], F32, qkv_scope, side="right")
    lb = E("sync", lambda e: e.dma_start(out=bq[:], in_=bqkv_d[:]), dma="const")
    lpe = E("sync", lambda e: e.dma_start(out=pe2[:], in_=pe2_d[:]), waits=LNB, dma="const")
    lbv = E("sync", lambda e: e.dma_start(out=bv[:], in_=bv_d[:]), waits=LNB, dma="const")
    wq_f = [ctx.sbuf(f"wq_f{i}", [128, 3072], F32, qkv_scope, side="right") for i in range(4)]
    wqkv_bf = [ctx.sbuf(f"wqkv_bf{i}", [128, 3072], BF16, qkv_scope, side="right") for i in range(NDT)]
    wq_cast = []
    prev_cast = {}
    for dt in range(NDT):
        b = dt % 4
        ldw = [prev_cast[b]] if b in prev_cast else []
        ld = E("sync", lambda e, dt=dt, b=b: e.dma_start(out=wq_f[b][:], in_=wqkv_d[dt * 128:(dt + 1) * 128, :]), waits=ldw + LNB, dma="wqf%d" % b)
        cs = E("vector", lambda e, dt=dt, b=b: e.tensor_copy(wqkv_bf[dt][:], wq_f[b][:]), waits=[ld])
        fl = ctx.flush("vector")
        prev_cast[b] = fl
        wq_cast.append(fl)

    ps_qk = [ctx.psum(f"ps_qk{i}", [128, 512], F32) for i in range(2)]
    kst = [ctx.sbuf(f"kst{i}", [128, 512], BF16, qkv_scope, side="right") for i in range(4)]
    k_stores = []
    q_stores = []
    pb = 0
    ps_free = [(), ()]
    for ot in range(16):
        for tc in range(4):
            pb ^= 1
            mms = None
            for dt in range(NDT):
                mms = E("tensor", lambda e, ot=ot, tc=tc, dt=dt, pb=pb: e.matmul(
                    ps_qk[pb][:], wqkv_bf[dt][:, ot * 128:(ot + 1) * 128],
                    xn_fm[:, dt * T + tc * 512: dt * T + (tc + 1) * 512],
                    start=(dt == 0), stop=(dt == NDT - 1)),
                    waits=(list(ps_free[pb]) + [XN, wq_cast[-1]] if dt == 0 else []))
            mms = ctx.flush("tensor")
            if ot < 8:
                h0, h1 = 2 * ot, 2 * ot + 1
                sb = tc % 2
                e1 = E("vector", lambda e, ot=ot, pb=pb, sb=sb: e.tensor_scalar(
                    kst[sb][:], ps_qk[pb][:], bq[:, ot:ot + 1], None, ALU.add),
                    waits=[mms, lb] + list(q_stores[-4:-2] if len(q_stores) >= 4 else []))
                fl = ctx.flush("vector")
                st1 = E("gpsimd", lambda e, h0=h0, tc=tc, sb=sb: e.dma_start(
                    out=QP_o[h0, 0:64, tc * 512:(tc + 1) * 512], in_=kst[sb][0:64, :]), waits=[fl], dma="qst%d" % sb) if not DBG_NOQ else ctx.flush("vector")
                st2 = E("gpsimd", lambda e, h1=h1, tc=tc, sb=sb: e.dma_start(
                    out=QP_o[h1, 0:64, tc * 512:(tc + 1) * 512], in_=kst[sb][64:128, :]), waits=[fl], dma="qst%d" % sb) if not DBG_NOQ else ctx.flush("vector")
                q_stores += [st1, st2]
                ps_free[pb] = (fl,)
            else:
                sb = 2 + tc % 2
                e1a = E("vector", lambda e, ot=ot, pb=pb, sb=sb: e.tensor_scalar(
                    kst[sb][:], ps_qk[pb][:], bq[:, ot:ot + 1], None, ALU.add),
                    waits=[mms, lb, lpe] + list(k_stores[-2:-1] if len(k_stores) >= 2 else []))
                e1 = E("vector", lambda e, tc=tc, sb=sb: e.tensor_tensor(
                    kst[sb][:], kst[sb][:], pe2[:, tc * 512:(tc + 1) * 512], ALU.add), waits=[e1a])
                fl = ctx.flush("vector")
                st = E("gpsimd", lambda e, ot=ot, tc=tc, sb=sb: e.dma_start(
                    out=K_o[(ot - 8) * 128:(ot - 7) * 128, tc * 512:(tc + 1) * 512], in_=kst[sb][:]), waits=[fl], dma="kst%d" % sb)
                k_stores.append(st)
                ps_free[pb] = (fl, st)
    KDONE = [p.dma_total("kst2"), p.dma_total("kst3")]

    vst = [ctx.sbuf(f"vst{i}", [128, 512], BF16, qkv_scope, side="right") for i in range(2)]
    v_stores = []
    ps_free = [(), ()]
    for tt in range(NT):
        for vc in range(2):
            pb ^= 1
            mms = None
            for dt in range(NDT):
                mms = E("tensor", lambda e, tt=tt, vc=vc, dt=dt, pb=pb: e.matmul(
                    ps_qk[pb][:], xn_fm[:, dt * T + tt * 128: dt * T + (tt + 1) * 128],
                    wqkv_bf[dt][:, 2048 + vc * 512: 2048 + (vc + 1) * 512],
                    start=(dt == 0), stop=(dt == NDT - 1)),
                    waits=(list(ps_free[pb]) if dt == 0 else []))
            mms = ctx.flush("tensor")
            sb = vc
            e1 = E("vector", lambda e, vc=vc, pb=pb, sb=sb: e.tensor_tensor(
                vst[sb][:], ps_qk[pb][:], bv[:, vc * 512:(vc + 1) * 512], ALU.add),
                waits=[mms, lbv] + (v_stores[-2:-1] if len(v_stores) >= 2 else []))
            fl = ctx.flush("vector")
            st = E("gpsimd", lambda e, tt=tt, vc=vc, sb=sb: e.dma_start(
                out=V_o[tt * 128:(tt + 1) * 128, vc * 512:(vc + 1) * 512], in_=vst[sb][:]), waits=[fl], dma="vst%d" % sb)
            v_stores.append(st)
            ps_free[pb] = (fl,)

    QDONE = [p.dma_total("qst0"), p.dma_total("qst1"), p.dma_total("vst0"), p.dma_total("vst1")]
    k0b = ctx.sbuf("k0b", [64, 128], BF16, qkv_scope, side="right")
    qtb = ctx.sbuf("qtb", [64, T], BF16, qkv_scope, side="right")
    ps_mrow = ctx.psum("ps_mrow", [1, 128], F32)
    mcol = ctx.sbuf("mcol", [128, 1], F32, qkv_scope, side="right")
    mrow_st = ctx.sbuf("mrow_st", [1, T], BF16, qkv_scope, side="right")
    qp_stores = []
    for h in range(NH):
        lk = E("scalar", lambda e, h=h: e.dma_start(out=k0b[:], in_=K_o[h * 64:(h + 1) * 64, 0:128]), waits=KDONE + QDONE, dma="k0b")
        lq = E("scalar", lambda e, h=h: e.dma_start(out=qtb[:], in_=QP_o[h, 0:64, :]), waits=QDONE + qp_stores[-1:], dma="qtb")
        prev = (lk, lq)
        for qt in range(NT):
            m1 = E("tensor", lambda e, h=h, qt=qt: e.matmul(
                ps_misc[:], qtb[:, qt * 128:(qt + 1) * 128], k0b[:], start=True, stop=True), waits=list(prev))
            m1d = ctx.flush("tensor")
            r1 = E("vector", lambda e: e.tensor_reduce(mcol[:], ps_misc[:], AX.X, ALU.max), waits=[m1d])
            r1d = ctx.flush("vector")
            t1 = E("tensor", lambda e: e.transpose(ps_mrow[:], mcol[:], identt[:]), waits=[r1d])
            t1d = ctx.flush("tensor")
            c1 = E("scalar", lambda e, h=h, qt=qt: e.copy(mrow_st[:, qt * 128:(qt + 1) * 128], ps_mrow[:]), waits=[t1d])
            fls = ctx.flush("scalar")
            prev = (fls,)
        qp_stores.append(E("gpsimd", lambda e, h=h: e.dma_start(out=QP_o[h, 64:65, :], in_=mrow_st[:]), waits=[prev[0]], dma="qpst"))
    QKVB = p.all_barriers()
    qkv_scope.close()


    ysum = ctx.sbuf("ysum", [128, NT * DM], F32)
    zs = E("vector", lambda e: e.memset(ysum[:], 0.0))
    wgb = [ctx.sbuf(f"wgb{i}", [128, 8 * 128], BF16) for i in range(2)]
    wub = [ctx.sbuf(f"wub{i}", [128, 8 * 128], BF16) for i in range(2)]
    wgf = [ctx.sbuf("wgf0", [128, 8 * 128], F32)] * 2
    wuf = [ctx.sbuf("wuf0", [128, 8 * 128], F32)] * 2
    wdb = ctx.sbuf("wdb", [128, 11 * DM], BF16)
    wdf = [ctx.sbuf(f"wdf{i}", [128, DM], F32) for i in range(2)]
    Pe = ctx.sbuf("Pe", [128, 11 * T], BF16)
    sg = ctx.sbuf("sg", [128, 512], F32)
    sg2 = ctx.sbuf("sg2", [128, 512], F32)
    ps_h = [ctx.psum(f"ps_h{i}", [128, 512], F32) for i in range(2)]

    def expert_block(tag, n_it, wg_src, wu_src, wd_src, gate_e, dep0):
        pg_dep = {0: ()}
        pu_dep = {1: ()}
        wprev = {0: None, 1: None}
        up_done = []
        wchain = [None]
        for it in range(n_it):
            b = it % 2
            wl = [wchain[0]] if wchain[0] is not None else []
            lw1 = E("scalar", lambda e, it=it, b=b: e.dma_start(out=wgf[b].ap().rearrange("p (a n) -> p a n", a=8), in_=wg_src(it).rearrange("a p n -> p a n")), waits=wl + list(dep0), dma="wgf")
            cw1_ = E("vector", lambda e, b=b: e.tensor_copy(wgb[b][:], wgf[b][:]), waits=[lw1])
            cw1 = ctx.flush("vector")
            lw2 = E("scalar", lambda e, it=it, b=b: e.dma_start(out=wuf[b].ap().rearrange("p (a n) -> p a n", a=8), in_=wu_src(it).rearrange("a p n -> p a n")), waits=[cw1], dma="wuf")
            cw2_ = E("vector", lambda e, b=b: e.tensor_copy(wub[b][:], wuf[b][:]), waits=[lw2])
            cw2 = ctx.flush("vector")
            wprev[b] = cw2
            for tc in range(4):
                pba = 0
                pbb = 1
                mg = None
                for dt in range(NDT):
                    mg = E("tensor", lambda e, it=it, tc=tc, dt=dt, b=b, pba=pba: e.matmul(
                        ps_h[pba][:], wgb[b][:, dt * 128:(dt + 1) * 128],
                        xn_fm[:, dt * T + tc * 512:dt * T + (tc + 1) * 512],
                        start=(dt == 0), stop=(dt == NDT - 1)),
                        waits=([cw1, XN] + list(pg_dep[pba]) if dt == 0 else []))
                mu_ = None
                for dt in range(NDT):
                    mu_ = E("tensor", lambda e, it=it, tc=tc, dt=dt, b=b, pbb=pbb: e.matmul(
                        ps_h[pbb][:], wub[b][:, dt * 128:(dt + 1) * 128],
                        xn_fm[:, dt * T + tc * 512:dt * T + (tc + 1) * 512],
                        start=(dt == 0), stop=(dt == NDT - 1)),
                        waits=([cw2] + list(pu_dep[pbb]) if dt == 0 else []))
                mgd = ctx.flush("tensor")
                a1_ = E("scalar", lambda e, pba=pba: e.activation(sg[:], ps_h[pba][:], AF.Silu), waits=[mgd])
                a1 = ctx.flush("scalar")
                mud = ctx.flush("tensor")
                a2_ = E("vector", lambda e, it=it, tc=tc, pbb=pbb: e.tensor_tensor(
                    Pe[:, it * T + tc * 512: it * T + (tc + 1) * 512], ps_h[pbb][:], sg[:], ALU.mult), waits=[mud, a1])
                a2 = ctx.flush("vector")
                pg_dep[pba] = (a1,)
                pu_dep[pbb] = (a2,)
                up_done.append(a2)
                wchain[0] = mu_
        wd_cast = []
        wdp = {0: None, 1: None}
        for it in range(n_it):
            b = it % 2
            wl = [wdp[b]] if wdp[b] is not None else []
            lw = E("scalar", lambda e, it=it, b=b: e.dma_start(out=wdf[b][:], in_=wd_src(it)), waits=wl, dma="wdf%d" % b)
            cw_ = E("vector", lambda e, it=it, b=b: e.tensor_copy(wdb[:, it * DM:(it + 1) * DM], wdf[b][:]), waits=[lw])
            cw = ctx.flush("vector")
            wdp[b] = cw
            wd_cast.append(cw)
        UP = up_done[-1]
        UPS = p.barrier("scalar")
        WD = wd_cast[-1]
        dn_dep = {0: (), 1: ()}
        outs = []
        for tt in range(NT):
            for hf in range(2):
                pb2 = hf
                md = None
                for it in range(n_it):
                    md = E("tensor", lambda e, tt=tt, hf=hf, it=it, pb2=pb2: e.matmul(
                        ps_h[pb2][:], Pe[:, it * T + tt * 128: it * T + (tt + 1) * 128],
                        wdb[:, it * DM + hf * 512: it * DM + (hf + 1) * 512],
                        start=(it == 0), stop=(it == n_it - 1)),
                        waits=([UP, UPS, WD] + list(dn_dep[pb2]) if it == 0 else []))
                mdd = ctx.flush("tensor")
                if gate_e is None:
                    o1 = E("vector", lambda e, tt=tt, hf=hf, pb2=pb2: e.tensor_tensor(
                        ysum[:, tt * DM + hf * 512: tt * DM + (hf + 1) * 512],
                        ysum[:, tt * DM + hf * 512: tt * DM + (hf + 1) * 512], ps_h[pb2][:], ALU.add), waits=[mdd, zs])
                else:
                    o0 = E("vector", lambda e, tt=tt, pb2=pb2, ge=gate_e: e.tensor_scalar(
                        sg2[:], ps_h[pb2][:], gates[:, tt * 8 + ge: tt * 8 + ge + 1], None, ALU.mult), waits=[mdd, GATES])
                    o1 = E("vector", lambda e, tt=tt, hf=hf: e.tensor_tensor(
                        ysum[:, tt * DM + hf * 512: tt * DM + (hf + 1) * 512],
                        ysum[:, tt * DM + hf * 512: tt * DM + (hf + 1) * 512], sg2[:], ALU.add), waits=[o0, zs])
                o1f = ctx.flush("vector")
                dn_dep[pb2] = (o1f,)
                outs.append(o1f)
        return outs[-1]

    dep0 = QKVB
    n_experts = 8
    for ex in range(n_experts):
        last = expert_block(f"e{ex}", 11,
                            (lambda it, ex=ex: wg_d[ex, it, :, :, :]),
                            (lambda it, ex=ex: wu_d[ex, it, :, :, :]),
                            (lambda it, ex=ex: wd_d[ex, it, :, :]),
                            ex, dep0)
        dep0 = p.all_barriers()
    for half in range(2):
        last = expert_block(f"sh{half}", 11,
                            (lambda it, half=half: swg_d[half * 11 + it, :, :, :]),
                            (lambda it, half=half: swu_d[half * 11 + it, :, :, :]),
                            (lambda it, half=half: swd_d[half * 11 + it, :, :]),
                            None, dep0)
        dep0 = p.all_barriers()

    g0st = [ctx.sbuf(f"g0st{i}", [128, DM], F32) for i in range(2)]
    g_stores = []
    for tt in range(NT):
        b = tt % 2
        wl = [g_stores[-2]] if len(g_stores) >= 2 else []
        a1 = E("scalar", lambda e, tt=tt, b=b: e.activation(g0st[b][:], ysum[:, tt * DM:(tt + 1) * DM], AF.Silu), waits=[last] + wl)
        fls = ctx.flush("scalar")
        st = E("sync", lambda e, tt=tt, b=b: e.dma_start(out=g0_o[tt * 128:(tt + 1) * 128, :], in_=g0st[b][:]), waits=[fls], dma="g0st%d" % b)
        g_stores.append(st)

    return





def phase2(ctx):
    p = ctx.p
    E = p.emit
    QP_d = ctx.din("QP", [NH, 65, T], BF16)
    K_d = ctx.din("K_all", [4 * DM, T], BF16)
    V_d = ctx.din("V_all", [S, DM], BF16)
    g0_d = ctx.din("g0", [T, DM])
    x_d = ctx.din("x", [T, DM])
    wout_d = ctx.din("wout", [DM, DM])
    bout_d = ctx.din("bout128", [128, DM])
    id_d = ctx.din("ident", [128, 128])

    x1_o = ctx.dout("x1", [T, DM])
    xn1_o = ctx.dout("xn1", [T, DM])
    xn1fm_o = ctx.dout("xn1fm", [128, NDT * T], BF16)

    identt = ctx.sbuf("identt", [128, 128], F32)
    identb = ctx.sbuf("identb", [128, 128], BF16)
    eps_t = ctx.sbuf("eps_t", [128, 1], F32)
    li = E("sync", lambda e: e.dma_start(out=identt[:], in_=id_d[:]), dma="const")
    ci = E("vector", lambda e: e.tensor_copy(identb[:], identt[:]), waits=[li])
    cid = ctx.flush("vector")
    mz1 = E("vector", lambda e: e.memset(eps_t[:], 1e-5))

    pt_guard = [None, None, None, None]
    AO = [ctx.sbuf(f"AO{i}", [128, T], BF16) for i in range(8)]
    qpb = [ctx.sbuf(f"qpb{i}", [65, T], BF16) for i in range(2)]
    Kh = [ctx.sbuf(f"Kh{i}", [65, S], BF16) for i in range(2)]
    vtm = [ctx.sbuf(f"vtm{i}", [128, 64 * 65], BF16) for i in range(2)]
    Pt = [ctx.sbuf(f"Pt{i}", [128, 512], BF16) for i in range(4)]
    ones64 = ctx.sbuf("ones64", [1, 64], F32)
    rzf = ctx.sbuf("rzf", [1, 512], F32)
    ofb = ctx.sbuf("ofb", [64, 512], BF16)
    ps_S = [ctx.psum(f"ps_S{i}", [128, 512], F32) for i in range(2)]
    ps_O = ctx.psum("ps_O", [65, 512], F32)
    ps_rz = ctx.psum("ps_rz", [64, 512], F32)
    ps_s0 = [ctx.psum(f"ps_s0{i}", [128, 512], F32) for i in range(2)]
    ps_S4 = [ps_S[0], ps_S[1], ps_s0[0], ps_s0[1]]
    ps_tp = ctx.psum("ps_tp", [128, 128], BF16)

    mz2 = E("vector", lambda e: e.memset(ones64[:], 1.0))
    for i in range(2):
        E("vector", lambda e, i=i: e.memset(Kh[i][64:65, :], -1.0))
        E("vector", lambda e, i=i: e.memset(vtm[i].ap().rearrange("p (a n) -> p a n", a=64)[:, :, 64:65], 1.0))
    mzdr = ctx.flush("vector")

    V_r = V_d.ap().rearrange("(a p) n -> p a n", p=128)
    head_guard = [None, None]
    qb_guard = None
    ao_done = []
    for h in range(NH):
        hb = h % 2
        hw = [x for x in [head_guard[hb], mzdr] if x is not None]
        lqp = E("sync", lambda e, h=h, hb=hb: e.dma_start(out=qpb[hb][0:65, :], in_=QP_d[h, :, :]),
                waits=hw, dma=f"qp{hb}")
        lks = []
        for r in range(4):
            lks.append(E("sync", lambda e, h=h, r=r, hb=hb: e.dma_start(
                out=Kh[hb][0:64, r * T:(r + 1) * T], in_=K_d[r * DM + h * 64: r * DM + (h + 1) * 64, :]),
                waits=hw, dma=f"kh{hb}"))
        lvs = []
        for q4 in range(4):
            lvs.append(E("gpsimd", lambda e, h=h, q4=q4, hb=hb: e.dma_start(
                out=vtm[hb].ap().rearrange("p (a n) -> p a n", a=64)[:, q4 * 16:(q4 + 1) * 16, 0:64],
                in_=V_r[:, q4 * 16:(q4 + 1) * 16, h * 64:(h + 1) * 64]),
                waits=hw, dma=f"vh{hb}"))
        for qb in range(4):
            for kt4 in range(16):
                for j in range(4):
                    kt = kt4 * 4 + j
                    E("tensor", lambda e, kt=kt, qb=qb, j=j, hb=hb: e.matmul(
                        ps_S4[j][:], Kh[hb][:, kt * 128:(kt + 1) * 128], qpb[hb][0:65, qb * 512:(qb + 1) * 512],
                        start=True, stop=True),
                        waits=[lks[kt // 16], lqp] + [x for x in [pt_guard[j]] if x is not None])
                m1d = ctx.flush("tensor")
                for j in range(4):
                    E("scalar", lambda e, j=j: e.activation(Pt[j][:], ps_S4[j][:], AF.Exp, scale=0.125), waits=[m1d])
                a1d = ctx.flush("scalar")
                for j in range(4):
                    pt_guard[j] = a1d
                for j in range(4):
                    kt = kt4 * 4 + j
                    E("tensor", lambda e, kt=kt, j=j, hb=hb: e.matmul(
                        ps_O[:], vtm[hb].ap().rearrange("p (a n) -> p a n", a=64)[:, kt:kt + 1, :].squeeze(1), Pt[j][:],
                        start=(kt == 0), stop=(kt == 63)),
                        waits=[a1d, lvs[kt // 16]] + [x for x in [qb_guard] if (kt == 0 and x is not None)])
            omd = ctx.flush("tensor")
            r1 = E("vector", lambda e: e.reciprocal(rzf[:], ps_O[64:65, :]), waits=[omd])
            r1d = ctx.flush("vector")
            mz_ = E("tensor", lambda e: e.matmul(ps_rz[:], ones64[:], rzf[:], start=True, stop=True), waits=[r1d, mz2])
            mzd = ctx.flush("tensor")
            c1 = E("scalar", lambda e: e.copy(ofb[:], ps_O[0:64, :]), waits=[omd])
            c1d = ctx.flush("scalar")
            o1 = E("vector", lambda e, h=h, qb=qb: e.tensor_tensor(
                AO[h // 2][(h % 2) * 64:(h % 2) * 64 + 64, qb * 512:(qb + 1) * 512], ofb[:], ps_rz[:], ALU.mult),
                waits=[mzd, c1d])
            o1d = ctx.flush("vector")
            ao_done.append(o1d)
            qb_guard = o1d
        head_guard[hb] = qb_guard
    AODONE = p.barrier("vector")

    wof = [ctx.sbuf(f"wof{i}", [128, DM], F32) for i in range(2)]
    wob = [ctx.sbuf(f"wob{i}", [128, DM], BF16) for i in range(8)]
    wcast = []
    wprev = {0: None, 1: None}
    for hp in range(8):
        b = hp % 2
        lw = E("sync", lambda e, hp=hp, b=b: e.dma_start(out=wof[b][:], in_=wout_d[hp * 128:(hp + 1) * 128, :]),
               waits=[x for x in [wprev[b]] if x is not None], dma=f"wof{b}")
        cw = E("vector", lambda e, hp=hp, b=b: e.tensor_copy(wob[hp][:], wof[b][:]), waits=[lw])
        cwd = ctx.flush("vector")
        wprev[b] = cwd
        wcast.append(cwd)
    bout = ctx.sbuf("bout", [128, DM], F32)
    lbo = E("sync", lambda e: e.dma_start(out=bout[:], in_=bout_d[:]), dma="const")

    tmp = {k: ctx.sbuf("lntmp_" + k, [128, 1], F32) for k in ["mu", "negmu", "ssq", "var", "rstd"]}
    tmp["sq"] = ctx.sbuf("lntmp_sq", [128, DM], F32)
    tmp["junk"] = ctx.sbuf("lntmp_junk", [128, DM], F32)
    tmp["eps_t"] = eps_t
    xb = [ctx.sbuf(f"xb{i}", [128, DM], F32) for i in range(2)]
    g0b = [ctx.sbuf(f"g0b{i}", [128, DM], F32) for i in range(2)]
    x1t = ctx.sbuf("x1t", [128, DM], F32)
    s0f = ctx.sbuf("s0f", [128, DM], F32)
    xn32 = ctx.sbuf("xn32", [128, DM], F32)
    xnb = ctx.sbuf("xnb", [128, DM], BF16)
    ctx.xnfm_st = ctx.sbuf("xnfm_st", [128, 128], BF16)
    xg_guard = {0: None, 1: None}
    st_guard = {"x1": None, "xn1": None, "s0": None, "t1last": None, "fmst": None}
    for tt in range(NT):
        b = tt % 2
        gw = [x for x in [xg_guard[b]] if x is not None]
        lx = E("sync", lambda e, tt=tt, b=b: e.dma_start(out=xb[b][:], in_=x_d[tt * 128:(tt + 1) * 128, :]), waits=gw, dma=f"xb{b}")
        lg = E("scalar", lambda e, tt=tt, b=b: e.dma_start(out=g0b[b][:], in_=g0_d[tt * 128:(tt + 1) * 128, :]), waits=gw, dma=f"g0b{b}")
        mm = {}
        for half in range(2):
            m_ = None
            for hp in range(8):
                m_ = E("tensor", lambda e, tt=tt, half=half, hp=hp: e.matmul(
                    ps_s0[half][:], AO[hp][:, tt * 128:(tt + 1) * 128], wob[hp][:, half * 512:(half + 1) * 512],
                    start=(hp == 0), stop=(hp == 7)),
                    waits=([AODONE, wcast[-1]] + [x for x in [st_guard["s0"]] if x is not None] if hp == 0 else []))
            mm[half] = m_
        mmd = ctx.flush("tensor")
        e1 = None
        for half in range(2):
            sl = slice(half * 512, (half + 1) * 512)
            e1 = E("vector", lambda e, half=half, sl=sl: e.tensor_tensor(s0f[:, sl], ps_s0[half][:], bout[:, sl], ALU.add),
                   waits=[mmd, lbo] + [x for x in [st_guard["x1"]] if x is not None])
        e2 = E("vector", lambda e, b=b: e.tensor_tensor(s0f[:], s0f[:], g0b[b][:], ALU.mult), waits=[e1, lg])
        e3 = E("vector", lambda e, b=b: e.tensor_tensor(x1t[:], s0f[:], xb[b][:], ALU.add), waits=[e2, lx])
        e3d = ctx.flush("vector")
        st_guard["s0"] = e3d
        xg_guard[b] = e3d
        sx = E("gpsimd", lambda e, tt=tt: e.dma_start(out=x1_o[tt * 128:(tt + 1) * 128, :], in_=x1t[:]),
               waits=[e3d] + [x for x in [st_guard["x1"]] if x is not None], dma="x1st")
        st_guard["x1"] = sx
        lnt = ln_tile(ctx, E, x1t, xn32, tmp, [e3d, mz1] + [x for x in [st_guard["xn1"]] if x is not None])
        lnd = ctx.flush("vector")
        sn = E("gpsimd", lambda e, tt=tt: e.dma_start(out=xn1_o[tt * 128:(tt + 1) * 128, :], in_=xn32[:]),
               waits=[lnd], dma="xn1st")
        st_guard["xn1"] = sn
        cb = E("vector", lambda e: e.tensor_copy(xnb[:], xn32[:]),
               waits=[lnd] + [x for x in [st_guard["t1last"]] if x is not None])
        cbd = ctx.flush("vector")
        for dt in range(NDT):
            t1 = E("tensor", lambda e, dt=dt: e.transpose(ps_tp[:], xnb[:, dt * 128:(dt + 1) * 128], identb[:]),
                   waits=[cbd, cid] + [x for x in [st_guard["fmst"]] if x is not None])
            t1d = ctx.flush("tensor")
            t2 = E("vector", lambda e: e.tensor_copy(ctx.xnfm_st[:], ps_tp[:]), waits=[t1d])
            t2d = ctx.flush("vector")
            t3 = E("gpsimd", lambda e, dt=dt, tt=tt: e.dma_start(
                out=xn1fm_o[:, dt * T + tt * 128: dt * T + (tt + 1) * 128], in_=ctx.xnfm_st[:]), waits=[t2d], dma="xnfmst")
            st_guard["fmst"] = t3
            st_guard["t1last"] = t1d
    return


def build_launch1():
    ctx = Ctx("l1")
    p = ctx.start(); ctx.p = p; ctx.mkflush()
    phase1(ctx)
    return ctx.done(p.all_barriers())


def build_launch2():
    ctx = Ctx("l2")
    p = ctx.start(); ctx.p = p; ctx.mkflush()
    phase2(ctx)
    return ctx.done(p.all_barriers())


def host_prep_l1(inputs):
    x = np.asarray(inputs["L0_attn_x"])
    wqkv = np.ascontiguousarray(inputs["L0_attn_W_qkv"]).astype(np.float32)
    bqkv = np.asarray(inputs["L0_attn_b_qkv"]).astype(np.float32)
    bqkv_r = np.ascontiguousarray(bqkv.reshape(24, 128).T)
    bv128 = np.ascontiguousarray(np.broadcast_to(bqkv[2048:3072][None, :], (128, 1024)))
    pe = rel_pos_enc(S, HD)
    gate_wT = np.ascontiguousarray(np.asarray(inputs["L0_ffn_gate_w"]).astype(np.float32).T)
    wg = np.asarray(inputs["L0_ffn_wg"]).astype(np.float32)
    wu = np.asarray(inputs["L0_ffn_wu"]).astype(np.float32)
    wd = np.asarray(inputs["L0_ffn_wd"]).astype(np.float32)
    wg_t = np.ascontiguousarray(wg.reshape(8, 8, 128, 11, 128).transpose(0, 3, 1, 2, 4))
    wu_t = np.ascontiguousarray(wu.reshape(8, 8, 128, 11, 128).transpose(0, 3, 1, 2, 4))
    wd_t = np.ascontiguousarray(wd.reshape(8, 11, 128, 1024))
    swg = np.asarray(inputs["L0_ffn_swg"]).astype(np.float32)
    swu = np.asarray(inputs["L0_ffn_swu"]).astype(np.float32)
    swd = np.asarray(inputs["L0_ffn_swd"]).astype(np.float32)
    swg_t = np.ascontiguousarray(swg.reshape(8, 128, 22, 128).transpose(2, 0, 1, 3))
    swu_t = np.ascontiguousarray(swu.reshape(8, 128, 22, 128).transpose(2, 0, 1, 3))
    swd_t = np.ascontiguousarray(swd.reshape(22, 128, 1024))
    ident = np.eye(128, dtype=np.float32)
    maps = []
    for c in range(NCORES):
        b, r = c // 4, c % 4
        sl = slice(r * T, (r + 1) * T)
        pe_own = pe[sl].T
        pe2 = np.ascontiguousarray(np.concatenate([pe_own, pe_own], 0))
        maps.append({
            "x": np.ascontiguousarray(x[b, sl]).astype(np.float32),
            "wqkv": wqkv, "bqkv_r": bqkv_r, "pe2": pe2, "bv128": bv128,
            "gate_wT": gate_wT, "wg_t": wg_t, "wu_t": wu_t, "wd_t": wd_t,
            "swg_t": swg_t, "swu_t": swu_t, "swd_t": swd_t, "ident": ident,
        })
    return maps





def phase3a(ctx, i):
    p = ctx.p
    E = p.emit
    mfm_d = ctx.din("m_fm", [128, NDT * T], BF16)
    w4_d = ctx.din("in_wT_t4", [4, 32, NDT, 128, 128])
    xz_o = ctx.dout("xz", [4096, T], BF16)

    mfm = ctx.sbuf("mfm", [128, NDT * T], BF16)
    lm = E("sync", lambda e: e.dma_start(out=mfm[:], in_=mfm_d[:]), dma="const")
    wf = [ctx.sbuf(f"wf{i}", [128, NDT * 128], F32) for i in range(2)]
    wb = [ctx.sbuf(f"wb{i}", [128, NDT * 128], BF16) for i in range(2)]
    stg = [ctx.sbuf(f"stg{i}", [128, 512], BF16) for i in range(2)]
    ps = [ctx.psum(f"ps{i}", [128, 512], F32) for i in range(2)]
    wprev = {0: None, 1: None}
    st_guard = {0: None, 1: None}
    ps_guard = {0: None, 1: None}
    for ot in range(32):
        b = ot % 2
        lw = E("sync", lambda e, ot=ot, b=b, i=i: e.dma_start(
            out=wf[b].ap().rearrange("p (a n) -> p a n", a=NDT), in_=w4_d[i, ot, :, :, :].rearrange("a p n -> p a n")),
            waits=[x for x in [wprev[b]] if x is not None], dma=f"wf{b}")
        cw_ = E("vector", lambda e, b=b: e.tensor_copy(wb[b][:], wf[b][:]), waits=[lw])
        cw = ctx.flush("vector")
        wprev[b] = cw
        for tc in range(4):
            pb = tc % 2
            m_ = None
            for dt in range(NDT):
                m_ = E("tensor", lambda e, ot=ot, tc=tc, dt=dt, b=b, pb=pb: e.matmul(
                    ps[pb][:], wb[b][:, dt * 128:(dt + 1) * 128], mfm[:, dt * T + tc * 512: dt * T + (tc + 1) * 512],
                    start=(dt == 0), stop=(dt == NDT - 1)),
                    waits=([cw, lm] + [x for x in [ps_guard[pb]] if x is not None] if dt == 0 else []))
            md = ctx.flush("tensor")
            sb = tc % 2
            c1 = E("scalar", lambda e, pb=pb, sb=sb: e.copy(stg[sb][:], ps[pb][:]),
                   waits=[md] + [x for x in [st_guard[sb]] if x is not None] + (p.all_barriers() if DBG_SER else []))
            c1d = ctx.flush("scalar")
            ps_guard[pb] = c1d
            st = E("gpsimd", lambda e, ot=ot, tc=tc, sb=sb: e.dma_start(
                out=xz_o[ot * 128:(ot + 1) * 128, tc * 512:(tc + 1) * 512], in_=stg[sb][:]), waits=[c1d], dma=f"st{sb}")
            st_guard[sb] = st
        wprev[b] = md
    return



def phase3b(ctx, i):
    p = ctx.p
    E = p.emit
    xz_d = ctx.din("xz", [4096, T], BF16)
    tails_d = ctx.din("tails_ag", [4 * 2048, 3], BF16)
    ohprev_d = ctx.din("ohprev", [128, 4])
    cw_d = ctx.din("cw_r4", [4, 16, 128, 4])[i]
    cb_d = ctx.din("cb_r4", [4, 128, 16])[i]
    xpwT_d = ctx.din("xp_wT_t4", [4, 16, 128, 96])[i]
    dtwT_d = ctx.din("dt_wT_t4", [4, 16, 64, 128])[i]
    dtb_d = ctx.din("dtb_r4", [4, 128, 16])[i]
    alog_d = ctx.din("alog_r4", [4, 16, 128, 16])[i]
    dskip_d = ctx.din("dskip_r4", [4, 128, 16])[i]
    id_d = ctx.din("ident", [128, 128])

    xc_o = ctx.dout("xc", [2048, T], BF16)
    cumd_o = ctx.dout("cumd", [2048, T])
    bc_o = ctx.dout("bc", [32, T], BF16)
    s_o = ctx.dout("s_sum", [2048, 16])
    pc_o = ctx.dout("p_sum", [2048, 16])
    yl_o = ctx.dout("y_local", [2048, T], BF16)

    identt = ctx.sbuf("identt", [128, 128], F32)
    identb = ctx.sbuf("identb", [128, 128], BF16)
    li = E("sync", lambda e: e.dma_start(out=identt[:], in_=id_d[:]), dma="const")
    ci_ = E("vector", lambda e: e.tensor_copy(identb[:], identt[:]), waits=[li])
    ci = ctx.flush("vector")
    cwt = ctx.sbuf("cwt", [128, 16 * 4], F32)
    cbt = ctx.sbuf("cbt", [128, 16], F32)
    xpw = ctx.sbuf("xpw", [128, 16 * 96], F32)
    xpwb = ctx.sbuf("xpwb", [128, 16 * 96], BF16)
    dtw = ctx.sbuf("dtw", [64, 16 * 128], F32)
    dtwb = ctx.sbuf("dtwb", [64, 16 * 128], BF16)
    dtb = ctx.sbuf("dtb", [128, 16], F32)
    alog = ctx.sbuf("alog", [128, 16 * 16], F32)
    at = ctx.sbuf("at", [128, 16 * 16], F32)
    dskip = ctx.sbuf("dskip", [128, 16], F32)
    lc = []
    lc.append(E("sync", lambda e: e.dma_start(out=cwt.ap().rearrange("p (a n) -> p a n", a=16), in_=cw_d.rearrange("a p n -> p a n")), dma="const"))
    lc.append(E("sync", lambda e: e.dma_start(out=cbt[:], in_=cb_d[:]), dma="const"))
    lc.append(E("sync", lambda e: e.dma_start(out=xpw.ap().rearrange("p (a n) -> p a n", a=16), in_=xpwT_d.rearrange("a p n -> p a n")), dma="const"))
    lc.append(E("sync", lambda e: e.dma_start(out=dtw.ap().rearrange("p (a n) -> p a n", a=16), in_=dtwT_d.rearrange("a p n -> p a n")), dma="const"))
    lc.append(E("sync", lambda e: e.dma_start(out=dtb[:], in_=dtb_d[:]), dma="const"))
    lc.append(E("sync", lambda e: e.dma_start(out=alog.ap().rearrange("p (a n) -> p a n", a=16), in_=alog_d.rearrange("a p n -> p a n")), dma="const"))
    lc.append(E("sync", lambda e: e.dma_start(out=dskip[:], in_=dskip_d[:]), dma="const"))
    LC = [p.dma_total("const")]
    c1 = E("vector", lambda e: e.tensor_copy(xpwb[:], xpw[:]), waits=LC)
    c2 = E("vector", lambda e: e.tensor_copy(dtwb[:], dtw[:]), waits=LC)
    CW = ctx.flush("vector")
    a1 = E("scalar", lambda e: e.activation(at[:], alog[:], AF.Exp), waits=LC)
    a2 = E("vector", lambda e: e.tensor_scalar(at[:], at[:], -1.0, None, ALU.mult), waits=[a1])
    AT = ctx.flush("vector")

    xsp = [ctx.sbuf(f"xsp{j}", [128, T + 3], BF16) for j in range(2)]
    ohp = ctx.sbuf("ohp", [128, 4], F32)
    lohp = E("sync", lambda e: e.dma_start(out=ohp[:], in_=ohprev_d[:]), dma="const")
    tl4 = ctx.sbuf("tl4", [128, 4 * 3], BF16)
    hacc = ctx.sbuf("hacc", [128, 3], F32)
    acc1 = ctx.sbuf("acc1", [128, T], F32)
    acc2 = ctx.sbuf("acc2", [128, T], F32)
    xct = [ctx.sbuf(f"xct{i}", [128, T], BF16) for i in range(2)]
    ps_dbl = [ctx.psum(f"ps_dbl{i}", [128, 512], F32) for i in range(4)]
    xs_guard = {0: None, 1: None}
    xc_guard = {0: None, 1: None}
    hguard = [None]
    xc_stores = []
    dbl_mm = None
    for cht in range(16):
        b = cht % 2
        gw = [x for x in [xs_guard[b]] if x is not None]
        lh4 = E("sync", lambda e, cht=cht: e.dma_start(
            out=tl4.ap().rearrange("p (a n) -> p a n", a=4),
            in_=tails_d.ap().rearrange("(a q) n -> a q n", a=4)[:, cht * 128:(cht + 1) * 128, :].rearrange("a p n -> p a n")),
            waits=gw + [x for x in [hguard[0]] if x is not None], dma="tl4")
        hc1 = E("vector", lambda e: e.tensor_scalar(hacc[:], tl4[:, 0:3], ohp[:, 0:1], None, ALU.mult), waits=[lh4, lohp])
        hc2 = E("vector", lambda e: e.scalar_tensor_tensor(hacc[:], tl4[:, 3:6], ohp[:, 1:2], hacc[:], ALU.mult, ALU.add), waits=[hc1])
        hc3 = E("vector", lambda e: e.scalar_tensor_tensor(hacc[:], tl4[:, 6:9], ohp[:, 2:3], hacc[:], ALU.mult, ALU.add), waits=[hc2])
        hc4 = E("vector", lambda e, b=b: e.tensor_copy(xsp[b][:, 0:3], hacc[:]), waits=[hc3])
        lh = ctx.flush("vector")
        hguard[0] = lh
        lx = E("sync", lambda e, cht=cht, b=b: e.dma_start(out=xsp[b][:, 3:T + 3], in_=xz_d[cht * 128:(cht + 1) * 128, :]), waits=gw, dma=f"xs{b}")
        e1 = E("vector", lambda e, b=b, cht=cht: e.tensor_scalar(acc1[:], xsp[b][:, 0:T], cwt[:, cht * 4:cht * 4 + 1], None, ALU.mult), waits=[lh, lx] + LC)
        e2 = E("vector", lambda e, b=b, cht=cht: e.scalar_tensor_tensor(acc2[:], xsp[b][:, 1:T + 1], cwt[:, cht * 4 + 1:cht * 4 + 2], acc1[:], ALU.mult, ALU.add), waits=[e1])
        e3 = E("vector", lambda e, b=b, cht=cht: e.scalar_tensor_tensor(acc1[:], xsp[b][:, 2:T + 2], cwt[:, cht * 4 + 2:cht * 4 + 3], acc2[:], ALU.mult, ALU.add), waits=[e2])
        e4 = E("vector", lambda e, b=b, cht=cht: e.scalar_tensor_tensor(acc2[:], xsp[b][:, 3:T + 3], cwt[:, cht * 4 + 3:cht * 4 + 4], acc1[:], ALU.mult, ALU.add), waits=[e3])
        e4d = ctx.flush("vector")
        xs_guard[b] = e4d
        a3 = E("scalar", lambda e, b=b, cht=cht: e.activation(xct[b][:], acc2[:], AF.Silu, bias=cbt[:, cht:cht + 1]),
               waits=[e4d] + [x for x in [xc_guard[b]] if x is not None])
        a3d = ctx.flush("scalar")
        stx = E("gpsimd", lambda e, cht=cht, b=b: e.dma_start(out=xc_o[cht * 128:(cht + 1) * 128, :], in_=xct[b][:]), waits=[a3d], dma=f"xcst{b}")
        xc_stores.append(stx)
        xc_guard[b] = stx
        for tc in range(4):
            dbl_mm = E("tensor", lambda e, cht=cht, tc=tc, b=b: e.matmul(
                ps_dbl[tc][0:96, :], xpwb[:, cht * 96:(cht + 1) * 96], xct[b][:, tc * 512:(tc + 1) * 512],
                start=(cht == 0), stop=(cht == 15)), waits=([a3d, CW] if True else []))
    dbld = ctx.flush("tensor")
    dts = ctx.sbuf("dts", [64, T], BF16)
    bcs = ctx.sbuf("bcs", [32, T], BF16)
    ev = []
    for tc in range(4):
        ev.append(E("scalar", lambda e, tc=tc: e.copy(dts[:, tc * 512:(tc + 1) * 512], ps_dbl[tc][0:64, :]), waits=[dbld]))
        ev.append(E("scalar", lambda e, tc=tc: e.copy(bcs[:, tc * 512:(tc + 1) * 512], ps_dbl[tc][64:96, :]), waits=[dbld]))
    evd = ctx.flush("scalar")
    stbc = E("gpsimd", lambda e: e.dma_start(out=bc_o[:, :], in_=bcs[:]), waits=[evd], dma="bcst")

    ones_t = ctx.sbuf("ones_t", [128, T], F32)
    mzo = E("vector", lambda e: e.memset(ones_t[:], 1.0))
    delta = ctx.sbuf("delta", [128, T], F32)
    cumd = ctx.sbuf("cumd_t", [128, T], F32)
    ut = ctx.sbuf("ut", [128, T], BF16)
    sp1 = ctx.sbuf("sp1", [128, T], F32)
    sp2 = ctx.sbuf("sp2", [128, T], F32)
    av = ctx.sbuf("av", [128, T], F32)
    rv = ctx.sbuf("rv", [128, T], BF16)
    hv = ctx.sbuf("hv", [128, T], BF16)
    pv = ctx.sbuf("pv", [128, T], BF16)
    udv = ctx.sbuf("udv", [128, T], BF16)
    bbb = [ctx.sbuf(f"bbb{i}", [128, T], BF16) for i in range(2)]
    cbb = [ctx.sbuf(f"cbb{i}", [128, T], BF16) for i in range(2)]
    carry = ctx.sbuf("carry", [128, 1], F32)
    ssum = ctx.sbuf("ssum", [128, 16], F32)
    psum_t = ctx.sbuf("psum_t", [128, 16], F32)
    xc2 = [ctx.sbuf(f"xc2{i}", [128, T], BF16) for i in range(2)]
    ps_del = [ctx.psum(f"ps_del{i}", [128, 512], F32) for i in range(2)]
    ps_y = ps_dbl
    totd = ctx.sbuf("totd", [128, 1], F32)
    yst = ctx.sbuf("yst", [128, T], BF16)
    xc2_guard = {0: None, 1: None}
    bcb_guard = {0: None, 1: None}
    st2_guard = {"cumd": None, "s": None, "pc": None, "yl": None}
    del_guard = [None]
    sy_guard = [None]
    for cht in range(16):
        b = cht % 2
        lxc = E("sync", lambda e, cht=cht, b=b: e.dma_start(out=xc2[b][:], in_=xc_o[cht * 128:(cht + 1) * 128, :]),
                waits=[xc_stores[cht]] + [x for x in [xc2_guard[b]] if x is not None], dma=f"xc2{b}")
        dmm = {}
        for tc in range(4):
            pb = tc % 2
            dmm[tc] = E("tensor", lambda e, cht=cht, tc=tc, pb=pb: e.matmul(
                ps_del[pb][:], dtwb[:, cht * 128:(cht + 1) * 128], dts[:, tc * 512:(tc + 1) * 512],
                start=True, stop=True), waits=([evd, CW, dbld] + [x for x in [del_guard[0]] if x is not None] if tc < 2 else ([del_guard[0]] if tc == 2 else [])))
            if tc == 1 or tc == 3:
                dd = ctx.flush("tensor")
                for tcc in ([0, 1] if tc == 1 else [2, 3]):
                    sl = slice(tcc * 512, (tcc + 1) * 512)
                    E("scalar", lambda e, cht=cht, tcc=tcc, sl=sl: e.activation(
                        sp1[:, sl], ps_del[tcc % 2][:], AF.Abs, bias=dtb[:, cht:cht + 1]), waits=[dd])
                    E("scalar", lambda e, cht=cht, tcc=tcc, sl=sl: e.activation(
                        sp2[:, sl], ps_del[tcc % 2][:], AF.Relu, bias=dtb[:, cht:cht + 1]), waits=[dd])
                del_guard[0] = ctx.flush("scalar")
        sfa = E("scalar", lambda e: e.activation(sp1[:], sp1[:], AF.Exp, scale=-1.0))
        sfb = E("scalar", lambda e: e.activation(sp1[:], sp1[:], AF.Ln, bias=1.0), waits=[sfa])
        sfd = ctx.flush("scalar")
        dl1 = E("vector", lambda e: e.tensor_tensor(delta[:], sp1[:], sp2[:], ALU.add), waits=[sfd])
        cd1 = E("vector", lambda e: e.tensor_tensor_scan(cumd[:], ones_t[:], delta[:], 0.0, ALU.mult, ALU.add), waits=[dl1, mzo])
        cdd = ctx.flush("vector")
        sc1 = E("gpsimd", lambda e, cht=cht: e.dma_start(out=cumd_o[cht * 128:(cht + 1) * 128, :], in_=cumd[:]),
                waits=[cdd] + [x for x in [st2_guard["cumd"]] if x is not None], dma="cumdst")
        st2_guard["cumd"] = sc1
        u1 = E("vector", lambda e, b=b: e.tensor_tensor(ut[:], delta[:], xc2[b][:], ALU.mult), waits=[cdd, lxc])
        tc1 = E("vector", lambda e: e.tensor_copy(totd[:], cumd[:, T - 1:T]), waits=[cdd])
        tcd = ctx.flush("vector")
        pc1 = E("scalar", lambda e, cht=cht: e.activation(psum_t[:], at[:, cht * 16:(cht + 1) * 16], AF.Exp, scale=totd[:]), waits=[tcd, AT])
        pcd = ctx.flush("scalar")
        sp_st = E("gpsimd", lambda e, cht=cht: e.dma_start(out=pc_o[cht * 128:(cht + 1) * 128, :], in_=psum_t[:]),
                  waits=[pcd] + [x for x in [st2_guard["pc"]] if x is not None], dma="pcst")
        st2_guard["pc"] = sp_st
        ymm = None
        for n in range(16):
            bb = n % 2
            lb_ = E("sync", lambda e, n=n, bb=bb: e.dma_start(out=bbb[bb].ap().unsqueeze(1), in_=bc_o[n:n + 1, :].partition_broadcast(128)),
                    waits=[stbc] + [x for x in [bcb_guard[bb]] if x is not None], dma=f"bcb{bb}")
            lc_ = E("scalar", lambda e, n=n, bb=bb: e.dma_start(out=cbb[bb].ap().unsqueeze(1), in_=bc_o[16 + n:17 + n, :].partition_broadcast(128)),
                    waits=[stbc] + [x for x in [bcb_guard[bb]] if x is not None], dma=f"ccb{bb}")
            aa1 = E("scalar", lambda e, cht=cht, n=n: e.activation(av[:], delta[:], AF.Exp, scale=at[:, cht * 16 + n:cht * 16 + n + 1]),
                    waits=[AT] + ([ymm] if ymm else []))
            aad = ctx.flush("scalar")
            rr1 = E("vector", lambda e, bb=bb: e.tensor_tensor(rv[:], ut[:], bbb[bb][:], ALU.mult), waits=[u1, lb_, aad])
            hh1 = E("vector", lambda e: e.tensor_tensor_scan(hv[:], av[:], rv[:], 0.0, ALU.mult, ALU.add), waits=[rr1])
            cc1 = E("vector", lambda e, n=n: e.tensor_copy(ssum[:, n:n + 1], hv[:, T - 1:T]), waits=[hh1])
            pp1 = E("vector", lambda e, bb=bb: e.tensor_tensor(pv[:], hv[:], cbb[bb][:], ALU.mult),
                    waits=[cc1, lc_] + ([ymm] if ymm else []))
            ppd = ctx.flush("vector")
            bcb_guard[bb] = ppd
            for tc in range(4):
                ymm = E("tensor", lambda e, n=n, tc=tc: e.matmul(
                    ps_y[tc][:], identb[:], pv[:, tc * 512:(tc + 1) * 512], start=(n == 0), stop=False),
                    waits=[ppd, ci] + ([sy_guard[0]] if (n == 0 and sy_guard[0] is not None) else []))
        ud1 = E("vector", lambda e, b=b, cht=cht: e.tensor_scalar(udv[:], xc2[b][:], dskip[:, cht:cht + 1], None, ALU.mult),
                waits=[lxc] + ([ymm] if ymm else []))
        udd = ctx.flush("vector")
        for tc in range(4):
            ymm = E("tensor", lambda e, tc=tc: e.matmul(
                ps_y[tc][:], identb[:], udv[:, tc * 512:(tc + 1) * 512], start=False, stop=True), waits=[udd])
        ymd = ctx.flush("tensor")
        xc2_guard[b] = ymd
        sy = []
        for tc in range(4):
            sy.append(E("scalar", lambda e, tc=tc: e.copy(yst[:, tc * 512:(tc + 1) * 512], ps_y[tc][:]),
                        waits=[ymd] + [x for x in [st2_guard["yl"]] if x is not None]))
        syd = ctx.flush("scalar")
        sy_guard[0] = syd
        sy_st = E("gpsimd", lambda e, cht=cht: e.dma_start(out=yl_o[cht * 128:(cht + 1) * 128, :], in_=yst[:]),
                  waits=[syd] + [x for x in [st2_guard["yl"]] if x is not None], dma="ylst")
        st2_guard["yl"] = sy_st
        ss_st = E("gpsimd", lambda e, cht=cht: e.dma_start(out=s_o[cht * 128:(cht + 1) * 128, :], in_=ssum[:]),
                  waits=[ppd] + [x for x in [st2_guard["s"]] if x is not None], dma="sst")
        st2_guard["s"] = ss_st
    return



def phase3c(ctx, i):
    p = ctx.p
    E = p.emit
    yl_d = ctx.din("y_local", [2048, T], BF16)
    cumd_d = ctx.din("cumd", [2048, T])
    bc_d = ctx.din("bc", [32, T], BF16)
    z_d = ctx.din("xz", [4096, T], BF16)
    s_all_d = ctx.din("s_ag", [4 * 2048, 16]).rearrange("(a q) n -> a q n", a=4)
    p_all_d = ctx.din("p_ag", [4 * 2048, 16]).rearrange("(a q) n -> a q n", a=4)
    oh_d = ctx.din("ohsel", [128, 4])
    alog_d = ctx.din("alog_r4", [4, 16, 128, 16])[i]
    outwT_d = ctx.din("out_wT_t4", [4, 16, 128, DM])[i]
    lnw_d = ctx.din("lnw128_4", [4, 128, DM])[i]
    lnb_d = ctx.din("lnb128_4", [4, 128, DM])[i]
    m_d = ctx.din("m_tok", [T, DM])
    id_d = ctx.din("ident", [128, 128])

    mn_o = ctx.dout("m_next", [T, DM])
    mnfm_o = ctx.dout("m_next_fm", [128, NDT * T], BF16)
    gated_o = ctx.dout("gated", [2048, T], BF16)

    identt = ctx.sbuf("identt", [128, 128], F32)
    identb = ctx.sbuf("identb", [128, 128], BF16)
    eps_t = ctx.sbuf("eps_t", [128, 1], F32)
    li = E("sync", lambda e: e.dma_start(out=identt[:], in_=id_d[:]), dma="const")
    ci_ = E("vector", lambda e: e.tensor_copy(identb[:], identt[:]), waits=[li])
    ci = ctx.flush("vector")
    mz1 = E("vector", lambda e: e.memset(eps_t[:], 1e-5))
    alog = ctx.sbuf("alog", [128, 16 * 16], F32)
    at = ctx.sbuf("at", [128, 16 * 16], F32)
    oh = ctx.sbuf("oh", [128, 4], F32)
    lnw = ctx.sbuf("lnw", [128, DM], F32)
    lnb = ctx.sbuf("lnb", [128, DM], F32)
    la = E("sync", lambda e: e.dma_start(out=alog.ap().rearrange("p (a n) -> p a n", a=16), in_=alog_d.rearrange("a p n -> p a n")), dma="const")
    lo = E("sync", lambda e: e.dma_start(out=oh[:], in_=oh_d[:]), dma="const")
    lw_ = E("sync", lambda e: e.dma_start(out=lnw[:], in_=lnw_d[:]), dma="const")
    lb_ = E("sync", lambda e: e.dma_start(out=lnb[:], in_=lnb_d[:]), dma="const")
    a1 = E("scalar", lambda e: e.activation(at[:], alog[:], AF.Exp), waits=[la])
    a2 = E("vector", lambda e: e.tensor_scalar(at[:], at[:], -1.0, None, ALU.mult), waits=[a1])
    AT = ctx.flush("vector")

    sti = [ctx.sbuf(f"sti{i}", [128, 4 * 16], F32) for i in range(2)]
    pti = [ctx.sbuf(f"pti{i}", [128, 4 * 16], F32) for i in range(2)]
    t0 = ctx.sbuf("t0", [128, 16], F32)
    t1_ = ctx.sbuf("t1_", [128, 16], F32)
    t2_ = ctx.sbuf("t2_", [128, 16], F32)
    hin = ctx.sbuf("hin", [128, 16], F32)
    tmpv = ctx.sbuf("tmpv", [128, 16], F32)
    cumd = [ctx.sbuf(f"cumd{i}", [128, T], F32) for i in range(2)]
    ylc = [ctx.sbuf(f"ylc{i}", [128, T], BF16) for i in range(2)]
    zc = [ctx.sbuf(f"zc{i}", [128, T], BF16) for i in range(2)]
    dfv = ctx.sbuf("dfv", [128, T], F32)
    termv = ctx.sbuf("termv", [128, T], BF16)
    szv = ctx.sbuf("szv", [128, T], BF16)
    gtd = ctx.sbuf("gtd", [128, T], BF16)
    cbb = [ctx.sbuf(f"cbb{i}", [128, T], BF16) for i in range(2)]
    ps_y = [ctx.psum(f"ps_y{i}", [128, 512], F32) for i in range(4)]
    ld_guard = {0: None, 1: None}
    bcb_guard = {0: None, 1: None}
    gst_guard = [None]
    for cht in range(16):
        b = cht % 2
        gw = [x for x in [ld_guard[b]] if x is not None]
        ls = E("sync", lambda e, cht=cht, b=b: e.dma_start(out=sti[b].ap().rearrange("p (a n) -> p a n", a=4), in_=s_all_d[:, cht * 128:(cht + 1) * 128, :].rearrange("a p n -> p a n")), waits=gw, dma=f"sti{b}")
        lp = E("sync", lambda e, cht=cht, b=b: e.dma_start(out=pti[b].ap().rearrange("p (a n) -> p a n", a=4), in_=p_all_d[:, cht * 128:(cht + 1) * 128, :].rearrange("a p n -> p a n")), waits=gw, dma=f"sti{b}")
        lcu = E("sync", lambda e, cht=cht, b=b: e.dma_start(out=cumd[b][:], in_=cumd_d[cht * 128:(cht + 1) * 128, :]), waits=gw, dma=f"cud{b}")
        lyl = E("scalar", lambda e, cht=cht, b=b: e.dma_start(out=ylc[b][:], in_=yl_d[cht * 128:(cht + 1) * 128, :]), waits=gw, dma=f"ylc{b}")
        lz = E("scalar", lambda e, cht=cht, b=b: e.dma_start(out=zc[b][:], in_=z_d[2048 + cht * 128: 2048 + (cht + 1) * 128, :]), waits=gw, dma=f"zc{b}")
        h1 = E("vector", lambda e, b=b: e.tensor_copy(t0[:], sti[b][:, 0:16]), waits=[ls, lp])
        h2 = E("vector", lambda e, b=b: e.tensor_tensor(tmpv[:], pti[b][:, 16:32], t0[:], ALU.mult))
        h3 = E("vector", lambda e, b=b: e.tensor_tensor(t1_[:], tmpv[:], sti[b][:, 16:32], ALU.add))
        h4 = E("vector", lambda e, b=b: e.tensor_tensor(tmpv[:], pti[b][:, 32:48], t1_[:], ALU.mult))
        h5 = E("vector", lambda e, b=b: e.tensor_tensor(t2_[:], tmpv[:], sti[b][:, 32:48], ALU.add))
        h6 = E("vector", lambda e: e.tensor_scalar(hin[:], t0[:], oh[:, 1:2], None, ALU.mult), waits=[lo])
        h7 = E("vector", lambda e: e.scalar_tensor_tensor(tmpv[:], t1_[:], oh[:, 2:3], hin[:], ALU.mult, ALU.add))
        h8 = E("vector", lambda e: e.scalar_tensor_tensor(hin[:], t2_[:], oh[:, 3:4], tmpv[:], ALU.mult, ALU.add))
        h8d = ctx.flush("vector")
        ld_guard[b] = h8d
        ymm = None
        for tc in range(4):
            ymm = E("tensor", lambda e, tc=tc, b=b: e.matmul(
                ps_y[tc][:], identb[:], ylc[b][:, tc * 512:(tc + 1) * 512], start=True, stop=False),
                waits=[lyl, ci] + [x for x in [gst_guard[0]] if x is not None])
        for n in range(16):
            bb = n % 2
            lc_ = E("scalar", lambda e, n=n, bb=bb: e.dma_start(out=cbb[bb].ap().unsqueeze(1), in_=bc_d[16 + n:17 + n, :].partition_broadcast(128)),
                    waits=[x for x in [bcb_guard[bb]] if x is not None], dma=f"ccb{bb}")
            aa1 = E("scalar", lambda e, cht=cht, n=n, b=b: e.activation(dfv[:], cumd[b][:], AF.Exp, scale=at[:, cht * 16 + n:cht * 16 + n + 1]),
                    waits=[AT, lcu] + ([ymm] if ymm else []))
            aad = ctx.flush("scalar")
            tt1 = E("vector", lambda e, n=n, bb=bb: e.scalar_tensor_tensor(termv[:], dfv[:], hin[:, n:n + 1], cbb[bb][:], ALU.mult, ALU.mult),
                    waits=[aad, lc_, h8d] + ([ymm] if ymm else []))
            ttd = ctx.flush("vector")
            bcb_guard[bb] = ttd
            for tc in range(4):
                ymm = E("tensor", lambda e, tc=tc, n=n: e.matmul(
                    ps_y[tc][:], identb[:], termv[:, tc * 512:(tc + 1) * 512], start=False, stop=(tc == 3 and n == 15)),
                    waits=[ttd])
        ymd = ctx.flush("tensor")
        sz1 = E("scalar", lambda e, b=b: e.activation(szv[:], zc[b][:], AF.Silu), waits=[lz, ymd])
        szd = ctx.flush("scalar")
        g1 = None
        for tc in range(4):
            sl = slice(tc * 512, (tc + 1) * 512)
            g1 = E("vector", lambda e, tc=tc, sl=sl: e.tensor_tensor(gtd[:, sl], ps_y[tc][:], szv[:, sl], ALU.mult), waits=[szd])
        g1d = ctx.flush("vector")
        gst = E("gpsimd", lambda e, cht=cht: e.dma_start(out=gated_o[cht * 128:(cht + 1) * 128, :], in_=gtd[:]),
                waits=[g1d] + [x for x in [gst_guard[0]] if x is not None], dma="gst")
        gst_guard[0] = gst
    GDONE = [p.dma_total("gst")]

    owf = [ctx.sbuf(f"owf{i}", [128, DM], F32) for i in range(2)]
    owb = [ctx.sbuf(f"owb{i}", [128, DM], BF16) for i in range(2)]
    gl = [ctx.sbuf(f"gl{i}", [128, 128], BF16) for i in range(2)]
    mt = [ctx.sbuf(f"mt{i}", [128, DM], F32) for i in range(2)]
    res = ctx.sbuf("res", [128, DM], F32)
    mn32 = ctx.sbuf("mn32", [128, DM], F32)
    mnb = ctx.sbuf("mnb", [128, DM], BF16)
    fm_st = ctx.sbuf("fm_st", [128, 128], BF16)
    tmp = {k: ctx.sbuf("lntmp_" + k, [128, 1], F32) for k in ["mu", "negmu", "ssq", "var", "rstd"]}
    tmp["sq"] = ctx.sbuf("lntmp_sq", [128, DM], F32)
    tmp["junk"] = ctx.sbuf("lntmp_junk", [128, DM], F32)
    tmp["eps_t"] = eps_t
    ps_o = [ctx.psum(f"ps_o{i}", [128, 512], F32) for i in range(2)]
    ps_tp = ctx.psum("ps_tp", [128, 128], BF16)
    st_guard = {"mn": None, "fm": None, "res": None, "t1last": None, "mt": {0: None, 1: None}, "gl": {0: None, 1: None}, "ow": {0: None, 1: None}, "owpe": {0: None, 1: None}}
    for tt in range(NT):
        b = tt % 2
        lm = E("sync", lambda e, tt=tt, b=b: e.dma_start(out=mt[b][:], in_=m_d[tt * 128:(tt + 1) * 128, :]),
               waits=[x for x in [st_guard["mt"][b]] if x is not None], dma=f"mt{b}")
        mm_ = {}
        for cht in range(16):
            gb = cht % 2
            lg = E("scalar", lambda e, cht=cht, tt=tt, gb=gb: e.dma_start(out=gl[gb][:], in_=gated_o[cht * 128:(cht + 1) * 128, tt * 128:(tt + 1) * 128]),
                   waits=GDONE + [x for x in [st_guard["gl"][gb]] if x is not None], dma=f"gl{gb}")
            ob = cht % 2
            low = E("sync", lambda e, cht=cht, ob=ob: e.dma_start(out=owf[ob][:], in_=outwT_d[cht, :, :]),
                    waits=[x for x in [st_guard["ow"][ob]] if x is not None], dma=f"ow{ob}")
            cow_ = E("vector", lambda e, ob=ob: e.tensor_copy(owb[ob][:], owf[ob][:]),
                     waits=[low] + [x for x in [st_guard["owpe"][ob]] if x is not None])
            cow = ctx.flush("vector")
            st_guard["ow"][ob] = cow
            gld = ctx.flush("scalar")
            for half in range(2):
                mm_[half] = E("tensor", lambda e, cht=cht, half=half, gb=gb, ob=ob: e.matmul(
                    ps_o[half][:], gl[gb][:], owb[ob][:, half * 512:(half + 1) * 512],
                    start=(cht == 0), stop=(cht == 15)),
                    waits=[lg, gld, cow] + ([x for x in [st_guard["res"]] if x is not None] if cht == 0 else []))
            st_guard["gl"][gb] = mm_[1]
            st_guard["owpe"][ob] = mm_[1]
        mmd = ctx.flush("tensor")
        r1 = None
        for half in range(2):
            sl = slice(half * 512, (half + 1) * 512)
            r1 = E("vector", lambda e, half=half, sl=sl, b=b: e.tensor_tensor(res[:, sl], ps_o[half][:], mt[b][:, sl], ALU.add), waits=[mmd, lm])
        r1d = ctx.flush("vector")
        st_guard["res"] = r1d
        st_guard["mt"][b] = r1d
        lnt = ln_tile(ctx, E, res, mn32, tmp, [r1d, mz1])
        w1_ = E("vector", lambda e: e.tensor_tensor(mn32[:], mn32[:], lnw[:], ALU.mult), waits=[lnt, lw_])
        w2_ = E("vector", lambda e: e.tensor_tensor(mn32[:], mn32[:], lnb[:], ALU.add), waits=[w1_, lb_] + [x for x in [st_guard["mn"]] if x is not None])
        w2d = ctx.flush("vector")
        smn = E("gpsimd", lambda e, tt=tt: e.dma_start(out=mn_o[tt * 128:(tt + 1) * 128, :], in_=mn32[:]), waits=[w2d], dma="mnst")
        st_guard["mn"] = smn
        cb = E("vector", lambda e: e.tensor_copy(mnb[:], mn32[:]), waits=[w2d] + [x for x in [st_guard["t1last"]] if x is not None])
        cbd = ctx.flush("vector")
        for dt in range(NDT):
            t1 = E("tensor", lambda e, dt=dt: e.transpose(ps_tp[:], mnb[:, dt * 128:(dt + 1) * 128], identb[:]),
                   waits=[cbd, ci] + [x for x in [st_guard["fm"]] if x is not None])
            t1d = ctx.flush("tensor")
            t2 = E("vector", lambda e: e.tensor_copy(fm_st[:], ps_tp[:]), waits=[t1d])
            t2d = ctx.flush("vector")
            t3 = E("gpsimd", lambda e, dt=dt, tt=tt: e.dma_start(
                out=mnfm_o[:, dt * T + tt * 128: dt * T + (tt + 1) * 128], in_=fm_st[:]), waits=[t2d], dma="fmst")
            st_guard["fm"] = t3
            st_guard["t1last"] = t1d
    return



def phase4(ctx):
    p = ctx.p
    E = p.emit
    xnfm_d = ctx.din("xn1fm", [128, NDT * T], BF16)
    xnt_d = ctx.din("xn1", [T, DM])
    x1_d = ctx.din("x1", [T, DM])
    s1_d = ctx.din("s1", [T, DM])
    rw_d = ctx.din("router_w", [DM, 8])
    w1_d = ctx.din("w1_t", [8, 32, NDT, 128, 128])
    b1_d = ctx.din("b1_r", [8, 128, 32])
    w2_d = ctx.din("w2_t", [8, 32, 128, DM])
    b2_d = ctx.din("b2", [8, DM])
    gam_d = ctx.din("gamma", [8, DM])
    bet_d = ctx.din("beta", [8, DM])
    id_d = ctx.din("ident", [128, 128])
    out_o = ctx.dout("out", [T, DM])

    identt = ctx.sbuf("identt", [128, 128], F32)
    eps_t = ctx.sbuf("eps_t", [128, 1], F32)
    li = E("sync", lambda e: e.dma_start(out=identt[:], in_=id_d[:]), dma="const")
    mz1 = E("vector", lambda e: e.memset(eps_t[:], 1e-6))
    one_col = ctx.sbuf("one_col", [128, 1], F32)
    mz2 = E("vector", lambda e: e.memset(one_col[:], 1.0))
    xnfm = ctx.sbuf("xnfm", [128, NDT * T], BF16)
    lxf = E("sync", lambda e: e.dma_start(out=xnfm[:], in_=xnfm_d[:]), dma="const")
    rw = ctx.sbuf("rw", [128, NDT * 8], F32)
    lrw = E("sync", lambda e: e.dma_start(out=rw.ap().rearrange("p (a n) -> p a n", a=NDT), in_=rw_d.ap().rearrange("(a p) n -> p a n", p=128)), dma="const")

    gates = ctx.sbuf("gates", [128, NT * 8], F32)
    xt32 = [ctx.sbuf(f"xt32{i}", [128, DM], F32) for i in range(2)]
    x32t = ctx.sbuf("x32t", [128, NDT * 128], F32)
    lgt = ctx.sbuf("lgt", [128, 8], F32)
    m8 = ctx.sbuf("m8", [128, 8], F32)
    d21 = ctx.sbuf("d21", [128, 1], F32)
    ez = ctx.sbuf("ez", [128, 1], F32)
    rz = ctx.sbuf("rz", [128, 1], F32)
    pex = ctx.sbuf("pex", [128, 8], F32)
    msk = ctx.sbuf("msk", [128, 8], F32)
    ps_dn = [ctx.psum(f"ps_dn{i}", [128, 512], F32) for i in range(8)]
    xl_guard = {0: None, 1: None}
    gprev = ()
    for tt in range(NT):
        b = tt % 2
        lx = E("sync", lambda e, tt=tt, b=b: e.dma_start(out=xt32[b][:], in_=xnt_d[tt * 128:(tt + 1) * 128, :]),
               waits=[x for x in [xl_guard[b]] if x is not None], dma=f"xt{b}")
        tpd = None
        for dt in range(NDT):
            t3 = E("tensor", lambda e, b=b, dt=dt: e.transpose(ps_dn[0][0:128, 0:128], xt32[b][:, dt * 128:(dt + 1) * 128], identt[:]),
                   waits=[lx, li] + ([tpd] if tpd else []))
            t3d = ctx.flush("tensor")
            t4 = E("vector", lambda e, dt=dt: e.tensor_copy(x32t[:, dt * 128:(dt + 1) * 128], ps_dn[0][0:128, 0:128]), waits=[t3d])
            tpd = ctx.flush("vector")
        xl_guard[b] = tpd
        mms = None
        for dt in range(NDT):
            mms = E("tensor", lambda e, dt=dt: e.matmul(
                ps_dn[1][0:128, 0:8], x32t[:, dt * 128:(dt + 1) * 128], rw[:, dt * 8:(dt + 1) * 8],
                start=(dt == 0), stop=(dt == NDT - 1)),
                waits=([tpd, lrw] + list(gprev) if dt == 0 else []))
        mmsd = ctx.flush("tensor")
        c1 = E("vector", lambda e: e.tensor_copy(lgt[:], ps_dn[1][0:128, 0:8]), waits=[mmsd])
        c2 = E("vector", lambda e: e.max(m8[:], lgt[:]), waits=[c1])
        c3 = E("vector", lambda e: e.tensor_scalar(d21[:], m8[:, 1:2], m8[:, 0:1], None, ALU.subtract), waits=[c2])
        c4 = E("scalar", lambda e: e.activation(ez[:], d21[:], AF.Exp), waits=[c3])
        c5 = E("vector", lambda e: e.tensor_tensor(ez[:], ez[:], one_col[:], ALU.add), waits=[c4, mz2])
        c6 = E("vector", lambda e: e.reciprocal(rz[:], ez[:]), waits=[c5])
        c7 = E("vector", lambda e: e.tensor_scalar(pex[:], lgt[:], m8[:, 0:1], None, ALU.subtract), waits=[c2])
        c8 = E("scalar", lambda e: e.activation(pex[:], pex[:], AF.Exp), waits=[c7])
        c9 = E("vector", lambda e: e.tensor_scalar(msk[:], lgt[:], m8[:, 1:2], None, ALU.is_ge), waits=[c8])
        c10 = E("vector", lambda e: e.tensor_tensor(pex[:], pex[:], msk[:], ALU.mult), waits=[c9])
        c11 = E("vector", lambda e, tt=tt: e.tensor_scalar(gates[:, tt * 8:(tt + 1) * 8], pex[:], rz[:], None, ALU.mult), waits=[c10, c6])
        gprev = (c11, c2)
    GATES = ctx.flush("vector")

    g1acc = ctx.sbuf("g1acc", [128, NT * DM], F32)
    zg = E("vector", lambda e: e.memset(g1acc[:], 0.0))
    hfm = ctx.sbuf("hfm", [128, 32 * 512], BF16)
    w1b = [ctx.sbuf(f"w1b{i}", [128, NDT * 128], BF16) for i in range(2)]
    w2b = [ctx.sbuf(f"w2b{i}", [128, DM], BF16) for i in range(2)]
    b1t = ctx.sbuf("b1t", [128, 32], F32)
    b2b = ctx.sbuf("b2b", [128, DM], F32)
    gab = ctx.sbuf("gab", [128, DM], F32)
    beb = ctx.sbuf("beb", [128, DM], F32)
    res = ctx.sbuf("res", [128, DM], F32)
    nrm = ctx.sbuf("nrm", [128, DM], F32)
    xnt2 = [ctx.sbuf(f"xnt2{i}", [128, DM], F32) for i in range(2)]
    tmp = {k: ctx.sbuf("lntmp_" + k, [128, 1], F32) for k in ["mu", "negmu", "ssq", "var", "rstd"]}
    tmp["sq"] = ctx.sbuf("lntmp_sq", [128, DM], F32)
    tmp["junk"] = ctx.sbuf("lntmp_junk", [128, DM], F32)
    tmp["eps_t"] = eps_t
    ps_up = [ps_dn[6], ps_dn[7]]
    w1_guard = {0: None, 1: None}
    w2_guard = {0: None, 1: None}
    up_guard = {0: None, 1: None}
    xn2_guard = {0: None, 1: None}
    dn_guard = {i: None for i in range(8)}
    hf_guard = [None]
    acc_guard = [zg]
    for ex in range(8):
        lb1 = E("sync", lambda e, ex=ex: e.dma_start(out=b1t[:], in_=b1_d[ex, :, :]), waits=acc_guard, dma="b1t")
        lb2 = E("scalar", lambda e, ex=ex: e.dma_start(out=b2b.ap().unsqueeze(1), in_=b2_d[ex:ex + 1, :].partition_broadcast(128)), waits=acc_guard, dma="b2b")
        lga = E("scalar", lambda e, ex=ex: e.dma_start(out=gab.ap().unsqueeze(1), in_=gam_d[ex:ex + 1, :].partition_broadcast(128)), waits=acc_guard, dma="gab")
        lbe = E("scalar", lambda e, ex=ex: e.dma_start(out=beb.ap().unsqueeze(1), in_=bet_d[ex:ex + 1, :].partition_broadcast(128)), waits=acc_guard, dma="beb")
        for tc in range(4):
            agd = None
            for fc2 in range(16):
                lw = {}
                for j in range(2):
                    fc = fc2 * 2 + j
                    wbuf = fc % 2
                    lw[j] = E("gpsimd", lambda e, ex=ex, fc=fc, wbuf=wbuf: e.dma_start(
                        out=w1b[wbuf].ap().rearrange("p (a n) -> p a n", a=NDT), in_=w1_d[ex, fc, :, :, :].rearrange("a p n -> p a n")),
                        waits=[x for x in [w1_guard[wbuf]] if x is not None], dma=f"w1{wbuf}")
                for j in range(2):
                    fc = fc2 * 2 + j
                    wbuf = fc % 2
                    pb = fc % 2
                    m_ = None
                    for dt in range(NDT):
                        m_ = E("tensor", lambda e, tc=tc, fc=fc, dt=dt, wbuf=wbuf, pb=pb: e.matmul(
                            ps_up[pb][:], w1b[wbuf][:, dt * 128:(dt + 1) * 128], xnfm[:, dt * T + tc * 512: dt * T + (tc + 1) * 512],
                            start=(dt == 0), stop=(dt == NDT - 1)),
                            waits=([lw[j], lxf] + [x for x in [up_guard[pb], dn_guard[6 + pb]] if x is not None] if dt == 0 else []))
                    w1_guard[wbuf] = m_
                md = ctx.flush("tensor")
                for j in range(2):
                    fc = fc2 * 2 + j
                    pb = fc % 2
                    ag = E("scalar", lambda e, fc=fc, pb=pb: e.activation(hfm[:, fc * 512:(fc + 1) * 512], ps_up[pb][:], AF.Gelu, bias=b1t[:, fc:fc + 1]),
                           waits=[md, lb1] + [x for x in [hf_guard[0]] if x is not None])
                agd = ctx.flush("scalar")
                up_guard[0] = agd
                up_guard[1] = agd
            dn_last = None
            for fc in range(32):
                wbuf = fc % 2
                lw2 = E("gpsimd", lambda e, ex=ex, fc=fc, wbuf=wbuf: e.dma_start(out=w2b[wbuf][:], in_=w2_d[ex, fc, :, :]),
                        waits=[x for x in [w2_guard[wbuf]] if x is not None], dma=f"w2{wbuf}")
                for sub in range(8):
                    ttl, half = sub // 2, sub % 2
                    dn_last = E("tensor", lambda e, fc=fc, ttl=ttl, half=half, wbuf=wbuf, sub=sub: e.matmul(
                        ps_dn[sub][:], hfm[:, fc * 512 + ttl * 128: fc * 512 + ttl * 128 + 128],
                        w2b[wbuf][:, half * 512:(half + 1) * 512],
                        start=(fc == 0), stop=(fc == 31)),
                        waits=([lw2, agd] + [x for x in [dn_guard[sub]] if x is not None] if fc == 0 else [lw2]))
                w2_guard[wbuf] = dn_last
            dnd = ctx.flush("tensor")
            hf_guard[0] = dnd
            for tl in range(4):
                tt_g = tc * 4 + tl
                xb2 = tl % 2
                lxn = E("sync", lambda e, tt_g=tt_g, xb2=xb2: e.dma_start(out=xnt2[xb2][:], in_=xnt_d[tt_g * 128:(tt_g + 1) * 128, :]),
                        waits=[x for x in [xn2_guard[xb2]] if x is not None], dma=f"xn2{xb2}")
                r1 = None
                for half in range(2):
                    sub = tl * 2 + half
                    sl = slice(half * 512, (half + 1) * 512)
                    r1 = E("vector", lambda e, sub=sub, sl=sl: e.tensor_tensor(res[:, sl], ps_dn[sub][:], b2b[:, sl], ALU.add), waits=[dnd, lb2])
                r2 = E("vector", lambda e, xb2=xb2: e.tensor_tensor(res[:], res[:], xnt2[xb2][:], ALU.add), waits=[r1, lxn])
                r2d = ctx.flush("vector")
                xn2_guard[xb2] = r2d
                dn_guard[tl * 2] = r2d
                dn_guard[tl * 2 + 1] = r2d
                lnt = ln_tile(ctx, E, res, nrm, tmp, [r2d, mz1], nf=1024.0)
                g1_ = E("vector", lambda e: e.tensor_tensor(nrm[:], nrm[:], gab[:], ALU.mult), waits=[lnt, lga])
                g2_ = E("vector", lambda e: e.tensor_tensor(nrm[:], nrm[:], beb[:], ALU.add), waits=[g1_, lbe])
                g3_ = E("vector", lambda e, ex=ex, tt_g=tt_g: e.tensor_scalar(nrm[:], nrm[:], gates[:, tt_g * 8 + ex: tt_g * 8 + ex + 1], None, ALU.mult), waits=[g2_, GATES])
                g4_ = E("vector", lambda e, tt_g=tt_g: e.tensor_tensor(g1acc[:, tt_g * DM:(tt_g + 1) * DM], g1acc[:, tt_g * DM:(tt_g + 1) * DM], nrm[:], ALU.add), waits=[g3_, zg])
        acc_guard = [ctx.flush("vector")]

    x1b = [xt32[0], xt32[1]]
    s1b = [tmp["sq"], tmp["junk"]]
    sg1 = nrm
    outb = res
    fx_guard = {0: None, 1: None}
    ost_guard = [None]
    for tt in range(NT):
        b = tt % 2
        gw = [x for x in [fx_guard[b]] if x is not None]
        lx1 = E("sync", lambda e, tt=tt, b=b: e.dma_start(out=x1b[b][:], in_=x1_d[tt * 128:(tt + 1) * 128, :]), waits=gw, dma=f"fx{b}")
        ls1 = E("scalar", lambda e, tt=tt, b=b: e.dma_start(out=s1b[b][:], in_=s1_d[tt * 128:(tt + 1) * 128, :]), waits=gw, dma=f"fs{b}")
        a1 = E("scalar", lambda e, tt=tt: e.activation(sg1[:], g1acc[:, tt * DM:(tt + 1) * DM], AF.Silu), waits=acc_guard + [x for x in [ost_guard[0]] if x is not None])
        a1d = ctx.flush("scalar")
        o1 = E("vector", lambda e, b=b: e.tensor_tensor(outb[:], sg1[:], s1b[b][:], ALU.mult), waits=[a1d, ls1])
        o2 = E("vector", lambda e, b=b: e.tensor_tensor(outb[:], outb[:], x1b[b][:], ALU.add), waits=[o1, lx1])
        o2d = ctx.flush("vector")
        fx_guard[b] = o2d
        so = E("gpsimd", lambda e, tt=tt: e.dma_start(out=out_o[tt * 128:(tt + 1) * 128, :], in_=outb[:]), waits=[o2d] + [x for x in [ost_guard[0]] if x is not None], dma="ost")
        ost_guard[0] = so
    return


def host_prep_l2(inputs, r1_results):
    x = np.asarray(inputs["L0_attn_x"])
    wout = np.ascontiguousarray(np.asarray(inputs["L0_attn_W_out"]).astype(np.float32))
    bout = np.asarray(inputs["L0_attn_b_out"]).astype(np.float32)
    bout128 = np.ascontiguousarray(np.broadcast_to(bout[None, :], (128, 1024)))
    ident = np.eye(128, dtype=np.float32)
    maps = []
    for c in range(NCORES):
        b, r = c // 4, c % 4
        grp = range(b * 4, b * 4 + 4)
        K_all = np.concatenate([np.asarray(r1_results[j]["K_own"]) for j in grp], axis=0)
        V_all = np.concatenate([np.asarray(r1_results[j]["V_own"]) for j in grp], axis=0)
        maps.append({
            "QP": np.asarray(r1_results[c]["QP_own"]),
            "K_all": np.ascontiguousarray(K_all),
            "V_all": np.ascontiguousarray(V_all),
            "g0": np.asarray(r1_results[c]["g0"]),
            "x": np.ascontiguousarray(x[b, r * T:(r + 1) * T]).astype(np.float32),
            "wout": wout, "bout128": bout128, "ident": ident,
        })
    return maps




def host_prep_l3a(inputs, i, r_prev):
    in_w = np.asarray(inputs["L1_attn_in_proj_w"][i]).astype(np.float32)
    in_wT_t = np.ascontiguousarray(in_w.T.reshape(8, 128, 32, 128).transpose(2, 0, 1, 3))
    return [{"m_fm": np.asarray(r_prev[c]["m_fm"]), "in_wT_t": in_wT_t} for c in range(NCORES)]


def host_prep_l3b(inputs, i, r3a):
    cw = np.asarray(inputs["L1_attn_conv_w"][i]).astype(np.float32)
    cb = np.asarray(inputs["L1_attn_conv_b"][i]).astype(np.float32)
    xpw = np.asarray(inputs["L1_attn_x_proj_w"][i]).astype(np.float32)
    dtw = np.asarray(inputs["L1_attn_dt_proj_w"][i]).astype(np.float32)
    dtb = np.asarray(inputs["L1_attn_dt_proj_b"][i]).astype(np.float32)
    alog = np.asarray(inputs["L1_attn_A_log"][i]).astype(np.float32)
    dskip = np.asarray(inputs["L1_attn_D_skip"][i]).astype(np.float32)
    com = {
        "cw_r": np.ascontiguousarray(cw.reshape(16, 128, 4)),
        "cb_r": np.ascontiguousarray(cb.reshape(16, 128).T),
        "xp_wT_t": np.ascontiguousarray(xpw.T.reshape(16, 128, 96)),
        "dt_wT_t": np.ascontiguousarray(dtw.T.reshape(64, 16, 128).transpose(1, 0, 2)),
        "dtb_r": np.ascontiguousarray(dtb.reshape(16, 128).T),
        "alog_r": np.ascontiguousarray(alog.reshape(16, 128, 16)),
        "dskip_r": np.ascontiguousarray(dskip.reshape(16, 128).T),
        "ident": np.eye(128, dtype=np.float32),
    }
    maps = []
    for c in range(NCORES):
        xz = np.asarray(r3a[c]["xz"])
        if c % 4 == 0:
            halo = np.zeros((2048, 3), dtype=BF)
        else:
            halo = np.ascontiguousarray(np.asarray(r3a[c - 1]["xz"])[0:2048, T - 3:T])
        m = dict(com)
        m["xz"] = xz
        m["halo"] = halo
        maps.append(m)
    return maps


def host_prep_l3c(inputs, i, r3a, r3b, m_tok_list):
    outw = np.asarray(inputs["L1_attn_out_proj_w"][i]).astype(np.float32)
    lnw = np.asarray(inputs["L1_attn_ln_w"][i]).astype(np.float32)
    lnb = np.asarray(inputs["L1_attn_ln_b"][i]).astype(np.float32)
    alog = np.asarray(inputs["L1_attn_A_log"][i]).astype(np.float32)
    com = {
        "out_wT_t": np.ascontiguousarray(outw.T.reshape(16, 128, DM)),
        "lnw128": np.ascontiguousarray(np.broadcast_to(lnw[None, :], (128, DM))),
        "lnb128": np.ascontiguousarray(np.broadcast_to(lnb[None, :], (128, DM))),
        "alog_r": np.ascontiguousarray(alog.reshape(16, 128, 16)),
        "ident": np.eye(128, dtype=np.float32),
    }
    maps = []
    for c in range(NCORES):
        b, r = c // 4, c % 4
        grp = range(b * 4, b * 4 + 4)
        s_all = np.stack([np.asarray(r3b[j]["s_sum"]) for j in grp], 0)
        p_all = np.stack([np.asarray(r3b[j]["p_sum"]) for j in grp], 0)
        oh = np.zeros((128, 4), np.float32)
        oh[:, r] = 1.0
        m = dict(com)
        m.update({
            "y_local": np.asarray(r3b[c]["y_local"]),
            "cumd": np.asarray(r3b[c]["cumd"]),
            "bc": np.asarray(r3b[c]["bc"]),
            "z": np.ascontiguousarray(np.asarray(r3a[c]["xz"])[2048:4096, :]),
            "s_all": np.ascontiguousarray(s_all),
            "p_all": np.ascontiguousarray(p_all),
            "ohsel": oh,
            "m_tok": np.asarray(m_tok_list[c]),
        })
        maps.append(m)
    return maps


def host_prep_l4(inputs, r2, s1_list):
    w1 = np.asarray(inputs["L1_ffn_w1"]).astype(np.float32)
    w2 = np.asarray(inputs["L1_ffn_w2"]).astype(np.float32)
    b1 = np.asarray(inputs["L1_ffn_b1"]).astype(np.float32)
    com = {
        "router_w": np.ascontiguousarray(np.asarray(inputs["L1_ffn_router_w"]).astype(np.float32)),
        "w1_t": np.ascontiguousarray(w1.reshape(8, 8, 128, 32, 128).transpose(0, 3, 1, 2, 4)),
        "b1_r": np.ascontiguousarray(b1.reshape(8, 32, 128).transpose(0, 2, 1)),
        "w2_t": np.ascontiguousarray(w2.reshape(8, 32, 128, DM)),
        "b2": np.ascontiguousarray(np.asarray(inputs["L1_ffn_b2"]).astype(np.float32)),
        "gamma": np.ascontiguousarray(np.asarray(inputs["L1_ffn_gamma"]).astype(np.float32)),
        "beta": np.ascontiguousarray(np.asarray(inputs["L1_ffn_beta"]).astype(np.float32)),
        "ident": np.eye(128, dtype=np.float32),
    }
    maps = []
    for c in range(NCORES):
        m = dict(com)
        m.update({
            "xn1fm": np.asarray(r2[c]["xn1fm"]),
            "xn1": np.asarray(r2[c]["xn1"]),
            "x1": np.asarray(r2[c]["x1"]),
            "s1": np.asarray(s1_list[c]),
        })
        maps.append(m)
    return maps





RG4 = [[0, 1, 2, 3], [4, 5, 6, 7]]


def build_fused():
    ctx = Ctx("fused")
    nc = ctx.nc
    p = ctx.start()
    ctx.p = p
    ctx.mkflush()
    E = p.emit

    D = {}
    def ext(name, shape, dtype=F32):
        D[name] = nc.dram_tensor(name, list(shape), dtype, kind="ExternalInput")
    def itn(name, shape, dtype=F32):
        D[name] = nc.dram_tensor(name, list(shape), dtype, kind="Internal")
    ext("x", [T, DM]); ext("wqkv", [DM, 3072]); ext("bqkv_r", [128, 24]); ext("pe2", [128, T])
    ext("bv128", [128, DM]); ext("gate_wT", [DM, 8])
    ext("wg_t", [8, 11, 8, 128, 128]); ext("wu_t", [8, 11, 8, 128, 128]); ext("wd_t", [8, 11, 128, DM])
    ext("swg_t", [22, 8, 128, 128]); ext("swu_t", [22, 8, 128, 128]); ext("swd_t", [22, 128, DM])
    ext("ident", [128, 128]); ext("wout", [DM, DM]); ext("bout128", [128, DM])
    ext("in_wT_t4", [4, 32, NDT, 128, 128])
    ext("cw_r4", [4, 16, 128, 4]); ext("cb_r4", [4, 128, 16]); ext("xp_wT_t4", [4, 16, 128, 96])
    ext("dt_wT_t4", [4, 16, 64, 128]); ext("dtb_r4", [4, 128, 16]); ext("alog_r4", [4, 16, 128, 16])
    ext("dskip_r4", [4, 128, 16]); ext("out_wT_t4", [4, 16, 128, DM])
    ext("lnw128_4", [4, 128, DM]); ext("lnb128_4", [4, 128, DM])
    ext("ohsel", [128, 4]); ext("ohprev", [128, 4])
    ext("router_w", [DM, 8]); ext("w1_t", [8, 32, NDT, 128, 128]); ext("b1_r", [8, 128, 32])
    ext("w2_t", [8, 32, 128, DM]); ext("b2", [8, DM]); ext("gamma", [8, DM]); ext("beta", [8, DM])
    itn("K_own", [DM, T], BF16); itn("V_own", [T, DM], BF16); itn("QP_own", [NH, 65, T], BF16)
    itn("g0", [T, DM]); itn("K_all", [4 * DM, T], BF16); itn("V_all", [S, DM], BF16)
    itn("x1", [T, DM]); itn("xn1", [T, DM]); itn("xn1fm", [128, NDT * T], BF16)
    itn("xz", [4096, T], BF16); itn("xc", [2048, T], BF16); itn("cumd", [2048, T])
    itn("bc", [32, T], BF16); itn("s_sum", [2048, 16]); itn("p_sum", [2048, 16])
    itn("y_local", [2048, T], BF16); itn("gated", [2048, T], BF16)
    itn("tail_own", [2048, 3], BF16); itn("tails_ag", [4 * 2048, 3], BF16)
    itn("s_ag", [4 * 2048, 16]); itn("p_ag", [4 * 2048, 16])
    itn("mA", [T, DM]); itn("mB", [T, DM])
    itn("mA_fm", [128, NDT * T], BF16); itn("mB_fm", [128, NDT * T], BF16)
    D["out"] = nc.dram_tensor("out", [T, DM], F32, kind="ExternalOutput")
    ctx.D = D

    ctx.phase_open("p1_", "")
    D["QP"] = D["QP_own"]
    phase1(ctx)
    ctx.phase_close()
    bar = p.all_barriers()
    E("gpsimd", lambda e: e.collective_compute("AllGather", ALU.bypass, replica_groups=RG4,
                                               ins=[D["K_own"].ap().opt()], outs=[D["K_all"].ap().opt()]), waits=bar)
    E("gpsimd", lambda e: e.collective_compute("AllGather", ALU.bypass, replica_groups=RG4,
                                               ins=[D["V_own"].ap().opt()], outs=[D["V_all"].ap().opt()]))
    ctx.phase_open("p2_", "")
    phase2(ctx)
    ctx.phase_close()

    cur_fm, cur_tok = "xn1fm", "xn1"
    for i in range(4):
        nxt_fm, nxt_tok = ("mA_fm", "mA") if i % 2 == 0 else ("mB_fm", "mB")
        D["m_fm"] = D[cur_fm]
        D["m_tok"] = D[cur_tok]
        D["m_next"] = D[nxt_tok]
        D["m_next_fm"] = D[nxt_fm]
        ctx.phase_open(f"a{i}_", "")
        phase3a(ctx, i)
        ctx.phase_close()
        bar = p.all_barriers()
        tcp = E("gpsimd", lambda e: e.dma_start(out=D["tail_own"][:, :], in_=D["xz"][0:2048, T - 3:T]), waits=bar, dma="tailcp")
        E("gpsimd", lambda e: e.collective_compute("AllGather", ALU.bypass, replica_groups=RG4,
                                                   ins=[D["tail_own"].ap().opt()], outs=[D["tails_ag"].ap().opt()]), waits=[tcp])
        ctx.phase_open(f"b{i}_", "")
        phase3b(ctx, i)
        ctx.phase_close()
        bar = p.all_barriers()
        E("gpsimd", lambda e: e.collective_compute("AllGather", ALU.bypass, replica_groups=RG4,
                                                   ins=[D["s_sum"].ap().opt()], outs=[D["s_ag"].ap().opt()]), waits=bar)
        E("gpsimd", lambda e: e.collective_compute("AllGather", ALU.bypass, replica_groups=RG4,
                                                   ins=[D["p_sum"].ap().opt()], outs=[D["p_ag"].ap().opt()]))
        ctx.phase_open(f"c{i}_", "")
        phase3c(ctx, i)
        ctx.phase_close()
        cur_fm, cur_tok = nxt_fm, nxt_tok

    D["s1"] = D[cur_tok]
    ctx.phase_open("p4_", "")
    phase4(ctx)
    ctx.phase_close()
    return ctx.done(p.all_barriers())


def host_prep_fused(inputs):
    x = np.asarray(inputs["L0_attn_x"])
    wqkv = np.ascontiguousarray(inputs["L0_attn_W_qkv"]).astype(np.float32)
    bqkv = np.asarray(inputs["L0_attn_b_qkv"]).astype(np.float32)
    pe = rel_pos_enc(S, HD)
    wg = np.asarray(inputs["L0_ffn_wg"]).astype(np.float32)
    wu = np.asarray(inputs["L0_ffn_wu"]).astype(np.float32)
    wd = np.asarray(inputs["L0_ffn_wd"]).astype(np.float32)
    swg = np.asarray(inputs["L0_ffn_swg"]).astype(np.float32)
    swu = np.asarray(inputs["L0_ffn_swu"]).astype(np.float32)
    swd = np.asarray(inputs["L0_ffn_swd"]).astype(np.float32)
    in_w = np.asarray(inputs["L1_attn_in_proj_w"]).astype(np.float32)
    cw = np.asarray(inputs["L1_attn_conv_w"]).astype(np.float32)
    cb = np.asarray(inputs["L1_attn_conv_b"]).astype(np.float32)
    xpw = np.asarray(inputs["L1_attn_x_proj_w"]).astype(np.float32)
    dtw = np.asarray(inputs["L1_attn_dt_proj_w"]).astype(np.float32)
    dtb = np.asarray(inputs["L1_attn_dt_proj_b"]).astype(np.float32)
    alog = np.asarray(inputs["L1_attn_A_log"]).astype(np.float32)
    dskip = np.asarray(inputs["L1_attn_D_skip"]).astype(np.float32)
    outw = np.asarray(inputs["L1_attn_out_proj_w"]).astype(np.float32)
    lnw = np.asarray(inputs["L1_attn_ln_w"]).astype(np.float32)
    lnb = np.asarray(inputs["L1_attn_ln_b"]).astype(np.float32)
    w1 = np.asarray(inputs["L1_ffn_w1"]).astype(np.float32)
    b1 = np.asarray(inputs["L1_ffn_b1"]).astype(np.float32)
    w2 = np.asarray(inputs["L1_ffn_w2"]).astype(np.float32)
    com = {
        "wqkv": wqkv,
        "bqkv_r": np.ascontiguousarray(bqkv.reshape(24, 128).T),
        "bv128": np.ascontiguousarray(np.broadcast_to(bqkv[2048:3072][None, :], (128, 1024))),
        "gate_wT": np.ascontiguousarray(np.asarray(inputs["L0_ffn_gate_w"]).astype(np.float32).T),
        "wg_t": np.ascontiguousarray(wg.reshape(8, 8, 128, 11, 128).transpose(0, 3, 1, 2, 4)),
        "wu_t": np.ascontiguousarray(wu.reshape(8, 8, 128, 11, 128).transpose(0, 3, 1, 2, 4)),
        "wd_t": np.ascontiguousarray(wd.reshape(8, 11, 128, 1024)),
        "swg_t": np.ascontiguousarray(swg.reshape(8, 128, 22, 128).transpose(2, 0, 1, 3)),
        "swu_t": np.ascontiguousarray(swu.reshape(8, 128, 22, 128).transpose(2, 0, 1, 3)),
        "swd_t": np.ascontiguousarray(swd.reshape(22, 128, 1024)),
        "ident": np.eye(128, dtype=np.float32),
        "wout": np.ascontiguousarray(np.asarray(inputs["L0_attn_W_out"]).astype(np.float32)),
        "bout128": np.ascontiguousarray(np.broadcast_to(np.asarray(inputs["L0_attn_b_out"]).astype(np.float32)[None, :], (128, 1024))),
        "in_wT_t4": np.ascontiguousarray(np.stack([in_w[i].T.reshape(8, 128, 32, 128).transpose(2, 0, 1, 3) for i in range(4)])),
        "cw_r4": np.ascontiguousarray(cw.reshape(4, 16, 128, 4)),
        "cb_r4": np.ascontiguousarray(np.stack([cb[i].reshape(16, 128).T for i in range(4)])),
        "xp_wT_t4": np.ascontiguousarray(np.stack([xpw[i].T.reshape(16, 128, 96) for i in range(4)])),
        "dt_wT_t4": np.ascontiguousarray(np.stack([dtw[i].T.reshape(64, 16, 128).transpose(1, 0, 2) for i in range(4)])),
        "dtb_r4": np.ascontiguousarray(np.stack([dtb[i].reshape(16, 128).T for i in range(4)])),
        "alog_r4": np.ascontiguousarray(alog.reshape(4, 16, 128, 16)),
        "dskip_r4": np.ascontiguousarray(np.stack([dskip[i].reshape(16, 128).T for i in range(4)])),
        "out_wT_t4": np.ascontiguousarray(np.stack([outw[i].T.reshape(16, 128, DM) for i in range(4)])),
        "lnw128_4": np.ascontiguousarray(np.stack([np.broadcast_to(lnw[i][None, :], (128, DM)) for i in range(4)])),
        "lnb128_4": np.ascontiguousarray(np.stack([np.broadcast_to(lnb[i][None, :], (128, DM)) for i in range(4)])),
        "router_w": np.ascontiguousarray(np.asarray(inputs["L1_ffn_router_w"]).astype(np.float32)),
        "w1_t": np.ascontiguousarray(w1.reshape(8, 8, 128, 32, 128).transpose(0, 3, 1, 2, 4)),
        "b1_r": np.ascontiguousarray(b1.reshape(8, 32, 128).transpose(0, 2, 1)),
        "w2_t": np.ascontiguousarray(w2.reshape(8, 32, 128, DM)),
        "b2": np.ascontiguousarray(np.asarray(inputs["L1_ffn_b2"]).astype(np.float32)),
        "gamma": np.ascontiguousarray(np.asarray(inputs["L1_ffn_gamma"]).astype(np.float32)),
        "beta": np.ascontiguousarray(np.asarray(inputs["L1_ffn_beta"]).astype(np.float32)),
    }
    maps = []
    for c in range(NCORES):
        b, r = c // 4, c % 4
        sl = slice(r * T, (r + 1) * T)
        pe_own = pe[sl].T
        oh = np.zeros((128, 4), np.float32); oh[:, r] = 1.0
        ohp = np.zeros((128, 4), np.float32)
        if r > 0:
            ohp[:, r - 1] = 1.0
        m = dict(com)
        m.update({
            "x": np.ascontiguousarray(x[b, sl]).astype(np.float32),
            "pe2": np.ascontiguousarray(np.concatenate([pe_own, pe_own], 0)),
            "ohsel": oh, "ohprev": ohp,
        })
        maps.append(m)
    return maps


def kernel(**inputs):
    cores = list(range(NCORES))
    nc = _cached("fused", build_fused)
    res = run_bass_kernel_spmd(nc, host_prep_fused(inputs), core_ids=cores).results
    out = np.zeros((2, S, DM), np.float32)
    for c in range(NCORES):
        b, r = c // 4, c % 4
        out[b, r * T:(r + 1) * T] = np.asarray(res[c]["out"])
    return out

